# version 01
# baseline (speedup 1.0000x reference)
import contextlib

import numpy as np

import concourse.bass as bass
import concourse.mybir as mybir
from concourse.bass_utils import run_bass_kernel_spmd

F32 = mybir.dt.float32
BF16 = mybir.dt.bfloat16
ALU = mybir.AluOpType
ACT = mybir.ActivationFunctionType
AX = mybir.AxisListType

NCORES = 8
D = 1024
T = 16384
TC = 2048
H = 134
EXT = TC + H
INNER = 2048
NH = 16
DH = 64
W = 128
SEG = 16
NSEG = T // SEG
NSEG_C = TC // SEG
KV0 = 6
KVLEN = EXT - KV0
E = 8
FF = 2048

NT = [512, 512, 512, 512, EXT - 2048]
NTO = [0, 512, 1024, 1536, 2048]
NN = len(NT)


class Prog:
    ENGINES = ("tensor", "vector", "scalar", "gpsimd", "sync")

    def __init__(self, nc, es):
        self.nc = nc
        self.es = es
        self.q = {e: [] for e in self.ENGINES}
        self.sems = {}
        self.cnt = {}

    def sem(self, name):
        if name not in self.sems:
            self.sems[name] = self.es.enter_context(self.nc.semaphore(name))
            self.cnt[name] = 0
        return self.sems[name]

    def val(self, name):
        self.sem(name)
        return self.cnt[name]

    AUTOSEM = {"tensor": "t", "vector": "v", "scalar": "s"}

    def emit(self, eng, fn, wait=(), inc=None, inc_n=1):
        if inc is not None and inc_n == 16:
            if not hasattr(self, "_fence"):
                self._fence = {}
            cur = self.val(inc)
            if cur > 0 and self._fence.get((eng, inc)) != cur:
                h0 = self.sem(inc)
                self.q[eng].append(lambda E, h=h0, v=cur: E.wait_ge(h, v))
                self._fence[(eng, inc)] = cur
        w = [(self.sem(s), v) for (s, v) in wait if v > 0]
        if eng in self.AUTOSEM:
            own = self.AUTOSEM[eng]
            prev = self.val(own)
            if prev > 0:
                w.append((self.sem(own), prev))
            assert inc is None or inc == own, (eng, inc)
            inc, inc_n = own, 1
        h = None
        ret = None
        if inc is not None:
            h = self.sem(inc)
            self.cnt[inc] += inc_n
            ret = self.cnt[inc]

        def step(E, fn=fn, w=w, h=h, n=inc_n):
            for (sh, v) in w:
                E.wait_ge(sh, v)
            inst = fn(E)
            if h is not None:
                inst.then_inc(h, n)

        self.q[eng].append(step)
        return ret

    def wait(self, eng, semname, value):
        if value <= 0:
            return
        h = self.sem(semname)
        self.q[eng].append(lambda E, h=h, v=value: E.wait_ge(h, v))

    def dma(self, eng, sem, fn, wait=()):
        return self.emit(eng, fn, wait=wait, inc=sem, inc_n=16)

    def barrier(self):
        h = self.sem("bar")
        for e in self.ENGINES:
            self.cnt["bar"] += 1
            self.q[e].append(lambda E, h=h: E.nop().then_inc(h, 1))
        tot = self.cnt["bar"]
        for e in self.ENGINES:
            self.wait(e, "bar", tot)

    def finalize(self):
        block = self.es.enter_context(self.nc.Block())
        qs = self.q

        @block.tensor
        def _(E):
            for s in qs["tensor"]:
                s(E)

        @block.vector
        def _(E):
            for s in qs["vector"]:
                s(E)

        @block.scalar
        def _(E):
            for s in qs["scalar"]:
                s(E)

        @block.gpsimd
        def _(E):
            for s in qs["gpsimd"]:
                s(E)

        @block.sync
        def _(E):
            for s in qs["sync"]:
                s(E)


class Arena:
    def __init__(self, nc, name, cols, dtype):
        self.t = nc.alloc_sbuf_tensor(name, [128, cols], dtype)
        self.cols = cols
        self.off = 0
        self.marks = []
        self.peak = 0

    def alloc(self, n, align=16):
        o = (self.off + align - 1) // align * align
        assert o + n <= self.cols, f"arena {self.t.name}: {o}+{n}>{self.cols}"
        self.off = o + n
        self.peak = max(self.peak, self.off)
        return o

    def push(self):
        self.marks.append(self.off)

    def pop(self):
        self.off = self.marks.pop()


class View:
    def __init__(self, arena, off, rows, cols):
        self.a = arena
        self.off = off
        self.rows = rows
        self.cols = cols

    def __getitem__(self, idx):
        rs, cs = idx
        if isinstance(rs, int):
            rs = slice(rs, rs + 1)
        if isinstance(cs, int):
            cs = slice(cs, cs + 1)
        r0 = rs.start or 0
        r1 = self.rows if rs.stop is None else rs.stop
        c0 = cs.start or 0
        c1 = self.cols if cs.stop is None else cs.stop
        return bass.AP(self.a.t, self.off + c0 + r0 * self.a.cols,
                       [[self.a.cols, r1 - r0], [1, c1 - c0]])

    def ap3(self, r0, nrow, dims, c0=0):
        return bass.AP(self.a.t, self.off + c0 + r0 * self.a.cols,
                       [[self.a.cols, nrow]] + [list(d) for d in dims])


def valloc(arena, rows, cols, align=16):
    return View(arena, arena.alloc(cols, align), rows, cols)


def host_prepare(inp):
    x = np.asarray(inp["L0_attn_x"]).reshape(T, D)
    sh = {}

    def T_(a):
        return np.ascontiguousarray(np.asarray(a).T)

    sh["Win"] = np.ascontiguousarray(inp["L0_attn_W_in"])
    sh["Wa"] = np.ascontiguousarray(inp["L0_attn_Wa"])
    sh["Wb"] = np.ascontiguousarray(inp["L0_attn_Wb"])
    sh["Wc"] = np.ascontiguousarray(inp["L0_attn_Wc"])
    sh["Wo"] = np.ascontiguousarray(inp["L0_attn_Wo"])
    sh["Wr"] = T_(inp["L0_ffn_W_r"])
    sh["Wk"] = T_(inp["L0_ffn_W_k"])
    sh["Wv"] = T_(inp["L0_ffn_W_v"])
    sh["Wq"] = np.ascontiguousarray(inp["L1_attn_Wq"])
    sh["Wkv"] = np.ascontiguousarray(inp["L1_attn_Wkv"])
    sh["Wout"] = np.ascontiguousarray(inp["L1_attn_Wout"])
    sh["Wproj"] = np.ascontiguousarray(np.asarray(inp["L1_attn_Wproj"]).reshape(DH, 1))
    sh["Wrout"] = T_(inp["L1_ffn_router_w"])
    sh["Wg"] = np.ascontiguousarray(np.swapaxes(np.asarray(inp["L1_ffn_w_gate"]), 1, 2))
    sh["Wu"] = np.ascontiguousarray(np.swapaxes(np.asarray(inp["L1_ffn_w_up"]), 1, 2))
    sh["Wd"] = np.ascontiguousarray(np.swapaxes(np.asarray(inp["L1_ffn_w_down"]), 1, 2))

    def pp(a, n):
        return np.ascontiguousarray(np.asarray(a).reshape(n, 128).T)

    sh["b_in"] = pp(inp["L0_attn_b_in"], 32)
    sh["conv_w"] = np.ascontiguousarray(np.asarray(inp["L0_attn_conv_w"]).reshape(16, 128, 3))
    sh["conv_b"] = pp(inp["L0_attn_conv_b"], 16)
    sh["ba"] = pp(inp["L0_attn_ba"], 16)
    sh["bb"] = pp(inp["L0_attn_bb"], 16)
    sh["bc"] = pp(inp["L0_attn_bc"], 16)
    sh["bo"] = pp(inp["L0_attn_bo"], 8)
    sh["gamma"] = pp(inp["L0_attn_gamma"], 8)
    sh["beta"] = pp(inp["L0_attn_beta"], 8)
    sh["mu_r"] = pp(inp["L0_ffn_mu_r"], 8)
    sh["mu_k"] = pp(inp["L0_ffn_mu_k"], 8)
    sh["bout"] = pp(inp["L1_attn_bout"], 8)
    for nm, key in (("lng_l", "L1_attn_ln_local_g"), ("lnb_l", "L1_attn_ln_local_b"),
                    ("lng_g", "L1_attn_ln_global_g"), ("lnb_g", "L1_attn_ln_global_b")):
        v = np.asarray(inp[key]).reshape(DH)
        sh[nm] = np.ascontiguousarray(np.concatenate([v, v]).reshape(128, 1))

    sh["ident"] = np.ascontiguousarray(np.eye(128, dtype=np.float32))
    prot = np.zeros((128, 128), np.float32)
    for hh in range(2):
        for d0 in range(32):
            prot[hh * 64 + d0 + 32, hh * 64 + d0] = -1.0
            prot[hh * 64 + d0, hh * 64 + d0 + 32] = 1.0
    sh["prot"] = prot
    sh["causal_own"] = (np.arange(W)[:, None] <= np.arange(W)[None, :]).astype(np.float32)
    hs = np.zeros((128, 2), np.float32)
    hs[:64, 0] = 1.0
    hs[64:, 1] = 1.0
    sh["halfsel"] = hs
    sh["sel2T"] = np.ascontiguousarray(hs.T)
    wp2 = np.zeros((128, 2), np.float32)
    wp = np.asarray(inp["L1_attn_Wproj"]).reshape(DH)
    wp2[:64, 0] = wp
    wp2[64:, 1] = wp
    sh["wproj2"] = wp2

    per = []
    for c in range(NCORES):
        d = {}
        t0 = TC * c
        lo = t0 - H
        sl = np.zeros((EXT, D), np.float32)
        a0 = max(0, lo)
        sl[a0 - lo:, :] = x[a0:t0 + TC, :]
        d["x0"] = np.ascontiguousarray(sl.T)
        d["prev_mask"] = np.full((128, 1), 0.0 if c == 0 else 1.0, np.float32)
        m = (np.arange(8) < c).astype(np.float32)
        d["foldm"] = np.ascontiguousarray(np.broadcast_to(m, (128, 8)))
        d["foldm1"] = np.ascontiguousarray(np.broadcast_to(1.0 - m, (128, 8)))
        tau = np.arange(t0 - H, t0 + TC)
        tau = np.maximum(tau, 0).astype(np.float64)
        invf = 1.0 / 10000.0 ** (np.arange(0, DH, 2, dtype=np.float64) / DH)
        ang = tau[None, :] * invf[:, None]
        cos = np.cos(ang).astype(np.float32)
        sin = np.sin(ang).astype(np.float32)
        cos64 = np.concatenate([cos, cos], 0)
        sinS = np.concatenate([sin, sin], 0)
        d["cos_t"] = np.ascontiguousarray(np.concatenate([cos64, cos64], 0))
        d["sin_t"] = np.ascontiguousarray(np.concatenate([sinS, sinS], 0))
        Mfull = np.zeros((4, 128, 8 * 512), np.float32)
        for j in range(4):
            qv = t0 + 512 * j + np.arange(512)
            for stile in range(8):
                segg = stile * 128 + np.arange(128)
                Mfull[j][:, stile * 512:(stile + 1) * 512] = (
                    qv[None, :] >= 16 * (segg[:, None] + 1))
        d["gmask"] = np.ascontiguousarray(Mfull)
        per.append(d)
    return sh, per


def build_program(nblocks=4, dbg=None, dbg_mode=None):
    nc = bass.Bass("TRN2", target_bir_lowering=False)
    es = contextlib.ExitStack()
    di = {}

    def IN(name, shape):
        di[name] = nc.dram_tensor(name, list(shape), F32, kind="ExternalInput")

    IN("x0", (D, EXT))
    IN("Win", (D, 2 * INNER)); IN("Wa", (INNER, INNER)); IN("Wb", (INNER, INNER))
    IN("Wc", (INNER, INNER)); IN("Wo", (INNER, D))
    IN("Wr", (D, D)); IN("Wk", (D, D)); IN("Wv", (D, D))
    IN("Wq", (D, D)); IN("Wkv", (D, D)); IN("Wout", (D, D)); IN("Wproj", (DH, 1))
    IN("Wrout", (D, E)); IN("Wg", (E, D, FF)); IN("Wu", (E, D, FF)); IN("Wd", (E, FF, D))
    IN("b_in", (128, 32)); IN("conv_w", (16, 128, 3)); IN("conv_b", (128, 16))
    IN("ba", (128, 16)); IN("bb", (128, 16)); IN("bc", (128, 16)); IN("bo", (128, 8))
    IN("gamma", (128, 8)); IN("beta", (128, 8))
    IN("mu_r", (128, 8)); IN("mu_k", (128, 8)); IN("bout", (128, 8))
    IN("lng_l", (128, 1)); IN("lnb_l", (128, 1)); IN("lng_g", (128, 1)); IN("lnb_g", (128, 1))
    IN("ident", (128, 128)); IN("prot", (128, 128)); IN("causal_own", (W, W))
    IN("prev_mask", (128, 1)); IN("foldm", (128, 8)); IN("foldm1", (128, 8))
    IN("cos_t", (128, EXT)); IN("sin_t", (128, EXT))
    IN("halfsel", (128, 2)); IN("sel2T", (2, 128)); IN("wproj2", (128, 2))
    IN("gmask", (4, 128, 8 * 512))

    out = nc.dram_tensor("out", [TC, D], F32, kind="ExternalOutput")
    dbg_out = None
    if dbg is not None:
        dbg_out = nc.dram_tensor("dbgo", [D, EXT], F32, kind="ExternalOutput")

    y_spill = nc.dram_tensor("y_spill", [16, 128, EXT], BF16)
    out_spill = nc.dram_tensor("out_spill", [8, 128, 2048], BF16)
    cp_spill = nc.dram_tensor("cp_spill", [16, 128, EXT], BF16)
    xn_spill = nc.dram_tensor("xn_spill", [8, 128, EXT], BF16)
    ag1_in = nc.dram_tensor("ag1_in", [128, 32], F32)
    ag1_out = nc.dram_tensor("ag1_out", [128 * 8, 32], F32)
    ag2_in = nc.dram_tensor("ag2_in", [NSEG_C, NH * DH], BF16)
    ag2_out = nc.dram_tensor("ag2_out", [NSEG, NH * DH], BF16)

    prog = Prog(nc, es)
    af = Arena(nc, "af", 26800, F32)
    ab = Arena(nc, "ab", 50800, BF16)

    xres = valloc(af, 128, 8 * EXT)

    def xres_t(k):
        return View(af, xres.off + k * EXT, 128, EXT)

    cst = valloc(af, 128, 600)
    co = {}
    _cc = [cst.off]

    def calloc(name, n):
        co[name] = _cc[0]
        _cc[0] += n

    for nm, n in (("b_in", 32), ("b_in_h", 32), ("conv_w", 48), ("conv_b", 16),
                  ("ba", 16), ("ba_h", 16), ("bb", 16), ("bb_h", 16), ("bc", 16),
                  ("bo", 8), ("gamma", 8), ("beta", 8), ("mu_r", 8), ("mu_r1", 8),
                  ("mu_k", 8), ("mu_k1", 8), ("bout", 8), ("lng_l", 1), ("lnb_l", 1),
                  ("lng_g", 1), ("lnb_g", 1), ("prev", 1), ("foldm", 8), ("foldm1", 8),
                  ("Sin", 16), ("exs", 16), ("exp", 16), ("onesrow", 128),
                  ("onescol", 1), ("ptail", 32), ("zero2", 2), ("wproj", 1), ("eps", 1),
                  ("maxk", 16), ("spare", 16)):
        calloc(nm, n)
    assert _cc[0] - cst.off <= 600, _cc[0] - cst.off

    def cv(name, j=0, rows=128, base=0, ncol=1):
        return bass.AP(af.t, co[name] + j + base * af.cols,
                       [[af.cols, rows], [1, ncol]])

    onescol_b = valloc(ab, 128, 16)
    ident_bf = valloc(ab, 128, 128)
    ident_f = valloc(af, 128, 128)
    prot_bf = valloc(ab, 128, 128)
    causal_bf = valloc(ab, W, W)

    ps = [es.enter_context(nc.psum_tensor("p%d" % i, [128, 512], F32)) for i in range(8)]

    def psap(i, rows=128, cols=512, r0=0, c0=0):
        return bass.AP(ps[i], c0 + r0 * 512, [[512, rows], [1, cols]])

    def gdma(dst, src, sem="wg", wait=()):
        return prog.dma("gpsimd", sem,
                        lambda E, d=dst, s=src: E.dma_start(out=d, in_=s), wait=wait)

    for nm, rows in (("b_in", 32), ("conv_b", 16), ("ba", 16), ("bb", 16), ("bc", 16),
                     ("bo", 8), ("gamma", 8), ("beta", 8), ("mu_r", 8), ("mu_k", 8),
                     ("bout", 8), ("lng_l", 1), ("lnb_l", 1), ("lng_g", 1), ("lnb_g", 1)):
        gdma(cv(nm, ncol=rows), bass.AP(di[nm], 0, [[rows, 128], [1, rows]]))
    for t in range(16):
        gdma(cv("conv_w", 3 * t, ncol=3), bass.AP(di["conv_w"], t * 384, [[3, 128], [1, 3]]))
    gdma(cv("prev"), di["prev_mask"].ap())
    gdma(cv("foldm", ncol=8), di["foldm"].ap())
    gdma(cv("foldm1", ncol=8), di["foldm1"].ap())
    gdma(cv("wproj", rows=DH), bass.AP(di["Wproj"], 0, [[1, DH], [1, 1]]))
    gdma(ident_bf[:, :], di["ident"].ap())
    gdma(ident_f[:, :], di["ident"].ap())
    gdma(prot_bf[:, :], di["prot"].ap())
    gdma(causal_bf[:, :], di["causal_own"].ap())
    for k in range(8):
        gdma(xres_t(k)[:, :], bass.AP(di["x0"], k * 128 * EXT, [[EXT, 128], [1, EXT]]))
    ld_done = prog.val("wg")

    prog.wait("vector", "wg", ld_done)
    prog.emit("vector", lambda E: E.memset(cv("onesrow", rows=1, ncol=128), 1.0), inc="v")
    prog.emit("vector", lambda E: E.memset(cv("onescol", ncol=1), 1.0), inc="v")
    prog.emit("vector", lambda E: E.memset(onescol_b[:, 0:1], 1.0), inc="v")
    prog.emit("vector", lambda E: E.memset(cv("zero2", ncol=2), 0.0), inc="v")
    prog.emit("vector", lambda E: E.memset(cv("eps", ncol=1), 1e-5), inc="v")
    prog.emit("vector", lambda E: E.tensor_scalar(
        cv("b_in_h", ncol=32), cv("b_in", ncol=32), 0.5, None, ALU.mult), inc="v")
    prog.emit("vector", lambda E: E.tensor_scalar(
        cv("ba_h", ncol=16), cv("ba", ncol=16), 0.5, None, ALU.mult), inc="v")
    prog.emit("vector", lambda E: E.tensor_scalar(
        cv("bb_h", ncol=16), cv("bb", ncol=16), 0.5, None, ALU.mult), inc="v")
    prog.emit("vector", lambda E: E.tensor_scalar(
        cv("mu_r1", ncol=8), cv("mu_r", ncol=8), -1.0, 1.0, ALU.mult, ALU.add), inc="v")
    prog.emit("vector", lambda E: E.tensor_scalar(
        cv("mu_k1", ncol=8), cv("mu_k", ncol=8), -1.0, 1.0, ALU.mult, ALU.add), inc="v")
    cdone = prog.val("v")
    prog.wait("tensor", "v", cdone)
    prog.wait("scalar", "v", cdone)
    prog.wait("sync", "v", cdone)

    st = dict(dbg_mode=dbg_mode, nc=nc, prog=prog, af=af, ab=ab, ps=ps, psap=psap, di=di, xres_t=xres_t,
              cv=cv, co=co, gdma=gdma, onescol_b=onescol_b, ident_f=ident_f, ident_bf=ident_bf, prot_bf=prot_bf, causal_bf=causal_bf,
              y_spill=y_spill, cp_spill=cp_spill, xn_spill=xn_spill,
              out_spill=out_spill,
              ag1_in=ag1_in, ag1_out=ag1_out, ag2_in=ag2_in, ag2_out=ag2_out,
              out=out, dbg_out=dbg_out)

    emit_b0(st)
    if nblocks >= 2:
        emit_b1(st)
    if nblocks >= 3:
        emit_b2(st)
    if nblocks >= 4:
        emit_b3(st)

    prog.barrier()
    if dbg is not None and dbg_mode is None:
        for k in range(8):
            dst = bass.AP(dbg_out, k * 128 * EXT, [[EXT, 128], [1, EXT]])
            prog.emit("sync", lambda E, d=dst, s=xres_t(k)[:, :]:
                      E.dma_start(out=d, in_=s), inc="st", inc_n=16)
    emit_output(st)
    prog.wait("sync", "st", prog.val("st"))
    prog.finalize()
    es.close()
    return nc


def emit_pre_ln(st, src_t, spill, n_list=None, dst_t=None):
    prog, af, ab, psap, cv = st["prog"], st["af"], st["ab"], st["psap"], st["cv"]
    n_list = list(n_list if n_list is not None else range(NN))
    af.push(); ab.push()
    sq = valloc(af, 128, 512)
    srow = valloc(af, 1, 1536)
    xt = [valloc(ab, 128, 512) for _ in range(2)]
    xt_busy = [0, 0]
    for n in n_list:
        nt, o = NT[n], NTO[n]
        sq_free = prog.val("t")
        for k in range(8):
            sv = prog.emit("scalar", lambda E, k=k, o=o, nt=nt: E.activation(
                sq[0:128, 0:nt], src_t(k)[:, o:o + nt], ACT.Square),
                wait=[("t", sq_free)], inc="s")
            prog.emit("tensor", lambda E, k=k, o=o, nt=nt: E.matmul(
                psap(6, 1, nt), cv("onescol"), src_t(k)[:, o:o + nt],
                start=(k == 0), stop=(k == 7)), inc="t")
            sq_free = prog.emit("tensor", lambda E, k=k, nt=nt: E.matmul(
                psap(7, 1, nt), cv("onescol"), sq[0:128, 0:nt],
                start=(k == 0), stop=(k == 7)), wait=[("s", sv)], inc="t")
        m_row = srow[0:1, 0:nt]
        r_row = bass.AP(srow.a.t, srow.off + 512, [[srow.a.cols, 1], [1, nt]])
        tmp = bass.AP(srow.a.t, srow.off + 1024, [[srow.a.cols, 1], [1, nt]])
        prog.emit("vector", lambda E, m_row=m_row, nt=nt: E.tensor_scalar(
            m_row, psap(6, 1, nt), 1.0 / D, None, ALU.mult),
            wait=[("t", sq_free)], inc="v")
        prog.emit("vector", lambda E, m_row=m_row, tmp=tmp: E.tensor_mul(
            tmp, m_row, m_row), inc="v")
        vv = prog.emit("vector", lambda E, r_row=r_row, tmp=tmp, nt=nt:
                       E.scalar_tensor_tensor(
            r_row, psap(7, 1, nt), 1.0 / D, tmp, ALU.mult, ALU.subtract), inc="v")
        sv = prog.emit("scalar", lambda E, r_row=r_row, tmp=tmp: E.activation(
            tmp, r_row, ACT.Ln, bias=cv("eps", rows=1)), wait=[("v", vv)], inc="s")
        sv = prog.emit("scalar", lambda E, r_row=r_row, tmp=tmp: E.activation(
            r_row, tmp, ACT.Exp, scale=-0.5), inc="s")
        if st.get("dbg_mode") == "lnrows" and n == n_list[0]:
            prog.emit("sync", lambda E: E.dma_start(
                out=bass.AP(st["dbg_out"], 0, [[EXT, 1], [1, 1536]]),
                in_=srow[0:1, 0:1536]), wait=[("s", sv)], inc="st", inc_n=16)
        tb = prog.emit("tensor", lambda E, nt=nt, m_row=m_row: E.matmul(
            psap(4, 128, nt), cv("onesrow", rows=1, ncol=128), m_row,
            start=True, stop=True), wait=[("s", sv)], inc="t")
        tb = prog.emit("tensor", lambda E, nt=nt, r_row=r_row: E.matmul(
            psap(5, 128, nt), cv("onesrow", rows=1, ncol=128), r_row,
            start=True, stop=True), inc="t")
        for k in range(8):
            if dst_t is not None:
                vv = prog.emit("vector", lambda E, k=k, o=o, nt=nt: E.tensor_sub(
                    xt[0][0:128, 0:nt], src_t(k)[:, o:o + nt], psap(4, 128, nt)),
                    wait=[("t", tb)], inc="v")
                vv = prog.emit("vector", lambda E, k=k, o=o, nt=nt: E.tensor_mul(
                    dst_t(k)[0:128, o:o + nt], xt[0][0:128, 0:nt], psap(5, 128, nt)),
                    inc="v")
                continue
            b = xt[k % 2]
            vv = prog.emit("vector", lambda E, k=k, b=b, o=o, nt=nt: E.tensor_sub(
                b[0:128, 0:nt], src_t(k)[:, o:o + nt], psap(4, 128, nt)),
                wait=[("t", tb), ("sy%d" % (k % 2), xt_busy[k % 2])], inc="v")
            vv = prog.emit("vector", lambda E, b=b, nt=nt: E.tensor_mul(
                b[0:128, 0:nt], b[0:128, 0:nt], psap(5, 128, nt)), inc="v")
            xt_busy[k % 2] = prog.emit("sync", lambda E, k=k, b=b, o=o, nt=nt:
                                       E.dma_start(
                out=bass.AP(spill, k * 128 * EXT + o, [[EXT, 128], [1, nt]]),
                in_=b[0:128, 0:nt]), wait=[("v", vv)], inc="sy%d" % (k % 2), inc_n=16)
        prog.wait("tensor", "v", prog.val("v"))
    out_done = ((prog.val("sy0"), prog.val("sy1")) if dst_t is None
                else prog.val("v"))
    af.pop(); ab.pop()
    return out_done


def emit_b0(st):
    prog, af, ab = st["prog"], st["af"], st["ab"]
    psap, cv, co = st["psap"], st["cv"], st["co"]
    gdma = st["gdma"]
    xres_t, di = st["xres_t"], st["di"]
    xn_spill = st["xn_spill"]
    y_spill, cp_spill = st["y_spill"], st["cp_spill"]

    af.push(); ab.push()
    sq = valloc(af, 128, 512)
    srow = valloc(af, 1, 2560)
    zt = [valloc(af, 128, 512) for _ in range(2)]
    z_bf = [valloc(ab, 128, 512) for _ in range(8)]
    xnt = [valloc(ab, 128, 512) for _ in range(2)]
    xnt_busy = [0, 0]

    onescol_b = st["onescol_b"]

    def stat_pass(src_fn, mcol, rcol, nt, ones_sum=None):
        if ones_sum is None:
            ones_sum = cv("onescol")
        sq_free = prog.val("t")
        for k in range(8):
            a, wv = src_fn(k)
            sv = prog.emit("scalar", lambda E, a=a, nt=nt: E.activation(
                sq[0:128, 0:nt], a, ACT.Square),
                wait=list(wv) + [("t", sq_free)], inc="s")
            prog.emit("tensor", lambda E, a=a, k=k, nt=nt, ones_sum=ones_sum: E.matmul(
                psap(6, 1, nt), ones_sum, a, start=(k == 0), stop=(k == 7)),
                wait=wv, inc="t")
            sq_free = prog.emit("tensor", lambda E, k=k, nt=nt: E.matmul(
                psap(7, 1, nt), cv("onescol"), sq[0:128, 0:nt],
                start=(k == 0), stop=(k == 7)), wait=[("s", sv)], inc="t")
        m_row = srow[0:1, mcol:mcol + nt]
        r_row = srow[0:1, rcol:rcol + nt]
        tmp = srow[0:1, 2048:2048 + nt]
        prog.emit("vector", lambda E, m_row=m_row, nt=nt: E.tensor_scalar(
            m_row, psap(6, 1, nt), 1.0 / D, None, ALU.mult),
            wait=[("t", sq_free)], inc="v")
        prog.emit("vector", lambda E, m_row=m_row, tmp=tmp: E.tensor_mul(
            tmp, m_row, m_row), inc="v")
        vv = prog.emit("vector", lambda E, r_row=r_row, tmp=tmp, nt=nt:
                       E.scalar_tensor_tensor(
            r_row, psap(7, 1, nt), 1.0 / D, tmp, ALU.mult, ALU.subtract), inc="v")
        sv = prog.emit("scalar", lambda E, r_row=r_row, tmp=tmp: E.activation(
            tmp, r_row, ACT.Ln, bias=cv("eps", rows=1)), wait=[("v", vv)], inc="s")
        sv = prog.emit("scalar", lambda E, r_row=r_row, tmp=tmp: E.activation(
            r_row, tmp, ACT.Exp, scale=-0.5), inc="s")
        return m_row, r_row, [("s", sv)]

    for n in range(NN):
        nt, o = NT[n], NTO[n]
        m1, r1, w1 = stat_pass(lambda k: (xres_t(k)[:, o:o + nt], ()), 0, 512, nt)
        tb = prog.emit("tensor", lambda E, m1=m1, nt=nt: E.matmul(
            psap(4, 128, nt), cv("onesrow", rows=1, ncol=128), m1,
            start=True, stop=True), wait=w1, inc="t")
        tb = prog.emit("tensor", lambda E, r1=r1, nt=nt: E.matmul(
            psap(5, 128, nt), cv("onesrow", rows=1, ncol=128), r1,
            start=True, stop=True), inc="t")
        zw = []
        for k in range(8):
            b = zt[k % 2]
            prog.emit("vector", lambda E, k=k, b=b, o=o, nt=nt: E.tensor_sub(
                b[0:128, 0:nt], xres_t(k)[:, o:o + nt], psap(4, 128, nt)),
                wait=[("t", tb)], inc="v")
            prog.emit("vector", lambda E, b=b, nt=nt: E.tensor_mul(
                b[0:128, 0:nt], b[0:128, 0:nt], psap(5, 128, nt)), inc="v")
            prog.emit("vector", lambda E, k=k, b=b, o=o, nt=nt: E.tensor_add(
                xres_t(k)[:, o:o + nt], xres_t(k)[:, o:o + nt], b[0:128, 0:nt]),
                inc="v")
            vv = prog.emit("vector", lambda E, k=k, b=b, nt=nt: E.tensor_copy(
                z_bf[k][0:128, 0:nt], b[0:128, 0:nt]), inc="v")
            zw.append(vv)
        m2, r2, w2 = stat_pass(
            lambda k: (z_bf[k][0:128, 0:nt], [("v", zw[k])]), 1024, 1536, nt,
            ones_sum=onescol_b[:, 0:1])
        tb2 = prog.emit("tensor", lambda E, m2=m2, nt=nt: E.matmul(
            psap(4, 128, nt), cv("onesrow", rows=1, ncol=128), m2,
            start=True, stop=True), wait=w2, inc="t")
        tb2 = prog.emit("tensor", lambda E, r2=r2, nt=nt: E.matmul(
            psap(5, 128, nt), cv("onesrow", rows=1, ncol=128), r2,
            start=True, stop=True), inc="t")
        for k in range(8):
            b = zt[k % 2]
            prog.emit("vector", lambda E, k=k, b=b, nt=nt: E.tensor_sub(
                b[0:128, 0:nt], z_bf[k][0:128, 0:nt], psap(4, 128, nt)),
                wait=[("t", tb2)], inc="v")
            prog.emit("vector", lambda E, b=b, nt=nt: E.tensor_mul(
                b[0:128, 0:nt], b[0:128, 0:nt], psap(5, 128, nt)), inc="v")
            xb = xnt[k % 2]
            vv = prog.emit("vector", lambda E, k=k, b=b, xb=xb, nt=nt: E.tensor_scalar(
                xb[0:128, 0:nt], b[0:128, 0:nt], cv("gamma", k), cv("beta", k),
                ALU.mult, ALU.add), wait=[("sy%d" % (k % 2), xnt_busy[k % 2])], inc="v")
            xnt_busy[k % 2] = prog.emit("sync", lambda E, k=k, xb=xb, o=o, nt=nt:
                                        E.dma_start(
                out=bass.AP(xn_spill, k * 128 * EXT + o, [[EXT, 128], [1, nt]]),
                in_=xb[0:128, 0:nt]), wait=[("v", vv)], inc="sy%d" % (k % 2), inc_n=16)
        prog.wait("tensor", "v", prog.val("v"))
    xn_done = (prog.val("sy0"), prog.val("sy1"))
    af.pop(); ab.pop()

    af.push()
    ab.push()
    proj = [valloc(af, 128, 516) for _ in range(2)]
    sg = [valloc(af, 128, 512) for _ in range(2)]
    t1 = valloc(af, 128, 512)
    t3 = valloc(af, 128, 512)
    u_bf = valloc(ab, 128, 16 * EXT)

    def u_t(m):
        return View(ab, u_bf.off + m * EXT, 128, EXT)

    st["u_t"] = u_t
    ab.push()
    xn_nt = [valloc(ab, 128, 512) for _ in range(8)]
    wbuf = [valloc(ab, 128, 2048) for _ in range(2)]
    wbuf_busy = [0, 0]
    psA_busy = [0, 0]
    pj_busy = [0, 0]
    prog.wait("gpsimd", "sy0", xn_done[0])
    prog.wait("gpsimd", "sy1", xn_done[1])
    for n in range(NN):
        nt, o = NTO[n] and (NT[n], NTO[n]) or (NT[n], NTO[n])
        nt, o = NT[n], NTO[n]
        xl = []
        for k in range(8):
            lv = gdma(xn_nt[k][0:128, 0:nt],
                      bass.AP(xn_spill, k * 128 * EXT + o, [[EXT, 128], [1, nt]]),
                      wait=[("t", st.get("_xn_rd_t", 0))] if k == 0 else (),
                      sem="wx")
            xl.append(lv)
        for mc in range(16):
            par = mc % 2
            wb = wbuf[par]
            lv = gdma(wb[0:128, 0:1024],
                      bass.AP(di["Win"], mc * 128,
                              [[4096, 128], [128 * 4096, 8], [1, 128]]),
                      wait=[("t", wbuf_busy[par])], sem="w%d" % par)
            lv = gdma(wb[0:128, 1024:2048],
                      bass.AP(di["Win"], 2048 + mc * 128,
                              [[4096, 128], [128 * 4096, 8], [1, 128]]),
                      sem="w%d" % par)
            prog.wait("tensor", "w%d" % par, lv)
            prog.wait("tensor", "wx", xl[-1])
            prog.wait("tensor", "s", psA_busy[par])
            for kt in range(8):
                prog.emit("tensor", lambda E, kt=kt, par=par, wb=wb, nt=nt: E.matmul(
                    psap(par, 128, nt), wb[0:128, kt * 128:kt * 128 + 128],
                    xn_nt[kt][0:128, 0:nt], start=(kt == 0), stop=(kt == 7)), inc="t")
            for kt in range(8):
                prog.emit("tensor", lambda E, kt=kt, par=par, wb=wb, nt=nt: E.matmul(
                    psap(2 + par, 128, nt),
                    wb[0:128, 1024 + kt * 128:1024 + kt * 128 + 128],
                    xn_nt[kt][0:128, 0:nt], start=(kt == 0), stop=(kt == 7)), inc="t")
            tmm = prog.val("t")
            wbuf_busy[par] = tmm
            st["_xn_rd_t"] = tmm
            pj, sgb = proj[par], sg[par]
            if n > 0:
                prog.emit("vector", lambda E, pj=pj, mc=mc: E.tensor_copy(
                    pj[0:128, 0:2], cv("ptail", 2 * mc, ncol=2)),
                    wait=[("v", pj_busy[par])], inc="v")
            else:
                prog.emit("vector", lambda E, pj=pj: E.tensor_copy(
                    pj[0:128, 0:2], cv("zero2", ncol=2)),
                    wait=[("v", pj_busy[par])], inc="v")
            halo_v = prog.val("v")
            sv = prog.emit("scalar", lambda E, pj=pj, mc=mc, par=par, nt=nt:
                           E.activation(pj[0:128, 2:2 + nt], psap(par, 128, nt),
                                        ACT.Identity, bias=cv("b_in", mc)),
                           wait=[("t", tmm), ("v", halo_v)], inc="s")
            sv = prog.emit("scalar", lambda E, sgb=sgb, mc=mc, par=par, nt=nt:
                           E.activation(sgb[0:128, 0:nt], psap(2 + par, 128, nt),
                                        ACT.Tanh, scale=0.5, bias=cv("b_in_h", 16 + mc)),
                           inc="s")
            psA_busy[par] = sv
            vw = [("s", sv)]
            if n == 0:
                vv = prog.emit("vector", lambda E, pj=pj: E.tensor_scalar(
                    pj[0:128, 2:136], pj[0:128, 2:136], cv("prev"), None, ALU.mult),
                    wait=vw, inc="v")
                vw = [("v", vv)]
            vv = prog.emit("vector", lambda E, pj=pj, mc=mc, nt=nt: E.tensor_copy(
                cv("ptail", 2 * mc, ncol=2), pj[0:128, nt:nt + 2]), wait=vw, inc="v")
            vv = prog.emit("vector", lambda E, pj=pj, mc=mc, nt=nt: E.tensor_scalar(
                t1[0:128, 0:nt], pj[0:128, 1:1 + nt], cv("conv_w", 3 * mc + 1),
                None, ALU.mult), inc="v")
            vv = prog.emit("vector", lambda E, pj=pj, mc=mc, nt=nt:
                           E.scalar_tensor_tensor(
                t1[0:128, 0:nt], pj[0:128, 0:nt], cv("conv_w", 3 * mc + 0),
                t1[0:128, 0:nt], ALU.mult, ALU.add), inc="v")
            vv = prog.emit("vector", lambda E, pj=pj, mc=mc, nt=nt:
                           E.scalar_tensor_tensor(
                t3[0:128, 0:nt], pj[0:128, 2:2 + nt], cv("conv_w", 3 * mc + 2),
                t1[0:128, 0:nt], ALU.mult, ALU.add), inc="v")
            sv = prog.emit("scalar", lambda E, mc=mc, nt=nt: E.activation(
                t1[0:128, 0:nt], t3[0:128, 0:nt], ACT.Silu, bias=cv("conv_b", mc)),
                wait=[("v", vv)], inc="s")
            vv = prog.emit("vector", lambda E, sgb=sgb, nt=nt: E.tensor_scalar(
                sgb[0:128, 0:nt], sgb[0:128, 0:nt], 0.5, 0.5, ALU.mult, ALU.add),
                wait=[("s", sv)], inc="v")
            vv = prog.emit("vector", lambda E, mc=mc, o=o, nt=nt, sgb=sgb: E.tensor_mul(
                u_t(mc)[0:128, o:o + nt], t1[0:128, 0:nt], sgb[0:128, 0:nt]), inc="v")
            if n == 0:
                prog.emit("vector", lambda E, mc=mc: E.memset(
                    u_t(mc)[0:128, 0:2], 0.0), inc="v")
                vv = prog.emit("vector", lambda E, mc=mc: E.tensor_scalar(
                    u_t(mc)[0:128, 2:134], u_t(mc)[0:128, 2:134], cv("prev"),
                    None, ALU.mult), inc="v")
            pj_busy[par] = vv
    u_done = prog.val("v")
    p2_end_t = prog.val("t")
    af.pop()
    ab.pop()

    af.push()
    ab.push()
    a_row = valloc(af, 128, EXT)
    w_row = valloc(af, 128, EXT)
    s_row = valloc(af, 128, EXT)
    ct = valloc(af, 128, 512)
    tg = valloc(af, 128, 512)
    zrow = valloc(ab, 128, EXT)
    wabc = valloc(ab, 128, 3 * 2048)
    ytile = valloc(ab, 128, EXT)
    cptile = valloc(ab, 128, EXT)
    prog.emit("vector", lambda E: E.memset(zrow[:, :], 0.0),
              wait=[("t", p2_end_t)], inc="v")
    prog.wait("tensor", "v", u_done)

    spill_done = 0
    wabc_busy = p2_end_t
    yv_done = 0
    ev_s = 0
    ev_v = 0
    for gi in range(16):
        for wi, wname in enumerate(("Wa", "Wb", "Wc")):
            lv = gdma(wabc[0:128, wi * 2048:(wi + 1) * 2048],
                      bass.AP(di[wname], gi * 128,
                              [[2048, 128], [128 * 2048, 16], [1, 128]]),
                      wait=[("t", wabc_busy)] if wi == 0 else (), sem="w0")
        prog.wait("tensor", "w0", lv)
        for which in range(2):
            for n in range(NN):
                nt, o = NT[n], NTO[n]
                par = n % 2
                prog.wait("tensor", "s", ev_s)
                for kt in range(16):
                    prog.emit("tensor", lambda E, kt=kt, par=par, o=o, nt=nt,
                              which=which: E.matmul(
                        psap(par, 128, nt),
                        wabc[0:128, which * 2048 + kt * 128:
                             which * 2048 + kt * 128 + 128],
                        u_t(kt)[0:128, o:o + nt], start=(kt == 0), stop=(kt == 15)),
                        inc="t")
                tmm = prog.val("t")
                bh = "ba_h" if which == 0 else "bb_h"
                sv = prog.emit("scalar", lambda E, par=par, gi=gi, nt=nt, bh=bh:
                               E.activation(tg[0:128, 0:nt], psap(par, 128, nt),
                                            ACT.Tanh, scale=0.5, bias=cv(bh, gi)),
                               wait=[("t", tmm), ("v", prog.val("v"))], inc="s")
                ev_s = sv
                if which == 0:
                    vv = prog.emit("vector", lambda E, o=o, nt=nt: E.tensor_scalar(
                        a_row[0:128, o:o + nt], tg[0:128, 0:nt], 0.5, 0.5,
                        ALU.mult, ALU.add), wait=[("s", sv)], inc="v")
                else:
                    vv = prog.emit("vector", lambda E, o=o, nt=nt: E.tensor_scalar(
                        tg[0:128, 0:nt], tg[0:128, 0:nt], 0.5, 0.5,
                        ALU.mult, ALU.add), wait=[("s", sv)], inc="v")
                    vv = prog.emit("vector", lambda E, o=o, nt=nt, gi=gi: E.tensor_mul(
                        w_row[0:128, o:o + nt], tg[0:128, 0:nt],
                        u_t(gi)[0:128, o:o + nt]), inc="v")
                ev_v = vv
        vv = prog.emit("vector", lambda E: E.tensor_tensor_scan(
            s_row[:, :], a_row[:, :], w_row[:, :], 0.0, ALU.mult, ALU.add), inc="v")
        vv = prog.emit("vector", lambda E: E.tensor_tensor_scan(
            w_row[:, :], a_row[:, :], zrow[:, :], 1.0, ALU.mult, ALU.add), inc="v")
        vv = prog.emit("vector", lambda E, gi=gi: E.tensor_copy(
            cv("exs", gi), s_row[0:128, 2049:2050]), inc="v")
        vv = prog.emit("vector", lambda E, gi=gi: E.tensor_copy(
            cv("exp", gi), w_row[0:128, 2049:2050]), inc="v")
        scan_done = vv
        ev_s2 = 0
        for n in range(NN):
            nt, o = NT[n], NTO[n]
            par = 4 + n % 2
            prog.wait("tensor", "s", ev_s2)
            for kt in range(16):
                prog.emit("tensor", lambda E, kt=kt, par=par, o=o, nt=nt: E.matmul(
                    psap(par, 128, nt),
                    wabc[0:128, 2 * 2048 + kt * 128:2 * 2048 + kt * 128 + 128],
                    u_t(kt)[0:128, o:o + nt], start=(kt == 0), stop=(kt == 15)),
                    inc="t")
            tmm = prog.val("t")
            sv = prog.emit("scalar", lambda E, par=par, gi=gi, nt=nt: E.activation(
                ct[0:128, 0:nt], psap(par, 128, nt), ACT.Tanh, bias=cv("bc", gi)),
                wait=[("t", tmm), ("v", prog.val("v"))], inc="s")
            ev_s2 = sv
            prog.emit("vector", lambda E, o=o, nt=nt: E.tensor_mul(
                tg[0:128, 0:nt], ct[0:128, 0:nt], s_row[0:128, o:o + nt]),
                wait=[("s", sv), ("v", scan_done), ("sy", spill_done)], inc="v")
            prog.emit("vector", lambda E, o=o, nt=nt, gi=gi: E.tensor_add(
                ytile[0:128, o:o + nt], tg[0:128, 0:nt], u_t(gi)[0:128, o:o + nt]),
                inc="v")
            yv_done = prog.emit("vector", lambda E, o=o, nt=nt: E.tensor_mul(
                cptile[0:128, o:o + nt], ct[0:128, 0:nt], w_row[0:128, o:o + nt]),
                inc="v")
        prog.emit("sync", lambda E, gi=gi: E.dma_start(
            out=bass.AP(y_spill, gi * 128 * EXT, [[EXT, 128], [1, EXT]]),
            in_=ytile[:, :]), wait=[("v", yv_done)], inc="sy", inc_n=16)
        spill_done = prog.emit("sync", lambda E, gi=gi: E.dma_start(
            out=bass.AP(cp_spill, gi * 128 * EXT, [[EXT, 128], [1, EXT]]),
            in_=cptile[:, :]), inc="sy", inc_n=16)
        wabc_busy = prog.val("t")
        ev_v = prog.val("v")
        ev_s = prog.val("s")
    p3_end_t = prog.val("t")
    af.pop()
    ab.pop()
    ab.pop()

    exdone = prog.val("v")
    gv = prog.emit("gpsimd", lambda E: E.dma_start(
        out=st["ag1_in"].ap(), in_=cv("exs", 0, ncol=32)),
        wait=[("v", exdone)], inc="wg", inc_n=16)
    cv1 = prog.emit("gpsimd", lambda E: E.collective_compute(
        "AllGather", ALU.bypass, replica_groups=[list(range(8))],
        ins=[st["ag1_in"].ap().opt()], outs=[st["ag1_out"].ap().opt()]),
        wait=[("wg", gv)], inc="cc", inc_n=1)
    af.push()
    agb = valloc(af, 128, 256)
    t2f = valloc(af, 128, 16)
    t3f = valloc(af, 128, 16)
    gv = prog.emit("gpsimd", lambda E: E.dma_start(
        out=agb[:, :], in_=bass.AP(st["ag1_out"], 0,
                                   [[32, 128], [128 * 32, 8], [1, 32]])),
        wait=[("cc", cv1)], inc="wg", inc_n=16)
    prog.wait("vector", "wg", gv)
    prog.emit("vector", lambda E: E.memset(cv("Sin", 0, ncol=16), 0.0), inc="v")
    for j in range(8):
        prog.emit("vector", lambda E, j=j: E.tensor_scalar(
            t2f[:, :], View(af, agb.off + j * 32 + 16, 128, 16)[:, :],
            cv("foldm", j), cv("foldm1", j), ALU.mult, ALU.add), inc="v")
        prog.emit("vector", lambda E: E.tensor_mul(
            t3f[:, :], cv("Sin", 0, ncol=16), t2f[:, :]), inc="v")
        prog.emit("vector", lambda E, j=j: E.tensor_scalar(
            t2f[:, :], View(af, agb.off + j * 32, 128, 16)[:, :],
            cv("foldm", j), None, ALU.mult), inc="v")
        prog.emit("vector", lambda E: E.tensor_add(
            cv("Sin", 0, ncol=16), t3f[:, :], t2f[:, :]), inc="v")
    fold_done = prog.val("v")
    af.pop()

    ab.push()
    wo_all = valloc(ab, 128, 16 * 1024)
    lv = gdma(wo_all[:, :], bass.AP(di["Wo"], 0,
                                    [[1024, 128], [128 * 1024, 16], [1, 1024]]),
              wait=[("t", p3_end_t)], sem="wx")
    ybuf = [valloc(ab, 128, 512) for _ in range(16)]
    yfin = [valloc(ab, 128, 512) for _ in range(16)]
    prog.wait("tensor", "wx", lv)
    bank_busy = [0, 0, 0, 0]
    yread_t = 0
    for n in range(NN):
        nt, o = NT[n], NTO[n]
        lw = []
        for gi in range(16):
            lv = gdma(ybuf[gi][0:128, 0:nt],
                      bass.AP(y_spill, gi * 128 * EXT + o, [[EXT, 128], [1, nt]]),
                      wait=[("sy", spill_done), ("t", yread_t)] if gi == 0 else (),
                      sem="wy")
            lv = gdma(yfin[gi][0:128, 0:nt],
                      bass.AP(cp_spill, gi * 128 * EXT + o, [[EXT, 128], [1, nt]]),
                      sem="wy")
            lw.append(lv)
        yv = []
        for gi in range(16):
            vv = prog.emit("vector", lambda E, gi=gi, nt=nt: E.scalar_tensor_tensor(
                yfin[gi][0:128, 0:nt], yfin[gi][0:128, 0:nt], cv("Sin", gi),
                ybuf[gi][0:128, 0:nt], ALU.mult, ALU.add),
                wait=[("wy", lw[-1]), ("v", fold_done)], inc="v")
            yv.append(vv)
        for mo in range(8):
            par = mo % 4
            prog.wait("tensor", "v", bank_busy[par])
            for kt in range(16):
                prog.emit("tensor", lambda E, kt=kt, mo=mo, par=par, nt=nt: E.matmul(
                    psap(par, 128, nt),
                    wo_all[0:128, kt * 1024 + mo * 128:kt * 1024 + mo * 128 + 128],
                    yfin[kt][0:128, 0:nt], start=(kt == 0), stop=(kt == 15)),
                    wait=[("v", yv[-1])] if kt == 0 else (), inc="t")
            tmm = prog.val("t")
            bank_busy[par] = prog.emit(
                "vector", lambda E, mo=mo, par=par, o=o, nt=nt:
                E.scalar_tensor_tensor(
                    xres_t(mo)[0:128, o:o + nt], psap(par, 128, nt), cv("bo", mo),
                    xres_t(mo)[0:128, o:o + nt], ALU.add, ALU.add),
                wait=[("t", tmm)], inc="v")
        yread_t = prog.val("t")
    for k in range(8):
        prog.emit("vector", lambda E, k=k: E.tensor_scalar(
            xres_t(k)[0:128, 0:H], xres_t(k)[0:128, 0:H], cv("prev"), None,
            ALU.mult), inc="v")
    ab.pop()
    x1_done = prog.val("v")
    prog.wait("tensor", "v", x1_done)
    prog.wait("scalar", "v", x1_done)
    prog.wait("gpsimd", "v", x1_done)
    prog.wait("sync", "v", x1_done)
    st["x1_done"] = x1_done


def emit_b1(st):
    prog, af, ab = st["prog"], st["af"], st["ab"]
    psap, cv, di = st["psap"], st["cv"], st["di"]
    gdma = st["gdma"]
    xres_t = st["xres_t"]
    af.push(); ab.push()
    x1n = valloc(ab, 128, 8 * EXT)

    def x1n_t(k):
        return View(ab, x1n.off + k * EXT, 128, EXT)

    emit_pre_ln(st, xres_t, None, dst_t=x1n_t)
    ln_done = prog.val("v")
    prog.wait("tensor", "v", ln_done)
    prog.wait("gpsimd", "v", ln_done)
    if st.get("dbg_mode") == "x1n":
        prog.wait("gpsimd", "v", ln_done)
        for k in range(8):
            dst = bass.AP(st["dbg_out"], k * 128 * EXT, [[EXT, 128], [1, EXT]])
            st["gdma"](dst, x1n_t(k)[:, :], sem="dbgs")
        prog.wait("gpsimd", "dbgs", prog.val("dbgs"))

    wkv = valloc(ab, 128, 2 * 8192)
    wr_s = valloc(ab, 128, 8192)
    lv = gdma(View(ab, wkv.off, 128, 8192)[:, :],
              bass.AP(di["Wk"], 0, [[1024, 128], [128 * 1024, 8], [1, 1024]]),
              sem="w0")
    lv = gdma(View(ab, wkv.off + 8192, 128, 8192)[:, :],
              bass.AP(di["Wv"], 0, [[1024, 128], [128 * 1024, 8], [1, 1024]]),
              sem="w0")
    lvr = gdma(wr_s[:, :], bass.AP(di["Wr"], 0, [[1024, 128], [128 * 1024, 8], [1, 1024]]),
               sem="w0")
    xrk = valloc(ab, 128, 8 * 512)
    ka = valloc(ab, 128, 8 * 512)
    rbuf = valloc(af, 128, 8 * 512)
    tb1 = valloc(af, 128, 512)
    tbr = valloc(af, 128, 512)
    tbk = valloc(af, 128, 512)
    prog.wait("tensor", "w0", lvr)

    def wslice(w, kt, m):
        base = wkv.off if w == "k" else (wkv.off + 8192 if w == "v" else wr_s.off)
        return View(ab, base + kt * 1024 + m * 128, 128, 128)[:, :]

    bank_s = [0] * 8
    bank_v = [0] * 8
    for n in range(NN):
        nt, o = NT[n], NTO[n]
        if n == 0:
            o, nt = 1, NT[0] - 1
        mixv = 0
        for which, (mu, mu1) in enumerate((("mu_r", "mu_r1"), ("mu_k", "mu_k1"))):
            dstv = xrk
            for k in range(8):
                prog.emit("vector", lambda E, k=k, o=o, nt=nt, mu1=mu1: E.tensor_scalar(
                    tb1[0:128, 0:nt], x1n_t(k)[0:128, o - 1:o - 1 + nt],
                    cv(mu1, k), None, ALU.mult),
                    wait=[("t", bank_v[7])] if which == 1 and k == 0 else (), inc="v")
                mixv = prog.emit("vector", lambda E, k=k, o=o, nt=nt, mu=mu, which=which:
                                 E.scalar_tensor_tensor(
                    xrk[0:128, k * 512:k * 512 + nt], x1n_t(k)[0:128, o:o + nt],
                    cv(mu, k), tb1[0:128, 0:nt], ALU.mult, ALU.add), inc="v")
            if which == 0:
                prog.wait("tensor", "v", mixv)
                for m in range(8):
                    pb = m % 4
                    prog.wait("tensor", "s", bank_s[pb])
                    for kt in range(8):
                        prog.emit("tensor", lambda E, kt=kt, m=m, pb=pb, nt=nt: E.matmul(
                            psap(pb, 128, nt), wslice("r", kt, m),
                            View(ab, xrk.off + kt * 512, 128, nt)[0:128, 0:nt],
                            start=(kt == 0), stop=(kt == 7)), inc="t")
                    tmm = prog.val("t")
                    bank_v[7] = tmm
                    sv = prog.emit("scalar", lambda E, m=m, pb=pb, nt=nt: E.activation(
                        tbr[0:128, 0:nt], psap(pb, 128, nt), ACT.Tanh, scale=0.5),
                        wait=[("t", tmm), ("v", prog.val("v"))], inc="s")
                    bank_s[pb] = sv
                    prog.emit("vector", lambda E, m=m, nt=nt: E.tensor_scalar(
                        View(af, rbuf.off + m * 512, 128, nt)[0:128, 0:nt],
                        tbr[0:128, 0:nt], 0.5, 0.5, ALU.mult, ALU.add),
                        wait=[("s", sv)], inc="v")
        prog.wait("tensor", "v", mixv)
        for m in range(8):
            pb = m % 4
            prog.wait("tensor", "s", bank_s[pb])
            for kt in range(8):
                prog.emit("tensor", lambda E, kt=kt, m=m, pb=pb, nt=nt: E.matmul(
                    psap(pb, 128, nt), wslice("k", kt, m),
                    View(ab, xrk.off + kt * 512, 128, nt)[0:128, 0:nt],
                    start=(kt == 0), stop=(kt == 7)), inc="t")
            tmm = prog.val("t")
            sv = prog.emit("scalar", lambda E, m=m, pb=pb, nt=nt: E.activation(
                tbk[0:128, 0:nt], psap(pb, 128, nt), ACT.Relu),
                wait=[("t", tmm), ("v", prog.val("v"))], inc="s")
            sv = prog.emit("scalar", lambda E, m=m, nt=nt: E.activation(
                View(ab, ka.off + m * 512, 128, nt)[0:128, 0:nt],
                tbk[0:128, 0:nt], ACT.Square), inc="s")
            bank_s[pb] = sv
        kadone = prog.val("s")
        prog.wait("tensor", "s", kadone)
        for m in range(8):
            pb = 4 + m % 2
            prog.wait("tensor", "v", bank_v[pb])
            for kt in range(8):
                prog.emit("tensor", lambda E, kt=kt, m=m, pb=pb, nt=nt: E.matmul(
                    psap(pb, 128, nt), wslice("v", kt, m),
                    View(ab, ka.off + kt * 512, 128, nt)[0:128, 0:nt],
                    start=(kt == 0), stop=(kt == 7)), inc="t")
            tmm = prog.val("t")
            vv = prog.emit("vector", lambda E, m=m, pb=pb, nt=nt: E.tensor_mul(
                tb1[0:128, 0:nt], psap(pb, 128, nt),
                View(af, rbuf.off + m * 512, 128, nt)[0:128, 0:nt]),
                wait=[("t", tmm)], inc="v")
            vv = prog.emit("vector", lambda E, m=m, o=o, nt=nt: E.tensor_add(
                xres_t(m)[0:128, o:o + nt], xres_t(m)[0:128, o:o + nt],
                tb1[0:128, 0:nt]), inc="v")
            bank_v[pb] = vv
        prog.wait("vector", "t", prog.val("t"))
    af.pop(); ab.pop()
    x2_done = prog.val("v")
    prog.wait("tensor", "v", x2_done)
    prog.wait("scalar", "v", x2_done)
    prog.wait("gpsimd", "v", x2_done)
    prog.wait("sync", "v", x2_done)


def emit_b2(st):
    prog, af, ab = st["prog"], st["af"], st["ab"]
    psap, cv, di = st["psap"], st["cv"], st["di"]
    gdma = st["gdma"]
    ps = st["ps"]
    xres_t = st["xres_t"]
    xn_spill = st["xn_spill"]
    out_spill = st["out_spill"]
    af.push(); ab.push()

    x2_done = emit_pre_ln(st, xres_t, xn_spill)
    prog.wait("gpsimd", "sy0", x2_done[0])
    prog.wait("gpsimd", "sy1", x2_done[1])

    q_rot = valloc(ab, 128, 8 * 2048)
    lkv = valloc(ab, 128, 8 * EXT)
    payload = valloc(ab, 128, 1024)
    hsel_b = valloc(ab, 128, 2)
    s2T_b = valloc(ab, 2, 128)
    wp2_b = valloc(ab, 128, 2)
    s2T_f = valloc(af, 2, 128)
    hsel_f = valloc(af, 128, 2)
    mq = valloc(af, 2, 8)
    lv = gdma(hsel_b[:, :], di["halfsel"].ap(), sem="wx")
    lv = gdma(s2T_b[0:2, :], di["sel2T"].ap(), sem="wx")
    lv = gdma(wp2_b[:, :], di["wproj2"].ap(), sem="wx")
    lv = gdma(s2T_f[0:2, :], di["sel2T"].ap(), sem="wx")
    lv = gdma(hsel_f[:, :], di["halfsel"].ap(), sem="wx")
    tabs = lv
    prog.wait("tensor", "wx", tabs)
    prog.wait("vector", "wx", tabs)
    prog.emit("vector", lambda E: E.memset(cv("spare", ncol=1, rows=2), 1.0), inc="v")
    ones2_set = prog.val("v")

    def qr(hp):
        return View(ab, q_rot.off + hp * 2048, 128, 2048)

    def lk(hp):
        return View(ab, lkv.off + hp * EXT, 128, EXT)

    KVC = [(6, 512), (518, 512), (1030, 512), (1542, 512), (2054, 128)]
    OWN = [(134 + 512 * c, 512) for c in range(4)]

    af.push(); ab.push()
    cos_b = valloc(ab, 128, EXT)
    sin_b = valloc(ab, 128, EXT)
    xnt = [valloc(ab, 128, 512) for _ in range(8)]
    kvrot = valloc(ab, 128, EXT)
    wslc = [valloc(ab, 128, 1024) for _ in range(2)]
    kvb = valloc(ab, 128, 512)
    srow = valloc(af, 2, 2048)
    s2 = valloc(af, 2, 2048)
    sc1 = valloc(af, 128, 512)
    sc2 = valloc(af, 128, 512)
    segt = valloc(af, 128, 35)
    comp = valloc(af, 128, 128)
    lv = gdma(cos_b[:, :], di["cos_t"].ap(), sem="wx")
    lv = gdma(sin_b[:, :], di["sin_t"].ap(), sem="wx")
    tabs2 = lv
    prog.wait("vector", "wx", tabs2)

    wq_busy = [0, 0]
    bank_v = {i: 0 for i in range(8)}
    bank_s = {i: 0 for i in range(8)}
    st["_b2xt"] = 0

    def stream_x2n(c0, nt):
        lvv = None
        for k in range(8):
            lvv = gdma(xnt[k][0:128, 0:nt],
                       bass.AP(xn_spill, k * 128 * EXT + c0, [[EXT, 128], [1, nt]]),
                       wait=[("t", st["_b2xt"])] if k == 0 else (), sem="wx")
        return lvv

    def rot_sweep(hp, wname, chunks, dst_fn, scale_q):
        par = 0 if wname == "Wkv" else 1
        wb = wslc[par]
        lvw = gdma(wb[:, :], bass.AP(di[wname], hp * 128,
                                     [[1024, 128], [128 * 1024, 8], [1, 128]]),
                   wait=[("t", wq_busy[par])], sem="w%d" % par)
        if scale_q:
            prog.emit("vector", lambda E, wb=wb: E.tensor_scalar(
                wb[:, :], wb[:, :], 0.125, None, ALU.mult),
                wait=[("w%d" % par, lvw)], inc="v")
            prog.wait("tensor", "v", prog.val("v"))
        else:
            prog.wait("tensor", "w%d" % par, lvw)
        for (c0, nt) in chunks:
            xv = stream_x2n(c0, nt)
            prog.wait("tensor", "wx", xv)
            prog.wait("tensor", "s", bank_s[0])
            prog.wait("tensor", "v", bank_v[0])
            for kt in range(8):
                prog.emit("tensor", lambda E, kt=kt, nt=nt, wb=wb: E.matmul(
                    psap(0, 128, nt), wb[0:128, kt * 128:kt * 128 + 128],
                    xnt[kt][0:128, 0:nt], start=(kt == 0), stop=(kt == 7)), inc="t")
            tmm = prog.val("t")
            st["_b2xt"] = tmm
            sv = prog.emit("scalar", lambda E, nt=nt: E.activation(
                kvb[0:128, 0:nt], psap(0, 128, nt), ACT.Copy),
                wait=[("t", tmm), ("v", bank_v[1])], inc="s")
            bank_s[0] = sv
            prog.wait("tensor", "s", sv)
            prog.wait("tensor", "v", bank_v[1])
            tv = prog.emit("tensor", lambda E, nt=nt: E.matmul(
                psap(1, 128, nt), st["prot_bf"][:, :], kvb[0:128, 0:nt],
                start=True, stop=True), inc="t")
            vv = prog.emit("vector", lambda E, c0=c0, nt=nt: E.tensor_mul(
                sc1[0:128, 0:nt], kvb[0:128, 0:nt], cos_b[0:128, c0:c0 + nt]),
                wait=[("s", sv)], inc="v")
            vv = prog.emit("vector", lambda E, c0=c0, nt=nt: E.tensor_mul(
                sc2[0:128, 0:nt], psap(1, 128, nt), sin_b[0:128, c0:c0 + nt]),
                wait=[("t", tv)], inc="v")
            vv = prog.emit("vector", lambda E, c0=c0, nt=nt, dst=dst_fn(c0, nt):
                           E.tensor_add(dst, sc1[0:128, 0:nt], sc2[0:128, 0:nt]),
                           inc="v")
            bank_v[1] = vv
        wq_busy[par] = prog.val("t")
        return prog.val("v")

    def ln64_stats(src_ap_fn, chunks, dst_write, ones_sum=None):
        osum = ones_sum if ones_sum is not None else hsel_b
        for (c0, nt) in chunks:
            srcap = src_ap_fn(c0, nt)
            sv = prog.emit("scalar", lambda E, srcap=srcap, nt=nt: E.activation(
                sc1[0:128, 0:nt], srcap, ACT.Square),
                wait=[("v", prog.val("v"))], inc="s")
            prog.wait("tensor", "s", sv)
            prog.wait("tensor", "v", bank_v[3])
            tv = prog.emit("tensor", lambda E, srcap=srcap, nt=nt, osum=osum: E.matmul(
                psap(2, 2, nt), osum[:, :], srcap, start=True, stop=True), inc="t")
            tv = prog.emit("tensor", lambda E, nt=nt: E.matmul(
                psap(3, 2, nt), hsel_f[:, :], sc1[0:128, 0:nt],
                start=True, stop=True), inc="t")
            m_row = srow[0:2, 0:nt]
            r_row = srow[0:2, 512:512 + nt]
            tmp = srow[0:2, 1024:1024 + nt]
            vv = prog.emit("vector", lambda E, m_row=m_row, nt=nt: E.tensor_scalar(
                m_row, psap(2, 2, nt), 1.0 / DH, None, ALU.mult),
                wait=[("t", tv)], inc="v")
            vv = prog.emit("vector", lambda E, m_row=m_row, tmp=tmp: E.tensor_mul(
                tmp, m_row, m_row), inc="v")
            vv = prog.emit("vector", lambda E, r_row=r_row, tmp=tmp, nt=nt:
                           E.scalar_tensor_tensor(
                r_row, psap(3, 2, nt), 1.0 / DH, tmp, ALU.mult, ALU.subtract),
                inc="v")
            sv = prog.emit("scalar", lambda E, r_row=r_row, tmp=tmp: E.activation(
                tmp, r_row, ACT.Ln, bias=cv("eps", rows=2)),
                wait=[("v", vv)], inc="s")
            sv = prog.emit("scalar", lambda E, r_row=r_row, tmp=tmp: E.activation(
                r_row, tmp, ACT.Exp, scale=-0.5), inc="s")
            prog.wait("tensor", "s", sv)
            tv = prog.emit("tensor", lambda E, m_row=m_row, nt=nt: E.matmul(
                psap(2, 128, nt), s2T_f[0:2, :], m_row, start=True, stop=True),
                inc="t")
            tv = prog.emit("tensor", lambda E, r_row=r_row, nt=nt: E.matmul(
                psap(3, 128, nt), s2T_f[0:2, :], r_row, start=True, stop=True),
                inc="t")
            vv = prog.emit("vector", lambda E, srcap=srcap, nt=nt: E.tensor_sub(
                sc1[0:128, 0:nt], srcap, psap(2, 128, nt)),
                wait=[("t", tv)], inc="v")
            vv = prog.emit("vector", lambda E, nt=nt: E.tensor_mul(
                sc1[0:128, 0:nt], sc1[0:128, 0:nt], psap(3, 128, nt)), inc="v")
            dst_write(c0, nt)
            bank_v[3] = prog.val("v")

    for hp in range(8):
        rot_sweep(hp, "Wkv", KVC, lambda c0, nt: kvrot[0:128, c0:c0 + nt], False)
        rot_sweep(hp, "Wq", OWN,
                  lambda c0, nt, hp=hp: qr(hp)[0:128, c0 - 134:c0 - 134 + nt], True)
        qv = prog.val("v")
        for ci, (c0, nt) in enumerate(OWN):
            sv = prog.emit("scalar", lambda E, c0=c0, nt=nt, hp=hp: E.activation(
                sc1[0:128, 0:nt], qr(hp)[0:128, c0 - 134:c0 - 134 + nt], ACT.Square),
                wait=[("v", qv), ("t", prog.val("t"))], inc="s")
            prog.wait("tensor", "s", sv)
            prog.wait("tensor", "v", bank_v[2])
            tv = prog.emit("tensor", lambda E, nt=nt: E.matmul(
                psap(2, 2, nt), hsel_f[:, :], sc1[0:128, 0:nt],
                start=True, stop=True), inc="t")
            bank_v[2] = prog.emit("vector", lambda E, ci=ci, nt=nt: E.reduce_max(
                srow[0:2, ci:ci + 1], psap(2, 2, nt), axis=AX.X),
                wait=[("t", tv)], inc="v")
        vv = prog.emit("vector", lambda E: E.reduce_max(
            srow[0:2, 4:5], srow[0:2, 0:4], axis=AX.X), inc="v")
        sv = prog.emit("scalar", lambda E: E.activation(
            srow[0:2, 1:2], srow[0:2, 4:5], ACT.Ln, bias=cv("eps", rows=2)),
            wait=[("v", vv)], inc="s")
        sv = prog.emit("scalar", lambda E, hp=hp: E.activation(
            mq[0:2, hp:hp + 1], srow[0:2, 1:2], ACT.Exp, scale=0.5), inc="s")
        ln64_stats(lambda c0, nt: kvrot[0:128, c0:c0 + nt], KVC,
                   lambda c0, nt, hp=hp: prog.emit(
                       "vector", lambda E, c0=c0, nt=nt, hp=hp: E.tensor_scalar(
                           lk(hp)[0:128, c0:c0 + nt], sc1[0:128, 0:nt],
                           cv("lng_l"), cv("lnb_l"), ALU.mult, ALU.add), inc="v"))
        for ci, (c0, nt) in enumerate(OWN):
            prog.wait("tensor", "v", bank_v[4])
            tv = prog.emit("tensor", lambda E, c0=c0, nt=nt: E.matmul(
                psap(4, 2, nt), wp2_b[:, :], kvrot[0:128, c0:c0 + nt],
                start=True, stop=True), inc="t")
            vv = prog.emit("vector", lambda E, ci=ci, nt=nt: E.tensor_copy(
                s2[0:2, ci * 512:ci * 512 + nt], psap(4, 2, nt)),
                wait=[("t", tv)], inc="v")
            bank_v[4] = vv
        sdone = prog.val("v")
        segdma = st.get("_segdma", 0)
        for h in range(2):
            dv = prog.emit("sync", lambda E, h=h: E.dma_start(
                out=bass.AP(af.t, segt.off, [[af.cols, 128], [1, 16]]),
                in_=bass.AP(af.t, s2.off + h * af.cols,
                            [[af.cols, 1], [16, 128], [1, 16]])),
                wait=[("v", prog.val("v"))], inc="sg", inc_n=16)
            dgv = prog.val("sg")
            vv = prog.emit("vector", lambda E: E.reduce_max(
                View(af, segt.off + 16, 128, 1)[0:128, 0:1],
                View(af, segt.off, 128, 16)[0:128, 0:16], axis=AX.X),
                wait=[("sg", dgv)], inc="v")
            vv = prog.emit("vector", lambda E: E.tensor_scalar(
                View(af, segt.off + 16, 128, 1)[0:128, 0:1],
                View(af, segt.off + 16, 128, 1)[0:128, 0:1], -1.0, None, ALU.mult),
                inc="v")
            sv = prog.emit("scalar", lambda E: E.activation(
                View(af, segt.off + 17, 128, 16)[0:128, 0:16],
                View(af, segt.off, 128, 16)[0:128, 0:16], ACT.Exp,
                bias=View(af, segt.off + 16, 128, 1)[0:128, 0:1],
                accum_out=View(af, segt.off + 33, 128, 1)[0:128, 0:1]),
                wait=[("v", vv)], inc="s")
            vv = prog.emit("vector", lambda E: E.reciprocal(
                View(af, segt.off + 34, 128, 1)[0:128, 0:1],
                View(af, segt.off + 33, 128, 1)[0:128, 0:1]),
                wait=[("s", sv)], inc="v")
            vv = prog.emit("vector", lambda E: E.tensor_scalar(
                View(af, segt.off + 17, 128, 16)[0:128, 0:16],
                View(af, segt.off + 17, 128, 16)[0:128, 0:16],
                View(af, segt.off + 34, 128, 1)[0:128, 0:1], None, ALU.mult),
                inc="v")
            dv = prog.emit("sync", lambda E, h=h: E.dma_start(
                out=bass.AP(af.t, s2.off + h * af.cols,
                            [[af.cols, 1], [16, 128], [1, 16]]),
                in_=bass.AP(af.t, segt.off + 17, [[af.cols, 128], [1, 16]])),
                wait=[("v", vv)], inc="sg", inc_n=16)
        st["_segdma"] = prog.val("sg")
        prog.wait("tensor", "sg", st["_segdma"])
        prog.wait("vector", "sg", st["_segdma"])
        for ci, (c0, nt) in enumerate(OWN):
            prog.wait("tensor", "v", bank_v[5])
            tv = prog.emit("tensor", lambda E, ci=ci, nt=nt: E.matmul(
                psap(5, 128, nt), s2T_f[0:2, :], s2[0:2, ci * 512:ci * 512 + nt],
                start=True, stop=True), inc="t")
            vv = prog.emit("vector", lambda E, c0=c0, nt=nt: E.tensor_mul(
                sc1[0:128, 0:nt], kvrot[0:128, c0:c0 + nt], psap(5, 128, nt)),
                wait=[("t", tv)], inc="v")
            vv = prog.emit("vector", lambda E, ci=ci: E.reduce_sum(
                comp[0:128, ci * 32:ci * 32 + 32],
                bass.AP(af.t, sc1.off, [[af.cols, 128], [16, 32], [1, 16]]),
                axis=AX.X), inc="v")
            bank_v[5] = vv
        def wr_gkv(c0, nt, hp=hp):
            vv = prog.emit("vector", lambda E, nt=nt: E.tensor_scalar(
                sc2[0:128, 0:nt], sc1[0:128, 0:nt], cv("lng_g"), cv("lnb_g"),
                ALU.mult, ALU.add), inc="v")
        ln64_stats(lambda c0, nt: comp[0:128, 0:128], [(0, 128)], wr_gkv,
                   ones_sum=hsel_f)
        prog.wait("tensor", "v", prog.val("v"))
        tv = prog.emit("tensor", lambda E: E.transpose(
            psap(0, 128, 128), sc2[0:128, 0:128], st["ident_f"][:, :]),
            wait=[("s", bank_s[0]), ("v", bank_v[1])], inc="t")
        vv = prog.emit("vector", lambda E, hp=hp: E.tensor_copy(
            payload[0:128, hp * 128:hp * 128 + 128], psap(0, 128, 128)),
            wait=[("t", tv)], inc="v")
        bank_v[0] = vv
    pay_done = prog.val("v")
    if st.get("dbg_mode") in ("b2q", "b2lkv", "b2pay"):
        prog.wait("gpsimd", "v", pay_done)
        mode = st["dbg_mode"]
        for hp in range(8):
            if mode == "b2q":
                srcv = qr(hp)[:, 0:2048]
                ncol = 2048
            elif mode == "b2lkv":
                srcv = lk(hp)[:, 0:EXT]
                ncol = EXT
            else:
                srcv = payload[0:128, hp * 128:hp * 128 + 128]
                ncol = 128
            dst = bass.AP(st["dbg_out"], hp * 128 * EXT, [[EXT, 128], [1, ncol]])
            gdma(dst, srcv, sem="dbgs")
        prog.wait("gpsimd", "dbgs", prog.val("dbgs"))
    af.pop(); ab.pop()

    gv = prog.emit("gpsimd", lambda E: E.dma_start(
        out=st["ag2_in"].ap(), in_=payload[:, :]),
        wait=[("v", pay_done)], inc="wg", inc_n=16)
    cv2 = prog.emit("gpsimd", lambda E: E.collective_compute(
        "AllGather", ALU.bypass, replica_groups=[list(range(8))],
        ins=[st["ag2_in"].ap().opt()], outs=[st["ag2_out"].ap().opt()]),
        wait=[("wg", gv)], inc="cc", inc_n=1)
    prog.wait("sync", "cc", cv2)

    af.push(); ab.push()
    gkv_fm = valloc(ab, 128, NSEG)
    vtokg = [valloc(ab, 128, 130) for _ in range(8)]
    vtokl = [valloc(ab, 128, 130) for _ in range(17)]
    gmask = valloc(ab, 128, 4096)
    attn = valloc(ab, 128, 4096)
    attl = valloc(ab, 128, 256)
    outb = valloc(ab, 128, 512)
    lsc = valloc(af, 128, 512)
    outf = valloc(af, 128, 512)
    rrow = valloc(af, 1, 512)
    gn = valloc(af, 2, 8)
    nbuf = valloc(af, 128, 1)
    for i in range(8):
        prog.emit("vector", lambda E, i=i: E.memset(vtokg[i][0:128, 64:65], 1.0), inc="v")
        prog.emit("vector", lambda E, i=i: E.memset(vtokg[i][0:128, 129:130], 1.0), inc="v")
    for i in range(17):
        prog.emit("vector", lambda E, i=i: E.memset(vtokl[i][0:128, 64:65], 1.0), inc="v")
        prog.emit("vector", lambda E, i=i: E.memset(vtokl[i][0:128, 129:130], 1.0), inc="v")
    ones_set = prog.val("v")
    prog.wait("sync", "v", ones_set)

    out_dma = 0
    gm_busy = 0
    for hp in range(8):
        dv = prog.emit("sync", lambda E, hp=hp: E.dma_start_transpose(
            gkv_fm[0:128, 0:NSEG],
            bass.AP(st["ag2_out"], hp * 128, [[1024, NSEG], [1, 128]])),
            wait=[("t", st.get("_l2t", 0))], inc="sg", inc_n=16)
        for stt in range(8):
            dv = prog.emit("sync", lambda E, hp=hp, stt=stt: E.dma_start(
                out=bass.AP(ab.t, vtokg[stt].off, [[ab.cols, 128], [65, 2], [1, 64]]),
                in_=bass.AP(st["ag2_out"], stt * 128 * 1024 + hp * 128,
                            [[1024, 128], [64, 2], [1, 64]])), inc="sg", inc_n=16)
        gld = prog.val("sg")
        for kw in range(17):
            sv = prog.emit("scalar", lambda E, kw=kw, hp=hp: E.activation(
                lsc[0:128, 0:128], lk(hp)[0:128, 6 + 128 * kw:6 + 128 * kw + 128],
                ACT.Copy), wait=[("t", prog.val("t")), ("sg", gld)], inc="s")
            prog.wait("tensor", "s", sv)
            tv = prog.emit("tensor", lambda E: E.transpose(
                psap(6, 128, 128), lsc[0:128, 0:128], st["ident_f"][:, :]),
                wait=[("v", prog.val("v"))], inc="t")
            vv = prog.emit("vector", lambda E, kw=kw: E.tensor_copy(
                bass.AP(ab.t, vtokl[kw].off, [[ab.cols, 128], [65, 2], [1, 64]]),
                bass.AP(ps[6], 0, [[512, 128], [64, 2], [1, 64]])),
                wait=[("t", tv)], inc="v")
        for ci in range(2):
            sv = prog.emit("scalar", lambda E, ci=ci: E.activation(
                lsc[0:128, 0:512], gkv_fm[0:128, ci * 512:ci * 512 + 512],
                ACT.Square), wait=[("t", prog.val("t")), ("sg", gld),
                                   ("v", prog.val("v"))], inc="s")
            prog.wait("tensor", "s", sv)
            tv = prog.emit("tensor", lambda E: E.matmul(
                psap(7, 2, 512), hsel_f[:, :], lsc[0:128, 0:512],
                start=True, stop=True), inc="t")
            prog.emit("vector", lambda E, ci=ci: E.reduce_max(
                gn[0:2, ci:ci + 1], psap(7, 2, 512), axis=AX.X),
                wait=[("t", tv)], inc="v")
        for ci, (c0, nt) in enumerate(KVC):
            sv = prog.emit("scalar", lambda E, c0=c0, nt=nt, hp=hp: E.activation(
                lsc[0:128, 0:nt], lk(hp)[0:128, c0:c0 + nt], ACT.Square),
                wait=[("t", prog.val("t")), ("v", prog.val("v"))], inc="s")
            prog.wait("tensor", "s", sv)
            tv = prog.emit("tensor", lambda E, nt=nt: E.matmul(
                psap(7, 2, nt), hsel_f[:, :], lsc[0:128, 0:nt],
                start=True, stop=True), inc="t")
            prog.emit("vector", lambda E, ci=ci, nt=nt: E.reduce_max(
                gn[0:2, 2 + ci:3 + ci], psap(7, 2, nt), axis=AX.X),
                wait=[("t", tv)], inc="v")
        vv = prog.emit("vector", lambda E: E.reduce_max(
            gn[0:2, 7:8], gn[0:2, 0:7], axis=AX.X), inc="v")
        vv = prog.emit("vector", lambda E: E.tensor_copy(
            gn[0:2, 2:3], gn[0:2, 7:8]), inc="v")
        sv = prog.emit("scalar", lambda E: E.activation(
            gn[0:2, 3:4], gn[0:2, 2:3], ACT.Ln, bias=cv("eps", rows=2)),
            wait=[("v", vv)], inc="s")
        sv = prog.emit("scalar", lambda E: E.activation(
            gn[0:2, 4:5], gn[0:2, 3:4], ACT.Exp, scale=0.5), inc="s")
        vv = prog.emit("vector", lambda E, hp=hp: E.tensor_mul(
            gn[0:2, 5:6], gn[0:2, 4:5], mq[0:2, hp:hp + 1]),
            wait=[("s", sv)], inc="v")
        prog.wait("tensor", "v", prog.val("v"))
        tv = prog.emit("tensor", lambda E: E.matmul(
            psap(7, 1, 1), cv("spare", rows=2), gn[0:2, 5:6],
            start=True, stop=True), inc="t")
        vv = prog.emit("vector", lambda E: E.tensor_scalar(
            gn[0:1, 6:7], bass.AP(ps[7], 0, [[512, 1], [1, 1]]),
            -1.0, None, ALU.mult), wait=[("t", tv)], inc="v")
        prog.wait("tensor", "v", vv)
        tv = prog.emit("tensor", lambda E: E.matmul(
            psap(7, 128, 1), cv("onesrow", rows=1, ncol=128),
            gn[0:1, 6:7], start=True, stop=True), inc="t")
        vv = prog.emit("vector", lambda E: E.tensor_copy(
            nbuf[0:128, 0:1], bass.AP(ps[7], 0, [[512, 128], [1, 1]])),
            wait=[("t", tv)], inc="v")
        nb_done = prog.val("v")
        for j in range(4):
            mv = gdma(gmask[:, :], bass.AP(di["gmask"], j * 128 * 4096,
                                           [[4096, 128], [1, 4096]]),
                      wait=[("v", gm_busy)], sem="wx")
            for h in range(2):
                hb = 64 * h
                for stt in range(8):
                    pb = stt % 2
                    prog.wait("tensor", "s", bank_s.get(("g", pb), 0))
                    tv = prog.emit("tensor", lambda E, stt=stt, h=h, hp=hp, j=j, pb=pb,
                                   hb=hb: E.matmul(
                        psap(pb, 128, 512),
                        View(ab, gkv_fm.off + stt * 128, 128, 128)[hb:hb + 64, 0:128],
                        qr(hp)[hb:hb + 64, j * 512:j * 512 + 512],
                        start=True, stop=True), inc="t")
                    sv = prog.emit("scalar", lambda E, stt=stt, pb=pb: E.activation(
                        View(ab, attn.off + stt * 512, 128, 512)[0:128, 0:512],
                        psap(pb, 128, 512), ACT.Exp, bias=nbuf[0:128, 0:1]),
                        wait=[("t", tv), ("v", nb_done)], inc="s")
                    bank_s[("g", pb)] = sv
                vv = prog.emit("vector", lambda E: E.tensor_mul(
                    attn[0:128, 0:4096], attn[0:128, 0:4096], gmask[0:128, 0:4096]),
                    wait=[("s", prog.val("s")), ("wx", mv)], inc="v")
                gm_busy_h = vv
                prog.wait("tensor", "v", vv)
                prog.wait("tensor", "s", bank_s.get("out", 0))
                for stt in range(8):
                    prog.emit("tensor", lambda E, stt=stt, h=h, hb=hb: E.matmul(
                        psap(6, 65, 512),
                        View(ab, vtokg[stt].off, 128, 130)[0:128, 65 * h:65 * h + 65],
                        View(ab, attn.off + stt * 512, 128, 512)[0:128, 0:512],
                        start=(stt == 0), stop=False), inc="t")
                for w in range(4):
                    wq = 4 * j + w
                    kvc_prev = 6 + 128 * wq
                    for t2, kc in ((0, kvc_prev), (1, kvc_prev + 128)):
                        pb = 2 + t2
                        prog.wait("tensor", "s", bank_s.get(("l", pb), 0))
                        tv = prog.emit("tensor", lambda E, kc=kc, h=h, hp=hp, wq=wq,
                                       pb=pb, hb=hb: E.matmul(
                            psap(pb, 128, 128), lk(hp)[hb:hb + 64, kc:kc + 128],
                            qr(hp)[hb:hb + 64, wq * 128:wq * 128 + 128],
                            start=True, stop=True), inc="t")
                        sv = prog.emit("scalar", lambda E, t2=t2, pb=pb: E.activation(
                            attl[0:128, t2 * 128:t2 * 128 + 128],
                            psap(pb, 128, 128), ACT.Exp, bias=nbuf[0:128, 0:1]),
                            wait=[("t", tv)], inc="s")
                        bank_s[("l", pb)] = sv
                    vv = prog.emit("vector", lambda E: E.tensor_mul(
                        attl[0:128, 128:256], attl[0:128, 128:256],
                        st["causal_bf"][:, :]), wait=[("s", prog.val("s"))], inc="v")
                    if wq == 0:
                        vv = prog.emit("vector", lambda E: E.tensor_scalar(
                            attl[0:128, 0:128], attl[0:128, 0:128],
                            cv("prev"), None, ALU.mult), inc="v")
                    prog.wait("tensor", "v", vv)
                    for t2 in (0, 1):
                        prog.emit("tensor", lambda E, wq=wq, w=w, t2=t2, h=h: E.matmul(
                            bass.AP(ps[6], w * 128, [[512, 65], [1, 128]]),
                            View(ab, vtokl[wq + t2].off, 128, 130)[0:128,
                                                                   65 * h:65 * h + 65],
                            attl[0:128, t2 * 128:t2 * 128 + 128],
                            start=False, stop=(w == 3 and t2 == 1)), inc="t")
                tmm = prog.val("t")
                vv = prog.emit("vector", lambda E: E.tensor_copy(
                    View(af, outf.off, 128, 512)[0:65, 0:512],
                    bass.AP(ps[6], 0, [[512, 65], [1, 512]])),
                    wait=[("t", tmm)], inc="v")
                vv = prog.emit("vector", lambda E: E.reciprocal(
                    rrow[0:1, 0:512],
                    View(af, outf.off, 128, 512)[64:65, 0:512]), inc="v")
                prog.wait("tensor", "v", vv)
                tv = prog.emit("tensor", lambda E: E.matmul(
                    psap(7, 64, 512), cv("onesrow", rows=1, ncol=64),
                    rrow[0:1, 0:512], start=True, stop=True), inc="t")
                vv = prog.emit("vector", lambda E, h=h: E.tensor_mul(
                    outb[64 * h:64 * h + 64, 0:512],
                    View(af, outf.off, 128, 512)[0:64, 0:512], psap(7, 64, 512)),
                    wait=[("t", tv)], inc="v")
                bank_s["out"] = prog.val("s")
            gm_busy = prog.val("v")
            out_dma = prog.emit("sync", lambda E, hp=hp, j=j: E.dma_start(
                out=bass.AP(out_spill, hp * 128 * 2048 + j * 512,
                            [[2048, 128], [1, 512]]),
                in_=outb[0:128, 0:512]), wait=[("v", gm_busy)], inc="sy", inc_n=16)
            prog.wait("vector", "sy", out_dma)
        st["_l2t"] = prog.val("t")
    af.pop(); ab.pop()

    af.push(); ab.push()
    wout = valloc(ab, 128, 8192)
    otiles = [valloc(ab, 128, 512) for _ in range(8)]
    lv = gdma(wout[:, :], bass.AP(di["Wout"], 0,
                                  [[1024, 128], [128 * 1024, 8], [1, 1024]]),
              wait=[("t", prog.val("t"))], sem="wx")
    prog.wait("tensor", "wx", lv)
    bank_vv = [0, 0, 0, 0]
    ot_busy = 0
    for j in range(4):
        lw = None
        for kt in range(8):
            lw = gdma(otiles[kt][0:128, 0:512],
                      bass.AP(out_spill, kt * 128 * 2048 + j * 512,
                              [[2048, 128], [1, 512]]),
                      wait=[("sy", out_dma), ("t", ot_busy)] if kt == 0 else (),
                      sem="wy")
        prog.wait("tensor", "wy", lw)
        for mo in range(8):
            pb = mo % 4
            prog.wait("tensor", "v", bank_vv[pb])
            for kt in range(8):
                prog.emit("tensor", lambda E, kt=kt, mo=mo, pb=pb: E.matmul(
                    psap(pb, 128, 512),
                    wout[0:128, kt * 1024 + mo * 128:kt * 1024 + mo * 128 + 128],
                    otiles[kt][0:128, 0:512], start=(kt == 0), stop=(kt == 7)),
                    inc="t")
            tmm = prog.val("t")
            bank_vv[pb] = prog.emit(
                "vector", lambda E, mo=mo, pb=pb, j=j: E.scalar_tensor_tensor(
                    xres_t(mo)[0:128, 134 + j * 512:134 + j * 512 + 512],
                    psap(pb, 128, 512), cv("bout", mo),
                    xres_t(mo)[0:128, 134 + j * 512:134 + j * 512 + 512],
                    ALU.add, ALU.add), wait=[("t", tmm)], inc="v")
        ot_busy = prog.val("t")
    af.pop(); ab.pop()
    af.pop(); ab.pop()
    x3_done = prog.val("v")
    prog.wait("tensor", "v", x3_done)
    prog.wait("scalar", "v", x3_done)
    prog.wait("gpsimd", "v", x3_done)
    prog.wait("sync", "v", x3_done)


def emit_b3(st):
    prog, af, ab = st["prog"], st["af"], st["ab"]
    psap, cv, di = st["psap"], st["cv"], st["di"]
    gdma = st["gdma"]
    ps = st["ps"]
    xres_t = st["xres_t"]
    xn_spill = st["xn_spill"]
    af.push(); ab.push()

    x3_done = emit_pre_ln(st, xres_t, xn_spill)
    prog.wait("gpsimd", "sy0", x3_done[0])
    prog.wait("gpsimd", "sy1", x3_done[1])

    xnt = [valloc(ab, 128, 1024) for _ in range(8)]
    wrt = valloc(ab, 128, 64)
    wsl = valloc(ab, 128, 3 * 1024)
    hbuf = valloc(ab, 128, 16 * 1024)
    dwall = valloc(ab, 1, 8 * 2048)
    onesrow_b = valloc(ab, 1, 128)
    dwr = valloc(af, 8, 2048)
    sm = valloc(af, 128, 8)
    ex = valloc(af, 128, 8)
    ab8 = valloc(af, 128, 8)
    fb = valloc(af, 128, 8)
    oh = valloc(af, 128, 8)
    dwt = valloc(af, 128, 8)
    smc = valloc(af, 128, 8)
    dwb = valloc(af, 128, 512)
    scf = valloc(af, 128, 512)
    sgf = valloc(af, 128, 512)
    JEPS2 = 0.02
    BIG = 1.0e30

    def C(j):
        return smc[0:128, j:j + 1]

    lv = gdma(wrt[:, :], bass.AP(di["Wrout"], 0, [[8, 128], [1024, 8], [1, 8]]),
              sem="wx")
    prog.emit("vector", lambda E: E.memset(onesrow_b[0:1, :], 1.0), inc="v")
    st["_b3xt"] = 0

    def stream_x3n(c0, nt):
        lvv = None
        for k in range(8):
            lvv = gdma(xnt[k][0:128, 0:nt],
                       bass.AP(xn_spill, k * 128 * EXT + c0, [[EXT, 128], [1, nt]]),
                       wait=[("t", st["_b3xt"])] if k == 0 else (), sem="wx")
        return lvv

    prog.wait("tensor", "wx", lv)
    xv = stream_x3n(134, 1024)
    prog.wait("tensor", "wx", xv)
    tp_v = 0
    for tt in range(16):
        if tt == 8:
            xv = stream_x3n(134 + 1024, 1024)
            prog.wait("tensor", "wx", xv)
        off = (tt % 8) * 128
        pb = tt % 2
        prog.wait("tensor", "v", st.get("_b3pv%d" % pb, 0))
        for kt in range(8):
            prog.emit("tensor", lambda E, kt=kt, off=off, pb=pb: E.matmul(
                psap(pb, 128, 8), xnt[kt][0:128, off:off + 128],
                wrt[0:128, kt * 8:kt * 8 + 8], start=(kt == 0), stop=(kt == 7)),
                inc="t")
        tmm = prog.val("t")
        st["_b3xt"] = tmm
        vv = prog.emit("vector", lambda E, pb=pb: E.tensor_copy(
            sm[:, :], psap(pb, 128, 8)), wait=[("t", tmm)], inc="v")
        st["_b3pv%d" % pb] = vv
        if st.get("dbg_mode") == "b3sm":
            prog.emit("sync", lambda E, tt=tt: E.dma_start(
                out=bass.AP(st["dbg_out"], tt * 8, [[EXT, 128], [1, 8]]),
                in_=sm[:, :]), wait=[("v", vv)], inc="dbgs", inc_n=16)
            prog.wait("vector", "dbgs", prog.val("dbgs"))
        prog.emit("vector", lambda E: E.reduce_max(C(0), sm[:, :], axis=AX.X), inc="v")
        prog.emit("vector", lambda E: E.tensor_scalar(
            C(1), C(0), -1.0, None, ALU.mult), inc="v")
        prog.emit("vector", lambda E: E.tensor_scalar(
            ab8[:, :], sm[:, :], -1.0, None, ALU.mult), inc="v")
        prog.emit("vector", lambda E: E.tensor_max(ab8[:, :], ab8[:, :], sm[:, :]),
                  inc="v")
        prog.emit("vector", lambda E: E.tensor_scalar(
            fb[:, :], ab8[:, :], C(0), None, ALU.max), inc="v")
        prog.emit("vector", lambda E: E.scalar_tensor_tensor(
            fb[:, :], fb[:, :], JEPS2, sm[:, :], ALU.mult, ALU.add), inc="v")
        vv = prog.emit("vector", lambda E: E.tensor_scalar(
            fb[:, :], fb[:, :], C(0), None, ALU.is_ge), inc="v")
        sv = prog.emit("scalar", lambda E: E.activation(
            ex[:, :], sm[:, :], ACT.Exp, bias=C(1)), wait=[("v", vv)], inc="s")
        prog.emit("vector", lambda E: E.tensor_mul(ex[:, :], ex[:, :], fb[:, :]),
                  wait=[("s", sv)], inc="v")
        prog.emit("vector", lambda E: E.reduce_sum(C(2), ex[:, :], axis=AX.X), inc="v")
        prog.emit("vector", lambda E: E.reciprocal(C(3), C(2)), inc="v")
        prog.emit("vector", lambda E: E.tensor_scalar(
            oh[:, :], sm[:, :], C(0), None, ALU.is_ge), inc="v")
        prog.emit("vector", lambda E: E.tensor_scalar(
            dwt[:, :], oh[:, :], C(3), None, ALU.mult), inc="v")
        prog.emit("vector", lambda E: E.scalar_tensor_tensor(
            fb[:, :], oh[:, :], -BIG, sm[:, :], ALU.mult, ALU.add), inc="v")
        prog.emit("vector", lambda E: E.reduce_max(C(4), fb[:, :], axis=AX.X), inc="v")
        prog.emit("vector", lambda E: E.tensor_scalar(
            C(5), C(4), -1.0, None, ALU.mult), inc="v")
        prog.emit("vector", lambda E: E.tensor_scalar(
            ab8[:, :], ab8[:, :], C(4), None, ALU.max), inc="v")
        prog.emit("vector", lambda E: E.scalar_tensor_tensor(
            ab8[:, :], ab8[:, :], JEPS2, sm[:, :], ALU.mult, ALU.add), inc="v")
        prog.emit("vector", lambda E: E.tensor_scalar(
            ab8[:, :], ab8[:, :], C(4), None, ALU.is_ge), inc="v")
        prog.emit("vector", lambda E: E.tensor_scalar(
            fb[:, :], fb[:, :], C(4), None, ALU.is_ge), inc="v")
        prog.emit("vector", lambda E: E.tensor_scalar(
            oh[:, :], oh[:, :], -1.0, 1.0, ALU.mult, ALU.add), inc="v")
        vv = prog.emit("vector", lambda E: E.tensor_mul(
            ab8[:, :], ab8[:, :], oh[:, :]), inc="v")
        sv = prog.emit("scalar", lambda E: E.activation(
            ex[:, :], sm[:, :], ACT.Exp, bias=C(5)), wait=[("v", vv)], inc="s")
        prog.emit("vector", lambda E: E.tensor_mul(ex[:, :], ex[:, :], ab8[:, :]),
                  wait=[("s", sv)], inc="v")
        prog.emit("vector", lambda E: E.reduce_sum(C(6), ex[:, :], axis=AX.X), inc="v")
        prog.emit("vector", lambda E: E.reciprocal(C(7), C(6)), inc="v")
        vv = prog.emit("vector", lambda E: E.scalar_tensor_tensor(
            dwt[:, :], fb[:, :], C(7), dwt[:, :], ALU.mult, ALU.add), inc="v")
        if st.get("dbg_mode") == "b3dw":
            prog.emit("sync", lambda E, tt=tt: E.dma_start(
                out=bass.AP(st["dbg_out"], tt * 8, [[EXT, 128], [1, 8]]),
                in_=dwt[:, :]), wait=[("v", vv)], inc="dbgs", inc_n=16)
            prog.wait("vector", "dbgs", prog.val("dbgs"))
        prog.wait("tensor", "v", vv)
        prog.wait("tensor", "v", tp_v)
        tv = prog.emit("tensor", lambda E, pb=pb: E.transpose(
            psap(2, 8, 128), dwt[:, :], st["ident_f"][:, :]), inc="t")
        tp_v = prog.emit("vector", lambda E, tt=tt: E.tensor_copy(
            dwr[0:8, tt * 128:tt * 128 + 128], psap(2, 8, 128)),
            wait=[("t", tv)], inc="v")
        gdma(bass.AP(ab.t, dwall.off + tt * 128, [[ab.cols, 1], [2048, 8], [1, 128]]),
             dwr[0:8, tt * 128:tt * 128 + 128], sem="dw", wait=[("v", tp_v)])

    bank_busy = {i: 0 for i in range(8)}
    dwr_done = prog.val("v")
    prog.wait("tensor", "dw", prog.val("dw"))
    wsl_busy = 0
    h_busy = 0
    for e in range(8):
        for half in range(2):
            hc0 = 134 + half * 1024
            st["_b3xt"] = prog.val("t")
            xv = stream_x3n(hc0, 1024)
            for kt in range(16):
                lvw = gdma(wsl[0:128, 0:1024],
                           bass.AP(di["Wg"], e * 1024 * FF + kt * 128,
                                   [[FF, 128], [128 * FF, 8], [1, 128]]),
                           wait=[("t", wsl_busy)], sem="w0")
                lvw = gdma(wsl[0:128, 1024:2048],
                           bass.AP(di["Wu"], e * 1024 * FF + kt * 128,
                                   [[FF, 128], [128 * FF, 8], [1, 128]]), sem="w0")
                prog.wait("tensor", "w0", lvw)
                prog.wait("tensor", "wx", xv)
                for sub in range(2):
                    so = sub * 512
                    prog.wait("tensor", "s", bank_busy.get("g", 0))
                    prog.wait("tensor", "v", bank_busy.get("u", 0))
                    prog.wait("tensor", "v", bank_busy[0])
                    prog.wait("tensor", "v", bank_busy[1])
                    for k8 in range(8):
                        prog.emit("tensor", lambda E, k8=k8, so=so: E.matmul(
                            psap(0, 128, 512), wsl[0:128, k8 * 128:k8 * 128 + 128],
                            xnt[k8][0:128, so:so + 512],
                            start=(k8 == 0), stop=(k8 == 7)), inc="t")
                    for k8 in range(8):
                        prog.emit("tensor", lambda E, k8=k8, so=so: E.matmul(
                            psap(1, 128, 512),
                            wsl[0:128, 1024 + k8 * 128:1024 + k8 * 128 + 128],
                            xnt[k8][0:128, so:so + 512],
                            start=(k8 == 0), stop=(k8 == 7)), inc="t")
                    tmm = prog.val("t")
                    sv = prog.emit("scalar", lambda E: E.activation(
                        sgf[:, :], psap(0, 128, 512), ACT.Silu),
                        wait=[("t", tmm), ("v", prog.val("v"))], inc="s")
                    bank_busy["g"] = sv
                    vv = prog.emit("vector", lambda E, kt=kt, so=so: E.tensor_mul(
                        View(ab, hbuf.off + kt * 1024, 128, 1024)[0:128, so:so + 512],
                        sgf[:, :], psap(1, 128, 512)),
                        wait=[("s", sv), ("t", h_busy)], inc="v")
                    bank_busy["u"] = vv
                wsl_busy = prog.val("t")
            hv = prog.val("v")
            for sub in range(2):
                so = sub * 512
                prog.wait("tensor", "v", bank_busy.get("dwb", 0))
                prog.wait("tensor", "v", bank_busy[2])
                prog.wait("tensor", "v", dwr_done)
                tv = prog.emit("tensor", lambda E, e=e, half=half, so=so: E.matmul(
                    psap(2, 128, 512), onesrow_b[0:1, :],
                    bass.AP(ab.t, dwall.off + e * 2048 + half * 1024 + so,
                            [[ab.cols, 1], [1, 512]]),
                    start=True, stop=True), wait=[("v", tp_v)], inc="t")
                vv = prog.emit("vector", lambda E: E.tensor_copy(
                    dwb[:, :], psap(2, 128, 512)), wait=[("t", tv)], inc="v")
                bank_busy["dwb"] = vv
                prog.wait("tensor", "v", vv)
                for mo in range(8):
                    prog.wait("tensor", "v", bank_busy[mo])
                for kt in range(16):
                    lvd = gdma(wsl[0:128, 2048:3072],
                               bass.AP(di["Wd"], e * FF * 1024 + kt * 128 * 1024,
                                       [[1024, 128], [0, 1], [1, 1024]]),
                               wait=[("t", st.get("_wd_busy", 0))], sem="w1")
                    prog.wait("tensor", "w1", lvd)
                    for mo in range(8):
                        prog.emit("tensor", lambda E, kt=kt, mo=mo, so=so: E.matmul(
                            psap(mo, 128, 512),
                            wsl[0:128, 2048 + mo * 128:2048 + mo * 128 + 128],
                            View(ab, hbuf.off + kt * 1024, 128, 1024)[0:128,
                                                                      so:so + 512],
                            start=(kt == 0), stop=(kt == 15)),
                            wait=[("v", hv)] if kt == 0 and mo == 0 else (), inc="t")
                    st["_wd_busy"] = prog.val("t")
                for mo in range(8):
                    tmm = prog.val("t")
                    vv = prog.emit("vector", lambda E, mo=mo: E.tensor_mul(
                        scf[:, :], psap(mo, 128, 512), dwb[:, :]),
                        wait=[("t", tmm)], inc="v")
                    vv = prog.emit("vector", lambda E, mo=mo, hc0=hc0, so=so:
                                   E.tensor_add(
                        xres_t(mo)[0:128, hc0 + so:hc0 + so + 512],
                        xres_t(mo)[0:128, hc0 + so:hc0 + so + 512], scf[:, :]),
                        inc="v")
                    bank_busy[mo] = vv
            h_busy = prog.val("t")
    af.pop(); ab.pop()
    x4_done = prog.val("v")
    prog.wait("tensor", "v", x4_done)
    prog.wait("scalar", "v", x4_done)
    prog.wait("gpsimd", "v", x4_done)
    prog.wait("sync", "v", x4_done)


def emit_output(st):
    prog, af = st["prog"], st["af"]
    psap = st["psap"]
    xres_t = st["xres_t"]
    ident = st["ident_f"]
    af.push()
    otile = valloc(af, 128, 1024)
    copies = []
    for tt in range(16):
        c0 = H + tt * 128
        for k in range(8):
            pv = k % 4
            guard = copies[-4] if len(copies) >= 4 else 0
            tv = prog.emit("tensor", lambda E, k=k, c0=c0, pv=pv: E.transpose(
                psap(pv, 128, 128), xres_t(k)[:, c0:c0 + 128], ident[:, :]),
                wait=[("v", guard)], inc="t")
            vv = prog.emit("vector", lambda E, k=k, pv=pv: E.tensor_copy(
                otile[0:128, k * 128:(k + 1) * 128], psap(pv, 128, 128)),
                wait=[("t", tv)], inc="v")
            copies.append(vv)
        dst = bass.AP(st["out"], tt * 128 * D, [[D, 128], [1, D]])
        prog.emit("sync", lambda E, d=dst: E.dma_start(out=d, in_=otile[:, :]),
                  wait=[("v", copies[-1])], inc="st", inc_n=16)
        prog.wait("vector", "st", prog.val("st"))
    af.pop()


def kernel(**inputs):
    shared, per_core = host_prepare(inputs)
    nc = build_program(nblocks=4)
    in_maps = []
    for c in range(NCORES):
        m = dict(shared)
        m.update(per_core[c])
        in_maps.append(m)
    res = run_bass_kernel_spmd(nc, in_maps, core_ids=list(range(NCORES)))
    outs = [np.asarray(res.results[c]["out"]) for c in range(NCORES)]
    return np.concatenate(outs, axis=0).reshape(1, T, D)


# version 02
# speedup vs baseline: 1.1771x; 1.1771x over previous
import contextlib

import numpy as np

import concourse.bass as bass
import concourse.mybir as mybir
from concourse.bass_utils import run_bass_kernel_spmd

F32 = mybir.dt.float32
BF16 = mybir.dt.bfloat16
ALU = mybir.AluOpType
ACT = mybir.ActivationFunctionType
AX = mybir.AxisListType

NCORES = 8
D = 1024
T = 16384
TC = 2048
H = 134
EXT = TC + H
INNER = 2048
NH = 16
DH = 64
W = 128
SEG = 16
NSEG = T // SEG
NSEG_C = TC // SEG
KV0 = 6
KVLEN = EXT - KV0
E = 8
FF = 2048

NT = [512, 512, 512, 512, EXT - 2048]
NTO = [0, 512, 1024, 1536, 2048]
NN = len(NT)


class Prog:
    ENGINES = ("tensor", "vector", "scalar", "gpsimd", "sync")

    def __init__(self, nc, es):
        self.nc = nc
        self.es = es
        self.q = {e: [] for e in self.ENGINES}
        self.sems = {}
        self.cnt = {}
        self.scope = None

    def sem(self, name):
        if name not in self.sems:
            self.sems[name] = self.es.enter_context(self.nc.semaphore(name))
            self.cnt[name] = 0
        return self.sems[name]

    def val(self, name):
        self.sem(name)
        return self.cnt[name]

    AUTOSEM = {"tensor": "t", "vector": "v", "scalar": "s"}

    def emit(self, eng, fn, wait=(), inc=None, inc_n=1):
        if inc is not None and inc_n == 16:
            if not hasattr(self, "_fence"):
                self._fence = {}
            cur = self.val(inc)
            if cur > 0 and self._fence.get((eng, inc)) != cur:
                h0 = self.sem(inc)
                self.q[eng].append((self.scope,
                                    lambda E, h=h0, v=cur: E.wait_ge(h, v)))
                self._fence[(eng, inc)] = cur
        w = [(self.sem(s), v) for (s, v) in wait if v > 0]
        if eng in self.AUTOSEM:
            own = self.AUTOSEM[eng]
            prev = self.val(own)
            if prev > 0:
                w.append((self.sem(own), prev))
            assert inc is None or inc == own, (eng, inc)
            inc, inc_n = own, 1
        h = None
        ret = None
        if inc is not None:
            h = self.sem(inc)
            self.cnt[inc] += inc_n
            ret = self.cnt[inc]

        def step(E, fn=fn, w=w, h=h, n=inc_n):
            for (sh, v) in w:
                E.wait_ge(sh, v)
            inst = fn(E)
            if h is not None:
                inst.then_inc(h, n)

        self.q[eng].append((self.scope, step))
        return ret

    def wait(self, eng, semname, value):
        if value <= 0:
            return
        h = self.sem(semname)
        self.q[eng].append((self.scope, lambda E, h=h, v=value: E.wait_ge(h, v)))

    def dma(self, eng, sem, fn, wait=()):
        return self.emit(eng, fn, wait=wait, inc=sem, inc_n=16)

    def barrier(self):
        h = self.sem("bar")
        for e in self.ENGINES:
            self.cnt["bar"] += 1
            self.q[e].append((self.scope, lambda E, h=h: E.nop().then_inc(h, 1)))
        tot = self.cnt["bar"]
        for e in self.ENGINES:
            self.wait(e, "bar", tot)

    def finalize(self):
        block = self.es.enter_context(self.nc.Block())
        qs = self.q
        nc = self.nc

        def replay(E, items):
            cur, sid = None, None
            for (sc, s) in items:
                if sc != cur:
                    if cur is not None:
                        nc.leave_named_scope(cur, sid, False)
                    cur, sid = sc, None
                    if cur is not None:
                        sid, _ = nc.enter_named_scope(cur, False)
                s(E)
            if cur is not None:
                nc.leave_named_scope(cur, sid, False)

        @block.tensor
        def _(E):
            replay(E, qs["tensor"])

        @block.vector
        def _(E):
            replay(E, qs["vector"])

        @block.scalar
        def _(E):
            replay(E, qs["scalar"])

        @block.gpsimd
        def _(E):
            replay(E, qs["gpsimd"])

        @block.sync
        def _(E):
            replay(E, qs["sync"])


class Arena:
    def __init__(self, nc, name, cols, dtype):
        self.t = nc.alloc_sbuf_tensor(name, [128, cols], dtype)
        self.cols = cols
        self.off = 0
        self.marks = []
        self.peak = 0

    def alloc(self, n, align=16):
        o = (self.off + align - 1) // align * align
        assert o + n <= self.cols, f"arena {self.t.name}: {o}+{n}>{self.cols}"
        self.off = o + n
        self.peak = max(self.peak, self.off)
        return o

    def push(self):
        self.marks.append(self.off)

    def pop(self):
        self.off = self.marks.pop()


class View:
    def __init__(self, arena, off, rows, cols):
        self.a = arena
        self.off = off
        self.rows = rows
        self.cols = cols

    def __getitem__(self, idx):
        rs, cs = idx
        if isinstance(rs, int):
            rs = slice(rs, rs + 1)
        if isinstance(cs, int):
            cs = slice(cs, cs + 1)
        r0 = rs.start or 0
        r1 = self.rows if rs.stop is None else rs.stop
        c0 = cs.start or 0
        c1 = self.cols if cs.stop is None else cs.stop
        return bass.AP(self.a.t, self.off + c0 + r0 * self.a.cols,
                       [[self.a.cols, r1 - r0], [1, c1 - c0]])

    def ap3(self, r0, nrow, dims, c0=0):
        return bass.AP(self.a.t, self.off + c0 + r0 * self.a.cols,
                       [[self.a.cols, nrow]] + [list(d) for d in dims])


def valloc(arena, rows, cols, align=16):
    return View(arena, arena.alloc(cols, align), rows, cols)


def host_prepare(inp):
    x = np.asarray(inp["L0_attn_x"]).reshape(T, D)
    sh = {}

    def T_(a):
        return np.ascontiguousarray(np.asarray(a).T)

    sh["Win"] = np.ascontiguousarray(inp["L0_attn_W_in"])
    sh["Wa"] = np.ascontiguousarray(inp["L0_attn_Wa"])
    sh["Wb"] = np.ascontiguousarray(inp["L0_attn_Wb"])
    sh["Wc"] = np.ascontiguousarray(inp["L0_attn_Wc"])
    sh["Wo"] = np.ascontiguousarray(inp["L0_attn_Wo"])
    sh["Wr"] = T_(inp["L0_ffn_W_r"])
    sh["Wk"] = T_(inp["L0_ffn_W_k"])
    sh["Wv"] = T_(inp["L0_ffn_W_v"])
    sh["Wq"] = np.ascontiguousarray(inp["L1_attn_Wq"])
    sh["Wkv"] = np.ascontiguousarray(inp["L1_attn_Wkv"])
    sh["Wout"] = np.ascontiguousarray(inp["L1_attn_Wout"])
    sh["Wproj"] = np.ascontiguousarray(np.asarray(inp["L1_attn_Wproj"]).reshape(DH, 1))
    sh["Wrout"] = T_(inp["L1_ffn_router_w"])
    sh["Wg"] = np.ascontiguousarray(np.swapaxes(np.asarray(inp["L1_ffn_w_gate"]), 1, 2))
    sh["Wu"] = np.ascontiguousarray(np.swapaxes(np.asarray(inp["L1_ffn_w_up"]), 1, 2))
    sh["Wd"] = np.ascontiguousarray(np.swapaxes(np.asarray(inp["L1_ffn_w_down"]), 1, 2))

    def pp(a, n):
        return np.ascontiguousarray(np.asarray(a).reshape(n, 128).T)

    sh["b_in"] = pp(inp["L0_attn_b_in"], 32)
    sh["conv_w"] = np.ascontiguousarray(np.asarray(inp["L0_attn_conv_w"]).reshape(16, 128, 3))
    sh["conv_b"] = pp(inp["L0_attn_conv_b"], 16)
    sh["ba"] = pp(inp["L0_attn_ba"], 16)
    sh["bb"] = pp(inp["L0_attn_bb"], 16)
    sh["bc"] = pp(inp["L0_attn_bc"], 16)
    sh["bo"] = pp(inp["L0_attn_bo"], 8)
    sh["gamma"] = pp(inp["L0_attn_gamma"], 8)
    sh["beta"] = pp(inp["L0_attn_beta"], 8)
    sh["mu_r"] = pp(inp["L0_ffn_mu_r"], 8)
    sh["mu_k"] = pp(inp["L0_ffn_mu_k"], 8)
    sh["bout"] = pp(inp["L1_attn_bout"], 8)
    for nm, key in (("lng_l", "L1_attn_ln_local_g"), ("lnb_l", "L1_attn_ln_local_b"),
                    ("lng_g", "L1_attn_ln_global_g"), ("lnb_g", "L1_attn_ln_global_b")):
        v = np.asarray(inp[key]).reshape(DH)
        sh[nm] = np.ascontiguousarray(np.concatenate([v, v]).reshape(128, 1))

    sh["ident"] = np.ascontiguousarray(np.eye(128, dtype=np.float32))
    prot = np.zeros((128, 128), np.float32)
    for hh in range(2):
        for d0 in range(32):
            prot[hh * 64 + d0 + 32, hh * 64 + d0] = -1.0
            prot[hh * 64 + d0, hh * 64 + d0 + 32] = 1.0
    sh["prot"] = prot
    sh["causal_own"] = (np.arange(W)[:, None] <= np.arange(W)[None, :]).astype(np.float32)
    hs = np.zeros((128, 2), np.float32)
    hs[:64, 0] = 1.0
    hs[64:, 1] = 1.0
    sh["halfsel"] = hs
    sh["sel2T"] = np.ascontiguousarray(hs.T)
    wp2 = np.zeros((128, 2), np.float32)
    wp = np.asarray(inp["L1_attn_Wproj"]).reshape(DH)
    wp2[:64, 0] = wp
    wp2[64:, 1] = wp
    sh["wproj2"] = wp2

    per = []
    for c in range(NCORES):
        d = {}
        t0 = TC * c
        lo = t0 - H
        sl = np.zeros((EXT, D), np.float32)
        a0 = max(0, lo)
        sl[a0 - lo:, :] = x[a0:t0 + TC, :]
        d["x0"] = np.ascontiguousarray(sl.T)
        d["prev_mask"] = np.full((128, 1), 0.0 if c == 0 else 1.0, np.float32)
        m = (np.arange(8) < c).astype(np.float32)
        d["foldm"] = np.ascontiguousarray(np.broadcast_to(m, (128, 8)))
        d["foldm1"] = np.ascontiguousarray(np.broadcast_to(1.0 - m, (128, 8)))
        tau = np.arange(t0 - H, t0 + TC)
        tau = np.maximum(tau, 0).astype(np.float64)
        invf = 1.0 / 10000.0 ** (np.arange(0, DH, 2, dtype=np.float64) / DH)
        ang = tau[None, :] * invf[:, None]
        cos = np.cos(ang).astype(np.float32)
        sin = np.sin(ang).astype(np.float32)
        cos64 = np.concatenate([cos, cos], 0)
        sinS = np.concatenate([sin, sin], 0)
        d["cos_t"] = np.ascontiguousarray(np.concatenate([cos64, cos64], 0))
        d["sin_t"] = np.ascontiguousarray(np.concatenate([sinS, sinS], 0))
        Mfull = np.zeros((4, 128, 8 * 512), np.float32)
        for j in range(4):
            qv = t0 + 512 * j + np.arange(512)
            for stile in range(8):
                segg = stile * 128 + np.arange(128)
                Mfull[j][:, stile * 512:(stile + 1) * 512] = (
                    qv[None, :] >= 16 * (segg[:, None] + 1))
        d["gmask"] = np.ascontiguousarray(Mfull)
        per.append(d)
    return sh, per


def build_program(nblocks=4, dbg=None, dbg_mode=None):
    nc = bass.Bass("TRN2", target_bir_lowering=False)
    es = contextlib.ExitStack()
    di = {}

    def IN(name, shape):
        di[name] = nc.dram_tensor(name, list(shape), F32, kind="ExternalInput")

    IN("x0", (D, EXT))
    IN("Win", (D, 2 * INNER)); IN("Wa", (INNER, INNER)); IN("Wb", (INNER, INNER))
    IN("Wc", (INNER, INNER)); IN("Wo", (INNER, D))
    IN("Wr", (D, D)); IN("Wk", (D, D)); IN("Wv", (D, D))
    IN("Wq", (D, D)); IN("Wkv", (D, D)); IN("Wout", (D, D)); IN("Wproj", (DH, 1))
    IN("Wrout", (D, E)); IN("Wg", (E, D, FF)); IN("Wu", (E, D, FF)); IN("Wd", (E, FF, D))
    IN("b_in", (128, 32)); IN("conv_w", (16, 128, 3)); IN("conv_b", (128, 16))
    IN("ba", (128, 16)); IN("bb", (128, 16)); IN("bc", (128, 16)); IN("bo", (128, 8))
    IN("gamma", (128, 8)); IN("beta", (128, 8))
    IN("mu_r", (128, 8)); IN("mu_k", (128, 8)); IN("bout", (128, 8))
    IN("lng_l", (128, 1)); IN("lnb_l", (128, 1)); IN("lng_g", (128, 1)); IN("lnb_g", (128, 1))
    IN("ident", (128, 128)); IN("prot", (128, 128)); IN("causal_own", (W, W))
    IN("prev_mask", (128, 1)); IN("foldm", (128, 8)); IN("foldm1", (128, 8))
    IN("cos_t", (128, EXT)); IN("sin_t", (128, EXT))
    IN("halfsel", (128, 2)); IN("sel2T", (2, 128)); IN("wproj2", (128, 2))
    IN("gmask", (4, 128, 8 * 512))

    out = nc.dram_tensor("out", [TC, D], F32, kind="ExternalOutput")
    dbg_out = None
    if dbg is not None:
        dbg_out = nc.dram_tensor("dbgo", [D, EXT], F32, kind="ExternalOutput")

    y_spill = nc.dram_tensor("y_spill", [16, 128, EXT], BF16)
    out_spill = nc.dram_tensor("out_spill", [8, 128, 2048], BF16)
    cp_spill = nc.dram_tensor("cp_spill", [16, 128, EXT], BF16)
    xn_spill = nc.dram_tensor("xn_spill", [8, 128, EXT], BF16)
    ag1_in = nc.dram_tensor("ag1_in", [128, 32], F32)
    ag1_out = nc.dram_tensor("ag1_out", [128 * 8, 32], F32)
    ag2_in = nc.dram_tensor("ag2_in", [NSEG_C, NH * DH], BF16)
    ag2_out = nc.dram_tensor("ag2_out", [NSEG, NH * DH], BF16)

    prog = Prog(nc, es)
    af = Arena(nc, "af", 26800, F32)
    ab = Arena(nc, "ab", 50800, BF16)

    xres = valloc(af, 128, 8 * EXT)

    def xres_t(k):
        return View(af, xres.off + k * EXT, 128, EXT)

    cst = valloc(af, 128, 600)
    co = {}
    _cc = [cst.off]

    def calloc(name, n):
        co[name] = _cc[0]
        _cc[0] += n

    for nm, n in (("b_in", 32), ("b_in_h", 32), ("conv_w", 48), ("conv_b", 16),
                  ("ba", 16), ("ba_h", 16), ("bb", 16), ("bb_h", 16), ("bc", 16),
                  ("bo", 8), ("gamma", 8), ("beta", 8), ("mu_r", 8), ("mu_r1", 8),
                  ("mu_k", 8), ("mu_k1", 8), ("bout", 8), ("lng_l", 1), ("lnb_l", 1),
                  ("lng_g", 1), ("lnb_g", 1), ("prev", 1), ("foldm", 8), ("foldm1", 8),
                  ("Sin", 16), ("exs", 16), ("exp", 16), ("onesrow", 128),
                  ("onescol", 1), ("ptail", 32), ("zero2", 2), ("wproj", 1), ("eps", 1),
                  ("maxk", 16), ("spare", 16)):
        calloc(nm, n)
    assert _cc[0] - cst.off <= 600, _cc[0] - cst.off

    def cv(name, j=0, rows=128, base=0, ncol=1):
        return bass.AP(af.t, co[name] + j + base * af.cols,
                       [[af.cols, rows], [1, ncol]])

    onescol_b = valloc(ab, 128, 16)
    ident_bf = valloc(ab, 128, 128)
    ident_f = valloc(af, 128, 128)
    prot_bf = valloc(ab, 128, 128)
    causal_bf = valloc(ab, W, W)

    ps = [es.enter_context(nc.psum_tensor("p%d" % i, [128, 512], F32)) for i in range(8)]

    def psap(i, rows=128, cols=512, r0=0, c0=0):
        return bass.AP(ps[i], c0 + r0 * 512, [[512, rows], [1, cols]])

    def gdma(dst, src, sem="wg", wait=()):
        return prog.dma("gpsimd", sem,
                        lambda E, d=dst, s=src: E.dma_start(out=d, in_=s), wait=wait)

    for nm, rows in (("b_in", 32), ("conv_b", 16), ("ba", 16), ("bb", 16), ("bc", 16),
                     ("bo", 8), ("gamma", 8), ("beta", 8), ("mu_r", 8), ("mu_k", 8),
                     ("bout", 8), ("lng_l", 1), ("lnb_l", 1), ("lng_g", 1), ("lnb_g", 1)):
        gdma(cv(nm, ncol=rows), bass.AP(di[nm], 0, [[rows, 128], [1, rows]]))
    for t in range(16):
        gdma(cv("conv_w", 3 * t, ncol=3), bass.AP(di["conv_w"], t * 384, [[3, 128], [1, 3]]))
    gdma(cv("prev"), di["prev_mask"].ap())
    gdma(cv("foldm", ncol=8), di["foldm"].ap())
    gdma(cv("foldm1", ncol=8), di["foldm1"].ap())
    gdma(cv("wproj", rows=DH), bass.AP(di["Wproj"], 0, [[1, DH], [1, 1]]))
    gdma(ident_bf[:, :], di["ident"].ap())
    gdma(ident_f[:, :], di["ident"].ap())
    gdma(prot_bf[:, :], di["prot"].ap())
    gdma(causal_bf[:, :], di["causal_own"].ap())
    for k in range(8):
        gdma(xres_t(k)[:, :], bass.AP(di["x0"], k * 128 * EXT, [[EXT, 128], [1, EXT]]))
    ld_done = prog.val("wg")

    prog.wait("vector", "wg", ld_done)
    prog.emit("vector", lambda E: E.memset(cv("onesrow", rows=1, ncol=128), 1.0), inc="v")
    prog.emit("vector", lambda E: E.memset(cv("onescol", ncol=1), 1.0), inc="v")
    prog.emit("vector", lambda E: E.memset(onescol_b[:, 0:1], 1.0), inc="v")
    prog.emit("vector", lambda E: E.memset(cv("zero2", ncol=2), 0.0), inc="v")
    prog.emit("vector", lambda E: E.memset(cv("eps", ncol=1), 1e-5), inc="v")
    prog.emit("vector", lambda E: E.tensor_scalar(
        cv("b_in_h", ncol=32), cv("b_in", ncol=32), 0.5, None, ALU.mult), inc="v")
    prog.emit("vector", lambda E: E.tensor_scalar(
        cv("ba_h", ncol=16), cv("ba", ncol=16), 0.5, None, ALU.mult), inc="v")
    prog.emit("vector", lambda E: E.tensor_scalar(
        cv("bb_h", ncol=16), cv("bb", ncol=16), 0.5, None, ALU.mult), inc="v")
    prog.emit("vector", lambda E: E.tensor_scalar(
        cv("mu_r1", ncol=8), cv("mu_r", ncol=8), -1.0, 1.0, ALU.mult, ALU.add), inc="v")
    prog.emit("vector", lambda E: E.tensor_scalar(
        cv("mu_k1", ncol=8), cv("mu_k", ncol=8), -1.0, 1.0, ALU.mult, ALU.add), inc="v")
    cdone = prog.val("v")
    prog.wait("tensor", "v", cdone)
    prog.wait("scalar", "v", cdone)
    prog.wait("sync", "v", cdone)

    st = dict(dbg_mode=dbg_mode, nc=nc, prog=prog, af=af, ab=ab, ps=ps, psap=psap, di=di, xres_t=xres_t,
              cv=cv, co=co, gdma=gdma, onescol_b=onescol_b, ident_f=ident_f, ident_bf=ident_bf, prot_bf=prot_bf, causal_bf=causal_bf,
              y_spill=y_spill, cp_spill=cp_spill, xn_spill=xn_spill,
              out_spill=out_spill,
              ag1_in=ag1_in, ag1_out=ag1_out, ag2_in=ag2_in, ag2_out=ag2_out,
              out=out, dbg_out=dbg_out)

    prog.scope = "b0"
    emit_b0(st)
    if nblocks >= 2:
        prog.scope = "b1"
        emit_b1(st)
    if nblocks >= 3:
        prog.scope = "b2"
        emit_b2(st)
    if nblocks >= 4:
        prog.scope = "b3"
        emit_b3(st)
    prog.scope = "fin"

    prog.barrier()
    if dbg is not None and dbg_mode is None:
        for k in range(8):
            dst = bass.AP(dbg_out, k * 128 * EXT, [[EXT, 128], [1, EXT]])
            prog.emit("sync", lambda E, d=dst, s=xres_t(k)[:, :]:
                      E.dma_start(out=d, in_=s), inc="st", inc_n=16)
    emit_output(st)
    prog.wait("sync", "st", prog.val("st"))
    prog.finalize()
    es.close()
    return nc


def emit_pre_ln(st, src_t, spill, n_list=None, dst_t=None):
    prog, af, ab, psap, cv = st["prog"], st["af"], st["ab"], st["psap"], st["cv"]
    n_list = list(n_list if n_list is not None else range(NN))
    af.push(); ab.push()
    sq = valloc(af, 128, 512)
    srow = valloc(af, 1, 1536)
    xt = [valloc(ab, 128, 512) for _ in range(2)]
    xt_busy = [0, 0]
    for n in n_list:
        nt, o = NT[n], NTO[n]
        sq_free = prog.val("t")
        for k in range(8):
            sv = prog.emit("scalar", lambda E, k=k, o=o, nt=nt: E.activation(
                sq[0:128, 0:nt], src_t(k)[:, o:o + nt], ACT.Square),
                wait=[("t", sq_free)], inc="s")
            prog.emit("tensor", lambda E, k=k, o=o, nt=nt: E.matmul(
                psap(6, 1, nt), cv("onescol"), src_t(k)[:, o:o + nt],
                start=(k == 0), stop=(k == 7)), inc="t")
            sq_free = prog.emit("tensor", lambda E, k=k, nt=nt: E.matmul(
                psap(7, 1, nt), cv("onescol"), sq[0:128, 0:nt],
                start=(k == 0), stop=(k == 7)), wait=[("s", sv)], inc="t")
        m_row = srow[0:1, 0:nt]
        r_row = bass.AP(srow.a.t, srow.off + 512, [[srow.a.cols, 1], [1, nt]])
        tmp = bass.AP(srow.a.t, srow.off + 1024, [[srow.a.cols, 1], [1, nt]])
        prog.emit("vector", lambda E, m_row=m_row, nt=nt: E.tensor_scalar(
            m_row, psap(6, 1, nt), 1.0 / D, None, ALU.mult),
            wait=[("t", sq_free)], inc="v")
        prog.emit("vector", lambda E, m_row=m_row, tmp=tmp: E.tensor_mul(
            tmp, m_row, m_row), inc="v")
        vv = prog.emit("vector", lambda E, r_row=r_row, tmp=tmp, nt=nt:
                       E.scalar_tensor_tensor(
            r_row, psap(7, 1, nt), 1.0 / D, tmp, ALU.mult, ALU.subtract), inc="v")
        sv = prog.emit("scalar", lambda E, r_row=r_row, tmp=tmp: E.activation(
            tmp, r_row, ACT.Ln, bias=cv("eps", rows=1)), wait=[("v", vv)], inc="s")
        sv = prog.emit("scalar", lambda E, r_row=r_row, tmp=tmp: E.activation(
            r_row, tmp, ACT.Exp, scale=-0.5), inc="s")
        if st.get("dbg_mode") == "lnrows" and n == n_list[0]:
            prog.emit("sync", lambda E: E.dma_start(
                out=bass.AP(st["dbg_out"], 0, [[EXT, 1], [1, 1536]]),
                in_=srow[0:1, 0:1536]), wait=[("s", sv)], inc="st", inc_n=16)
        tb = prog.emit("tensor", lambda E, nt=nt, m_row=m_row: E.matmul(
            psap(4, 128, nt), cv("onesrow", rows=1, ncol=128), m_row,
            start=True, stop=True), wait=[("s", sv)], inc="t")
        tb = prog.emit("tensor", lambda E, nt=nt, r_row=r_row: E.matmul(
            psap(5, 128, nt), cv("onesrow", rows=1, ncol=128), r_row,
            start=True, stop=True), inc="t")
        for k in range(8):
            if dst_t is not None:
                vv = prog.emit("vector", lambda E, k=k, o=o, nt=nt: E.tensor_sub(
                    xt[0][0:128, 0:nt], src_t(k)[:, o:o + nt], psap(4, 128, nt)),
                    wait=[("t", tb)], inc="v")
                vv = prog.emit("vector", lambda E, k=k, o=o, nt=nt: E.tensor_mul(
                    dst_t(k)[0:128, o:o + nt], xt[0][0:128, 0:nt], psap(5, 128, nt)),
                    inc="v")
                continue
            b = xt[k % 2]
            vv = prog.emit("vector", lambda E, k=k, b=b, o=o, nt=nt: E.tensor_sub(
                b[0:128, 0:nt], src_t(k)[:, o:o + nt], psap(4, 128, nt)),
                wait=[("t", tb), ("sy%d" % (k % 2), xt_busy[k % 2])], inc="v")
            vv = prog.emit("vector", lambda E, b=b, nt=nt: E.tensor_mul(
                b[0:128, 0:nt], b[0:128, 0:nt], psap(5, 128, nt)), inc="v")
            xt_busy[k % 2] = prog.emit("sync", lambda E, k=k, b=b, o=o, nt=nt:
                                       E.dma_start(
                out=bass.AP(spill, k * 128 * EXT + o, [[EXT, 128], [1, nt]]),
                in_=b[0:128, 0:nt]), wait=[("v", vv)], inc="sy%d" % (k % 2), inc_n=16)
        prog.wait("tensor", "v", prog.val("v"))
    out_done = ((prog.val("sy0"), prog.val("sy1")) if dst_t is None
                else prog.val("v"))
    af.pop(); ab.pop()
    return out_done


def emit_b0(st):
    prog, af, ab = st["prog"], st["af"], st["ab"]
    psap, cv, co = st["psap"], st["cv"], st["co"]
    gdma = st["gdma"]
    xres_t, di = st["xres_t"], st["di"]
    xn_spill = st["xn_spill"]
    y_spill, cp_spill = st["y_spill"], st["cp_spill"]

    af.push(); ab.push()
    sq = valloc(af, 128, 512)
    srow = valloc(af, 1, 2560)
    zt = [valloc(af, 128, 512) for _ in range(2)]
    z_bf = [valloc(ab, 128, 512) for _ in range(8)]
    xnt = [valloc(ab, 128, 512) for _ in range(2)]
    xnt_busy = [0, 0]

    onescol_b = st["onescol_b"]

    def stat_pass(src_fn, mcol, rcol, nt, ones_sum=None):
        if ones_sum is None:
            ones_sum = cv("onescol")
        sq_free = prog.val("t")
        for k in range(8):
            a, wv = src_fn(k)
            sv = prog.emit("scalar", lambda E, a=a, nt=nt: E.activation(
                sq[0:128, 0:nt], a, ACT.Square),
                wait=list(wv) + [("t", sq_free)], inc="s")
            prog.emit("tensor", lambda E, a=a, k=k, nt=nt, ones_sum=ones_sum: E.matmul(
                psap(6, 1, nt), ones_sum, a, start=(k == 0), stop=(k == 7)),
                wait=wv, inc="t")
            sq_free = prog.emit("tensor", lambda E, k=k, nt=nt: E.matmul(
                psap(7, 1, nt), cv("onescol"), sq[0:128, 0:nt],
                start=(k == 0), stop=(k == 7)), wait=[("s", sv)], inc="t")
        m_row = srow[0:1, mcol:mcol + nt]
        r_row = srow[0:1, rcol:rcol + nt]
        tmp = srow[0:1, 2048:2048 + nt]
        prog.emit("vector", lambda E, m_row=m_row, nt=nt: E.tensor_scalar(
            m_row, psap(6, 1, nt), 1.0 / D, None, ALU.mult),
            wait=[("t", sq_free)], inc="v")
        prog.emit("vector", lambda E, m_row=m_row, tmp=tmp: E.tensor_mul(
            tmp, m_row, m_row), inc="v")
        vv = prog.emit("vector", lambda E, r_row=r_row, tmp=tmp, nt=nt:
                       E.scalar_tensor_tensor(
            r_row, psap(7, 1, nt), 1.0 / D, tmp, ALU.mult, ALU.subtract), inc="v")
        sv = prog.emit("scalar", lambda E, r_row=r_row, tmp=tmp: E.activation(
            tmp, r_row, ACT.Ln, bias=cv("eps", rows=1)), wait=[("v", vv)], inc="s")
        sv = prog.emit("scalar", lambda E, r_row=r_row, tmp=tmp: E.activation(
            r_row, tmp, ACT.Exp, scale=-0.5), inc="s")
        return m_row, r_row, [("s", sv)]

    for n in range(NN):
        nt, o = NT[n], NTO[n]
        m1, r1, w1 = stat_pass(lambda k: (xres_t(k)[:, o:o + nt], ()), 0, 512, nt)
        tb = prog.emit("tensor", lambda E, m1=m1, nt=nt: E.matmul(
            psap(4, 128, nt), cv("onesrow", rows=1, ncol=128), m1,
            start=True, stop=True), wait=w1, inc="t")
        tb = prog.emit("tensor", lambda E, r1=r1, nt=nt: E.matmul(
            psap(5, 128, nt), cv("onesrow", rows=1, ncol=128), r1,
            start=True, stop=True), inc="t")
        zw = []
        for k in range(8):
            b = zt[k % 2]
            prog.emit("vector", lambda E, k=k, b=b, o=o, nt=nt: E.tensor_sub(
                b[0:128, 0:nt], xres_t(k)[:, o:o + nt], psap(4, 128, nt)),
                wait=[("t", tb)], inc="v")
            prog.emit("vector", lambda E, b=b, nt=nt: E.tensor_mul(
                b[0:128, 0:nt], b[0:128, 0:nt], psap(5, 128, nt)), inc="v")
            prog.emit("vector", lambda E, k=k, b=b, o=o, nt=nt: E.tensor_add(
                xres_t(k)[:, o:o + nt], xres_t(k)[:, o:o + nt], b[0:128, 0:nt]),
                inc="v")
            vv = prog.emit("vector", lambda E, k=k, b=b, nt=nt: E.tensor_copy(
                z_bf[k][0:128, 0:nt], b[0:128, 0:nt]), inc="v")
            zw.append(vv)
        m2, r2, w2 = stat_pass(
            lambda k: (z_bf[k][0:128, 0:nt], [("v", zw[k])]), 1024, 1536, nt,
            ones_sum=onescol_b[:, 0:1])
        tb2 = prog.emit("tensor", lambda E, m2=m2, nt=nt: E.matmul(
            psap(4, 128, nt), cv("onesrow", rows=1, ncol=128), m2,
            start=True, stop=True), wait=w2, inc="t")
        tb2 = prog.emit("tensor", lambda E, r2=r2, nt=nt: E.matmul(
            psap(5, 128, nt), cv("onesrow", rows=1, ncol=128), r2,
            start=True, stop=True), inc="t")
        for k in range(8):
            b = zt[k % 2]
            prog.emit("vector", lambda E, k=k, b=b, nt=nt: E.tensor_sub(
                b[0:128, 0:nt], z_bf[k][0:128, 0:nt], psap(4, 128, nt)),
                wait=[("t", tb2)], inc="v")
            prog.emit("vector", lambda E, b=b, nt=nt: E.tensor_mul(
                b[0:128, 0:nt], b[0:128, 0:nt], psap(5, 128, nt)), inc="v")
            xb = xnt[k % 2]
            vv = prog.emit("vector", lambda E, k=k, b=b, xb=xb, nt=nt: E.tensor_scalar(
                xb[0:128, 0:nt], b[0:128, 0:nt], cv("gamma", k), cv("beta", k),
                ALU.mult, ALU.add), wait=[("sy%d" % (k % 2), xnt_busy[k % 2])], inc="v")
            xnt_busy[k % 2] = prog.emit("sync", lambda E, k=k, xb=xb, o=o, nt=nt:
                                        E.dma_start(
                out=bass.AP(xn_spill, k * 128 * EXT + o, [[EXT, 128], [1, nt]]),
                in_=xb[0:128, 0:nt]), wait=[("v", vv)], inc="sy%d" % (k % 2), inc_n=16)
        prog.wait("tensor", "v", prog.val("v"))
    xn_done = (prog.val("sy0"), prog.val("sy1"))
    af.pop(); ab.pop()

    st["prog"].scope = "b0.p2"
    af.push()
    ab.push()
    proj = [valloc(af, 128, 516) for _ in range(2)]
    sg = [valloc(af, 128, 512) for _ in range(2)]
    t1 = valloc(af, 128, 512)
    t3 = valloc(af, 128, 512)
    u_bf = valloc(ab, 128, 16 * EXT)

    def u_t(m):
        return View(ab, u_bf.off + m * EXT, 128, EXT)

    st["u_t"] = u_t
    ab.push()
    xn_nt = [valloc(ab, 128, 512) for _ in range(8)]
    wbuf = [valloc(ab, 128, 2048) for _ in range(2)]
    wbuf_busy = [0, 0]
    psA_busy = [0, 0]
    pj_busy = [0, 0]
    prog.wait("gpsimd", "sy0", xn_done[0])
    prog.wait("gpsimd", "sy1", xn_done[1])
    for n in range(NN):
        nt, o = NTO[n] and (NT[n], NTO[n]) or (NT[n], NTO[n])
        nt, o = NT[n], NTO[n]
        xl = []
        for k in range(8):
            lv = gdma(xn_nt[k][0:128, 0:nt],
                      bass.AP(xn_spill, k * 128 * EXT + o, [[EXT, 128], [1, nt]]),
                      wait=[("t", st.get("_xn_rd_t", 0))] if k == 0 else (),
                      sem="wx")
            xl.append(lv)
        for mc in range(16):
            par = mc % 2
            wb = wbuf[par]
            lv = gdma(wb[0:128, 0:1024],
                      bass.AP(di["Win"], mc * 128,
                              [[4096, 128], [128 * 4096, 8], [1, 128]]),
                      wait=[("t", wbuf_busy[par])], sem="w%d" % par)
            lv = gdma(wb[0:128, 1024:2048],
                      bass.AP(di["Win"], 2048 + mc * 128,
                              [[4096, 128], [128 * 4096, 8], [1, 128]]),
                      sem="w%d" % par)
            prog.wait("tensor", "w%d" % par, lv)
            prog.wait("tensor", "wx", xl[-1])
            prog.wait("tensor", "s", psA_busy[par])
            for kt in range(8):
                prog.emit("tensor", lambda E, kt=kt, par=par, wb=wb, nt=nt: E.matmul(
                    psap(par, 128, nt), wb[0:128, kt * 128:kt * 128 + 128],
                    xn_nt[kt][0:128, 0:nt], start=(kt == 0), stop=(kt == 7)), inc="t")
            for kt in range(8):
                prog.emit("tensor", lambda E, kt=kt, par=par, wb=wb, nt=nt: E.matmul(
                    psap(2 + par, 128, nt),
                    wb[0:128, 1024 + kt * 128:1024 + kt * 128 + 128],
                    xn_nt[kt][0:128, 0:nt], start=(kt == 0), stop=(kt == 7)), inc="t")
            tmm = prog.val("t")
            wbuf_busy[par] = tmm
            st["_xn_rd_t"] = tmm
            pj, sgb = proj[par], sg[par]
            if n > 0:
                prog.emit("vector", lambda E, pj=pj, mc=mc: E.tensor_copy(
                    pj[0:128, 0:2], cv("ptail", 2 * mc, ncol=2)),
                    wait=[("v", pj_busy[par])], inc="v")
            else:
                prog.emit("vector", lambda E, pj=pj: E.tensor_copy(
                    pj[0:128, 0:2], cv("zero2", ncol=2)),
                    wait=[("v", pj_busy[par])], inc="v")
            halo_v = prog.val("v")
            sv = prog.emit("scalar", lambda E, pj=pj, mc=mc, par=par, nt=nt:
                           E.activation(pj[0:128, 2:2 + nt], psap(par, 128, nt),
                                        ACT.Identity, bias=cv("b_in", mc)),
                           wait=[("t", tmm), ("v", halo_v)], inc="s")
            sv = prog.emit("scalar", lambda E, sgb=sgb, mc=mc, par=par, nt=nt:
                           E.activation(sgb[0:128, 0:nt], psap(2 + par, 128, nt),
                                        ACT.Tanh, scale=0.5, bias=cv("b_in_h", 16 + mc)),
                           inc="s")
            psA_busy[par] = sv
            vw = [("s", sv)]
            if n == 0:
                vv = prog.emit("vector", lambda E, pj=pj: E.tensor_scalar(
                    pj[0:128, 2:136], pj[0:128, 2:136], cv("prev"), None, ALU.mult),
                    wait=vw, inc="v")
                vw = [("v", vv)]
            vv = prog.emit("vector", lambda E, pj=pj, mc=mc, nt=nt: E.tensor_copy(
                cv("ptail", 2 * mc, ncol=2), pj[0:128, nt:nt + 2]), wait=vw, inc="v")
            vv = prog.emit("vector", lambda E, pj=pj, mc=mc, nt=nt: E.tensor_scalar(
                t1[0:128, 0:nt], pj[0:128, 1:1 + nt], cv("conv_w", 3 * mc + 1),
                None, ALU.mult), inc="v")
            vv = prog.emit("vector", lambda E, pj=pj, mc=mc, nt=nt:
                           E.scalar_tensor_tensor(
                t1[0:128, 0:nt], pj[0:128, 0:nt], cv("conv_w", 3 * mc + 0),
                t1[0:128, 0:nt], ALU.mult, ALU.add), inc="v")
            vv = prog.emit("vector", lambda E, pj=pj, mc=mc, nt=nt:
                           E.scalar_tensor_tensor(
                t3[0:128, 0:nt], pj[0:128, 2:2 + nt], cv("conv_w", 3 * mc + 2),
                t1[0:128, 0:nt], ALU.mult, ALU.add), inc="v")
            sv = prog.emit("scalar", lambda E, mc=mc, nt=nt: E.activation(
                t1[0:128, 0:nt], t3[0:128, 0:nt], ACT.Silu, bias=cv("conv_b", mc)),
                wait=[("v", vv)], inc="s")
            vv = prog.emit("vector", lambda E, sgb=sgb, nt=nt: E.tensor_scalar(
                sgb[0:128, 0:nt], sgb[0:128, 0:nt], 0.5, 0.5, ALU.mult, ALU.add),
                wait=[("s", sv)], inc="v")
            vv = prog.emit("vector", lambda E, mc=mc, o=o, nt=nt, sgb=sgb: E.tensor_mul(
                u_t(mc)[0:128, o:o + nt], t1[0:128, 0:nt], sgb[0:128, 0:nt]), inc="v")
            if n == 0:
                prog.emit("vector", lambda E, mc=mc: E.memset(
                    u_t(mc)[0:128, 0:2], 0.0), inc="v")
                vv = prog.emit("vector", lambda E, mc=mc: E.tensor_scalar(
                    u_t(mc)[0:128, 2:134], u_t(mc)[0:128, 2:134], cv("prev"),
                    None, ALU.mult), inc="v")
            pj_busy[par] = vv
    u_done = prog.val("v")
    p2_end_t = prog.val("t")
    af.pop()
    ab.pop()

    st["prog"].scope = "b0.p3"
    af.push()
    ab.push()
    a_row = valloc(af, 128, EXT)
    w_row = valloc(af, 128, EXT)
    s_row = valloc(af, 128, EXT)
    ct = valloc(af, 128, 512)
    tg = valloc(af, 128, 512)
    zrow = valloc(ab, 128, EXT)
    wabc = valloc(ab, 128, 3 * 2048)
    ytile = valloc(ab, 128, EXT)
    cptile = valloc(ab, 128, EXT)
    prog.emit("vector", lambda E: E.memset(zrow[:, :], 0.0),
              wait=[("t", p2_end_t)], inc="v")
    prog.wait("tensor", "v", u_done)

    spill_done = 0
    wabc_busy = p2_end_t
    yv_done = 0
    ev_s = 0
    ev_v = 0
    for gi in range(16):
        for wi, wname in enumerate(("Wa", "Wb", "Wc")):
            lv = gdma(wabc[0:128, wi * 2048:(wi + 1) * 2048],
                      bass.AP(di[wname], gi * 128,
                              [[2048, 128], [128 * 2048, 16], [1, 128]]),
                      wait=[("t", wabc_busy)] if wi == 0 else (), sem="w0")
        prog.wait("tensor", "w0", lv)
        for which in range(2):
            for n in range(NN):
                nt, o = NT[n], NTO[n]
                par = n % 2
                prog.wait("tensor", "s", ev_s)
                for kt in range(16):
                    prog.emit("tensor", lambda E, kt=kt, par=par, o=o, nt=nt,
                              which=which: E.matmul(
                        psap(par, 128, nt),
                        wabc[0:128, which * 2048 + kt * 128:
                             which * 2048 + kt * 128 + 128],
                        u_t(kt)[0:128, o:o + nt], start=(kt == 0), stop=(kt == 15)),
                        inc="t")
                tmm = prog.val("t")
                bh = "ba_h" if which == 0 else "bb_h"
                sv = prog.emit("scalar", lambda E, par=par, gi=gi, nt=nt, bh=bh:
                               E.activation(tg[0:128, 0:nt], psap(par, 128, nt),
                                            ACT.Tanh, scale=0.5, bias=cv(bh, gi)),
                               wait=[("t", tmm), ("v", prog.val("v"))], inc="s")
                ev_s = sv
                if which == 0:
                    vv = prog.emit("vector", lambda E, o=o, nt=nt: E.tensor_scalar(
                        a_row[0:128, o:o + nt], tg[0:128, 0:nt], 0.5, 0.5,
                        ALU.mult, ALU.add), wait=[("s", sv)], inc="v")
                else:
                    vv = prog.emit("vector", lambda E, o=o, nt=nt: E.tensor_scalar(
                        tg[0:128, 0:nt], tg[0:128, 0:nt], 0.5, 0.5,
                        ALU.mult, ALU.add), wait=[("s", sv)], inc="v")
                    vv = prog.emit("vector", lambda E, o=o, nt=nt, gi=gi: E.tensor_mul(
                        w_row[0:128, o:o + nt], tg[0:128, 0:nt],
                        u_t(gi)[0:128, o:o + nt]), inc="v")
                ev_v = vv
        vv = prog.emit("vector", lambda E: E.tensor_tensor_scan(
            s_row[:, :], a_row[:, :], w_row[:, :], 0.0, ALU.mult, ALU.add), inc="v")
        vv = prog.emit("vector", lambda E: E.tensor_tensor_scan(
            w_row[:, :], a_row[:, :], zrow[:, :], 1.0, ALU.mult, ALU.add), inc="v")
        vv = prog.emit("vector", lambda E, gi=gi: E.tensor_copy(
            cv("exs", gi), s_row[0:128, 2049:2050]), inc="v")
        vv = prog.emit("vector", lambda E, gi=gi: E.tensor_copy(
            cv("exp", gi), w_row[0:128, 2049:2050]), inc="v")
        scan_done = vv
        ev_s2 = 0
        for n in range(NN):
            nt, o = NT[n], NTO[n]
            par = 4 + n % 2
            prog.wait("tensor", "s", ev_s2)
            for kt in range(16):
                prog.emit("tensor", lambda E, kt=kt, par=par, o=o, nt=nt: E.matmul(
                    psap(par, 128, nt),
                    wabc[0:128, 2 * 2048 + kt * 128:2 * 2048 + kt * 128 + 128],
                    u_t(kt)[0:128, o:o + nt], start=(kt == 0), stop=(kt == 15)),
                    inc="t")
            tmm = prog.val("t")
            sv = prog.emit("scalar", lambda E, par=par, gi=gi, nt=nt: E.activation(
                ct[0:128, 0:nt], psap(par, 128, nt), ACT.Tanh, bias=cv("bc", gi)),
                wait=[("t", tmm), ("v", prog.val("v"))], inc="s")
            ev_s2 = sv
            prog.emit("vector", lambda E, o=o, nt=nt: E.tensor_mul(
                tg[0:128, 0:nt], ct[0:128, 0:nt], s_row[0:128, o:o + nt]),
                wait=[("s", sv), ("v", scan_done), ("sy", spill_done)], inc="v")
            prog.emit("vector", lambda E, o=o, nt=nt, gi=gi: E.tensor_add(
                ytile[0:128, o:o + nt], tg[0:128, 0:nt], u_t(gi)[0:128, o:o + nt]),
                inc="v")
            yv_done = prog.emit("vector", lambda E, o=o, nt=nt: E.tensor_mul(
                cptile[0:128, o:o + nt], ct[0:128, 0:nt], w_row[0:128, o:o + nt]),
                inc="v")
        prog.emit("sync", lambda E, gi=gi: E.dma_start(
            out=bass.AP(y_spill, gi * 128 * EXT, [[EXT, 128], [1, EXT]]),
            in_=ytile[:, :]), wait=[("v", yv_done)], inc="sy", inc_n=16)
        spill_done = prog.emit("sync", lambda E, gi=gi: E.dma_start(
            out=bass.AP(cp_spill, gi * 128 * EXT, [[EXT, 128], [1, EXT]]),
            in_=cptile[:, :]), inc="sy", inc_n=16)
        wabc_busy = prog.val("t")
        ev_v = prog.val("v")
        ev_s = prog.val("s")
    p3_end_t = prog.val("t")
    af.pop()
    ab.pop()
    ab.pop()

    st["prog"].scope = "b0.ag"
    exdone = prog.val("v")
    gv = prog.emit("gpsimd", lambda E: E.dma_start(
        out=st["ag1_in"].ap(), in_=cv("exs", 0, ncol=32)),
        wait=[("v", exdone)], inc="wg", inc_n=16)
    cv1 = prog.emit("gpsimd", lambda E: E.collective_compute(
        "AllGather", ALU.bypass, replica_groups=[list(range(8))],
        ins=[st["ag1_in"].ap().opt()], outs=[st["ag1_out"].ap().opt()]),
        wait=[("wg", gv)], inc="cc", inc_n=1)
    af.push()
    agb = valloc(af, 128, 256)
    t2f = valloc(af, 128, 16)
    t3f = valloc(af, 128, 16)
    gv = prog.emit("gpsimd", lambda E: E.dma_start(
        out=agb[:, :], in_=bass.AP(st["ag1_out"], 0,
                                   [[32, 128], [128 * 32, 8], [1, 32]])),
        wait=[("cc", cv1)], inc="wg", inc_n=16)
    prog.wait("vector", "wg", gv)
    prog.emit("vector", lambda E: E.memset(cv("Sin", 0, ncol=16), 0.0), inc="v")
    for j in range(8):
        prog.emit("vector", lambda E, j=j: E.tensor_scalar(
            t2f[:, :], View(af, agb.off + j * 32 + 16, 128, 16)[:, :],
            cv("foldm", j), cv("foldm1", j), ALU.mult, ALU.add), inc="v")
        prog.emit("vector", lambda E: E.tensor_mul(
            t3f[:, :], cv("Sin", 0, ncol=16), t2f[:, :]), inc="v")
        prog.emit("vector", lambda E, j=j: E.tensor_scalar(
            t2f[:, :], View(af, agb.off + j * 32, 128, 16)[:, :],
            cv("foldm", j), None, ALU.mult), inc="v")
        prog.emit("vector", lambda E: E.tensor_add(
            cv("Sin", 0, ncol=16), t3f[:, :], t2f[:, :]), inc="v")
    fold_done = prog.val("v")
    af.pop()

    st["prog"].scope = "b0.p4"
    ab.push()
    wo_all = valloc(ab, 128, 16 * 1024)
    lv = gdma(wo_all[:, :], bass.AP(di["Wo"], 0,
                                    [[1024, 128], [128 * 1024, 16], [1, 1024]]),
              wait=[("t", p3_end_t)], sem="wx")
    ybuf = [valloc(ab, 128, 512) for _ in range(16)]
    yfin = [valloc(ab, 128, 512) for _ in range(16)]
    prog.wait("tensor", "wx", lv)
    bank_busy = [0, 0, 0, 0]
    yread_t = 0
    for n in range(NN):
        nt, o = NT[n], NTO[n]
        lw = []
        for gi in range(16):
            lv = gdma(ybuf[gi][0:128, 0:nt],
                      bass.AP(y_spill, gi * 128 * EXT + o, [[EXT, 128], [1, nt]]),
                      wait=[("sy", spill_done), ("t", yread_t)] if gi == 0 else (),
                      sem="wy")
            lv = gdma(yfin[gi][0:128, 0:nt],
                      bass.AP(cp_spill, gi * 128 * EXT + o, [[EXT, 128], [1, nt]]),
                      sem="wy")
            lw.append(lv)
        yv = []
        for gi in range(16):
            vv = prog.emit("vector", lambda E, gi=gi, nt=nt: E.scalar_tensor_tensor(
                yfin[gi][0:128, 0:nt], yfin[gi][0:128, 0:nt], cv("Sin", gi),
                ybuf[gi][0:128, 0:nt], ALU.mult, ALU.add),
                wait=[("wy", lw[-1]), ("v", fold_done)], inc="v")
            yv.append(vv)
        for mo in range(8):
            par = mo % 4
            prog.wait("tensor", "v", bank_busy[par])
            for kt in range(16):
                prog.emit("tensor", lambda E, kt=kt, mo=mo, par=par, nt=nt: E.matmul(
                    psap(par, 128, nt),
                    wo_all[0:128, kt * 1024 + mo * 128:kt * 1024 + mo * 128 + 128],
                    yfin[kt][0:128, 0:nt], start=(kt == 0), stop=(kt == 15)),
                    wait=[("v", yv[-1])] if kt == 0 else (), inc="t")
            tmm = prog.val("t")
            bank_busy[par] = prog.emit(
                "vector", lambda E, mo=mo, par=par, o=o, nt=nt:
                E.scalar_tensor_tensor(
                    xres_t(mo)[0:128, o:o + nt], psap(par, 128, nt), cv("bo", mo),
                    xres_t(mo)[0:128, o:o + nt], ALU.add, ALU.add),
                wait=[("t", tmm)], inc="v")
        yread_t = prog.val("t")
    for k in range(8):
        prog.emit("vector", lambda E, k=k: E.tensor_scalar(
            xres_t(k)[0:128, 0:H], xres_t(k)[0:128, 0:H], cv("prev"), None,
            ALU.mult), inc="v")
    ab.pop()
    x1_done = prog.val("v")
    prog.wait("tensor", "v", x1_done)
    prog.wait("scalar", "v", x1_done)
    prog.wait("gpsimd", "v", x1_done)
    prog.wait("sync", "v", x1_done)
    st["x1_done"] = x1_done


def emit_b1(st):
    prog, af, ab = st["prog"], st["af"], st["ab"]
    psap, cv, di = st["psap"], st["cv"], st["di"]
    gdma = st["gdma"]
    xres_t = st["xres_t"]
    af.push(); ab.push()
    x1n = valloc(ab, 128, 8 * EXT)

    def x1n_t(k):
        return View(ab, x1n.off + k * EXT, 128, EXT)

    emit_pre_ln(st, xres_t, None, dst_t=x1n_t)
    ln_done = prog.val("v")
    prog.wait("tensor", "v", ln_done)
    prog.wait("gpsimd", "v", ln_done)
    if st.get("dbg_mode") == "x1n":
        prog.wait("gpsimd", "v", ln_done)
        for k in range(8):
            dst = bass.AP(st["dbg_out"], k * 128 * EXT, [[EXT, 128], [1, EXT]])
            st["gdma"](dst, x1n_t(k)[:, :], sem="dbgs")
        prog.wait("gpsimd", "dbgs", prog.val("dbgs"))

    wkv = valloc(ab, 128, 2 * 8192)
    wr_s = valloc(ab, 128, 8192)
    lv = gdma(View(ab, wkv.off, 128, 8192)[:, :],
              bass.AP(di["Wk"], 0, [[1024, 128], [128 * 1024, 8], [1, 1024]]),
              sem="w0")
    lv = gdma(View(ab, wkv.off + 8192, 128, 8192)[:, :],
              bass.AP(di["Wv"], 0, [[1024, 128], [128 * 1024, 8], [1, 1024]]),
              sem="w0")
    lvr = gdma(wr_s[:, :], bass.AP(di["Wr"], 0, [[1024, 128], [128 * 1024, 8], [1, 1024]]),
               sem="w0")
    xrk = valloc(ab, 128, 8 * 512)
    ka = valloc(ab, 128, 8 * 512)
    rbuf = valloc(af, 128, 8 * 512)
    tb1 = valloc(af, 128, 512)
    tbr = valloc(af, 128, 512)
    tbk = valloc(af, 128, 512)
    prog.wait("tensor", "w0", lvr)

    def wslice(w, kt, m):
        base = wkv.off if w == "k" else (wkv.off + 8192 if w == "v" else wr_s.off)
        return View(ab, base + kt * 1024 + m * 128, 128, 128)[:, :]

    bank_s = [0] * 8
    bank_v = [0] * 8
    for n in range(NN):
        nt, o = NT[n], NTO[n]
        if n == 0:
            o, nt = 1, NT[0] - 1
        mixv = 0
        for which, (mu, mu1) in enumerate((("mu_r", "mu_r1"), ("mu_k", "mu_k1"))):
            dstv = xrk
            for k in range(8):
                prog.emit("vector", lambda E, k=k, o=o, nt=nt, mu1=mu1: E.tensor_scalar(
                    tb1[0:128, 0:nt], x1n_t(k)[0:128, o - 1:o - 1 + nt],
                    cv(mu1, k), None, ALU.mult),
                    wait=[("t", bank_v[7])] if which == 1 and k == 0 else (), inc="v")
                mixv = prog.emit("vector", lambda E, k=k, o=o, nt=nt, mu=mu, which=which:
                                 E.scalar_tensor_tensor(
                    xrk[0:128, k * 512:k * 512 + nt], x1n_t(k)[0:128, o:o + nt],
                    cv(mu, k), tb1[0:128, 0:nt], ALU.mult, ALU.add), inc="v")
            if which == 0:
                prog.wait("tensor", "v", mixv)
                for m in range(8):
                    pb = m % 4
                    prog.wait("tensor", "s", bank_s[pb])
                    for kt in range(8):
                        prog.emit("tensor", lambda E, kt=kt, m=m, pb=pb, nt=nt: E.matmul(
                            psap(pb, 128, nt), wslice("r", kt, m),
                            View(ab, xrk.off + kt * 512, 128, nt)[0:128, 0:nt],
                            start=(kt == 0), stop=(kt == 7)), inc="t")
                    tmm = prog.val("t")
                    bank_v[7] = tmm
                    sv = prog.emit("scalar", lambda E, m=m, pb=pb, nt=nt: E.activation(
                        tbr[0:128, 0:nt], psap(pb, 128, nt), ACT.Tanh, scale=0.5),
                        wait=[("t", tmm), ("v", prog.val("v"))], inc="s")
                    bank_s[pb] = sv
                    prog.emit("vector", lambda E, m=m, nt=nt: E.tensor_scalar(
                        View(af, rbuf.off + m * 512, 128, nt)[0:128, 0:nt],
                        tbr[0:128, 0:nt], 0.5, 0.5, ALU.mult, ALU.add),
                        wait=[("s", sv)], inc="v")
        prog.wait("tensor", "v", mixv)
        for m in range(8):
            pb = m % 4
            prog.wait("tensor", "s", bank_s[pb])
            for kt in range(8):
                prog.emit("tensor", lambda E, kt=kt, m=m, pb=pb, nt=nt: E.matmul(
                    psap(pb, 128, nt), wslice("k", kt, m),
                    View(ab, xrk.off + kt * 512, 128, nt)[0:128, 0:nt],
                    start=(kt == 0), stop=(kt == 7)), inc="t")
            tmm = prog.val("t")
            sv = prog.emit("scalar", lambda E, m=m, pb=pb, nt=nt: E.activation(
                tbk[0:128, 0:nt], psap(pb, 128, nt), ACT.Relu),
                wait=[("t", tmm), ("v", prog.val("v"))], inc="s")
            sv = prog.emit("scalar", lambda E, m=m, nt=nt: E.activation(
                View(ab, ka.off + m * 512, 128, nt)[0:128, 0:nt],
                tbk[0:128, 0:nt], ACT.Square), inc="s")
            bank_s[pb] = sv
        kadone = prog.val("s")
        prog.wait("tensor", "s", kadone)
        for m in range(8):
            pb = 4 + m % 2
            prog.wait("tensor", "v", bank_v[pb])
            for kt in range(8):
                prog.emit("tensor", lambda E, kt=kt, m=m, pb=pb, nt=nt: E.matmul(
                    psap(pb, 128, nt), wslice("v", kt, m),
                    View(ab, ka.off + kt * 512, 128, nt)[0:128, 0:nt],
                    start=(kt == 0), stop=(kt == 7)), inc="t")
            tmm = prog.val("t")
            vv = prog.emit("vector", lambda E, m=m, pb=pb, nt=nt: E.tensor_mul(
                tb1[0:128, 0:nt], psap(pb, 128, nt),
                View(af, rbuf.off + m * 512, 128, nt)[0:128, 0:nt]),
                wait=[("t", tmm)], inc="v")
            vv = prog.emit("vector", lambda E, m=m, o=o, nt=nt: E.tensor_add(
                xres_t(m)[0:128, o:o + nt], xres_t(m)[0:128, o:o + nt],
                tb1[0:128, 0:nt]), inc="v")
            bank_v[pb] = vv
        prog.wait("vector", "t", prog.val("t"))
    af.pop(); ab.pop()
    x2_done = prog.val("v")
    prog.wait("tensor", "v", x2_done)
    prog.wait("scalar", "v", x2_done)
    prog.wait("gpsimd", "v", x2_done)
    prog.wait("sync", "v", x2_done)


def emit_b2(st):
    prog, af, ab = st["prog"], st["af"], st["ab"]
    psap, cv, di = st["psap"], st["cv"], st["di"]
    gdma = st["gdma"]
    ps = st["ps"]
    xres_t = st["xres_t"]
    xn_spill = st["xn_spill"]
    out_spill = st["out_spill"]
    af.push(); ab.push()

    x2_done = emit_pre_ln(st, xres_t, xn_spill)
    prog.wait("gpsimd", "sy0", x2_done[0])
    prog.wait("gpsimd", "sy1", x2_done[1])

    q_rot = valloc(ab, 128, 8 * 2048)
    lkv = valloc(ab, 128, 8 * EXT)
    payload = valloc(ab, 128, 1024)
    hsel_b = valloc(ab, 128, 2)
    s2T_b = valloc(ab, 2, 128)
    wp2_b = valloc(ab, 128, 2)
    s2T_f = valloc(af, 2, 128)
    hsel_f = valloc(af, 128, 2)
    mq = valloc(af, 2, 8)
    lv = gdma(hsel_b[:, :], di["halfsel"].ap(), sem="wx")
    lv = gdma(s2T_b[0:2, :], di["sel2T"].ap(), sem="wx")
    lv = gdma(wp2_b[:, :], di["wproj2"].ap(), sem="wx")
    lv = gdma(s2T_f[0:2, :], di["sel2T"].ap(), sem="wx")
    lv = gdma(hsel_f[:, :], di["halfsel"].ap(), sem="wx")
    tabs = lv
    prog.wait("tensor", "wx", tabs)
    prog.wait("vector", "wx", tabs)
    prog.emit("vector", lambda E: E.memset(cv("spare", ncol=1, rows=2), 1.0), inc="v")
    ones2_set = prog.val("v")

    def qr(hp):
        return View(ab, q_rot.off + hp * 2048, 128, 2048)

    def lk(hp):
        return View(ab, lkv.off + hp * EXT, 128, EXT)

    KVC = [(6, 512), (518, 512), (1030, 512), (1542, 512), (2054, 128)]
    OWN = [(134 + 512 * c, 512) for c in range(4)]

    st["prog"].scope = "b2.l1"
    af.push(); ab.push()
    cos_b = valloc(ab, 128, EXT)
    sin_b = valloc(ab, 128, EXT)
    xnt = [valloc(ab, 128, 512) for _ in range(8)]
    kvrot = valloc(ab, 128, EXT)
    wslc = [valloc(ab, 128, 1024) for _ in range(2)]
    kvb = valloc(ab, 128, 512)
    srow = valloc(af, 2, 2048)
    s2 = valloc(af, 2, 2048)
    sc1 = valloc(af, 128, 512)
    sc2 = valloc(af, 128, 512)
    segt = valloc(af, 128, 35)
    comp = valloc(af, 128, 128)
    lv = gdma(cos_b[:, :], di["cos_t"].ap(), sem="wx")
    lv = gdma(sin_b[:, :], di["sin_t"].ap(), sem="wx")
    tabs2 = lv
    prog.wait("vector", "wx", tabs2)

    wq_busy = [0, 0]
    bank_v = {i: 0 for i in range(8)}
    bank_s = {i: 0 for i in range(8)}
    st["_b2xt"] = 0

    def stream_x2n(c0, nt):
        lvv = None
        for k in range(8):
            lvv = gdma(xnt[k][0:128, 0:nt],
                       bass.AP(xn_spill, k * 128 * EXT + c0, [[EXT, 128], [1, nt]]),
                       wait=[("t", st["_b2xt"])] if k == 0 else (), sem="wx")
        return lvv

    def rot_sweep(hp, wname, chunks, dst_fn, scale_q):
        par = 0 if wname == "Wkv" else 1
        wb = wslc[par]
        lvw = gdma(wb[:, :], bass.AP(di[wname], hp * 128,
                                     [[1024, 128], [128 * 1024, 8], [1, 128]]),
                   wait=[("t", wq_busy[par])], sem="w%d" % par)
        if scale_q:
            prog.emit("vector", lambda E, wb=wb: E.tensor_scalar(
                wb[:, :], wb[:, :], 0.125, None, ALU.mult),
                wait=[("w%d" % par, lvw)], inc="v")
            prog.wait("tensor", "v", prog.val("v"))
        else:
            prog.wait("tensor", "w%d" % par, lvw)
        for (c0, nt) in chunks:
            xv = stream_x2n(c0, nt)
            prog.wait("tensor", "wx", xv)
            prog.wait("tensor", "s", bank_s[0])
            prog.wait("tensor", "v", bank_v[0])
            for kt in range(8):
                prog.emit("tensor", lambda E, kt=kt, nt=nt, wb=wb: E.matmul(
                    psap(0, 128, nt), wb[0:128, kt * 128:kt * 128 + 128],
                    xnt[kt][0:128, 0:nt], start=(kt == 0), stop=(kt == 7)), inc="t")
            tmm = prog.val("t")
            st["_b2xt"] = tmm
            sv = prog.emit("scalar", lambda E, nt=nt: E.activation(
                kvb[0:128, 0:nt], psap(0, 128, nt), ACT.Copy),
                wait=[("t", tmm), ("v", bank_v[1])], inc="s")
            bank_s[0] = sv
            prog.wait("tensor", "s", sv)
            prog.wait("tensor", "v", bank_v[1])
            tv = prog.emit("tensor", lambda E, nt=nt: E.matmul(
                psap(1, 128, nt), st["prot_bf"][:, :], kvb[0:128, 0:nt],
                start=True, stop=True), inc="t")
            vv = prog.emit("vector", lambda E, c0=c0, nt=nt: E.tensor_mul(
                sc1[0:128, 0:nt], kvb[0:128, 0:nt], cos_b[0:128, c0:c0 + nt]),
                wait=[("s", sv)], inc="v")
            vv = prog.emit("vector", lambda E, c0=c0, nt=nt: E.tensor_mul(
                sc2[0:128, 0:nt], psap(1, 128, nt), sin_b[0:128, c0:c0 + nt]),
                wait=[("t", tv)], inc="v")
            vv = prog.emit("vector", lambda E, c0=c0, nt=nt, dst=dst_fn(c0, nt):
                           E.tensor_add(dst, sc1[0:128, 0:nt], sc2[0:128, 0:nt]),
                           inc="v")
            bank_v[1] = vv
        wq_busy[par] = prog.val("t")
        return prog.val("v")

    def ln64_stats(src_ap_fn, chunks, dst_write, ones_sum=None):
        osum = ones_sum if ones_sum is not None else hsel_b
        for (c0, nt) in chunks:
            srcap = src_ap_fn(c0, nt)
            sv = prog.emit("scalar", lambda E, srcap=srcap, nt=nt: E.activation(
                sc1[0:128, 0:nt], srcap, ACT.Square),
                wait=[("v", prog.val("v"))], inc="s")
            prog.wait("tensor", "s", sv)
            prog.wait("tensor", "v", bank_v[3])
            tv = prog.emit("tensor", lambda E, srcap=srcap, nt=nt, osum=osum: E.matmul(
                psap(2, 2, nt), osum[:, :], srcap, start=True, stop=True), inc="t")
            tv = prog.emit("tensor", lambda E, nt=nt: E.matmul(
                psap(3, 2, nt), hsel_f[:, :], sc1[0:128, 0:nt],
                start=True, stop=True), inc="t")
            m_row = srow[0:2, 0:nt]
            r_row = srow[0:2, 512:512 + nt]
            tmp = srow[0:2, 1024:1024 + nt]
            vv = prog.emit("vector", lambda E, m_row=m_row, nt=nt: E.tensor_scalar(
                m_row, psap(2, 2, nt), 1.0 / DH, None, ALU.mult),
                wait=[("t", tv)], inc="v")
            vv = prog.emit("vector", lambda E, m_row=m_row, tmp=tmp: E.tensor_mul(
                tmp, m_row, m_row), inc="v")
            vv = prog.emit("vector", lambda E, r_row=r_row, tmp=tmp, nt=nt:
                           E.scalar_tensor_tensor(
                r_row, psap(3, 2, nt), 1.0 / DH, tmp, ALU.mult, ALU.subtract),
                inc="v")
            sv = prog.emit("scalar", lambda E, r_row=r_row, tmp=tmp: E.activation(
                tmp, r_row, ACT.Ln, bias=cv("eps", rows=2)),
                wait=[("v", vv)], inc="s")
            sv = prog.emit("scalar", lambda E, r_row=r_row, tmp=tmp: E.activation(
                r_row, tmp, ACT.Exp, scale=-0.5), inc="s")
            prog.wait("tensor", "s", sv)
            tv = prog.emit("tensor", lambda E, m_row=m_row, nt=nt: E.matmul(
                psap(2, 128, nt), s2T_f[0:2, :], m_row, start=True, stop=True),
                inc="t")
            tv = prog.emit("tensor", lambda E, r_row=r_row, nt=nt: E.matmul(
                psap(3, 128, nt), s2T_f[0:2, :], r_row, start=True, stop=True),
                inc="t")
            vv = prog.emit("vector", lambda E, srcap=srcap, nt=nt: E.tensor_sub(
                sc1[0:128, 0:nt], srcap, psap(2, 128, nt)),
                wait=[("t", tv)], inc="v")
            vv = prog.emit("vector", lambda E, nt=nt: E.tensor_mul(
                sc1[0:128, 0:nt], sc1[0:128, 0:nt], psap(3, 128, nt)), inc="v")
            dst_write(c0, nt)
            bank_v[3] = prog.val("v")

    for hp in range(8):
        rot_sweep(hp, "Wkv", KVC, lambda c0, nt: kvrot[0:128, c0:c0 + nt], False)
        rot_sweep(hp, "Wq", OWN,
                  lambda c0, nt, hp=hp: qr(hp)[0:128, c0 - 134:c0 - 134 + nt], True)
        qv = prog.val("v")
        for ci, (c0, nt) in enumerate(OWN):
            sv = prog.emit("scalar", lambda E, c0=c0, nt=nt, hp=hp: E.activation(
                sc1[0:128, 0:nt], qr(hp)[0:128, c0 - 134:c0 - 134 + nt], ACT.Square),
                wait=[("v", qv), ("t", prog.val("t"))], inc="s")
            prog.wait("tensor", "s", sv)
            prog.wait("tensor", "v", bank_v[2])
            tv = prog.emit("tensor", lambda E, nt=nt: E.matmul(
                psap(2, 2, nt), hsel_f[:, :], sc1[0:128, 0:nt],
                start=True, stop=True), inc="t")
            bank_v[2] = prog.emit("vector", lambda E, ci=ci, nt=nt: E.reduce_max(
                srow[0:2, ci:ci + 1], psap(2, 2, nt), axis=AX.X),
                wait=[("t", tv)], inc="v")
        vv = prog.emit("vector", lambda E: E.reduce_max(
            srow[0:2, 4:5], srow[0:2, 0:4], axis=AX.X), inc="v")
        sv = prog.emit("scalar", lambda E: E.activation(
            srow[0:2, 1:2], srow[0:2, 4:5], ACT.Ln, bias=cv("eps", rows=2)),
            wait=[("v", vv)], inc="s")
        sv = prog.emit("scalar", lambda E, hp=hp: E.activation(
            mq[0:2, hp:hp + 1], srow[0:2, 1:2], ACT.Exp, scale=0.5), inc="s")
        ln64_stats(lambda c0, nt: kvrot[0:128, c0:c0 + nt], KVC,
                   lambda c0, nt, hp=hp: prog.emit(
                       "vector", lambda E, c0=c0, nt=nt, hp=hp: E.tensor_scalar(
                           lk(hp)[0:128, c0:c0 + nt], sc1[0:128, 0:nt],
                           cv("lng_l"), cv("lnb_l"), ALU.mult, ALU.add), inc="v"))
        for ci, (c0, nt) in enumerate(OWN):
            prog.wait("tensor", "v", bank_v[4])
            tv = prog.emit("tensor", lambda E, c0=c0, nt=nt: E.matmul(
                psap(4, 2, nt), wp2_b[:, :], kvrot[0:128, c0:c0 + nt],
                start=True, stop=True), inc="t")
            vv = prog.emit("vector", lambda E, ci=ci, nt=nt: E.tensor_copy(
                s2[0:2, ci * 512:ci * 512 + nt], psap(4, 2, nt)),
                wait=[("t", tv)], inc="v")
            bank_v[4] = vv
        sdone = prog.val("v")
        segdma = st.get("_segdma", 0)
        for h in range(2):
            dv = prog.emit("sync", lambda E, h=h: E.dma_start(
                out=bass.AP(af.t, segt.off, [[af.cols, 128], [1, 16]]),
                in_=bass.AP(af.t, s2.off + h * af.cols,
                            [[af.cols, 1], [16, 128], [1, 16]])),
                wait=[("v", prog.val("v"))], inc="sg", inc_n=16)
            dgv = prog.val("sg")
            vv = prog.emit("vector", lambda E: E.reduce_max(
                View(af, segt.off + 16, 128, 1)[0:128, 0:1],
                View(af, segt.off, 128, 16)[0:128, 0:16], axis=AX.X),
                wait=[("sg", dgv)], inc="v")
            vv = prog.emit("vector", lambda E: E.tensor_scalar(
                View(af, segt.off + 16, 128, 1)[0:128, 0:1],
                View(af, segt.off + 16, 128, 1)[0:128, 0:1], -1.0, None, ALU.mult),
                inc="v")
            sv = prog.emit("scalar", lambda E: E.activation(
                View(af, segt.off + 17, 128, 16)[0:128, 0:16],
                View(af, segt.off, 128, 16)[0:128, 0:16], ACT.Exp,
                bias=View(af, segt.off + 16, 128, 1)[0:128, 0:1],
                accum_out=View(af, segt.off + 33, 128, 1)[0:128, 0:1]),
                wait=[("v", vv)], inc="s")
            vv = prog.emit("vector", lambda E: E.reciprocal(
                View(af, segt.off + 34, 128, 1)[0:128, 0:1],
                View(af, segt.off + 33, 128, 1)[0:128, 0:1]),
                wait=[("s", sv)], inc="v")
            vv = prog.emit("vector", lambda E: E.tensor_scalar(
                View(af, segt.off + 17, 128, 16)[0:128, 0:16],
                View(af, segt.off + 17, 128, 16)[0:128, 0:16],
                View(af, segt.off + 34, 128, 1)[0:128, 0:1], None, ALU.mult),
                inc="v")
            dv = prog.emit("sync", lambda E, h=h: E.dma_start(
                out=bass.AP(af.t, s2.off + h * af.cols,
                            [[af.cols, 1], [16, 128], [1, 16]]),
                in_=bass.AP(af.t, segt.off + 17, [[af.cols, 128], [1, 16]])),
                wait=[("v", vv)], inc="sg", inc_n=16)
        st["_segdma"] = prog.val("sg")
        prog.wait("tensor", "sg", st["_segdma"])
        prog.wait("vector", "sg", st["_segdma"])
        for ci, (c0, nt) in enumerate(OWN):
            prog.wait("tensor", "v", bank_v[5])
            tv = prog.emit("tensor", lambda E, ci=ci, nt=nt: E.matmul(
                psap(5, 128, nt), s2T_f[0:2, :], s2[0:2, ci * 512:ci * 512 + nt],
                start=True, stop=True), inc="t")
            vv = prog.emit("vector", lambda E, c0=c0, nt=nt: E.tensor_mul(
                sc1[0:128, 0:nt], kvrot[0:128, c0:c0 + nt], psap(5, 128, nt)),
                wait=[("t", tv)], inc="v")
            vv = prog.emit("vector", lambda E, ci=ci: E.reduce_sum(
                comp[0:128, ci * 32:ci * 32 + 32],
                bass.AP(af.t, sc1.off, [[af.cols, 128], [16, 32], [1, 16]]),
                axis=AX.X), inc="v")
            bank_v[5] = vv
        def wr_gkv(c0, nt, hp=hp):
            vv = prog.emit("vector", lambda E, nt=nt: E.tensor_scalar(
                sc2[0:128, 0:nt], sc1[0:128, 0:nt], cv("lng_g"), cv("lnb_g"),
                ALU.mult, ALU.add), inc="v")
        ln64_stats(lambda c0, nt: comp[0:128, 0:128], [(0, 128)], wr_gkv,
                   ones_sum=hsel_f)
        prog.wait("tensor", "v", prog.val("v"))
        tv = prog.emit("tensor", lambda E: E.transpose(
            psap(0, 128, 128), sc2[0:128, 0:128], st["ident_f"][:, :]),
            wait=[("s", bank_s[0]), ("v", bank_v[1])], inc="t")
        vv = prog.emit("vector", lambda E, hp=hp: E.tensor_copy(
            payload[0:128, hp * 128:hp * 128 + 128], psap(0, 128, 128)),
            wait=[("t", tv)], inc="v")
        bank_v[0] = vv
    pay_done = prog.val("v")
    if st.get("dbg_mode") in ("b2q", "b2lkv", "b2pay"):
        prog.wait("gpsimd", "v", pay_done)
        mode = st["dbg_mode"]
        for hp in range(8):
            if mode == "b2q":
                srcv = qr(hp)[:, 0:2048]
                ncol = 2048
            elif mode == "b2lkv":
                srcv = lk(hp)[:, 0:EXT]
                ncol = EXT
            else:
                srcv = payload[0:128, hp * 128:hp * 128 + 128]
                ncol = 128
            dst = bass.AP(st["dbg_out"], hp * 128 * EXT, [[EXT, 128], [1, ncol]])
            gdma(dst, srcv, sem="dbgs")
        prog.wait("gpsimd", "dbgs", prog.val("dbgs"))
    af.pop(); ab.pop()

    gv = prog.emit("gpsimd", lambda E: E.dma_start(
        out=st["ag2_in"].ap(), in_=payload[:, :]),
        wait=[("v", pay_done)], inc="wg", inc_n=16)
    cv2 = prog.emit("gpsimd", lambda E: E.collective_compute(
        "AllGather", ALU.bypass, replica_groups=[list(range(8))],
        ins=[st["ag2_in"].ap().opt()], outs=[st["ag2_out"].ap().opt()]),
        wait=[("wg", gv)], inc="cc", inc_n=1)
    prog.wait("sync", "cc", cv2)

    st["prog"].scope = "b2.l2"
    af.push(); ab.push()
    gkv_fm = valloc(ab, 128, NSEG)
    vtokg = [valloc(ab, 128, 130) for _ in range(8)]
    vtokl = [valloc(ab, 128, 130) for _ in range(17)]
    gmask = valloc(ab, 128, 4096)
    attn = valloc(ab, 128, 4096)
    attl = valloc(ab, 128, 256)
    outb = valloc(ab, 128, 512)
    lsc = valloc(af, 128, 512)
    outf = valloc(af, 128, 512)
    rrow = valloc(af, 1, 512)
    gn = valloc(af, 2, 8)
    nbuf = valloc(af, 128, 1)
    for i in range(8):
        prog.emit("vector", lambda E, i=i: E.memset(vtokg[i][0:128, 64:65], 1.0), inc="v")
        prog.emit("vector", lambda E, i=i: E.memset(vtokg[i][0:128, 129:130], 1.0), inc="v")
    for i in range(17):
        prog.emit("vector", lambda E, i=i: E.memset(vtokl[i][0:128, 64:65], 1.0), inc="v")
        prog.emit("vector", lambda E, i=i: E.memset(vtokl[i][0:128, 129:130], 1.0), inc="v")
    ones_set = prog.val("v")
    prog.wait("sync", "v", ones_set)

    out_dma = 0
    gm_busy = 0
    for hp in range(8):
        dv = prog.emit("sync", lambda E, hp=hp: E.dma_start_transpose(
            gkv_fm[0:128, 0:NSEG],
            bass.AP(st["ag2_out"], hp * 128, [[1024, NSEG], [1, 128]])),
            wait=[("t", st.get("_l2t", 0))], inc="sg", inc_n=16)
        for stt in range(8):
            dv = prog.emit("sync", lambda E, hp=hp, stt=stt: E.dma_start(
                out=bass.AP(ab.t, vtokg[stt].off, [[ab.cols, 128], [65, 2], [1, 64]]),
                in_=bass.AP(st["ag2_out"], stt * 128 * 1024 + hp * 128,
                            [[1024, 128], [64, 2], [1, 64]])), inc="sg", inc_n=16)
        gld = prog.val("sg")
        for kw in range(17):
            sv = prog.emit("scalar", lambda E, kw=kw, hp=hp: E.activation(
                lsc[0:128, 0:128], lk(hp)[0:128, 6 + 128 * kw:6 + 128 * kw + 128],
                ACT.Copy), wait=[("t", prog.val("t")), ("sg", gld)], inc="s")
            prog.wait("tensor", "s", sv)
            tv = prog.emit("tensor", lambda E: E.transpose(
                psap(6, 128, 128), lsc[0:128, 0:128], st["ident_f"][:, :]),
                wait=[("v", prog.val("v"))], inc="t")
            vv = prog.emit("vector", lambda E, kw=kw: E.tensor_copy(
                bass.AP(ab.t, vtokl[kw].off, [[ab.cols, 128], [65, 2], [1, 64]]),
                bass.AP(ps[6], 0, [[512, 128], [64, 2], [1, 64]])),
                wait=[("t", tv)], inc="v")
        for ci in range(2):
            sv = prog.emit("scalar", lambda E, ci=ci: E.activation(
                lsc[0:128, 0:512], gkv_fm[0:128, ci * 512:ci * 512 + 512],
                ACT.Square), wait=[("t", prog.val("t")), ("sg", gld),
                                   ("v", prog.val("v"))], inc="s")
            prog.wait("tensor", "s", sv)
            tv = prog.emit("tensor", lambda E: E.matmul(
                psap(7, 2, 512), hsel_f[:, :], lsc[0:128, 0:512],
                start=True, stop=True), inc="t")
            prog.emit("vector", lambda E, ci=ci: E.reduce_max(
                gn[0:2, ci:ci + 1], psap(7, 2, 512), axis=AX.X),
                wait=[("t", tv)], inc="v")
        for ci, (c0, nt) in enumerate(KVC):
            sv = prog.emit("scalar", lambda E, c0=c0, nt=nt, hp=hp: E.activation(
                lsc[0:128, 0:nt], lk(hp)[0:128, c0:c0 + nt], ACT.Square),
                wait=[("t", prog.val("t")), ("v", prog.val("v"))], inc="s")
            prog.wait("tensor", "s", sv)
            tv = prog.emit("tensor", lambda E, nt=nt: E.matmul(
                psap(7, 2, nt), hsel_f[:, :], lsc[0:128, 0:nt],
                start=True, stop=True), inc="t")
            prog.emit("vector", lambda E, ci=ci, nt=nt: E.reduce_max(
                gn[0:2, 2 + ci:3 + ci], psap(7, 2, nt), axis=AX.X),
                wait=[("t", tv)], inc="v")
        vv = prog.emit("vector", lambda E: E.reduce_max(
            gn[0:2, 7:8], gn[0:2, 0:7], axis=AX.X), inc="v")
        vv = prog.emit("vector", lambda E: E.tensor_copy(
            gn[0:2, 2:3], gn[0:2, 7:8]), inc="v")
        sv = prog.emit("scalar", lambda E: E.activation(
            gn[0:2, 3:4], gn[0:2, 2:3], ACT.Ln, bias=cv("eps", rows=2)),
            wait=[("v", vv)], inc="s")
        sv = prog.emit("scalar", lambda E: E.activation(
            gn[0:2, 4:5], gn[0:2, 3:4], ACT.Exp, scale=0.5), inc="s")
        vv = prog.emit("vector", lambda E, hp=hp: E.tensor_mul(
            gn[0:2, 5:6], gn[0:2, 4:5], mq[0:2, hp:hp + 1]),
            wait=[("s", sv)], inc="v")
        prog.wait("tensor", "v", prog.val("v"))
        tv = prog.emit("tensor", lambda E: E.matmul(
            psap(7, 1, 1), cv("spare", rows=2), gn[0:2, 5:6],
            start=True, stop=True), inc="t")
        vv = prog.emit("vector", lambda E: E.tensor_scalar(
            gn[0:1, 6:7], bass.AP(ps[7], 0, [[512, 1], [1, 1]]),
            -1.0, None, ALU.mult), wait=[("t", tv)], inc="v")
        prog.wait("tensor", "v", vv)
        tv = prog.emit("tensor", lambda E: E.matmul(
            psap(7, 128, 1), cv("onesrow", rows=1, ncol=128),
            gn[0:1, 6:7], start=True, stop=True), inc="t")
        vv = prog.emit("vector", lambda E: E.tensor_copy(
            nbuf[0:128, 0:1], bass.AP(ps[7], 0, [[512, 128], [1, 1]])),
            wait=[("t", tv)], inc="v")
        nb_done = prog.val("v")
        for j in range(4):
            mv = gdma(gmask[:, :], bass.AP(di["gmask"], j * 128 * 4096,
                                           [[4096, 128], [1, 4096]]),
                      wait=[("v", gm_busy)], sem="wx")
            for h in range(2):
                hb = 64 * h
                for stt in range(8):
                    pb = stt % 2
                    prog.wait("tensor", "s", bank_s.get(("g", pb), 0))
                    tv = prog.emit("tensor", lambda E, stt=stt, h=h, hp=hp, j=j, pb=pb,
                                   hb=hb: E.matmul(
                        psap(pb, 128, 512),
                        View(ab, gkv_fm.off + stt * 128, 128, 128)[hb:hb + 64, 0:128],
                        qr(hp)[hb:hb + 64, j * 512:j * 512 + 512],
                        start=True, stop=True), inc="t")
                    sv = prog.emit("scalar", lambda E, stt=stt, pb=pb: E.activation(
                        View(ab, attn.off + stt * 512, 128, 512)[0:128, 0:512],
                        psap(pb, 128, 512), ACT.Exp, bias=nbuf[0:128, 0:1]),
                        wait=[("t", tv), ("v", nb_done)], inc="s")
                    bank_s[("g", pb)] = sv
                vv = prog.emit("vector", lambda E: E.tensor_mul(
                    attn[0:128, 0:4096], attn[0:128, 0:4096], gmask[0:128, 0:4096]),
                    wait=[("s", prog.val("s")), ("wx", mv)], inc="v")
                gm_busy_h = vv
                prog.wait("tensor", "v", vv)
                prog.wait("tensor", "s", bank_s.get("out", 0))
                for stt in range(8):
                    prog.emit("tensor", lambda E, stt=stt, h=h, hb=hb: E.matmul(
                        psap(6, 65, 512),
                        View(ab, vtokg[stt].off, 128, 130)[0:128, 65 * h:65 * h + 65],
                        View(ab, attn.off + stt * 512, 128, 512)[0:128, 0:512],
                        start=(stt == 0), stop=False), inc="t")
                for w in range(4):
                    wq = 4 * j + w
                    kvc_prev = 6 + 128 * wq
                    for t2, kc in ((0, kvc_prev), (1, kvc_prev + 128)):
                        pb = 2 + t2
                        prog.wait("tensor", "s", bank_s.get(("l", pb), 0))
                        tv = prog.emit("tensor", lambda E, kc=kc, h=h, hp=hp, wq=wq,
                                       pb=pb, hb=hb: E.matmul(
                            psap(pb, 128, 128), lk(hp)[hb:hb + 64, kc:kc + 128],
                            qr(hp)[hb:hb + 64, wq * 128:wq * 128 + 128],
                            start=True, stop=True), inc="t")
                        sv = prog.emit("scalar", lambda E, t2=t2, pb=pb: E.activation(
                            attl[0:128, t2 * 128:t2 * 128 + 128],
                            psap(pb, 128, 128), ACT.Exp, bias=nbuf[0:128, 0:1]),
                            wait=[("t", tv)], inc="s")
                        bank_s[("l", pb)] = sv
                    vv = prog.emit("vector", lambda E: E.tensor_mul(
                        attl[0:128, 128:256], attl[0:128, 128:256],
                        st["causal_bf"][:, :]), wait=[("s", prog.val("s"))], inc="v")
                    if wq == 0:
                        vv = prog.emit("vector", lambda E: E.tensor_scalar(
                            attl[0:128, 0:128], attl[0:128, 0:128],
                            cv("prev"), None, ALU.mult), inc="v")
                    prog.wait("tensor", "v", vv)
                    for t2 in (0, 1):
                        prog.emit("tensor", lambda E, wq=wq, w=w, t2=t2, h=h: E.matmul(
                            bass.AP(ps[6], w * 128, [[512, 65], [1, 128]]),
                            View(ab, vtokl[wq + t2].off, 128, 130)[0:128,
                                                                   65 * h:65 * h + 65],
                            attl[0:128, t2 * 128:t2 * 128 + 128],
                            start=False, stop=(w == 3 and t2 == 1)), inc="t")
                tmm = prog.val("t")
                vv = prog.emit("vector", lambda E: E.tensor_copy(
                    View(af, outf.off, 128, 512)[0:65, 0:512],
                    bass.AP(ps[6], 0, [[512, 65], [1, 512]])),
                    wait=[("t", tmm)], inc="v")
                vv = prog.emit("vector", lambda E: E.reciprocal(
                    rrow[0:1, 0:512],
                    View(af, outf.off, 128, 512)[64:65, 0:512]), inc="v")
                prog.wait("tensor", "v", vv)
                tv = prog.emit("tensor", lambda E: E.matmul(
                    psap(7, 64, 512), cv("onesrow", rows=1, ncol=64),
                    rrow[0:1, 0:512], start=True, stop=True), inc="t")
                vv = prog.emit("vector", lambda E, h=h: E.tensor_mul(
                    outb[64 * h:64 * h + 64, 0:512],
                    View(af, outf.off, 128, 512)[0:64, 0:512], psap(7, 64, 512)),
                    wait=[("t", tv)], inc="v")
                bank_s["out"] = prog.val("s")
            gm_busy = prog.val("v")
            out_dma = prog.emit("sync", lambda E, hp=hp, j=j: E.dma_start(
                out=bass.AP(out_spill, hp * 128 * 2048 + j * 512,
                            [[2048, 128], [1, 512]]),
                in_=outb[0:128, 0:512]), wait=[("v", gm_busy)], inc="sy", inc_n=16)
            prog.wait("vector", "sy", out_dma)
        st["_l2t"] = prog.val("t")
    af.pop(); ab.pop()

    st["prog"].scope = "b2.l3"
    af.push(); ab.push()
    wout = valloc(ab, 128, 8192)
    otiles = [valloc(ab, 128, 512) for _ in range(8)]
    lv = gdma(wout[:, :], bass.AP(di["Wout"], 0,
                                  [[1024, 128], [128 * 1024, 8], [1, 1024]]),
              wait=[("t", prog.val("t"))], sem="wx")
    prog.wait("tensor", "wx", lv)
    bank_vv = [0, 0, 0, 0]
    ot_busy = 0
    for j in range(4):
        lw = None
        for kt in range(8):
            lw = gdma(otiles[kt][0:128, 0:512],
                      bass.AP(out_spill, kt * 128 * 2048 + j * 512,
                              [[2048, 128], [1, 512]]),
                      wait=[("sy", out_dma), ("t", ot_busy)] if kt == 0 else (),
                      sem="wy")
        prog.wait("tensor", "wy", lw)
        for mo in range(8):
            pb = mo % 4
            prog.wait("tensor", "v", bank_vv[pb])
            for kt in range(8):
                prog.emit("tensor", lambda E, kt=kt, mo=mo, pb=pb: E.matmul(
                    psap(pb, 128, 512),
                    wout[0:128, kt * 1024 + mo * 128:kt * 1024 + mo * 128 + 128],
                    otiles[kt][0:128, 0:512], start=(kt == 0), stop=(kt == 7)),
                    inc="t")
            tmm = prog.val("t")
            bank_vv[pb] = prog.emit(
                "vector", lambda E, mo=mo, pb=pb, j=j: E.scalar_tensor_tensor(
                    xres_t(mo)[0:128, 134 + j * 512:134 + j * 512 + 512],
                    psap(pb, 128, 512), cv("bout", mo),
                    xres_t(mo)[0:128, 134 + j * 512:134 + j * 512 + 512],
                    ALU.add, ALU.add), wait=[("t", tmm)], inc="v")
        ot_busy = prog.val("t")
    af.pop(); ab.pop()
    af.pop(); ab.pop()
    x3_done = prog.val("v")
    prog.wait("tensor", "v", x3_done)
    prog.wait("scalar", "v", x3_done)
    prog.wait("gpsimd", "v", x3_done)
    prog.wait("sync", "v", x3_done)


def emit_b3(st):
    prog, af, ab = st["prog"], st["af"], st["ab"]
    psap, cv, di = st["psap"], st["cv"], st["di"]
    gdma = st["gdma"]
    ps = st["ps"]
    xres_t = st["xres_t"]
    xn_spill = st["xn_spill"]
    af.push(); ab.push()

    x3_done = emit_pre_ln(st, xres_t, xn_spill)
    prog.wait("gpsimd", "sy0", x3_done[0])
    prog.wait("gpsimd", "sy1", x3_done[1])

    xnt = [valloc(ab, 128, 1024) for _ in range(8)]
    wrt = valloc(ab, 128, 64)
    wsl = valloc(ab, 128, 6 * 1024)
    hbuf = valloc(ab, 128, 16 * 1024)
    dwall = valloc(ab, 1, 8 * 2048)
    onesrow_b = valloc(ab, 1, 128)
    dwr = valloc(af, 8, 2048)
    sm = valloc(af, 128, 8)
    ex = valloc(af, 128, 8)
    ab8 = valloc(af, 128, 8)
    fb = valloc(af, 128, 8)
    oh = valloc(af, 128, 8)
    dwt = valloc(af, 128, 8)
    smc = valloc(af, 128, 8)
    dwb = valloc(af, 128, 512)
    scf = valloc(af, 128, 512)
    sgf = valloc(af, 128, 512)
    JEPS2 = 0.02
    BIG = 1.0e30

    def C(j):
        return smc[0:128, j:j + 1]

    lv = gdma(wrt[:, :], bass.AP(di["Wrout"], 0, [[8, 128], [1024, 8], [1, 8]]),
              sem="wx")
    prog.emit("vector", lambda E: E.memset(onesrow_b[0:1, :], 1.0), inc="v")
    st["_b3xt"] = 0

    def stream_x3n(c0, nt):
        lvv = None
        for k in range(8):
            lvv = gdma(xnt[k][0:128, 0:nt],
                       bass.AP(xn_spill, k * 128 * EXT + c0, [[EXT, 128], [1, nt]]),
                       wait=[("t", st["_b3xt"])] if k == 0 else (), sem="wx")
        return lvv

    st["prog"].scope = "b3.rt"
    prog.wait("tensor", "wx", lv)
    xv = stream_x3n(134, 1024)
    prog.wait("tensor", "wx", xv)
    tp_v = 0
    for tt in range(16):
        if tt == 8:
            xv = stream_x3n(134 + 1024, 1024)
            prog.wait("tensor", "wx", xv)
        off = (tt % 8) * 128
        pb = tt % 2
        prog.wait("tensor", "v", st.get("_b3pv%d" % pb, 0))
        for kt in range(8):
            prog.emit("tensor", lambda E, kt=kt, off=off, pb=pb: E.matmul(
                psap(pb, 128, 8), xnt[kt][0:128, off:off + 128],
                wrt[0:128, kt * 8:kt * 8 + 8], start=(kt == 0), stop=(kt == 7)),
                inc="t")
        tmm = prog.val("t")
        st["_b3xt"] = tmm
        vv = prog.emit("vector", lambda E, pb=pb: E.tensor_copy(
            sm[:, :], psap(pb, 128, 8)), wait=[("t", tmm)], inc="v")
        st["_b3pv%d" % pb] = vv
        if st.get("dbg_mode") == "b3sm":
            prog.emit("sync", lambda E, tt=tt: E.dma_start(
                out=bass.AP(st["dbg_out"], tt * 8, [[EXT, 128], [1, 8]]),
                in_=sm[:, :]), wait=[("v", vv)], inc="dbgs", inc_n=16)
            prog.wait("vector", "dbgs", prog.val("dbgs"))
        prog.emit("vector", lambda E: E.reduce_max(C(0), sm[:, :], axis=AX.X), inc="v")
        prog.emit("vector", lambda E: E.tensor_scalar(
            C(1), C(0), -1.0, None, ALU.mult), inc="v")
        prog.emit("vector", lambda E: E.tensor_scalar(
            ab8[:, :], sm[:, :], -1.0, None, ALU.mult), inc="v")
        prog.emit("vector", lambda E: E.tensor_max(ab8[:, :], ab8[:, :], sm[:, :]),
                  inc="v")
        prog.emit("vector", lambda E: E.tensor_scalar(
            fb[:, :], ab8[:, :], C(0), None, ALU.max), inc="v")
        prog.emit("vector", lambda E: E.scalar_tensor_tensor(
            fb[:, :], fb[:, :], JEPS2, sm[:, :], ALU.mult, ALU.add), inc="v")
        vv = prog.emit("vector", lambda E: E.tensor_scalar(
            fb[:, :], fb[:, :], C(0), None, ALU.is_ge), inc="v")
        sv = prog.emit("scalar", lambda E: E.activation(
            ex[:, :], sm[:, :], ACT.Exp, bias=C(1)), wait=[("v", vv)], inc="s")
        prog.emit("vector", lambda E: E.tensor_mul(ex[:, :], ex[:, :], fb[:, :]),
                  wait=[("s", sv)], inc="v")
        prog.emit("vector", lambda E: E.reduce_sum(C(2), ex[:, :], axis=AX.X), inc="v")
        prog.emit("vector", lambda E: E.reciprocal(C(3), C(2)), inc="v")
        prog.emit("vector", lambda E: E.tensor_scalar(
            oh[:, :], sm[:, :], C(0), None, ALU.is_ge), inc="v")
        prog.emit("vector", lambda E: E.tensor_scalar(
            dwt[:, :], oh[:, :], C(3), None, ALU.mult), inc="v")
        prog.emit("vector", lambda E: E.scalar_tensor_tensor(
            fb[:, :], oh[:, :], -BIG, sm[:, :], ALU.mult, ALU.add), inc="v")
        prog.emit("vector", lambda E: E.reduce_max(C(4), fb[:, :], axis=AX.X), inc="v")
        prog.emit("vector", lambda E: E.tensor_scalar(
            C(5), C(4), -1.0, None, ALU.mult), inc="v")
        prog.emit("vector", lambda E: E.tensor_scalar(
            ab8[:, :], ab8[:, :], C(4), None, ALU.max), inc="v")
        prog.emit("vector", lambda E: E.scalar_tensor_tensor(
            ab8[:, :], ab8[:, :], JEPS2, sm[:, :], ALU.mult, ALU.add), inc="v")
        prog.emit("vector", lambda E: E.tensor_scalar(
            ab8[:, :], ab8[:, :], C(4), None, ALU.is_ge), inc="v")
        prog.emit("vector", lambda E: E.tensor_scalar(
            fb[:, :], fb[:, :], C(4), None, ALU.is_ge), inc="v")
        prog.emit("vector", lambda E: E.tensor_scalar(
            oh[:, :], oh[:, :], -1.0, 1.0, ALU.mult, ALU.add), inc="v")
        vv = prog.emit("vector", lambda E: E.tensor_mul(
            ab8[:, :], ab8[:, :], oh[:, :]), inc="v")
        sv = prog.emit("scalar", lambda E: E.activation(
            ex[:, :], sm[:, :], ACT.Exp, bias=C(5)), wait=[("v", vv)], inc="s")
        prog.emit("vector", lambda E: E.tensor_mul(ex[:, :], ex[:, :], ab8[:, :]),
                  wait=[("s", sv)], inc="v")
        prog.emit("vector", lambda E: E.reduce_sum(C(6), ex[:, :], axis=AX.X), inc="v")
        prog.emit("vector", lambda E: E.reciprocal(C(7), C(6)), inc="v")
        vv = prog.emit("vector", lambda E: E.scalar_tensor_tensor(
            dwt[:, :], fb[:, :], C(7), dwt[:, :], ALU.mult, ALU.add), inc="v")
        if st.get("dbg_mode") == "b3dw":
            prog.emit("sync", lambda E, tt=tt: E.dma_start(
                out=bass.AP(st["dbg_out"], tt * 8, [[EXT, 128], [1, 8]]),
                in_=dwt[:, :]), wait=[("v", vv)], inc="dbgs", inc_n=16)
            prog.wait("vector", "dbgs", prog.val("dbgs"))
        prog.wait("tensor", "v", vv)
        prog.wait("tensor", "v", tp_v)
        tv = prog.emit("tensor", lambda E, pb=pb: E.transpose(
            psap(2, 8, 128), dwt[:, :], st["ident_f"][:, :]), inc="t")
        tp_v = prog.emit("vector", lambda E, tt=tt: E.tensor_copy(
            dwr[0:8, tt * 128:tt * 128 + 128], psap(2, 8, 128)),
            wait=[("t", tv)], inc="v")
        gdma(bass.AP(ab.t, dwall.off + tt * 128, [[ab.cols, 1], [2048, 8], [1, 128]]),
             dwr[0:8, tt * 128:tt * 128 + 128], sem="dw", wait=[("v", tp_v)])

    st["prog"].scope = "b3.ex"
    bank_busy = {i: 0 for i in range(8)}
    dwr_done = prog.val("v")
    prog.wait("tensor", "dw", prog.val("dw"))
    wslb = [0, 0]
    wdb = [0, 0]
    h_busy = 0
    for e in range(8):
        for half in range(2):
            hc0 = 134 + half * 1024
            st["_b3xt"] = prog.val("t")
            xv = stream_x3n(hc0, 1024)
            gu_load = {}

            def issue_gu(kt, e=e):
                pb = (kt % 2) * 2048
                gdma(wsl[0:128, pb:pb + 1024],
                     bass.AP(di["Wg"], e * 1024 * FF + kt * 128,
                             [[FF, 128], [128 * FF, 8], [1, 128]]),
                     wait=[("t", wslb[kt % 2])], sem="w0")
                gu_load[kt] = gdma(
                     wsl[0:128, pb + 1024:pb + 2048],
                     bass.AP(di["Wu"], e * 1024 * FF + kt * 128,
                             [[FF, 128], [128 * FF, 8], [1, 128]]), sem="w0")

            issue_gu(0)
            issue_gu(1)
            for kt in range(16):
                pb = (kt % 2) * 2048
                prog.wait("tensor", "w0", gu_load[kt])
                if kt == 0:
                    prog.wait("tensor", "wx", xv)
                for sub in range(2):
                    so = sub * 512
                    prog.wait("tensor", "s", bank_busy.get("g", 0))
                    prog.wait("tensor", "v", bank_busy.get("u", 0))
                    prog.wait("tensor", "v", bank_busy[0])
                    prog.wait("tensor", "v", bank_busy[1])
                    for k8 in range(8):
                        prog.emit("tensor", lambda E, k8=k8, so=so, pb=pb: E.matmul(
                            psap(0, 128, 512),
                            wsl[0:128, pb + k8 * 128:pb + k8 * 128 + 128],
                            xnt[k8][0:128, so:so + 512],
                            start=(k8 == 0), stop=(k8 == 7)), inc="t")
                    for k8 in range(8):
                        prog.emit("tensor", lambda E, k8=k8, so=so, pb=pb: E.matmul(
                            psap(1, 128, 512),
                            wsl[0:128, pb + 1024 + k8 * 128:pb + 1024 + k8 * 128 + 128],
                            xnt[k8][0:128, so:so + 512],
                            start=(k8 == 0), stop=(k8 == 7)), inc="t")
                    tmm = prog.val("t")
                    sv = prog.emit("scalar", lambda E: E.activation(
                        sgf[:, :], psap(0, 128, 512), ACT.Silu),
                        wait=[("t", tmm), ("v", prog.val("v"))], inc="s")
                    bank_busy["g"] = sv
                    vv = prog.emit("vector", lambda E, kt=kt, so=so: E.tensor_mul(
                        View(ab, hbuf.off + kt * 1024, 128, 1024)[0:128, so:so + 512],
                        sgf[:, :], psap(1, 128, 512)),
                        wait=[("s", sv), ("t", h_busy)], inc="v")
                    bank_busy["u"] = vv
                wslb[kt % 2] = prog.val("t")
                if kt + 2 < 16:
                    issue_gu(kt + 2)
            hv = prog.val("v")
            for sub in range(2):
                so = sub * 512
                prog.wait("tensor", "v", bank_busy.get("dwb", 0))
                prog.wait("tensor", "v", bank_busy[2])
                prog.wait("tensor", "v", dwr_done)
                tv = prog.emit("tensor", lambda E, e=e, half=half, so=so: E.matmul(
                    psap(2, 128, 512), onesrow_b[0:1, :],
                    bass.AP(ab.t, dwall.off + e * 2048 + half * 1024 + so,
                            [[ab.cols, 1], [1, 512]]),
                    start=True, stop=True), wait=[("v", tp_v)], inc="t")
                vv = prog.emit("vector", lambda E: E.tensor_copy(
                    dwb[:, :], psap(2, 128, 512)), wait=[("t", tv)], inc="v")
                bank_busy["dwb"] = vv
                prog.wait("tensor", "v", vv)
                for mo in range(8):
                    prog.wait("tensor", "v", bank_busy[mo])
                wd_load = {}

                def issue_wd(kt, e=e):
                    db = 4096 + (kt % 2) * 1024
                    wd_load[kt] = gdma(
                        wsl[0:128, db:db + 1024],
                        bass.AP(di["Wd"], e * FF * 1024 + kt * 128 * 1024,
                                [[1024, 128], [0, 1], [1, 1024]]),
                        wait=[("t", wdb[kt % 2])], sem="w1")

                issue_wd(0)
                issue_wd(1)
                for kt in range(16):
                    db = 4096 + (kt % 2) * 1024
                    prog.wait("tensor", "w1", wd_load[kt])
                    for mo in range(8):
                        prog.emit("tensor", lambda E, kt=kt, mo=mo, so=so, db=db:
                                  E.matmul(
                            psap(mo, 128, 512),
                            wsl[0:128, db + mo * 128:db + mo * 128 + 128],
                            View(ab, hbuf.off + kt * 1024, 128, 1024)[0:128,
                                                                      so:so + 512],
                            start=(kt == 0), stop=(kt == 15)),
                            wait=[("v", hv)] if kt == 0 and mo == 0 else (), inc="t")
                    wdb[kt % 2] = prog.val("t")
                    if kt + 2 < 16:
                        issue_wd(kt + 2)
                for mo in range(8):
                    tmm = prog.val("t")
                    vv = prog.emit("vector", lambda E, mo=mo: E.tensor_mul(
                        scf[:, :], psap(mo, 128, 512), dwb[:, :]),
                        wait=[("t", tmm)], inc="v")
                    vv = prog.emit("vector", lambda E, mo=mo, hc0=hc0, so=so:
                                   E.tensor_add(
                        xres_t(mo)[0:128, hc0 + so:hc0 + so + 512],
                        xres_t(mo)[0:128, hc0 + so:hc0 + so + 512], scf[:, :]),
                        inc="v")
                    bank_busy[mo] = vv
            h_busy = prog.val("t")
    af.pop(); ab.pop()
    x4_done = prog.val("v")
    prog.wait("tensor", "v", x4_done)
    prog.wait("scalar", "v", x4_done)
    prog.wait("gpsimd", "v", x4_done)
    prog.wait("sync", "v", x4_done)


def emit_output(st):
    prog, af = st["prog"], st["af"]
    psap = st["psap"]
    xres_t = st["xres_t"]
    ident = st["ident_f"]
    af.push()
    otile = valloc(af, 128, 1024)
    copies = []
    for tt in range(16):
        c0 = H + tt * 128
        for k in range(8):
            pv = k % 4
            guard = copies[-4] if len(copies) >= 4 else 0
            tv = prog.emit("tensor", lambda E, k=k, c0=c0, pv=pv: E.transpose(
                psap(pv, 128, 128), xres_t(k)[:, c0:c0 + 128], ident[:, :]),
                wait=[("v", guard)], inc="t")
            vv = prog.emit("vector", lambda E, k=k, pv=pv: E.tensor_copy(
                otile[0:128, k * 128:(k + 1) * 128], psap(pv, 128, 128)),
                wait=[("t", tv)], inc="v")
            copies.append(vv)
        dst = bass.AP(st["out"], tt * 128 * D, [[D, 128], [1, D]])
        prog.emit("sync", lambda E, d=dst: E.dma_start(out=d, in_=otile[:, :]),
                  wait=[("v", copies[-1])], inc="st", inc_n=16)
        prog.wait("vector", "st", prog.val("st"))
    af.pop()


def kernel(**inputs):
    shared, per_core = host_prepare(inputs)
    nc = build_program(nblocks=4)
    in_maps = []
    for c in range(NCORES):
        m = dict(shared)
        m.update(per_core[c])
        in_maps.append(m)
    res = run_bass_kernel_spmd(nc, in_maps, core_ids=list(range(NCORES)))
    outs = [np.asarray(res.results[c]["out"]) for c in range(NCORES)]
    return np.concatenate(outs, axis=0).reshape(1, T, D)


# version 03
# speedup vs baseline: 1.2000x; 1.0195x over previous
import contextlib

import numpy as np

import concourse.bass as bass
import concourse.mybir as mybir
from concourse.bass_utils import run_bass_kernel_spmd

F32 = mybir.dt.float32
BF16 = mybir.dt.bfloat16
ALU = mybir.AluOpType
ACT = mybir.ActivationFunctionType
AX = mybir.AxisListType

NCORES = 8
D = 1024
T = 16384
TC = 2048
H = 134
EXT = TC + H
INNER = 2048
NH = 16
DH = 64
W = 128
SEG = 16
NSEG = T // SEG
NSEG_C = TC // SEG
KV0 = 6
KVLEN = EXT - KV0
E = 8
FF = 2048

NT = [512, 512, 512, 512, EXT - 2048]
NTO = [0, 512, 1024, 1536, 2048]
NN = len(NT)


class Prog:
    ENGINES = ("tensor", "vector", "scalar", "gpsimd", "sync")

    def __init__(self, nc, es):
        self.nc = nc
        self.es = es
        self.q = {e: [] for e in self.ENGINES}
        self.sems = {}
        self.cnt = {}
        self.scope = None

    def sem(self, name):
        if name not in self.sems:
            self.sems[name] = self.es.enter_context(self.nc.semaphore(name))
            self.cnt[name] = 0
        return self.sems[name]

    def val(self, name):
        self.sem(name)
        return self.cnt[name]

    AUTOSEM = {"tensor": "t", "vector": "v", "scalar": "s"}

    def emit(self, eng, fn, wait=(), inc=None, inc_n=1):
        if inc is not None and inc_n == 16:
            if not hasattr(self, "_fence"):
                self._fence = {}
            cur = self.val(inc)
            if cur > 0 and self._fence.get((eng, inc)) != cur:
                h0 = self.sem(inc)
                self.q[eng].append((self.scope,
                                    lambda E, h=h0, v=cur: E.wait_ge(h, v)))
                self._fence[(eng, inc)] = cur
        w = [(self.sem(s), v) for (s, v) in wait if v > 0]
        if eng in self.AUTOSEM:
            own = self.AUTOSEM[eng]
            prev = self.val(own)
            if prev > 0:
                w.append((self.sem(own), prev))
            assert inc is None or inc == own, (eng, inc)
            inc, inc_n = own, 1
        h = None
        ret = None
        if inc is not None:
            h = self.sem(inc)
            self.cnt[inc] += inc_n
            ret = self.cnt[inc]

        def step(E, fn=fn, w=w, h=h, n=inc_n):
            for (sh, v) in w:
                E.wait_ge(sh, v)
            inst = fn(E)
            if h is not None:
                inst.then_inc(h, n)

        self.q[eng].append((self.scope, step))
        return ret

    def wait(self, eng, semname, value):
        if value <= 0:
            return
        h = self.sem(semname)
        self.q[eng].append((self.scope, lambda E, h=h, v=value: E.wait_ge(h, v)))

    def dma(self, eng, sem, fn, wait=()):
        return self.emit(eng, fn, wait=wait, inc=sem, inc_n=16)

    def barrier(self):
        h = self.sem("bar")
        for e in self.ENGINES:
            self.cnt["bar"] += 1
            self.q[e].append((self.scope, lambda E, h=h: E.nop().then_inc(h, 1)))
        tot = self.cnt["bar"]
        for e in self.ENGINES:
            self.wait(e, "bar", tot)

    def finalize(self):
        block = self.es.enter_context(self.nc.Block())
        qs = self.q
        nc = self.nc

        def replay(E, items):
            cur, sid = None, None
            for (sc, s) in items:
                if sc != cur:
                    if cur is not None:
                        nc.leave_named_scope(cur, sid, False)
                    cur, sid = sc, None
                    if cur is not None:
                        sid, _ = nc.enter_named_scope(cur, False)
                s(E)
            if cur is not None:
                nc.leave_named_scope(cur, sid, False)

        @block.tensor
        def _(E):
            replay(E, qs["tensor"])

        @block.vector
        def _(E):
            replay(E, qs["vector"])

        @block.scalar
        def _(E):
            replay(E, qs["scalar"])

        @block.gpsimd
        def _(E):
            replay(E, qs["gpsimd"])

        @block.sync
        def _(E):
            replay(E, qs["sync"])


class Arena:
    def __init__(self, nc, name, cols, dtype):
        self.t = nc.alloc_sbuf_tensor(name, [128, cols], dtype)
        self.cols = cols
        self.off = 0
        self.marks = []
        self.peak = 0

    def alloc(self, n, align=16):
        o = (self.off + align - 1) // align * align
        assert o + n <= self.cols, f"arena {self.t.name}: {o}+{n}>{self.cols}"
        self.off = o + n
        self.peak = max(self.peak, self.off)
        return o

    def push(self):
        self.marks.append(self.off)

    def pop(self):
        self.off = self.marks.pop()


class View:
    def __init__(self, arena, off, rows, cols):
        self.a = arena
        self.off = off
        self.rows = rows
        self.cols = cols

    def __getitem__(self, idx):
        rs, cs = idx
        if isinstance(rs, int):
            rs = slice(rs, rs + 1)
        if isinstance(cs, int):
            cs = slice(cs, cs + 1)
        r0 = rs.start or 0
        r1 = self.rows if rs.stop is None else rs.stop
        c0 = cs.start or 0
        c1 = self.cols if cs.stop is None else cs.stop
        return bass.AP(self.a.t, self.off + c0 + r0 * self.a.cols,
                       [[self.a.cols, r1 - r0], [1, c1 - c0]])

    def ap3(self, r0, nrow, dims, c0=0):
        return bass.AP(self.a.t, self.off + c0 + r0 * self.a.cols,
                       [[self.a.cols, nrow]] + [list(d) for d in dims])


def valloc(arena, rows, cols, align=16):
    return View(arena, arena.alloc(cols, align), rows, cols)


def host_prepare(inp):
    x = np.asarray(inp["L0_attn_x"]).reshape(T, D)
    sh = {}

    def T_(a):
        return np.ascontiguousarray(np.asarray(a).T)

    sh["Win"] = np.ascontiguousarray(
        np.asarray(inp["L0_attn_W_in"]).reshape(8, 128, 32, 128)
        .transpose(2, 1, 0, 3).reshape(32, 128, 1024))
    for nm, key in (("Wa", "L0_attn_Wa"), ("Wb", "L0_attn_Wb"),
                    ("Wc", "L0_attn_Wc")):
        sh[nm] = np.ascontiguousarray(
            np.asarray(inp[key]).reshape(16, 128, 16, 128)
            .transpose(2, 1, 0, 3).reshape(16, 128, 2048))
    sh["Wo"] = np.ascontiguousarray(inp["L0_attn_Wo"])
    sh["Wr"] = T_(inp["L0_ffn_W_r"])
    sh["Wk"] = T_(inp["L0_ffn_W_k"])
    sh["Wv"] = T_(inp["L0_ffn_W_v"])
    sh["Wq"] = np.ascontiguousarray(inp["L1_attn_Wq"])
    sh["Wkv"] = np.ascontiguousarray(inp["L1_attn_Wkv"])
    sh["Wout"] = np.ascontiguousarray(inp["L1_attn_Wout"])
    sh["Wproj"] = np.ascontiguousarray(np.asarray(inp["L1_attn_Wproj"]).reshape(DH, 1))
    sh["Wrout"] = T_(inp["L1_ffn_router_w"])
    for nm, key in (("Wg", "L1_ffn_w_gate"), ("Wu", "L1_ffn_w_up")):
        sh[nm] = np.ascontiguousarray(
            np.asarray(inp[key]).reshape(8, 16, 128, 8, 128)
            .transpose(0, 1, 4, 3, 2).reshape(8, 16, 128, 1024))
    sh["Wd"] = np.ascontiguousarray(np.swapaxes(np.asarray(inp["L1_ffn_w_down"]), 1, 2))

    def pp(a, n):
        return np.ascontiguousarray(np.asarray(a).reshape(n, 128).T)

    sh["b_in"] = pp(inp["L0_attn_b_in"], 32)
    sh["conv_w"] = np.ascontiguousarray(np.asarray(inp["L0_attn_conv_w"]).reshape(16, 128, 3))
    sh["conv_b"] = pp(inp["L0_attn_conv_b"], 16)
    sh["ba"] = pp(inp["L0_attn_ba"], 16)
    sh["bb"] = pp(inp["L0_attn_bb"], 16)
    sh["bc"] = pp(inp["L0_attn_bc"], 16)
    sh["bo"] = pp(inp["L0_attn_bo"], 8)
    sh["gamma"] = pp(inp["L0_attn_gamma"], 8)
    sh["beta"] = pp(inp["L0_attn_beta"], 8)
    sh["mu_r"] = pp(inp["L0_ffn_mu_r"], 8)
    sh["mu_k"] = pp(inp["L0_ffn_mu_k"], 8)
    sh["bout"] = pp(inp["L1_attn_bout"], 8)
    for nm, key in (("lng_l", "L1_attn_ln_local_g"), ("lnb_l", "L1_attn_ln_local_b"),
                    ("lng_g", "L1_attn_ln_global_g"), ("lnb_g", "L1_attn_ln_global_b")):
        v = np.asarray(inp[key]).reshape(DH)
        sh[nm] = np.ascontiguousarray(np.concatenate([v, v]).reshape(128, 1))

    sh["ident"] = np.ascontiguousarray(np.eye(128, dtype=np.float32))
    prot = np.zeros((128, 128), np.float32)
    for hh in range(2):
        for d0 in range(32):
            prot[hh * 64 + d0 + 32, hh * 64 + d0] = -1.0
            prot[hh * 64 + d0, hh * 64 + d0 + 32] = 1.0
    sh["prot"] = prot
    sh["causal_own"] = (np.arange(W)[:, None] <= np.arange(W)[None, :]).astype(np.float32)
    hs = np.zeros((128, 2), np.float32)
    hs[:64, 0] = 1.0
    hs[64:, 1] = 1.0
    sh["halfsel"] = hs
    sh["sel2T"] = np.ascontiguousarray(hs.T)
    wp2 = np.zeros((128, 2), np.float32)
    wp = np.asarray(inp["L1_attn_Wproj"]).reshape(DH)
    wp2[:64, 0] = wp
    wp2[64:, 1] = wp
    sh["wproj2"] = wp2

    per = []
    for c in range(NCORES):
        d = {}
        t0 = TC * c
        lo = t0 - H
        sl = np.zeros((EXT, D), np.float32)
        a0 = max(0, lo)
        sl[a0 - lo:, :] = x[a0:t0 + TC, :]
        d["x0"] = np.ascontiguousarray(sl.T)
        d["prev_mask"] = np.full((128, 1), 0.0 if c == 0 else 1.0, np.float32)
        m = (np.arange(8) < c).astype(np.float32)
        d["foldm"] = np.ascontiguousarray(np.broadcast_to(m, (128, 8)))
        d["foldm1"] = np.ascontiguousarray(np.broadcast_to(1.0 - m, (128, 8)))
        tau = np.arange(t0 - H, t0 + TC)
        tau = np.maximum(tau, 0).astype(np.float64)
        invf = 1.0 / 10000.0 ** (np.arange(0, DH, 2, dtype=np.float64) / DH)
        ang = tau[None, :] * invf[:, None]
        cos = np.cos(ang).astype(np.float32)
        sin = np.sin(ang).astype(np.float32)
        cos64 = np.concatenate([cos, cos], 0)
        sinS = np.concatenate([sin, sin], 0)
        d["cos_t"] = np.ascontiguousarray(np.concatenate([cos64, cos64], 0))
        d["sin_t"] = np.ascontiguousarray(np.concatenate([sinS, sinS], 0))
        Mfull = np.zeros((4, 128, 8 * 512), np.float32)
        for j in range(4):
            qv = t0 + 512 * j + np.arange(512)
            for stile in range(8):
                segg = stile * 128 + np.arange(128)
                Mfull[j][:, stile * 512:(stile + 1) * 512] = (
                    qv[None, :] >= 16 * (segg[:, None] + 1))
        d["gmask"] = np.ascontiguousarray(Mfull)
        per.append(d)
    return sh, per


def build_program(nblocks=4, dbg=None, dbg_mode=None):
    nc = bass.Bass("TRN2", target_bir_lowering=False)
    es = contextlib.ExitStack()
    di = {}

    def IN(name, shape):
        di[name] = nc.dram_tensor(name, list(shape), F32, kind="ExternalInput")

    IN("x0", (D, EXT))
    IN("Win", (32, 128, 1024)); IN("Wa", (16, 128, INNER)); IN("Wb", (16, 128, INNER))
    IN("Wc", (16, 128, INNER)); IN("Wo", (INNER, D))
    IN("Wr", (D, D)); IN("Wk", (D, D)); IN("Wv", (D, D))
    IN("Wq", (D, D)); IN("Wkv", (D, D)); IN("Wout", (D, D)); IN("Wproj", (DH, 1))
    IN("Wrout", (D, E)); IN("Wg", (E, 16, 128, 1024)); IN("Wu", (E, 16, 128, 1024)); IN("Wd", (E, FF, D))
    IN("b_in", (128, 32)); IN("conv_w", (16, 128, 3)); IN("conv_b", (128, 16))
    IN("ba", (128, 16)); IN("bb", (128, 16)); IN("bc", (128, 16)); IN("bo", (128, 8))
    IN("gamma", (128, 8)); IN("beta", (128, 8))
    IN("mu_r", (128, 8)); IN("mu_k", (128, 8)); IN("bout", (128, 8))
    IN("lng_l", (128, 1)); IN("lnb_l", (128, 1)); IN("lng_g", (128, 1)); IN("lnb_g", (128, 1))
    IN("ident", (128, 128)); IN("prot", (128, 128)); IN("causal_own", (W, W))
    IN("prev_mask", (128, 1)); IN("foldm", (128, 8)); IN("foldm1", (128, 8))
    IN("cos_t", (128, EXT)); IN("sin_t", (128, EXT))
    IN("halfsel", (128, 2)); IN("sel2T", (2, 128)); IN("wproj2", (128, 2))
    IN("gmask", (4, 128, 8 * 512))

    out = nc.dram_tensor("out", [TC, D], F32, kind="ExternalOutput")
    dbg_out = None
    if dbg is not None:
        dbg_out = nc.dram_tensor("dbgo", [D, EXT], F32, kind="ExternalOutput")

    y_spill = nc.dram_tensor("y_spill", [16, 128, EXT], BF16)
    out_spill = nc.dram_tensor("out_spill", [8, 128, 2048], BF16)
    cp_spill = nc.dram_tensor("cp_spill", [16, 128, EXT], BF16)
    xn_spill = nc.dram_tensor("xn_spill", [8, 128, EXT], BF16)
    ag1_in = nc.dram_tensor("ag1_in", [128, 32], F32)
    ag1_out = nc.dram_tensor("ag1_out", [128 * 8, 32], F32)
    ag2_in = nc.dram_tensor("ag2_in", [NSEG_C, NH * DH], BF16)
    ag2_out = nc.dram_tensor("ag2_out", [NSEG, NH * DH], BF16)

    prog = Prog(nc, es)
    af = Arena(nc, "af", 26800, F32)
    ab = Arena(nc, "ab", 50800, BF16)

    xres = valloc(af, 128, 8 * EXT)

    def xres_t(k):
        return View(af, xres.off + k * EXT, 128, EXT)

    cst = valloc(af, 128, 600)
    co = {}
    _cc = [cst.off]

    def calloc(name, n):
        co[name] = _cc[0]
        _cc[0] += n

    for nm, n in (("b_in", 32), ("b_in_h", 32), ("conv_w", 48), ("conv_b", 16),
                  ("ba", 16), ("ba_h", 16), ("bb", 16), ("bb_h", 16), ("bc", 16),
                  ("bo", 8), ("gamma", 8), ("beta", 8), ("mu_r", 8), ("mu_r1", 8),
                  ("mu_k", 8), ("mu_k1", 8), ("bout", 8), ("lng_l", 1), ("lnb_l", 1),
                  ("lng_g", 1), ("lnb_g", 1), ("prev", 1), ("foldm", 8), ("foldm1", 8),
                  ("Sin", 16), ("exs", 16), ("exp", 16), ("onesrow", 128),
                  ("onescol", 1), ("ptail", 32), ("zero2", 2), ("wproj", 1), ("eps", 1),
                  ("maxk", 16), ("spare", 16)):
        calloc(nm, n)
    assert _cc[0] - cst.off <= 600, _cc[0] - cst.off

    def cv(name, j=0, rows=128, base=0, ncol=1):
        return bass.AP(af.t, co[name] + j + base * af.cols,
                       [[af.cols, rows], [1, ncol]])

    onescol_b = valloc(ab, 128, 16)
    ident_bf = valloc(ab, 128, 128)
    ident_f = valloc(af, 128, 128)
    prot_bf = valloc(ab, 128, 128)
    causal_bf = valloc(ab, W, W)

    ps = [es.enter_context(nc.psum_tensor("p%d" % i, [128, 512], F32)) for i in range(8)]

    def psap(i, rows=128, cols=512, r0=0, c0=0):
        return bass.AP(ps[i], c0 + r0 * 512, [[512, rows], [1, cols]])

    def gdma(dst, src, sem="wg", wait=()):
        return prog.dma("gpsimd", sem,
                        lambda E, d=dst, s=src: E.dma_start(out=d, in_=s), wait=wait)

    for nm, rows in (("b_in", 32), ("conv_b", 16), ("ba", 16), ("bb", 16), ("bc", 16),
                     ("bo", 8), ("gamma", 8), ("beta", 8), ("mu_r", 8), ("mu_k", 8),
                     ("bout", 8), ("lng_l", 1), ("lnb_l", 1), ("lng_g", 1), ("lnb_g", 1)):
        gdma(cv(nm, ncol=rows), bass.AP(di[nm], 0, [[rows, 128], [1, rows]]))
    for t in range(16):
        gdma(cv("conv_w", 3 * t, ncol=3), bass.AP(di["conv_w"], t * 384, [[3, 128], [1, 3]]))
    gdma(cv("prev"), di["prev_mask"].ap())
    gdma(cv("foldm", ncol=8), di["foldm"].ap())
    gdma(cv("foldm1", ncol=8), di["foldm1"].ap())
    gdma(cv("wproj", rows=DH), bass.AP(di["Wproj"], 0, [[1, DH], [1, 1]]))
    gdma(ident_bf[:, :], di["ident"].ap())
    gdma(ident_f[:, :], di["ident"].ap())
    gdma(prot_bf[:, :], di["prot"].ap())
    gdma(causal_bf[:, :], di["causal_own"].ap())
    for k in range(8):
        gdma(xres_t(k)[:, :], bass.AP(di["x0"], k * 128 * EXT, [[EXT, 128], [1, EXT]]))
    ld_done = prog.val("wg")

    prog.wait("vector", "wg", ld_done)
    prog.emit("vector", lambda E: E.memset(cv("onesrow", rows=1, ncol=128), 1.0), inc="v")
    prog.emit("vector", lambda E: E.memset(cv("onescol", ncol=1), 1.0), inc="v")
    prog.emit("vector", lambda E: E.memset(onescol_b[:, 0:1], 1.0), inc="v")
    prog.emit("vector", lambda E: E.memset(cv("zero2", ncol=2), 0.0), inc="v")
    prog.emit("vector", lambda E: E.memset(cv("eps", ncol=1), 1e-5), inc="v")
    prog.emit("vector", lambda E: E.tensor_scalar(
        cv("b_in_h", ncol=32), cv("b_in", ncol=32), 0.5, None, ALU.mult), inc="v")
    prog.emit("vector", lambda E: E.tensor_scalar(
        cv("ba_h", ncol=16), cv("ba", ncol=16), 0.5, None, ALU.mult), inc="v")
    prog.emit("vector", lambda E: E.tensor_scalar(
        cv("bb_h", ncol=16), cv("bb", ncol=16), 0.5, None, ALU.mult), inc="v")
    prog.emit("vector", lambda E: E.tensor_scalar(
        cv("mu_r1", ncol=8), cv("mu_r", ncol=8), -1.0, 1.0, ALU.mult, ALU.add), inc="v")
    prog.emit("vector", lambda E: E.tensor_scalar(
        cv("mu_k1", ncol=8), cv("mu_k", ncol=8), -1.0, 1.0, ALU.mult, ALU.add), inc="v")
    cdone = prog.val("v")
    prog.wait("tensor", "v", cdone)
    prog.wait("scalar", "v", cdone)
    prog.wait("sync", "v", cdone)

    st = dict(dbg_mode=dbg_mode, nc=nc, prog=prog, af=af, ab=ab, ps=ps, psap=psap, di=di, xres_t=xres_t,
              cv=cv, co=co, gdma=gdma, onescol_b=onescol_b, ident_f=ident_f, ident_bf=ident_bf, prot_bf=prot_bf, causal_bf=causal_bf,
              y_spill=y_spill, cp_spill=cp_spill, xn_spill=xn_spill,
              out_spill=out_spill,
              ag1_in=ag1_in, ag1_out=ag1_out, ag2_in=ag2_in, ag2_out=ag2_out,
              out=out, dbg_out=dbg_out)

    prog.scope = "b0"
    emit_b0(st)
    if nblocks >= 2:
        prog.scope = "b1"
        emit_b1(st)
    if nblocks >= 3:
        prog.scope = "b2"
        emit_b2(st)
    if nblocks >= 4:
        prog.scope = "b3"
        emit_b3(st)
    prog.scope = "fin"

    prog.barrier()
    if dbg is not None and dbg_mode is None:
        for k in range(8):
            dst = bass.AP(dbg_out, k * 128 * EXT, [[EXT, 128], [1, EXT]])
            prog.emit("sync", lambda E, d=dst, s=xres_t(k)[:, :]:
                      E.dma_start(out=d, in_=s), inc="st", inc_n=16)
    emit_output(st)
    prog.wait("sync", "st", prog.val("st"))
    prog.finalize()
    es.close()
    return nc


def emit_pre_ln(st, src_t, spill, n_list=None, dst_t=None):
    prog, af, ab, psap, cv = st["prog"], st["af"], st["ab"], st["psap"], st["cv"]
    n_list = list(n_list if n_list is not None else range(NN))
    af.push(); ab.push()
    sq = valloc(af, 128, 512)
    srow = valloc(af, 1, 1536)
    xt = [valloc(ab, 128, 512) for _ in range(2)]
    xt_busy = [0, 0]
    for n in n_list:
        nt, o = NT[n], NTO[n]
        sq_free = prog.val("t")
        for k in range(8):
            sv = prog.emit("scalar", lambda E, k=k, o=o, nt=nt: E.activation(
                sq[0:128, 0:nt], src_t(k)[:, o:o + nt], ACT.Square),
                wait=[("t", sq_free)], inc="s")
            prog.emit("tensor", lambda E, k=k, o=o, nt=nt: E.matmul(
                psap(6, 1, nt), cv("onescol"), src_t(k)[:, o:o + nt],
                start=(k == 0), stop=(k == 7)), inc="t")
            sq_free = prog.emit("tensor", lambda E, k=k, nt=nt: E.matmul(
                psap(7, 1, nt), cv("onescol"), sq[0:128, 0:nt],
                start=(k == 0), stop=(k == 7)), wait=[("s", sv)], inc="t")
        m_row = srow[0:1, 0:nt]
        r_row = bass.AP(srow.a.t, srow.off + 512, [[srow.a.cols, 1], [1, nt]])
        tmp = bass.AP(srow.a.t, srow.off + 1024, [[srow.a.cols, 1], [1, nt]])
        prog.emit("vector", lambda E, m_row=m_row, nt=nt: E.tensor_scalar(
            m_row, psap(6, 1, nt), 1.0 / D, None, ALU.mult),
            wait=[("t", sq_free)], inc="v")
        prog.emit("vector", lambda E, m_row=m_row, tmp=tmp: E.tensor_mul(
            tmp, m_row, m_row), inc="v")
        vv = prog.emit("vector", lambda E, r_row=r_row, tmp=tmp, nt=nt:
                       E.scalar_tensor_tensor(
            r_row, psap(7, 1, nt), 1.0 / D, tmp, ALU.mult, ALU.subtract), inc="v")
        sv = prog.emit("scalar", lambda E, r_row=r_row, tmp=tmp: E.activation(
            tmp, r_row, ACT.Ln, bias=cv("eps", rows=1)), wait=[("v", vv)], inc="s")
        sv = prog.emit("scalar", lambda E, r_row=r_row, tmp=tmp: E.activation(
            r_row, tmp, ACT.Exp, scale=-0.5), inc="s")
        if st.get("dbg_mode") == "lnrows" and n == n_list[0]:
            prog.emit("sync", lambda E: E.dma_start(
                out=bass.AP(st["dbg_out"], 0, [[EXT, 1], [1, 1536]]),
                in_=srow[0:1, 0:1536]), wait=[("s", sv)], inc="st", inc_n=16)
        tb = prog.emit("tensor", lambda E, nt=nt, m_row=m_row: E.matmul(
            psap(4, 128, nt), cv("onesrow", rows=1, ncol=128), m_row,
            start=True, stop=True), wait=[("s", sv)], inc="t")
        tb = prog.emit("tensor", lambda E, nt=nt, r_row=r_row: E.matmul(
            psap(5, 128, nt), cv("onesrow", rows=1, ncol=128), r_row,
            start=True, stop=True), inc="t")
        for k in range(8):
            if dst_t is not None:
                vv = prog.emit("vector", lambda E, k=k, o=o, nt=nt: E.tensor_sub(
                    xt[0][0:128, 0:nt], src_t(k)[:, o:o + nt], psap(4, 128, nt)),
                    wait=[("t", tb)], inc="v")
                vv = prog.emit("vector", lambda E, k=k, o=o, nt=nt: E.tensor_mul(
                    dst_t(k)[0:128, o:o + nt], xt[0][0:128, 0:nt], psap(5, 128, nt)),
                    inc="v")
                continue
            b = xt[k % 2]
            vv = prog.emit("vector", lambda E, k=k, b=b, o=o, nt=nt: E.tensor_sub(
                b[0:128, 0:nt], src_t(k)[:, o:o + nt], psap(4, 128, nt)),
                wait=[("t", tb), ("sy%d" % (k % 2), xt_busy[k % 2])], inc="v")
            vv = prog.emit("vector", lambda E, b=b, nt=nt: E.tensor_mul(
                b[0:128, 0:nt], b[0:128, 0:nt], psap(5, 128, nt)), inc="v")
            xt_busy[k % 2] = prog.emit("sync", lambda E, k=k, b=b, o=o, nt=nt:
                                       E.dma_start(
                out=bass.AP(spill, k * 128 * EXT + o, [[EXT, 128], [1, nt]]),
                in_=b[0:128, 0:nt]), wait=[("v", vv)], inc="sy%d" % (k % 2), inc_n=16)
        prog.wait("tensor", "v", prog.val("v"))
    out_done = ((prog.val("sy0"), prog.val("sy1")) if dst_t is None
                else prog.val("v"))
    af.pop(); ab.pop()
    return out_done


def emit_b0(st):
    prog, af, ab = st["prog"], st["af"], st["ab"]
    psap, cv, co = st["psap"], st["cv"], st["co"]
    gdma = st["gdma"]
    xres_t, di = st["xres_t"], st["di"]
    xn_spill = st["xn_spill"]
    y_spill, cp_spill = st["y_spill"], st["cp_spill"]

    af.push(); ab.push()
    sq = valloc(af, 128, 512)
    srow = valloc(af, 1, 2560)
    zt = [valloc(af, 128, 512) for _ in range(2)]
    z_bf = [valloc(ab, 128, 512) for _ in range(8)]
    xnt = [valloc(ab, 128, 512) for _ in range(2)]
    xnt_busy = [0, 0]

    onescol_b = st["onescol_b"]

    def stat_pass(src_fn, mcol, rcol, nt, ones_sum=None):
        if ones_sum is None:
            ones_sum = cv("onescol")
        sq_free = prog.val("t")
        for k in range(8):
            a, wv = src_fn(k)
            sv = prog.emit("scalar", lambda E, a=a, nt=nt: E.activation(
                sq[0:128, 0:nt], a, ACT.Square),
                wait=list(wv) + [("t", sq_free)], inc="s")
            prog.emit("tensor", lambda E, a=a, k=k, nt=nt, ones_sum=ones_sum: E.matmul(
                psap(6, 1, nt), ones_sum, a, start=(k == 0), stop=(k == 7)),
                wait=wv, inc="t")
            sq_free = prog.emit("tensor", lambda E, k=k, nt=nt: E.matmul(
                psap(7, 1, nt), cv("onescol"), sq[0:128, 0:nt],
                start=(k == 0), stop=(k == 7)), wait=[("s", sv)], inc="t")
        m_row = srow[0:1, mcol:mcol + nt]
        r_row = srow[0:1, rcol:rcol + nt]
        tmp = srow[0:1, 2048:2048 + nt]
        prog.emit("vector", lambda E, m_row=m_row, nt=nt: E.tensor_scalar(
            m_row, psap(6, 1, nt), 1.0 / D, None, ALU.mult),
            wait=[("t", sq_free)], inc="v")
        prog.emit("vector", lambda E, m_row=m_row, tmp=tmp: E.tensor_mul(
            tmp, m_row, m_row), inc="v")
        vv = prog.emit("vector", lambda E, r_row=r_row, tmp=tmp, nt=nt:
                       E.scalar_tensor_tensor(
            r_row, psap(7, 1, nt), 1.0 / D, tmp, ALU.mult, ALU.subtract), inc="v")
        sv = prog.emit("scalar", lambda E, r_row=r_row, tmp=tmp: E.activation(
            tmp, r_row, ACT.Ln, bias=cv("eps", rows=1)), wait=[("v", vv)], inc="s")
        sv = prog.emit("scalar", lambda E, r_row=r_row, tmp=tmp: E.activation(
            r_row, tmp, ACT.Exp, scale=-0.5), inc="s")
        return m_row, r_row, [("s", sv)]

    for n in range(NN):
        nt, o = NT[n], NTO[n]
        m1, r1, w1 = stat_pass(lambda k: (xres_t(k)[:, o:o + nt], ()), 0, 512, nt)
        tb = prog.emit("tensor", lambda E, m1=m1, nt=nt: E.matmul(
            psap(4, 128, nt), cv("onesrow", rows=1, ncol=128), m1,
            start=True, stop=True), wait=w1, inc="t")
        tb = prog.emit("tensor", lambda E, r1=r1, nt=nt: E.matmul(
            psap(5, 128, nt), cv("onesrow", rows=1, ncol=128), r1,
            start=True, stop=True), inc="t")
        zw = []
        for k in range(8):
            b = zt[k % 2]
            prog.emit("vector", lambda E, k=k, b=b, o=o, nt=nt: E.tensor_sub(
                b[0:128, 0:nt], xres_t(k)[:, o:o + nt], psap(4, 128, nt)),
                wait=[("t", tb)], inc="v")
            prog.emit("vector", lambda E, b=b, nt=nt: E.tensor_mul(
                b[0:128, 0:nt], b[0:128, 0:nt], psap(5, 128, nt)), inc="v")
            prog.emit("vector", lambda E, k=k, b=b, o=o, nt=nt: E.tensor_add(
                xres_t(k)[:, o:o + nt], xres_t(k)[:, o:o + nt], b[0:128, 0:nt]),
                inc="v")
            vv = prog.emit("vector", lambda E, k=k, b=b, nt=nt: E.tensor_copy(
                z_bf[k][0:128, 0:nt], b[0:128, 0:nt]), inc="v")
            zw.append(vv)
        m2, r2, w2 = stat_pass(
            lambda k: (z_bf[k][0:128, 0:nt], [("v", zw[k])]), 1024, 1536, nt,
            ones_sum=onescol_b[:, 0:1])
        tb2 = prog.emit("tensor", lambda E, m2=m2, nt=nt: E.matmul(
            psap(4, 128, nt), cv("onesrow", rows=1, ncol=128), m2,
            start=True, stop=True), wait=w2, inc="t")
        tb2 = prog.emit("tensor", lambda E, r2=r2, nt=nt: E.matmul(
            psap(5, 128, nt), cv("onesrow", rows=1, ncol=128), r2,
            start=True, stop=True), inc="t")
        for k in range(8):
            b = zt[k % 2]
            prog.emit("vector", lambda E, k=k, b=b, nt=nt: E.tensor_sub(
                b[0:128, 0:nt], z_bf[k][0:128, 0:nt], psap(4, 128, nt)),
                wait=[("t", tb2)], inc="v")
            prog.emit("vector", lambda E, b=b, nt=nt: E.tensor_mul(
                b[0:128, 0:nt], b[0:128, 0:nt], psap(5, 128, nt)), inc="v")
            xb = xnt[k % 2]
            vv = prog.emit("vector", lambda E, k=k, b=b, xb=xb, nt=nt: E.tensor_scalar(
                xb[0:128, 0:nt], b[0:128, 0:nt], cv("gamma", k), cv("beta", k),
                ALU.mult, ALU.add), wait=[("sy%d" % (k % 2), xnt_busy[k % 2])], inc="v")
            xnt_busy[k % 2] = prog.emit("sync", lambda E, k=k, xb=xb, o=o, nt=nt:
                                        E.dma_start(
                out=bass.AP(xn_spill, k * 128 * EXT + o, [[EXT, 128], [1, nt]]),
                in_=xb[0:128, 0:nt]), wait=[("v", vv)], inc="sy%d" % (k % 2), inc_n=16)
        prog.wait("tensor", "v", prog.val("v"))
    xn_done = (prog.val("sy0"), prog.val("sy1"))
    af.pop(); ab.pop()

    st["prog"].scope = "b0.p2"
    af.push()
    ab.push()
    proj = [valloc(af, 128, 516) for _ in range(2)]
    sg = [valloc(af, 128, 512) for _ in range(2)]
    t1 = valloc(af, 128, 512)
    t3 = valloc(af, 128, 512)
    u_bf = valloc(ab, 128, 16 * EXT)

    def u_t(m):
        return View(ab, u_bf.off + m * EXT, 128, EXT)

    st["u_t"] = u_t
    ab.push()
    xn_nt = [valloc(ab, 128, 512) for _ in range(8)]
    wbuf = [valloc(ab, 128, 2048) for _ in range(2)]
    wbuf_busy = [0, 0]
    psA_busy = [0, 0]
    pj_busy = [0, 0]
    prog.wait("gpsimd", "sy0", xn_done[0])
    prog.wait("gpsimd", "sy1", xn_done[1])
    for n in range(NN):
        nt, o = NTO[n] and (NT[n], NTO[n]) or (NT[n], NTO[n])
        nt, o = NT[n], NTO[n]
        xl = []
        for k in range(8):
            lv = gdma(xn_nt[k][0:128, 0:nt],
                      bass.AP(xn_spill, k * 128 * EXT + o, [[EXT, 128], [1, nt]]),
                      wait=[("t", st.get("_xn_rd_t", 0))] if k == 0 else (),
                      sem="wx")
            xl.append(lv)
        for mc in range(16):
            par = mc % 2
            wb = wbuf[par]
            lv = gdma(wb[0:128, 0:1024],
                      bass.AP(di["Win"], mc * 128 * 1024,
                              [[1024, 128], [1, 1024]]),
                      wait=[("t", wbuf_busy[par])], sem="w%d" % par)
            lv = gdma(wb[0:128, 1024:2048],
                      bass.AP(di["Win"], (16 + mc) * 128 * 1024,
                              [[1024, 128], [1, 1024]]),
                      sem="w%d" % par)
            prog.wait("tensor", "w%d" % par, lv)
            prog.wait("tensor", "wx", xl[-1])
            prog.wait("tensor", "s", psA_busy[par])
            for kt in range(8):
                prog.emit("tensor", lambda E, kt=kt, par=par, wb=wb, nt=nt: E.matmul(
                    psap(par, 128, nt), wb[0:128, kt * 128:kt * 128 + 128],
                    xn_nt[kt][0:128, 0:nt], start=(kt == 0), stop=(kt == 7)), inc="t")
            for kt in range(8):
                prog.emit("tensor", lambda E, kt=kt, par=par, wb=wb, nt=nt: E.matmul(
                    psap(2 + par, 128, nt),
                    wb[0:128, 1024 + kt * 128:1024 + kt * 128 + 128],
                    xn_nt[kt][0:128, 0:nt], start=(kt == 0), stop=(kt == 7)), inc="t")
            tmm = prog.val("t")
            wbuf_busy[par] = tmm
            st["_xn_rd_t"] = tmm
            pj, sgb = proj[par], sg[par]
            if n > 0:
                prog.emit("vector", lambda E, pj=pj, mc=mc: E.tensor_copy(
                    pj[0:128, 0:2], cv("ptail", 2 * mc, ncol=2)),
                    wait=[("v", pj_busy[par])], inc="v")
            else:
                prog.emit("vector", lambda E, pj=pj: E.tensor_copy(
                    pj[0:128, 0:2], cv("zero2", ncol=2)),
                    wait=[("v", pj_busy[par])], inc="v")
            halo_v = prog.val("v")
            sv = prog.emit("scalar", lambda E, pj=pj, mc=mc, par=par, nt=nt:
                           E.activation(pj[0:128, 2:2 + nt], psap(par, 128, nt),
                                        ACT.Identity, bias=cv("b_in", mc)),
                           wait=[("t", tmm), ("v", halo_v)], inc="s")
            sv = prog.emit("scalar", lambda E, sgb=sgb, mc=mc, par=par, nt=nt:
                           E.activation(sgb[0:128, 0:nt], psap(2 + par, 128, nt),
                                        ACT.Tanh, scale=0.5, bias=cv("b_in_h", 16 + mc)),
                           inc="s")
            psA_busy[par] = sv
            vw = [("s", sv)]
            if n == 0:
                vv = prog.emit("vector", lambda E, pj=pj: E.tensor_scalar(
                    pj[0:128, 2:136], pj[0:128, 2:136], cv("prev"), None, ALU.mult),
                    wait=vw, inc="v")
                vw = [("v", vv)]
            vv = prog.emit("vector", lambda E, pj=pj, mc=mc, nt=nt: E.tensor_copy(
                cv("ptail", 2 * mc, ncol=2), pj[0:128, nt:nt + 2]), wait=vw, inc="v")
            vv = prog.emit("vector", lambda E, pj=pj, mc=mc, nt=nt: E.tensor_scalar(
                t1[0:128, 0:nt], pj[0:128, 1:1 + nt], cv("conv_w", 3 * mc + 1),
                None, ALU.mult), inc="v")
            vv = prog.emit("vector", lambda E, pj=pj, mc=mc, nt=nt:
                           E.scalar_tensor_tensor(
                t1[0:128, 0:nt], pj[0:128, 0:nt], cv("conv_w", 3 * mc + 0),
                t1[0:128, 0:nt], ALU.mult, ALU.add), inc="v")
            vv = prog.emit("vector", lambda E, pj=pj, mc=mc, nt=nt:
                           E.scalar_tensor_tensor(
                t3[0:128, 0:nt], pj[0:128, 2:2 + nt], cv("conv_w", 3 * mc + 2),
                t1[0:128, 0:nt], ALU.mult, ALU.add), inc="v")
            sv = prog.emit("scalar", lambda E, mc=mc, nt=nt: E.activation(
                t1[0:128, 0:nt], t3[0:128, 0:nt], ACT.Silu, bias=cv("conv_b", mc)),
                wait=[("v", vv)], inc="s")
            vv = prog.emit("vector", lambda E, sgb=sgb, nt=nt: E.tensor_scalar(
                sgb[0:128, 0:nt], sgb[0:128, 0:nt], 0.5, 0.5, ALU.mult, ALU.add),
                wait=[("s", sv)], inc="v")
            vv = prog.emit("vector", lambda E, mc=mc, o=o, nt=nt, sgb=sgb: E.tensor_mul(
                u_t(mc)[0:128, o:o + nt], t1[0:128, 0:nt], sgb[0:128, 0:nt]), inc="v")
            if n == 0:
                prog.emit("vector", lambda E, mc=mc: E.memset(
                    u_t(mc)[0:128, 0:2], 0.0), inc="v")
                vv = prog.emit("vector", lambda E, mc=mc: E.tensor_scalar(
                    u_t(mc)[0:128, 2:134], u_t(mc)[0:128, 2:134], cv("prev"),
                    None, ALU.mult), inc="v")
            pj_busy[par] = vv
    u_done = prog.val("v")
    p2_end_t = prog.val("t")
    af.pop()
    ab.pop()

    st["prog"].scope = "b0.p3"
    af.push()
    ab.push()
    a_row = valloc(af, 128, EXT)
    w_row = valloc(af, 128, EXT)
    s_row = valloc(af, 128, EXT)
    ct = valloc(af, 128, 512)
    tg = valloc(af, 128, 512)
    zrow = valloc(ab, 128, EXT)
    wabc = valloc(ab, 128, 3 * 2048)
    ytile = valloc(ab, 128, EXT)
    cptile = valloc(ab, 128, EXT)
    prog.emit("vector", lambda E: E.memset(zrow[:, :], 0.0),
              wait=[("t", p2_end_t)], inc="v")
    prog.wait("tensor", "v", u_done)

    spill_done = 0
    yv_done = 0
    ev_s = 0
    ev_v = 0
    wload = {0: {}, 1: {}, 2: {}}
    slot_busy = {0: p2_end_t, 1: p2_end_t, 2: p2_end_t}
    WNAMES = ("Wa", "Wb", "Wc")

    def load_w(wi, gi):
        wload[wi][gi] = gdma(
            wabc[0:128, wi * 2048:(wi + 1) * 2048],
            bass.AP(di[WNAMES[wi]], gi * 128 * 2048, [[2048, 128], [1, 2048]]),
            wait=[("t", slot_busy[wi])], sem="w0")

    for wi in range(3):
        load_w(wi, 0)
    for gi in range(16):
        for which in range(2):
            prog.wait("tensor", "w0", wload[which][gi])
            for n in range(NN):
                nt, o = NT[n], NTO[n]
                par = n % 2
                prog.wait("tensor", "s", ev_s)
                for kt in range(16):
                    prog.emit("tensor", lambda E, kt=kt, par=par, o=o, nt=nt,
                              which=which: E.matmul(
                        psap(par, 128, nt),
                        wabc[0:128, which * 2048 + kt * 128:
                             which * 2048 + kt * 128 + 128],
                        u_t(kt)[0:128, o:o + nt], start=(kt == 0), stop=(kt == 15)),
                        inc="t")
                tmm = prog.val("t")
                bh = "ba_h" if which == 0 else "bb_h"
                sv = prog.emit("scalar", lambda E, par=par, gi=gi, nt=nt, bh=bh:
                               E.activation(tg[0:128, 0:nt], psap(par, 128, nt),
                                            ACT.Tanh, scale=0.5, bias=cv(bh, gi)),
                               wait=[("t", tmm), ("v", prog.val("v"))], inc="s")
                ev_s = sv
                if which == 0:
                    vv = prog.emit("vector", lambda E, o=o, nt=nt: E.tensor_scalar(
                        a_row[0:128, o:o + nt], tg[0:128, 0:nt], 0.5, 0.5,
                        ALU.mult, ALU.add), wait=[("s", sv)], inc="v")
                else:
                    vv = prog.emit("vector", lambda E, o=o, nt=nt: E.tensor_scalar(
                        tg[0:128, 0:nt], tg[0:128, 0:nt], 0.5, 0.5,
                        ALU.mult, ALU.add), wait=[("s", sv)], inc="v")
                    vv = prog.emit("vector", lambda E, o=o, nt=nt, gi=gi: E.tensor_mul(
                        w_row[0:128, o:o + nt], tg[0:128, 0:nt],
                        u_t(gi)[0:128, o:o + nt]), inc="v")
                ev_v = vv
            slot_busy[which] = prog.val("t")
            if gi + 1 < 16:
                load_w(which, gi + 1)
        vv = prog.emit("vector", lambda E: E.tensor_tensor_scan(
            s_row[:, :], a_row[:, :], w_row[:, :], 0.0, ALU.mult, ALU.add), inc="v")
        vv = prog.emit("vector", lambda E: E.tensor_tensor_scan(
            w_row[:, :], a_row[:, :], zrow[:, :], 1.0, ALU.mult, ALU.add), inc="v")
        vv = prog.emit("vector", lambda E, gi=gi: E.tensor_copy(
            cv("exs", gi), s_row[0:128, 2049:2050]), inc="v")
        vv = prog.emit("vector", lambda E, gi=gi: E.tensor_copy(
            cv("exp", gi), w_row[0:128, 2049:2050]), inc="v")
        scan_done = vv
        ev_s2 = 0
        prog.wait("tensor", "w0", wload[2][gi])
        for n in range(NN):
            nt, o = NT[n], NTO[n]
            par = 4 + n % 2
            prog.wait("tensor", "s", ev_s2)
            for kt in range(16):
                prog.emit("tensor", lambda E, kt=kt, par=par, o=o, nt=nt: E.matmul(
                    psap(par, 128, nt),
                    wabc[0:128, 2 * 2048 + kt * 128:2 * 2048 + kt * 128 + 128],
                    u_t(kt)[0:128, o:o + nt], start=(kt == 0), stop=(kt == 15)),
                    inc="t")
            tmm = prog.val("t")
            sv = prog.emit("scalar", lambda E, par=par, gi=gi, nt=nt: E.activation(
                ct[0:128, 0:nt], psap(par, 128, nt), ACT.Tanh, bias=cv("bc", gi)),
                wait=[("t", tmm), ("v", prog.val("v"))], inc="s")
            ev_s2 = sv
            prog.emit("vector", lambda E, o=o, nt=nt: E.tensor_mul(
                tg[0:128, 0:nt], ct[0:128, 0:nt], s_row[0:128, o:o + nt]),
                wait=[("s", sv), ("v", scan_done), ("sy", spill_done)], inc="v")
            prog.emit("vector", lambda E, o=o, nt=nt, gi=gi: E.tensor_add(
                ytile[0:128, o:o + nt], tg[0:128, 0:nt], u_t(gi)[0:128, o:o + nt]),
                inc="v")
            yv_done = prog.emit("vector", lambda E, o=o, nt=nt: E.tensor_mul(
                cptile[0:128, o:o + nt], ct[0:128, 0:nt], w_row[0:128, o:o + nt]),
                inc="v")
        prog.emit("sync", lambda E, gi=gi: E.dma_start(
            out=bass.AP(y_spill, gi * 128 * EXT, [[EXT, 128], [1, EXT]]),
            in_=ytile[:, :]), wait=[("v", yv_done)], inc="sy", inc_n=16)
        spill_done = prog.emit("sync", lambda E, gi=gi: E.dma_start(
            out=bass.AP(cp_spill, gi * 128 * EXT, [[EXT, 128], [1, EXT]]),
            in_=cptile[:, :]), inc="sy", inc_n=16)
        slot_busy[2] = prog.val("t")
        if gi + 1 < 16:
            load_w(2, gi + 1)
        ev_v = prog.val("v")
        ev_s = prog.val("s")
    p3_end_t = prog.val("t")
    af.pop()
    ab.pop()
    ab.pop()

    st["prog"].scope = "b0.ag"
    exdone = prog.val("v")
    gv = prog.emit("gpsimd", lambda E: E.dma_start(
        out=st["ag1_in"].ap(), in_=cv("exs", 0, ncol=32)),
        wait=[("v", exdone)], inc="wg", inc_n=16)
    cv1 = prog.emit("gpsimd", lambda E: E.collective_compute(
        "AllGather", ALU.bypass, replica_groups=[list(range(8))],
        ins=[st["ag1_in"].ap().opt()], outs=[st["ag1_out"].ap().opt()]),
        wait=[("wg", gv)], inc="cc", inc_n=1)
    af.push()
    agb = valloc(af, 128, 256)
    t2f = valloc(af, 128, 16)
    t3f = valloc(af, 128, 16)
    gv = prog.emit("gpsimd", lambda E: E.dma_start(
        out=agb[:, :], in_=bass.AP(st["ag1_out"], 0,
                                   [[32, 128], [128 * 32, 8], [1, 32]])),
        wait=[("cc", cv1)], inc="wg", inc_n=16)
    prog.wait("vector", "wg", gv)
    prog.emit("vector", lambda E: E.memset(cv("Sin", 0, ncol=16), 0.0), inc="v")
    for j in range(8):
        prog.emit("vector", lambda E, j=j: E.tensor_scalar(
            t2f[:, :], View(af, agb.off + j * 32 + 16, 128, 16)[:, :],
            cv("foldm", j), cv("foldm1", j), ALU.mult, ALU.add), inc="v")
        prog.emit("vector", lambda E: E.tensor_mul(
            t3f[:, :], cv("Sin", 0, ncol=16), t2f[:, :]), inc="v")
        prog.emit("vector", lambda E, j=j: E.tensor_scalar(
            t2f[:, :], View(af, agb.off + j * 32, 128, 16)[:, :],
            cv("foldm", j), None, ALU.mult), inc="v")
        prog.emit("vector", lambda E: E.tensor_add(
            cv("Sin", 0, ncol=16), t3f[:, :], t2f[:, :]), inc="v")
    fold_done = prog.val("v")
    af.pop()

    st["prog"].scope = "b0.p4"
    ab.push()
    wo_all = valloc(ab, 128, 16 * 1024)
    lv = gdma(wo_all[:, :], bass.AP(di["Wo"], 0,
                                    [[1024, 128], [128 * 1024, 16], [1, 1024]]),
              wait=[("t", p3_end_t)], sem="wx")
    ybuf = [valloc(ab, 128, 512) for _ in range(16)]
    yfin = [valloc(ab, 128, 512) for _ in range(16)]
    prog.wait("tensor", "wx", lv)
    bank_busy = [0, 0, 0, 0]
    yread_t = 0
    for n in range(NN):
        nt, o = NT[n], NTO[n]
        lw = []
        for gi in range(16):
            lv = gdma(ybuf[gi][0:128, 0:nt],
                      bass.AP(y_spill, gi * 128 * EXT + o, [[EXT, 128], [1, nt]]),
                      wait=[("sy", spill_done), ("t", yread_t)] if gi == 0 else (),
                      sem="wy")
            lv = gdma(yfin[gi][0:128, 0:nt],
                      bass.AP(cp_spill, gi * 128 * EXT + o, [[EXT, 128], [1, nt]]),
                      sem="wy")
            lw.append(lv)
        yv = []
        for gi in range(16):
            vv = prog.emit("vector", lambda E, gi=gi, nt=nt: E.scalar_tensor_tensor(
                yfin[gi][0:128, 0:nt], yfin[gi][0:128, 0:nt], cv("Sin", gi),
                ybuf[gi][0:128, 0:nt], ALU.mult, ALU.add),
                wait=[("wy", lw[-1]), ("v", fold_done)], inc="v")
            yv.append(vv)
        for mo in range(8):
            par = mo % 4
            prog.wait("tensor", "v", bank_busy[par])
            for kt in range(16):
                prog.emit("tensor", lambda E, kt=kt, mo=mo, par=par, nt=nt: E.matmul(
                    psap(par, 128, nt),
                    wo_all[0:128, kt * 1024 + mo * 128:kt * 1024 + mo * 128 + 128],
                    yfin[kt][0:128, 0:nt], start=(kt == 0), stop=(kt == 15)),
                    wait=[("v", yv[-1])] if kt == 0 else (), inc="t")
            tmm = prog.val("t")
            bank_busy[par] = prog.emit(
                "vector", lambda E, mo=mo, par=par, o=o, nt=nt:
                E.scalar_tensor_tensor(
                    xres_t(mo)[0:128, o:o + nt], psap(par, 128, nt), cv("bo", mo),
                    xres_t(mo)[0:128, o:o + nt], ALU.add, ALU.add),
                wait=[("t", tmm)], inc="v")
        yread_t = prog.val("t")
    for k in range(8):
        prog.emit("vector", lambda E, k=k: E.tensor_scalar(
            xres_t(k)[0:128, 0:H], xres_t(k)[0:128, 0:H], cv("prev"), None,
            ALU.mult), inc="v")
    ab.pop()
    x1_done = prog.val("v")
    prog.wait("tensor", "v", x1_done)
    prog.wait("scalar", "v", x1_done)
    prog.wait("gpsimd", "v", x1_done)
    prog.wait("sync", "v", x1_done)
    st["x1_done"] = x1_done


def emit_b1(st):
    prog, af, ab = st["prog"], st["af"], st["ab"]
    psap, cv, di = st["psap"], st["cv"], st["di"]
    gdma = st["gdma"]
    xres_t = st["xres_t"]
    af.push(); ab.push()
    x1n = valloc(ab, 128, 8 * EXT)

    def x1n_t(k):
        return View(ab, x1n.off + k * EXT, 128, EXT)

    emit_pre_ln(st, xres_t, None, dst_t=x1n_t)
    ln_done = prog.val("v")
    prog.wait("tensor", "v", ln_done)
    prog.wait("gpsimd", "v", ln_done)
    if st.get("dbg_mode") == "x1n":
        prog.wait("gpsimd", "v", ln_done)
        for k in range(8):
            dst = bass.AP(st["dbg_out"], k * 128 * EXT, [[EXT, 128], [1, EXT]])
            st["gdma"](dst, x1n_t(k)[:, :], sem="dbgs")
        prog.wait("gpsimd", "dbgs", prog.val("dbgs"))

    wkv = valloc(ab, 128, 2 * 8192)
    wr_s = valloc(ab, 128, 8192)
    lv = gdma(View(ab, wkv.off, 128, 8192)[:, :],
              bass.AP(di["Wk"], 0, [[1024, 128], [128 * 1024, 8], [1, 1024]]),
              sem="w0")
    lv = gdma(View(ab, wkv.off + 8192, 128, 8192)[:, :],
              bass.AP(di["Wv"], 0, [[1024, 128], [128 * 1024, 8], [1, 1024]]),
              sem="w0")
    lvr = gdma(wr_s[:, :], bass.AP(di["Wr"], 0, [[1024, 128], [128 * 1024, 8], [1, 1024]]),
               sem="w0")
    xrk = valloc(ab, 128, 8 * 512)
    ka = valloc(ab, 128, 8 * 512)
    rbuf = valloc(af, 128, 8 * 512)
    tb1 = valloc(af, 128, 512)
    tbr = valloc(af, 128, 512)
    tbk = valloc(af, 128, 512)
    prog.wait("tensor", "w0", lvr)

    def wslice(w, kt, m):
        base = wkv.off if w == "k" else (wkv.off + 8192 if w == "v" else wr_s.off)
        return View(ab, base + kt * 1024 + m * 128, 128, 128)[:, :]

    bank_s = [0] * 8
    bank_v = [0] * 8
    for n in range(NN):
        nt, o = NT[n], NTO[n]
        if n == 0:
            o, nt = 1, NT[0] - 1
        mixv = 0
        for which, (mu, mu1) in enumerate((("mu_r", "mu_r1"), ("mu_k", "mu_k1"))):
            dstv = xrk
            for k in range(8):
                prog.emit("vector", lambda E, k=k, o=o, nt=nt, mu1=mu1: E.tensor_scalar(
                    tb1[0:128, 0:nt], x1n_t(k)[0:128, o - 1:o - 1 + nt],
                    cv(mu1, k), None, ALU.mult),
                    wait=[("t", bank_v[7])] if which == 1 and k == 0 else (), inc="v")
                mixv = prog.emit("vector", lambda E, k=k, o=o, nt=nt, mu=mu, which=which:
                                 E.scalar_tensor_tensor(
                    xrk[0:128, k * 512:k * 512 + nt], x1n_t(k)[0:128, o:o + nt],
                    cv(mu, k), tb1[0:128, 0:nt], ALU.mult, ALU.add), inc="v")
            if which == 0:
                prog.wait("tensor", "v", mixv)
                for m in range(8):
                    pb = m % 4
                    prog.wait("tensor", "s", bank_s[pb])
                    for kt in range(8):
                        prog.emit("tensor", lambda E, kt=kt, m=m, pb=pb, nt=nt: E.matmul(
                            psap(pb, 128, nt), wslice("r", kt, m),
                            View(ab, xrk.off + kt * 512, 128, nt)[0:128, 0:nt],
                            start=(kt == 0), stop=(kt == 7)), inc="t")
                    tmm = prog.val("t")
                    bank_v[7] = tmm
                    sv = prog.emit("scalar", lambda E, m=m, pb=pb, nt=nt: E.activation(
                        tbr[0:128, 0:nt], psap(pb, 128, nt), ACT.Tanh, scale=0.5),
                        wait=[("t", tmm), ("v", prog.val("v"))], inc="s")
                    bank_s[pb] = sv
                    prog.emit("vector", lambda E, m=m, nt=nt: E.tensor_scalar(
                        View(af, rbuf.off + m * 512, 128, nt)[0:128, 0:nt],
                        tbr[0:128, 0:nt], 0.5, 0.5, ALU.mult, ALU.add),
                        wait=[("s", sv)], inc="v")
        prog.wait("tensor", "v", mixv)
        for m in range(8):
            pb = m % 4
            prog.wait("tensor", "s", bank_s[pb])
            for kt in range(8):
                prog.emit("tensor", lambda E, kt=kt, m=m, pb=pb, nt=nt: E.matmul(
                    psap(pb, 128, nt), wslice("k", kt, m),
                    View(ab, xrk.off + kt * 512, 128, nt)[0:128, 0:nt],
                    start=(kt == 0), stop=(kt == 7)), inc="t")
            tmm = prog.val("t")
            sv = prog.emit("scalar", lambda E, m=m, pb=pb, nt=nt: E.activation(
                tbk[0:128, 0:nt], psap(pb, 128, nt), ACT.Relu),
                wait=[("t", tmm), ("v", prog.val("v"))], inc="s")
            sv = prog.emit("scalar", lambda E, m=m, nt=nt: E.activation(
                View(ab, ka.off + m * 512, 128, nt)[0:128, 0:nt],
                tbk[0:128, 0:nt], ACT.Square), inc="s")
            bank_s[pb] = sv
        kadone = prog.val("s")
        prog.wait("tensor", "s", kadone)
        for m in range(8):
            pb = 4 + m % 2
            prog.wait("tensor", "v", bank_v[pb])
            for kt in range(8):
                prog.emit("tensor", lambda E, kt=kt, m=m, pb=pb, nt=nt: E.matmul(
                    psap(pb, 128, nt), wslice("v", kt, m),
                    View(ab, ka.off + kt * 512, 128, nt)[0:128, 0:nt],
                    start=(kt == 0), stop=(kt == 7)), inc="t")
            tmm = prog.val("t")
            vv = prog.emit("vector", lambda E, m=m, pb=pb, nt=nt: E.tensor_mul(
                tb1[0:128, 0:nt], psap(pb, 128, nt),
                View(af, rbuf.off + m * 512, 128, nt)[0:128, 0:nt]),
                wait=[("t", tmm)], inc="v")
            vv = prog.emit("vector", lambda E, m=m, o=o, nt=nt: E.tensor_add(
                xres_t(m)[0:128, o:o + nt], xres_t(m)[0:128, o:o + nt],
                tb1[0:128, 0:nt]), inc="v")
            bank_v[pb] = vv
        prog.wait("vector", "t", prog.val("t"))
    af.pop(); ab.pop()
    x2_done = prog.val("v")
    prog.wait("tensor", "v", x2_done)
    prog.wait("scalar", "v", x2_done)
    prog.wait("gpsimd", "v", x2_done)
    prog.wait("sync", "v", x2_done)


def emit_b2(st):
    prog, af, ab = st["prog"], st["af"], st["ab"]
    psap, cv, di = st["psap"], st["cv"], st["di"]
    gdma = st["gdma"]
    ps = st["ps"]
    xres_t = st["xres_t"]
    xn_spill = st["xn_spill"]
    out_spill = st["out_spill"]
    af.push(); ab.push()

    x2_done = emit_pre_ln(st, xres_t, xn_spill)
    prog.wait("gpsimd", "sy0", x2_done[0])
    prog.wait("gpsimd", "sy1", x2_done[1])

    q_rot = valloc(ab, 128, 8 * 2048)
    lkv = valloc(ab, 128, 8 * EXT)
    payload = valloc(ab, 128, 1024)
    hsel_b = valloc(ab, 128, 2)
    s2T_b = valloc(ab, 2, 128)
    wp2_b = valloc(ab, 128, 2)
    s2T_f = valloc(af, 2, 128)
    hsel_f = valloc(af, 128, 2)
    mq = valloc(af, 2, 8)
    lv = gdma(hsel_b[:, :], di["halfsel"].ap(), sem="wx")
    lv = gdma(s2T_b[0:2, :], di["sel2T"].ap(), sem="wx")
    lv = gdma(wp2_b[:, :], di["wproj2"].ap(), sem="wx")
    lv = gdma(s2T_f[0:2, :], di["sel2T"].ap(), sem="wx")
    lv = gdma(hsel_f[:, :], di["halfsel"].ap(), sem="wx")
    tabs = lv
    prog.wait("tensor", "wx", tabs)
    prog.wait("vector", "wx", tabs)
    prog.emit("vector", lambda E: E.memset(cv("spare", ncol=1, rows=2), 1.0), inc="v")
    ones2_set = prog.val("v")

    def qr(hp):
        return View(ab, q_rot.off + hp * 2048, 128, 2048)

    def lk(hp):
        return View(ab, lkv.off + hp * EXT, 128, EXT)

    KVC = [(6, 512), (518, 512), (1030, 512), (1542, 512), (2054, 128)]
    OWN = [(134 + 512 * c, 512) for c in range(4)]

    st["prog"].scope = "b2.l1"
    af.push(); ab.push()
    cos_b = valloc(ab, 128, EXT)
    sin_b = valloc(ab, 128, EXT)
    xnt = [valloc(ab, 128, 512) for _ in range(8)]
    kvrot = valloc(ab, 128, EXT)
    wslc = [valloc(ab, 128, 1024) for _ in range(2)]
    kvb = valloc(ab, 128, 512)
    srow = valloc(af, 2, 2048)
    s2 = valloc(af, 2, 2048)
    sc1 = valloc(af, 128, 512)
    sc2 = valloc(af, 128, 512)
    segt = valloc(af, 128, 35)
    comp = valloc(af, 128, 128)
    lv = gdma(cos_b[:, :], di["cos_t"].ap(), sem="wx")
    lv = gdma(sin_b[:, :], di["sin_t"].ap(), sem="wx")
    tabs2 = lv
    prog.wait("vector", "wx", tabs2)

    wq_busy = [0, 0]
    bank_v = {i: 0 for i in range(8)}
    bank_s = {i: 0 for i in range(8)}
    st["_b2xt"] = 0

    def stream_x2n(c0, nt):
        lvv = None
        for k in range(8):
            lvv = gdma(xnt[k][0:128, 0:nt],
                       bass.AP(xn_spill, k * 128 * EXT + c0, [[EXT, 128], [1, nt]]),
                       wait=[("t", st["_b2xt"])] if k == 0 else (), sem="wx")
        return lvv

    def rot_sweep(hp, wname, chunks, dst_fn, scale_q):
        par = 0 if wname == "Wkv" else 1
        wb = wslc[par]
        lvw = gdma(wb[:, :], bass.AP(di[wname], hp * 128,
                                     [[1024, 128], [128 * 1024, 8], [1, 128]]),
                   wait=[("t", wq_busy[par])], sem="w%d" % par)
        if scale_q:
            prog.emit("vector", lambda E, wb=wb: E.tensor_scalar(
                wb[:, :], wb[:, :], 0.125, None, ALU.mult),
                wait=[("w%d" % par, lvw)], inc="v")
            prog.wait("tensor", "v", prog.val("v"))
        else:
            prog.wait("tensor", "w%d" % par, lvw)
        for (c0, nt) in chunks:
            xv = stream_x2n(c0, nt)
            prog.wait("tensor", "wx", xv)
            prog.wait("tensor", "s", bank_s[0])
            prog.wait("tensor", "v", bank_v[0])
            for kt in range(8):
                prog.emit("tensor", lambda E, kt=kt, nt=nt, wb=wb: E.matmul(
                    psap(0, 128, nt), wb[0:128, kt * 128:kt * 128 + 128],
                    xnt[kt][0:128, 0:nt], start=(kt == 0), stop=(kt == 7)), inc="t")
            tmm = prog.val("t")
            st["_b2xt"] = tmm
            sv = prog.emit("scalar", lambda E, nt=nt: E.activation(
                kvb[0:128, 0:nt], psap(0, 128, nt), ACT.Copy),
                wait=[("t", tmm), ("v", bank_v[1])], inc="s")
            bank_s[0] = sv
            prog.wait("tensor", "s", sv)
            prog.wait("tensor", "v", bank_v[1])
            tv = prog.emit("tensor", lambda E, nt=nt: E.matmul(
                psap(1, 128, nt), st["prot_bf"][:, :], kvb[0:128, 0:nt],
                start=True, stop=True), inc="t")
            vv = prog.emit("vector", lambda E, c0=c0, nt=nt: E.tensor_mul(
                sc1[0:128, 0:nt], kvb[0:128, 0:nt], cos_b[0:128, c0:c0 + nt]),
                wait=[("s", sv)], inc="v")
            vv = prog.emit("vector", lambda E, c0=c0, nt=nt: E.tensor_mul(
                sc2[0:128, 0:nt], psap(1, 128, nt), sin_b[0:128, c0:c0 + nt]),
                wait=[("t", tv)], inc="v")
            vv = prog.emit("vector", lambda E, c0=c0, nt=nt, dst=dst_fn(c0, nt):
                           E.tensor_add(dst, sc1[0:128, 0:nt], sc2[0:128, 0:nt]),
                           inc="v")
            bank_v[1] = vv
        wq_busy[par] = prog.val("t")
        return prog.val("v")

    def ln64_stats(src_ap_fn, chunks, dst_write, ones_sum=None):
        osum = ones_sum if ones_sum is not None else hsel_b
        for (c0, nt) in chunks:
            srcap = src_ap_fn(c0, nt)
            sv = prog.emit("scalar", lambda E, srcap=srcap, nt=nt: E.activation(
                sc1[0:128, 0:nt], srcap, ACT.Square),
                wait=[("v", prog.val("v"))], inc="s")
            prog.wait("tensor", "s", sv)
            prog.wait("tensor", "v", bank_v[3])
            tv = prog.emit("tensor", lambda E, srcap=srcap, nt=nt, osum=osum: E.matmul(
                psap(2, 2, nt), osum[:, :], srcap, start=True, stop=True), inc="t")
            tv = prog.emit("tensor", lambda E, nt=nt: E.matmul(
                psap(3, 2, nt), hsel_f[:, :], sc1[0:128, 0:nt],
                start=True, stop=True), inc="t")
            m_row = srow[0:2, 0:nt]
            r_row = srow[0:2, 512:512 + nt]
            tmp = srow[0:2, 1024:1024 + nt]
            vv = prog.emit("vector", lambda E, m_row=m_row, nt=nt: E.tensor_scalar(
                m_row, psap(2, 2, nt), 1.0 / DH, None, ALU.mult),
                wait=[("t", tv)], inc="v")
            vv = prog.emit("vector", lambda E, m_row=m_row, tmp=tmp: E.tensor_mul(
                tmp, m_row, m_row), inc="v")
            vv = prog.emit("vector", lambda E, r_row=r_row, tmp=tmp, nt=nt:
                           E.scalar_tensor_tensor(
                r_row, psap(3, 2, nt), 1.0 / DH, tmp, ALU.mult, ALU.subtract),
                inc="v")
            sv = prog.emit("scalar", lambda E, r_row=r_row, tmp=tmp: E.activation(
                tmp, r_row, ACT.Ln, bias=cv("eps", rows=2)),
                wait=[("v", vv)], inc="s")
            sv = prog.emit("scalar", lambda E, r_row=r_row, tmp=tmp: E.activation(
                r_row, tmp, ACT.Exp, scale=-0.5), inc="s")
            prog.wait("tensor", "s", sv)
            tv = prog.emit("tensor", lambda E, m_row=m_row, nt=nt: E.matmul(
                psap(2, 128, nt), s2T_f[0:2, :], m_row, start=True, stop=True),
                inc="t")
            tv = prog.emit("tensor", lambda E, r_row=r_row, nt=nt: E.matmul(
                psap(3, 128, nt), s2T_f[0:2, :], r_row, start=True, stop=True),
                inc="t")
            vv = prog.emit("vector", lambda E, srcap=srcap, nt=nt: E.tensor_sub(
                sc1[0:128, 0:nt], srcap, psap(2, 128, nt)),
                wait=[("t", tv)], inc="v")
            vv = prog.emit("vector", lambda E, nt=nt: E.tensor_mul(
                sc1[0:128, 0:nt], sc1[0:128, 0:nt], psap(3, 128, nt)), inc="v")
            dst_write(c0, nt)
            bank_v[3] = prog.val("v")

    for hp in range(8):
        rot_sweep(hp, "Wkv", KVC, lambda c0, nt: kvrot[0:128, c0:c0 + nt], False)
        rot_sweep(hp, "Wq", OWN,
                  lambda c0, nt, hp=hp: qr(hp)[0:128, c0 - 134:c0 - 134 + nt], True)
        qv = prog.val("v")
        for ci, (c0, nt) in enumerate(OWN):
            sv = prog.emit("scalar", lambda E, c0=c0, nt=nt, hp=hp: E.activation(
                sc1[0:128, 0:nt], qr(hp)[0:128, c0 - 134:c0 - 134 + nt], ACT.Square),
                wait=[("v", qv), ("t", prog.val("t"))], inc="s")
            prog.wait("tensor", "s", sv)
            prog.wait("tensor", "v", bank_v[2])
            tv = prog.emit("tensor", lambda E, nt=nt: E.matmul(
                psap(2, 2, nt), hsel_f[:, :], sc1[0:128, 0:nt],
                start=True, stop=True), inc="t")
            bank_v[2] = prog.emit("vector", lambda E, ci=ci, nt=nt: E.reduce_max(
                srow[0:2, ci:ci + 1], psap(2, 2, nt), axis=AX.X),
                wait=[("t", tv)], inc="v")
        vv = prog.emit("vector", lambda E: E.reduce_max(
            srow[0:2, 4:5], srow[0:2, 0:4], axis=AX.X), inc="v")
        sv = prog.emit("scalar", lambda E: E.activation(
            srow[0:2, 1:2], srow[0:2, 4:5], ACT.Ln, bias=cv("eps", rows=2)),
            wait=[("v", vv)], inc="s")
        sv = prog.emit("scalar", lambda E, hp=hp: E.activation(
            mq[0:2, hp:hp + 1], srow[0:2, 1:2], ACT.Exp, scale=0.5), inc="s")
        ln64_stats(lambda c0, nt: kvrot[0:128, c0:c0 + nt], KVC,
                   lambda c0, nt, hp=hp: prog.emit(
                       "vector", lambda E, c0=c0, nt=nt, hp=hp: E.tensor_scalar(
                           lk(hp)[0:128, c0:c0 + nt], sc1[0:128, 0:nt],
                           cv("lng_l"), cv("lnb_l"), ALU.mult, ALU.add), inc="v"))
        for ci, (c0, nt) in enumerate(OWN):
            prog.wait("tensor", "v", bank_v[4])
            tv = prog.emit("tensor", lambda E, c0=c0, nt=nt: E.matmul(
                psap(4, 2, nt), wp2_b[:, :], kvrot[0:128, c0:c0 + nt],
                start=True, stop=True), inc="t")
            vv = prog.emit("vector", lambda E, ci=ci, nt=nt: E.tensor_copy(
                s2[0:2, ci * 512:ci * 512 + nt], psap(4, 2, nt)),
                wait=[("t", tv)], inc="v")
            bank_v[4] = vv
        sdone = prog.val("v")
        segdma = st.get("_segdma", 0)
        for h in range(2):
            dv = prog.emit("sync", lambda E, h=h: E.dma_start(
                out=bass.AP(af.t, segt.off, [[af.cols, 128], [1, 16]]),
                in_=bass.AP(af.t, s2.off + h * af.cols,
                            [[af.cols, 1], [16, 128], [1, 16]])),
                wait=[("v", prog.val("v"))], inc="sg", inc_n=16)
            dgv = prog.val("sg")
            vv = prog.emit("vector", lambda E: E.reduce_max(
                View(af, segt.off + 16, 128, 1)[0:128, 0:1],
                View(af, segt.off, 128, 16)[0:128, 0:16], axis=AX.X),
                wait=[("sg", dgv)], inc="v")
            vv = prog.emit("vector", lambda E: E.tensor_scalar(
                View(af, segt.off + 16, 128, 1)[0:128, 0:1],
                View(af, segt.off + 16, 128, 1)[0:128, 0:1], -1.0, None, ALU.mult),
                inc="v")
            sv = prog.emit("scalar", lambda E: E.activation(
                View(af, segt.off + 17, 128, 16)[0:128, 0:16],
                View(af, segt.off, 128, 16)[0:128, 0:16], ACT.Exp,
                bias=View(af, segt.off + 16, 128, 1)[0:128, 0:1],
                accum_out=View(af, segt.off + 33, 128, 1)[0:128, 0:1]),
                wait=[("v", vv)], inc="s")
            vv = prog.emit("vector", lambda E: E.reciprocal(
                View(af, segt.off + 34, 128, 1)[0:128, 0:1],
                View(af, segt.off + 33, 128, 1)[0:128, 0:1]),
                wait=[("s", sv)], inc="v")
            vv = prog.emit("vector", lambda E: E.tensor_scalar(
                View(af, segt.off + 17, 128, 16)[0:128, 0:16],
                View(af, segt.off + 17, 128, 16)[0:128, 0:16],
                View(af, segt.off + 34, 128, 1)[0:128, 0:1], None, ALU.mult),
                inc="v")
            dv = prog.emit("sync", lambda E, h=h: E.dma_start(
                out=bass.AP(af.t, s2.off + h * af.cols,
                            [[af.cols, 1], [16, 128], [1, 16]]),
                in_=bass.AP(af.t, segt.off + 17, [[af.cols, 128], [1, 16]])),
                wait=[("v", vv)], inc="sg", inc_n=16)
        st["_segdma"] = prog.val("sg")
        prog.wait("tensor", "sg", st["_segdma"])
        prog.wait("vector", "sg", st["_segdma"])
        for ci, (c0, nt) in enumerate(OWN):
            prog.wait("tensor", "v", bank_v[5])
            tv = prog.emit("tensor", lambda E, ci=ci, nt=nt: E.matmul(
                psap(5, 128, nt), s2T_f[0:2, :], s2[0:2, ci * 512:ci * 512 + nt],
                start=True, stop=True), inc="t")
            vv = prog.emit("vector", lambda E, c0=c0, nt=nt: E.tensor_mul(
                sc1[0:128, 0:nt], kvrot[0:128, c0:c0 + nt], psap(5, 128, nt)),
                wait=[("t", tv)], inc="v")
            vv = prog.emit("vector", lambda E, ci=ci: E.reduce_sum(
                comp[0:128, ci * 32:ci * 32 + 32],
                bass.AP(af.t, sc1.off, [[af.cols, 128], [16, 32], [1, 16]]),
                axis=AX.X), inc="v")
            bank_v[5] = vv
        def wr_gkv(c0, nt, hp=hp):
            vv = prog.emit("vector", lambda E, nt=nt: E.tensor_scalar(
                sc2[0:128, 0:nt], sc1[0:128, 0:nt], cv("lng_g"), cv("lnb_g"),
                ALU.mult, ALU.add), inc="v")
        ln64_stats(lambda c0, nt: comp[0:128, 0:128], [(0, 128)], wr_gkv,
                   ones_sum=hsel_f)
        prog.wait("tensor", "v", prog.val("v"))
        tv = prog.emit("tensor", lambda E: E.transpose(
            psap(0, 128, 128), sc2[0:128, 0:128], st["ident_f"][:, :]),
            wait=[("s", bank_s[0]), ("v", bank_v[1])], inc="t")
        vv = prog.emit("vector", lambda E, hp=hp: E.tensor_copy(
            payload[0:128, hp * 128:hp * 128 + 128], psap(0, 128, 128)),
            wait=[("t", tv)], inc="v")
        bank_v[0] = vv
    pay_done = prog.val("v")
    if st.get("dbg_mode") in ("b2q", "b2lkv", "b2pay"):
        prog.wait("gpsimd", "v", pay_done)
        mode = st["dbg_mode"]
        for hp in range(8):
            if mode == "b2q":
                srcv = qr(hp)[:, 0:2048]
                ncol = 2048
            elif mode == "b2lkv":
                srcv = lk(hp)[:, 0:EXT]
                ncol = EXT
            else:
                srcv = payload[0:128, hp * 128:hp * 128 + 128]
                ncol = 128
            dst = bass.AP(st["dbg_out"], hp * 128 * EXT, [[EXT, 128], [1, ncol]])
            gdma(dst, srcv, sem="dbgs")
        prog.wait("gpsimd", "dbgs", prog.val("dbgs"))
    af.pop(); ab.pop()

    gv = prog.emit("gpsimd", lambda E: E.dma_start(
        out=st["ag2_in"].ap(), in_=payload[:, :]),
        wait=[("v", pay_done)], inc="wg", inc_n=16)
    cv2 = prog.emit("gpsimd", lambda E: E.collective_compute(
        "AllGather", ALU.bypass, replica_groups=[list(range(8))],
        ins=[st["ag2_in"].ap().opt()], outs=[st["ag2_out"].ap().opt()]),
        wait=[("wg", gv)], inc="cc", inc_n=1)
    prog.wait("sync", "cc", cv2)

    st["prog"].scope = "b2.l2"
    af.push(); ab.push()
    gkv_fm = valloc(ab, 128, NSEG)
    vtokg = [valloc(ab, 128, 130) for _ in range(8)]
    vtokl = [valloc(ab, 128, 130) for _ in range(17)]
    gmask = valloc(ab, 128, 4096)
    attn = valloc(ab, 128, 4096)
    attl = valloc(ab, 128, 256)
    outb = valloc(ab, 128, 512)
    lsc = valloc(af, 128, 512)
    outf = valloc(af, 128, 512)
    rrow = valloc(af, 1, 512)
    gn = valloc(af, 2, 8)
    nbuf = valloc(af, 128, 1)
    for i in range(8):
        prog.emit("vector", lambda E, i=i: E.memset(vtokg[i][0:128, 64:65], 1.0), inc="v")
        prog.emit("vector", lambda E, i=i: E.memset(vtokg[i][0:128, 129:130], 1.0), inc="v")
    for i in range(17):
        prog.emit("vector", lambda E, i=i: E.memset(vtokl[i][0:128, 64:65], 1.0), inc="v")
        prog.emit("vector", lambda E, i=i: E.memset(vtokl[i][0:128, 129:130], 1.0), inc="v")
    ones_set = prog.val("v")
    prog.wait("sync", "v", ones_set)

    out_dma = 0
    gm_busy = 0
    for hp in range(8):
        dv = prog.emit("sync", lambda E, hp=hp: E.dma_start_transpose(
            gkv_fm[0:128, 0:NSEG],
            bass.AP(st["ag2_out"], hp * 128, [[1024, NSEG], [1, 128]])),
            wait=[("t", st.get("_l2t", 0))], inc="sg", inc_n=16)
        for stt in range(8):
            dv = prog.emit("sync", lambda E, hp=hp, stt=stt: E.dma_start(
                out=bass.AP(ab.t, vtokg[stt].off, [[ab.cols, 128], [65, 2], [1, 64]]),
                in_=bass.AP(st["ag2_out"], stt * 128 * 1024 + hp * 128,
                            [[1024, 128], [64, 2], [1, 64]])), inc="sg", inc_n=16)
        gld = prog.val("sg")
        for kw in range(17):
            sv = prog.emit("scalar", lambda E, kw=kw, hp=hp: E.activation(
                lsc[0:128, 0:128], lk(hp)[0:128, 6 + 128 * kw:6 + 128 * kw + 128],
                ACT.Copy), wait=[("t", prog.val("t")), ("sg", gld)], inc="s")
            prog.wait("tensor", "s", sv)
            tv = prog.emit("tensor", lambda E: E.transpose(
                psap(6, 128, 128), lsc[0:128, 0:128], st["ident_f"][:, :]),
                wait=[("v", prog.val("v"))], inc="t")
            vv = prog.emit("vector", lambda E, kw=kw: E.tensor_copy(
                bass.AP(ab.t, vtokl[kw].off, [[ab.cols, 128], [65, 2], [1, 64]]),
                bass.AP(ps[6], 0, [[512, 128], [64, 2], [1, 64]])),
                wait=[("t", tv)], inc="v")
        for ci in range(2):
            sv = prog.emit("scalar", lambda E, ci=ci: E.activation(
                lsc[0:128, 0:512], gkv_fm[0:128, ci * 512:ci * 512 + 512],
                ACT.Square), wait=[("t", prog.val("t")), ("sg", gld),
                                   ("v", prog.val("v"))], inc="s")
            prog.wait("tensor", "s", sv)
            tv = prog.emit("tensor", lambda E: E.matmul(
                psap(7, 2, 512), hsel_f[:, :], lsc[0:128, 0:512],
                start=True, stop=True), inc="t")
            prog.emit("vector", lambda E, ci=ci: E.reduce_max(
                gn[0:2, ci:ci + 1], psap(7, 2, 512), axis=AX.X),
                wait=[("t", tv)], inc="v")
        for ci, (c0, nt) in enumerate(KVC):
            sv = prog.emit("scalar", lambda E, c0=c0, nt=nt, hp=hp: E.activation(
                lsc[0:128, 0:nt], lk(hp)[0:128, c0:c0 + nt], ACT.Square),
                wait=[("t", prog.val("t")), ("v", prog.val("v"))], inc="s")
            prog.wait("tensor", "s", sv)
            tv = prog.emit("tensor", lambda E, nt=nt: E.matmul(
                psap(7, 2, nt), hsel_f[:, :], lsc[0:128, 0:nt],
                start=True, stop=True), inc="t")
            prog.emit("vector", lambda E, ci=ci, nt=nt: E.reduce_max(
                gn[0:2, 2 + ci:3 + ci], psap(7, 2, nt), axis=AX.X),
                wait=[("t", tv)], inc="v")
        vv = prog.emit("vector", lambda E: E.reduce_max(
            gn[0:2, 7:8], gn[0:2, 0:7], axis=AX.X), inc="v")
        vv = prog.emit("vector", lambda E: E.tensor_copy(
            gn[0:2, 2:3], gn[0:2, 7:8]), inc="v")
        sv = prog.emit("scalar", lambda E: E.activation(
            gn[0:2, 3:4], gn[0:2, 2:3], ACT.Ln, bias=cv("eps", rows=2)),
            wait=[("v", vv)], inc="s")
        sv = prog.emit("scalar", lambda E: E.activation(
            gn[0:2, 4:5], gn[0:2, 3:4], ACT.Exp, scale=0.5), inc="s")
        vv = prog.emit("vector", lambda E, hp=hp: E.tensor_mul(
            gn[0:2, 5:6], gn[0:2, 4:5], mq[0:2, hp:hp + 1]),
            wait=[("s", sv)], inc="v")
        prog.wait("tensor", "v", prog.val("v"))
        tv = prog.emit("tensor", lambda E: E.matmul(
            psap(7, 1, 1), cv("spare", rows=2), gn[0:2, 5:6],
            start=True, stop=True), inc="t")
        vv = prog.emit("vector", lambda E: E.tensor_scalar(
            gn[0:1, 6:7], bass.AP(ps[7], 0, [[512, 1], [1, 1]]),
            -1.0, None, ALU.mult), wait=[("t", tv)], inc="v")
        prog.wait("tensor", "v", vv)
        tv = prog.emit("tensor", lambda E: E.matmul(
            psap(7, 128, 1), cv("onesrow", rows=1, ncol=128),
            gn[0:1, 6:7], start=True, stop=True), inc="t")
        vv = prog.emit("vector", lambda E: E.tensor_copy(
            nbuf[0:128, 0:1], bass.AP(ps[7], 0, [[512, 128], [1, 1]])),
            wait=[("t", tv)], inc="v")
        nb_done = prog.val("v")
        for j in range(4):
            mv = gdma(gmask[:, :], bass.AP(di["gmask"], j * 128 * 4096,
                                           [[4096, 128], [1, 4096]]),
                      wait=[("v", gm_busy)], sem="wx")
            for h in range(2):
                hb = 64 * h
                for stt in range(8):
                    pb = stt % 2
                    prog.wait("tensor", "s", bank_s.get(("g", pb), 0))
                    tv = prog.emit("tensor", lambda E, stt=stt, h=h, hp=hp, j=j, pb=pb,
                                   hb=hb: E.matmul(
                        psap(pb, 128, 512),
                        View(ab, gkv_fm.off + stt * 128, 128, 128)[hb:hb + 64, 0:128],
                        qr(hp)[hb:hb + 64, j * 512:j * 512 + 512],
                        start=True, stop=True), inc="t")
                    sv = prog.emit("scalar", lambda E, stt=stt, pb=pb: E.activation(
                        View(ab, attn.off + stt * 512, 128, 512)[0:128, 0:512],
                        psap(pb, 128, 512), ACT.Exp, bias=nbuf[0:128, 0:1]),
                        wait=[("t", tv), ("v", nb_done)], inc="s")
                    bank_s[("g", pb)] = sv
                vv = prog.emit("vector", lambda E: E.tensor_mul(
                    attn[0:128, 0:4096], attn[0:128, 0:4096], gmask[0:128, 0:4096]),
                    wait=[("s", prog.val("s")), ("wx", mv)], inc="v")
                gm_busy_h = vv
                prog.wait("tensor", "v", vv)
                prog.wait("tensor", "s", bank_s.get("out", 0))
                for stt in range(8):
                    prog.emit("tensor", lambda E, stt=stt, h=h, hb=hb: E.matmul(
                        psap(6, 65, 512),
                        View(ab, vtokg[stt].off, 128, 130)[0:128, 65 * h:65 * h + 65],
                        View(ab, attn.off + stt * 512, 128, 512)[0:128, 0:512],
                        start=(stt == 0), stop=False), inc="t")
                for w in range(4):
                    wq = 4 * j + w
                    kvc_prev = 6 + 128 * wq
                    for t2, kc in ((0, kvc_prev), (1, kvc_prev + 128)):
                        pb = 2 + t2
                        prog.wait("tensor", "s", bank_s.get(("l", pb), 0))
                        tv = prog.emit("tensor", lambda E, kc=kc, h=h, hp=hp, wq=wq,
                                       pb=pb, hb=hb: E.matmul(
                            psap(pb, 128, 128), lk(hp)[hb:hb + 64, kc:kc + 128],
                            qr(hp)[hb:hb + 64, wq * 128:wq * 128 + 128],
                            start=True, stop=True), inc="t")
                        sv = prog.emit("scalar", lambda E, t2=t2, pb=pb: E.activation(
                            attl[0:128, t2 * 128:t2 * 128 + 128],
                            psap(pb, 128, 128), ACT.Exp, bias=nbuf[0:128, 0:1]),
                            wait=[("t", tv)], inc="s")
                        bank_s[("l", pb)] = sv
                    vv = prog.emit("vector", lambda E: E.tensor_mul(
                        attl[0:128, 128:256], attl[0:128, 128:256],
                        st["causal_bf"][:, :]), wait=[("s", prog.val("s"))], inc="v")
                    if wq == 0:
                        vv = prog.emit("vector", lambda E: E.tensor_scalar(
                            attl[0:128, 0:128], attl[0:128, 0:128],
                            cv("prev"), None, ALU.mult), inc="v")
                    prog.wait("tensor", "v", vv)
                    for t2 in (0, 1):
                        prog.emit("tensor", lambda E, wq=wq, w=w, t2=t2, h=h: E.matmul(
                            bass.AP(ps[6], w * 128, [[512, 65], [1, 128]]),
                            View(ab, vtokl[wq + t2].off, 128, 130)[0:128,
                                                                   65 * h:65 * h + 65],
                            attl[0:128, t2 * 128:t2 * 128 + 128],
                            start=False, stop=(w == 3 and t2 == 1)), inc="t")
                tmm = prog.val("t")
                vv = prog.emit("vector", lambda E: E.tensor_copy(
                    View(af, outf.off, 128, 512)[0:65, 0:512],
                    bass.AP(ps[6], 0, [[512, 65], [1, 512]])),
                    wait=[("t", tmm)], inc="v")
                vv = prog.emit("vector", lambda E: E.reciprocal(
                    rrow[0:1, 0:512],
                    View(af, outf.off, 128, 512)[64:65, 0:512]), inc="v")
                prog.wait("tensor", "v", vv)
                tv = prog.emit("tensor", lambda E: E.matmul(
                    psap(7, 64, 512), cv("onesrow", rows=1, ncol=64),
                    rrow[0:1, 0:512], start=True, stop=True), inc="t")
                vv = prog.emit("vector", lambda E, h=h: E.tensor_mul(
                    outb[64 * h:64 * h + 64, 0:512],
                    View(af, outf.off, 128, 512)[0:64, 0:512], psap(7, 64, 512)),
                    wait=[("t", tv)], inc="v")
                bank_s["out"] = prog.val("s")
            gm_busy = prog.val("v")
            out_dma = prog.emit("sync", lambda E, hp=hp, j=j: E.dma_start(
                out=bass.AP(out_spill, hp * 128 * 2048 + j * 512,
                            [[2048, 128], [1, 512]]),
                in_=outb[0:128, 0:512]), wait=[("v", gm_busy)], inc="sy", inc_n=16)
            prog.wait("vector", "sy", out_dma)
        st["_l2t"] = prog.val("t")
    af.pop(); ab.pop()

    st["prog"].scope = "b2.l3"
    af.push(); ab.push()
    wout = valloc(ab, 128, 8192)
    otiles = [valloc(ab, 128, 512) for _ in range(8)]
    lv = gdma(wout[:, :], bass.AP(di["Wout"], 0,
                                  [[1024, 128], [128 * 1024, 8], [1, 1024]]),
              wait=[("t", prog.val("t"))], sem="wx")
    prog.wait("tensor", "wx", lv)
    bank_vv = [0, 0, 0, 0]
    ot_busy = 0
    for j in range(4):
        lw = None
        for kt in range(8):
            lw = gdma(otiles[kt][0:128, 0:512],
                      bass.AP(out_spill, kt * 128 * 2048 + j * 512,
                              [[2048, 128], [1, 512]]),
                      wait=[("sy", out_dma), ("t", ot_busy)] if kt == 0 else (),
                      sem="wy")
        prog.wait("tensor", "wy", lw)
        for mo in range(8):
            pb = mo % 4
            prog.wait("tensor", "v", bank_vv[pb])
            for kt in range(8):
                prog.emit("tensor", lambda E, kt=kt, mo=mo, pb=pb: E.matmul(
                    psap(pb, 128, 512),
                    wout[0:128, kt * 1024 + mo * 128:kt * 1024 + mo * 128 + 128],
                    otiles[kt][0:128, 0:512], start=(kt == 0), stop=(kt == 7)),
                    inc="t")
            tmm = prog.val("t")
            bank_vv[pb] = prog.emit(
                "vector", lambda E, mo=mo, pb=pb, j=j: E.scalar_tensor_tensor(
                    xres_t(mo)[0:128, 134 + j * 512:134 + j * 512 + 512],
                    psap(pb, 128, 512), cv("bout", mo),
                    xres_t(mo)[0:128, 134 + j * 512:134 + j * 512 + 512],
                    ALU.add, ALU.add), wait=[("t", tmm)], inc="v")
        ot_busy = prog.val("t")
    af.pop(); ab.pop()
    af.pop(); ab.pop()
    x3_done = prog.val("v")
    prog.wait("tensor", "v", x3_done)
    prog.wait("scalar", "v", x3_done)
    prog.wait("gpsimd", "v", x3_done)
    prog.wait("sync", "v", x3_done)


def emit_b3(st):
    prog, af, ab = st["prog"], st["af"], st["ab"]
    psap, cv, di = st["psap"], st["cv"], st["di"]
    gdma = st["gdma"]
    ps = st["ps"]
    xres_t = st["xres_t"]
    xn_spill = st["xn_spill"]
    af.push(); ab.push()

    x3_done = emit_pre_ln(st, xres_t, xn_spill)
    prog.wait("gpsimd", "sy0", x3_done[0])
    prog.wait("gpsimd", "sy1", x3_done[1])

    xnt = [valloc(ab, 128, 1024) for _ in range(8)]
    wrt = valloc(ab, 128, 64)
    wsl = valloc(ab, 128, 6 * 1024)
    hbuf = valloc(ab, 128, 16 * 1024)
    dwall = valloc(ab, 1, 8 * 2048)
    onesrow_b = valloc(ab, 1, 128)
    dwr = valloc(af, 8, 2048)
    sm = valloc(af, 128, 8)
    ex = valloc(af, 128, 8)
    ab8 = valloc(af, 128, 8)
    fb = valloc(af, 128, 8)
    oh = valloc(af, 128, 8)
    dwt = valloc(af, 128, 8)
    smc = valloc(af, 128, 8)
    dwb = valloc(af, 128, 512)
    scf = valloc(af, 128, 512)
    sgf = valloc(af, 128, 512)
    JEPS2 = 0.02
    BIG = 1.0e30

    def C(j):
        return smc[0:128, j:j + 1]

    lv = gdma(wrt[:, :], bass.AP(di["Wrout"], 0, [[8, 128], [1024, 8], [1, 8]]),
              sem="wx")
    prog.emit("vector", lambda E: E.memset(onesrow_b[0:1, :], 1.0), inc="v")
    st["_b3xt"] = 0

    def stream_x3n(c0, nt):
        lvv = None
        for k in range(8):
            lvv = gdma(xnt[k][0:128, 0:nt],
                       bass.AP(xn_spill, k * 128 * EXT + c0, [[EXT, 128], [1, nt]]),
                       wait=[("t", st["_b3xt"])] if k == 0 else (), sem="wx")
        return lvv

    st["prog"].scope = "b3.rt"
    prog.wait("tensor", "wx", lv)
    xv = stream_x3n(134, 1024)
    prog.wait("tensor", "wx", xv)
    tp_v = 0
    for tt in range(16):
        if tt == 8:
            xv = stream_x3n(134 + 1024, 1024)
            prog.wait("tensor", "wx", xv)
        off = (tt % 8) * 128
        pb = tt % 2
        prog.wait("tensor", "v", st.get("_b3pv%d" % pb, 0))
        for kt in range(8):
            prog.emit("tensor", lambda E, kt=kt, off=off, pb=pb: E.matmul(
                psap(pb, 128, 8), xnt[kt][0:128, off:off + 128],
                wrt[0:128, kt * 8:kt * 8 + 8], start=(kt == 0), stop=(kt == 7)),
                inc="t")
        tmm = prog.val("t")
        st["_b3xt"] = tmm
        vv = prog.emit("vector", lambda E, pb=pb: E.tensor_copy(
            sm[:, :], psap(pb, 128, 8)), wait=[("t", tmm)], inc="v")
        st["_b3pv%d" % pb] = vv
        if st.get("dbg_mode") == "b3sm":
            prog.emit("sync", lambda E, tt=tt: E.dma_start(
                out=bass.AP(st["dbg_out"], tt * 8, [[EXT, 128], [1, 8]]),
                in_=sm[:, :]), wait=[("v", vv)], inc="dbgs", inc_n=16)
            prog.wait("vector", "dbgs", prog.val("dbgs"))
        prog.emit("vector", lambda E: E.reduce_max(C(0), sm[:, :], axis=AX.X), inc="v")
        prog.emit("vector", lambda E: E.tensor_scalar(
            C(1), C(0), -1.0, None, ALU.mult), inc="v")
        prog.emit("vector", lambda E: E.tensor_scalar(
            ab8[:, :], sm[:, :], -1.0, None, ALU.mult), inc="v")
        prog.emit("vector", lambda E: E.tensor_max(ab8[:, :], ab8[:, :], sm[:, :]),
                  inc="v")
        prog.emit("vector", lambda E: E.tensor_scalar(
            fb[:, :], ab8[:, :], C(0), None, ALU.max), inc="v")
        prog.emit("vector", lambda E: E.scalar_tensor_tensor(
            fb[:, :], fb[:, :], JEPS2, sm[:, :], ALU.mult, ALU.add), inc="v")
        vv = prog.emit("vector", lambda E: E.tensor_scalar(
            fb[:, :], fb[:, :], C(0), None, ALU.is_ge), inc="v")
        sv = prog.emit("scalar", lambda E: E.activation(
            ex[:, :], sm[:, :], ACT.Exp, bias=C(1)), wait=[("v", vv)], inc="s")
        prog.emit("vector", lambda E: E.tensor_mul(ex[:, :], ex[:, :], fb[:, :]),
                  wait=[("s", sv)], inc="v")
        prog.emit("vector", lambda E: E.reduce_sum(C(2), ex[:, :], axis=AX.X), inc="v")
        prog.emit("vector", lambda E: E.reciprocal(C(3), C(2)), inc="v")
        prog.emit("vector", lambda E: E.tensor_scalar(
            oh[:, :], sm[:, :], C(0), None, ALU.is_ge), inc="v")
        prog.emit("vector", lambda E: E.tensor_scalar(
            dwt[:, :], oh[:, :], C(3), None, ALU.mult), inc="v")
        prog.emit("vector", lambda E: E.scalar_tensor_tensor(
            fb[:, :], oh[:, :], -BIG, sm[:, :], ALU.mult, ALU.add), inc="v")
        prog.emit("vector", lambda E: E.reduce_max(C(4), fb[:, :], axis=AX.X), inc="v")
        prog.emit("vector", lambda E: E.tensor_scalar(
            C(5), C(4), -1.0, None, ALU.mult), inc="v")
        prog.emit("vector", lambda E: E.tensor_scalar(
            ab8[:, :], ab8[:, :], C(4), None, ALU.max), inc="v")
        prog.emit("vector", lambda E: E.scalar_tensor_tensor(
            ab8[:, :], ab8[:, :], JEPS2, sm[:, :], ALU.mult, ALU.add), inc="v")
        prog.emit("vector", lambda E: E.tensor_scalar(
            ab8[:, :], ab8[:, :], C(4), None, ALU.is_ge), inc="v")
        prog.emit("vector", lambda E: E.tensor_scalar(
            fb[:, :], fb[:, :], C(4), None, ALU.is_ge), inc="v")
        prog.emit("vector", lambda E: E.tensor_scalar(
            oh[:, :], oh[:, :], -1.0, 1.0, ALU.mult, ALU.add), inc="v")
        vv = prog.emit("vector", lambda E: E.tensor_mul(
            ab8[:, :], ab8[:, :], oh[:, :]), inc="v")
        sv = prog.emit("scalar", lambda E: E.activation(
            ex[:, :], sm[:, :], ACT.Exp, bias=C(5)), wait=[("v", vv)], inc="s")
        prog.emit("vector", lambda E: E.tensor_mul(ex[:, :], ex[:, :], ab8[:, :]),
                  wait=[("s", sv)], inc="v")
        prog.emit("vector", lambda E: E.reduce_sum(C(6), ex[:, :], axis=AX.X), inc="v")
        prog.emit("vector", lambda E: E.reciprocal(C(7), C(6)), inc="v")
        vv = prog.emit("vector", lambda E: E.scalar_tensor_tensor(
            dwt[:, :], fb[:, :], C(7), dwt[:, :], ALU.mult, ALU.add), inc="v")
        if st.get("dbg_mode") == "b3dw":
            prog.emit("sync", lambda E, tt=tt: E.dma_start(
                out=bass.AP(st["dbg_out"], tt * 8, [[EXT, 128], [1, 8]]),
                in_=dwt[:, :]), wait=[("v", vv)], inc="dbgs", inc_n=16)
            prog.wait("vector", "dbgs", prog.val("dbgs"))
        prog.wait("tensor", "v", vv)
        prog.wait("tensor", "v", tp_v)
        tv = prog.emit("tensor", lambda E, pb=pb: E.transpose(
            psap(2, 8, 128), dwt[:, :], st["ident_f"][:, :]), inc="t")
        tp_v = prog.emit("vector", lambda E, tt=tt: E.tensor_copy(
            dwr[0:8, tt * 128:tt * 128 + 128], psap(2, 8, 128)),
            wait=[("t", tv)], inc="v")
        gdma(bass.AP(ab.t, dwall.off + tt * 128, [[ab.cols, 1], [2048, 8], [1, 128]]),
             dwr[0:8, tt * 128:tt * 128 + 128], sem="dw", wait=[("v", tp_v)])

    st["prog"].scope = "b3.ex"
    bank_busy = {i: 0 for i in range(8)}
    dwr_done = prog.val("v")
    prog.wait("tensor", "dw", prog.val("dw"))
    wslb = [0, 0]
    wdb = [0, 0]
    h_busy = 0
    for e in range(8):
        for half in range(2):
            hc0 = 134 + half * 1024
            st["_b3xt"] = prog.val("t")
            xv = stream_x3n(hc0, 1024)
            gu_load = {}

            def issue_gu(kt, e=e):
                pb = (kt % 2) * 2048
                gdma(wsl[0:128, pb:pb + 1024],
                     bass.AP(di["Wg"], (e * 16 + kt) * 128 * 1024,
                             [[1024, 128], [1, 1024]]),
                     wait=[("t", wslb[kt % 2])], sem="w0")
                gu_load[kt] = gdma(
                     wsl[0:128, pb + 1024:pb + 2048],
                     bass.AP(di["Wu"], (e * 16 + kt) * 128 * 1024,
                             [[1024, 128], [1, 1024]]), sem="w0")

            issue_gu(0)
            issue_gu(1)
            for kt in range(16):
                pb = (kt % 2) * 2048
                prog.wait("tensor", "w0", gu_load[kt])
                if kt == 0:
                    prog.wait("tensor", "wx", xv)
                for sub in range(2):
                    so = sub * 512
                    prog.wait("tensor", "s", bank_busy.get("g", 0))
                    prog.wait("tensor", "v", bank_busy.get("u", 0))
                    prog.wait("tensor", "v", bank_busy[0])
                    prog.wait("tensor", "v", bank_busy[1])
                    for k8 in range(8):
                        prog.emit("tensor", lambda E, k8=k8, so=so, pb=pb: E.matmul(
                            psap(0, 128, 512),
                            wsl[0:128, pb + k8 * 128:pb + k8 * 128 + 128],
                            xnt[k8][0:128, so:so + 512],
                            start=(k8 == 0), stop=(k8 == 7)), inc="t")
                    for k8 in range(8):
                        prog.emit("tensor", lambda E, k8=k8, so=so, pb=pb: E.matmul(
                            psap(1, 128, 512),
                            wsl[0:128, pb + 1024 + k8 * 128:pb + 1024 + k8 * 128 + 128],
                            xnt[k8][0:128, so:so + 512],
                            start=(k8 == 0), stop=(k8 == 7)), inc="t")
                    tmm = prog.val("t")
                    sv = prog.emit("scalar", lambda E: E.activation(
                        sgf[:, :], psap(0, 128, 512), ACT.Silu),
                        wait=[("t", tmm), ("v", prog.val("v"))], inc="s")
                    bank_busy["g"] = sv
                    vv = prog.emit("vector", lambda E, kt=kt, so=so: E.tensor_mul(
                        View(ab, hbuf.off + kt * 1024, 128, 1024)[0:128, so:so + 512],
                        sgf[:, :], psap(1, 128, 512)),
                        wait=[("s", sv), ("t", h_busy)], inc="v")
                    bank_busy["u"] = vv
                wslb[kt % 2] = prog.val("t")
                if kt + 2 < 16:
                    issue_gu(kt + 2)
            hv = prog.val("v")
            for sub in range(2):
                so = sub * 512
                prog.wait("tensor", "v", bank_busy.get("dwb", 0))
                prog.wait("tensor", "v", bank_busy[2])
                prog.wait("tensor", "v", dwr_done)
                tv = prog.emit("tensor", lambda E, e=e, half=half, so=so: E.matmul(
                    psap(2, 128, 512), onesrow_b[0:1, :],
                    bass.AP(ab.t, dwall.off + e * 2048 + half * 1024 + so,
                            [[ab.cols, 1], [1, 512]]),
                    start=True, stop=True), wait=[("v", tp_v)], inc="t")
                vv = prog.emit("vector", lambda E: E.tensor_copy(
                    dwb[:, :], psap(2, 128, 512)), wait=[("t", tv)], inc="v")
                bank_busy["dwb"] = vv
                prog.wait("tensor", "v", vv)
                for mo in range(8):
                    prog.wait("tensor", "v", bank_busy[mo])
                wd_load = {}

                def issue_wd(kt, e=e):
                    db = 4096 + (kt % 2) * 1024
                    wd_load[kt] = gdma(
                        wsl[0:128, db:db + 1024],
                        bass.AP(di["Wd"], e * FF * 1024 + kt * 128 * 1024,
                                [[1024, 128], [0, 1], [1, 1024]]),
                        wait=[("t", wdb[kt % 2])], sem="w1")

                issue_wd(0)
                issue_wd(1)
                for kt in range(16):
                    db = 4096 + (kt % 2) * 1024
                    prog.wait("tensor", "w1", wd_load[kt])
                    for mo in range(8):
                        prog.emit("tensor", lambda E, kt=kt, mo=mo, so=so, db=db:
                                  E.matmul(
                            psap(mo, 128, 512),
                            wsl[0:128, db + mo * 128:db + mo * 128 + 128],
                            View(ab, hbuf.off + kt * 1024, 128, 1024)[0:128,
                                                                      so:so + 512],
                            start=(kt == 0), stop=(kt == 15)),
                            wait=[("v", hv)] if kt == 0 and mo == 0 else (), inc="t")
                    wdb[kt % 2] = prog.val("t")
                    if kt + 2 < 16:
                        issue_wd(kt + 2)
                for mo in range(8):
                    tmm = prog.val("t")
                    vv = prog.emit("vector", lambda E, mo=mo: E.tensor_mul(
                        scf[:, :], psap(mo, 128, 512), dwb[:, :]),
                        wait=[("t", tmm)], inc="v")
                    vv = prog.emit("vector", lambda E, mo=mo, hc0=hc0, so=so:
                                   E.tensor_add(
                        xres_t(mo)[0:128, hc0 + so:hc0 + so + 512],
                        xres_t(mo)[0:128, hc0 + so:hc0 + so + 512], scf[:, :]),
                        inc="v")
                    bank_busy[mo] = vv
            h_busy = prog.val("t")
    af.pop(); ab.pop()
    x4_done = prog.val("v")
    prog.wait("tensor", "v", x4_done)
    prog.wait("scalar", "v", x4_done)
    prog.wait("gpsimd", "v", x4_done)
    prog.wait("sync", "v", x4_done)


def emit_output(st):
    prog, af = st["prog"], st["af"]
    psap = st["psap"]
    xres_t = st["xres_t"]
    ident = st["ident_f"]
    af.push()
    otile = valloc(af, 128, 1024)
    copies = []
    for tt in range(16):
        c0 = H + tt * 128
        for k in range(8):
            pv = k % 4
            guard = copies[-4] if len(copies) >= 4 else 0
            tv = prog.emit("tensor", lambda E, k=k, c0=c0, pv=pv: E.transpose(
                psap(pv, 128, 128), xres_t(k)[:, c0:c0 + 128], ident[:, :]),
                wait=[("v", guard)], inc="t")
            vv = prog.emit("vector", lambda E, k=k, pv=pv: E.tensor_copy(
                otile[0:128, k * 128:(k + 1) * 128], psap(pv, 128, 128)),
                wait=[("t", tv)], inc="v")
            copies.append(vv)
        dst = bass.AP(st["out"], tt * 128 * D, [[D, 128], [1, D]])
        prog.emit("sync", lambda E, d=dst: E.dma_start(out=d, in_=otile[:, :]),
                  wait=[("v", copies[-1])], inc="st", inc_n=16)
        prog.wait("vector", "st", prog.val("st"))
    af.pop()


def kernel(**inputs):
    shared, per_core = host_prepare(inputs)
    nc = build_program(nblocks=4)
    in_maps = []
    for c in range(NCORES):
        m = dict(shared)
        m.update(per_core[c])
        in_maps.append(m)
    res = run_bass_kernel_spmd(nc, in_maps, core_ids=list(range(NCORES)))
    outs = [np.asarray(res.results[c]["out"]) for c in range(NCORES)]
    return np.concatenate(outs, axis=0).reshape(1, T, D)


# version 04
# speedup vs baseline: 1.6669x; 1.3891x over previous
import contextlib

import numpy as np

import concourse.bass as bass
import concourse.mybir as mybir
from concourse.bass_utils import run_bass_kernel_spmd

F32 = mybir.dt.float32
BF16 = mybir.dt.bfloat16
ALU = mybir.AluOpType
ACT = mybir.ActivationFunctionType
AX = mybir.AxisListType

NCORES = 8
D = 1024
T = 16384
TC = 2048
H = 134
EXT = TC + H
INNER = 2048
NH = 16
DH = 64
W = 128
SEG = 16
NSEG = T // SEG
NSEG_C = TC // SEG
KV0 = 6
KVLEN = EXT - KV0
E = 8
FF = 2048

NT = [512, 512, 512, 512, EXT - 2048]
NTO = [0, 512, 1024, 1536, 2048]
NN = len(NT)


class Prog:
    ENGINES = ("tensor", "vector", "scalar", "gpsimd", "sync")

    def __init__(self, nc, es):
        self.nc = nc
        self.es = es
        self.q = {e: [] for e in self.ENGINES}
        self.sems = {}
        self.cnt = {}
        self.scope = None

    def sem(self, name):
        if name not in self.sems:
            self.sems[name] = self.es.enter_context(self.nc.semaphore(name))
            self.cnt[name] = 0
        return self.sems[name]

    def val(self, name):
        self.sem(name)
        return self.cnt[name]

    AUTOSEM = {"tensor": "t", "vector": "v", "scalar": "s"}

    def emit(self, eng, fn, wait=(), inc=None, inc_n=1):
        if inc is not None and inc_n == 16:
            if not hasattr(self, "_fence"):
                self._fence = {}
            cur = self.val(inc)
            if cur > 0 and self._fence.get((eng, inc)) != cur:
                h0 = self.sem(inc)
                self.q[eng].append((self.scope,
                                    lambda E, h=h0, v=cur: E.wait_ge(h, v)))
                self._fence[(eng, inc)] = cur
        w = [(self.sem(s), v) for (s, v) in wait if v > 0]
        if eng in self.AUTOSEM:
            own = self.AUTOSEM[eng]
            prev = self.val(own)
            if prev > 0:
                w.append((self.sem(own), prev))
            assert inc is None or inc == own, (eng, inc)
            inc, inc_n = own, 1
        h = None
        ret = None
        if inc is not None:
            h = self.sem(inc)
            self.cnt[inc] += inc_n
            ret = self.cnt[inc]

        def step(E, fn=fn, w=w, h=h, n=inc_n):
            for (sh, v) in w:
                E.wait_ge(sh, v)
            inst = fn(E)
            if h is not None:
                inst.then_inc(h, n)

        self.q[eng].append((self.scope, step))
        return ret

    def wait(self, eng, semname, value):
        if value <= 0:
            return
        h = self.sem(semname)
        self.q[eng].append((self.scope, lambda E, h=h, v=value: E.wait_ge(h, v)))

    def dma(self, eng, sem, fn, wait=()):
        return self.emit(eng, fn, wait=wait, inc=sem, inc_n=16)

    def barrier(self):
        h = self.sem("bar")
        for e in self.ENGINES:
            self.cnt["bar"] += 1
            self.q[e].append((self.scope, lambda E, h=h: E.nop().then_inc(h, 1)))
        tot = self.cnt["bar"]
        for e in self.ENGINES:
            self.wait(e, "bar", tot)

    def finalize(self):
        block = self.es.enter_context(self.nc.Block())
        qs = self.q
        nc = self.nc

        def replay(E, items):
            cur, sid = None, None
            for (sc, s) in items:
                if sc != cur:
                    if cur is not None:
                        nc.leave_named_scope(cur, sid, False)
                    cur, sid = sc, None
                    if cur is not None:
                        sid, _ = nc.enter_named_scope(cur, False)
                s(E)
            if cur is not None:
                nc.leave_named_scope(cur, sid, False)

        @block.tensor
        def _(E):
            replay(E, qs["tensor"])

        @block.vector
        def _(E):
            replay(E, qs["vector"])

        @block.scalar
        def _(E):
            replay(E, qs["scalar"])

        @block.gpsimd
        def _(E):
            replay(E, qs["gpsimd"])

        @block.sync
        def _(E):
            replay(E, qs["sync"])


class Arena:
    def __init__(self, nc, name, cols, dtype):
        self.t = nc.alloc_sbuf_tensor(name, [128, cols], dtype)
        self.cols = cols
        self.off = 0
        self.marks = []
        self.peak = 0

    def alloc(self, n, align=16):
        o = (self.off + align - 1) // align * align
        assert o + n <= self.cols, f"arena {self.t.name}: {o}+{n}>{self.cols}"
        self.off = o + n
        self.peak = max(self.peak, self.off)
        return o

    def push(self):
        self.marks.append(self.off)

    def pop(self):
        self.off = self.marks.pop()


class View:
    def __init__(self, arena, off, rows, cols):
        self.a = arena
        self.off = off
        self.rows = rows
        self.cols = cols

    def __getitem__(self, idx):
        rs, cs = idx
        if isinstance(rs, int):
            rs = slice(rs, rs + 1)
        if isinstance(cs, int):
            cs = slice(cs, cs + 1)
        r0 = rs.start or 0
        r1 = self.rows if rs.stop is None else rs.stop
        c0 = cs.start or 0
        c1 = self.cols if cs.stop is None else cs.stop
        return bass.AP(self.a.t, self.off + c0 + r0 * self.a.cols,
                       [[self.a.cols, r1 - r0], [1, c1 - c0]])

    def ap3(self, r0, nrow, dims, c0=0):
        return bass.AP(self.a.t, self.off + c0 + r0 * self.a.cols,
                       [[self.a.cols, nrow]] + [list(d) for d in dims])


def valloc(arena, rows, cols, align=16):
    return View(arena, arena.alloc(cols, align), rows, cols)


def host_prepare(inp):
    x = np.asarray(inp["L0_attn_x"]).reshape(T, D)
    sh = {}

    def T_(a):
        return np.ascontiguousarray(np.asarray(a).T)

    sh["Win"] = np.ascontiguousarray(
        np.asarray(inp["L0_attn_W_in"]).reshape(8, 128, 32, 128)
        .transpose(2, 1, 0, 3).reshape(32, 128, 1024))
    for nm, key in (("Wa", "L0_attn_Wa"), ("Wb", "L0_attn_Wb"),
                    ("Wc", "L0_attn_Wc")):
        sh[nm] = np.ascontiguousarray(
            np.asarray(inp[key]).reshape(16, 128, 16, 128)
            .transpose(2, 1, 0, 3).reshape(16, 128, 2048))
    sh["Wo"] = np.ascontiguousarray(inp["L0_attn_Wo"])
    sh["Wr"] = T_(inp["L0_ffn_W_r"])
    sh["Wk"] = T_(inp["L0_ffn_W_k"])
    sh["Wv"] = T_(inp["L0_ffn_W_v"])
    sh["Wq"] = np.ascontiguousarray(inp["L1_attn_Wq"])
    sh["Wkv"] = np.ascontiguousarray(inp["L1_attn_Wkv"])
    sh["Wout"] = np.ascontiguousarray(inp["L1_attn_Wout"])
    sh["Wproj"] = np.ascontiguousarray(np.asarray(inp["L1_attn_Wproj"]).reshape(DH, 1))
    sh["Wrout"] = T_(inp["L1_ffn_router_w"])
    for nm, key in (("Wg", "L1_ffn_w_gate"), ("Wu", "L1_ffn_w_up")):
        sh[nm] = np.ascontiguousarray(
            np.asarray(inp[key]).reshape(8, 16, 128, 8, 128)
            .transpose(0, 1, 4, 3, 2).reshape(8, 16, 128, 1024))
    sh["Wd"] = np.ascontiguousarray(np.swapaxes(np.asarray(inp["L1_ffn_w_down"]), 1, 2))

    def pp(a, n):
        return np.ascontiguousarray(np.asarray(a).reshape(n, 128).T)

    sh["b_in"] = pp(inp["L0_attn_b_in"], 32)
    sh["conv_w"] = np.ascontiguousarray(np.asarray(inp["L0_attn_conv_w"]).reshape(16, 128, 3))
    sh["conv_b"] = pp(inp["L0_attn_conv_b"], 16)
    sh["ba"] = pp(inp["L0_attn_ba"], 16)
    sh["bb"] = pp(inp["L0_attn_bb"], 16)
    sh["bc"] = pp(inp["L0_attn_bc"], 16)
    sh["bo"] = pp(inp["L0_attn_bo"], 8)
    sh["gamma"] = pp(inp["L0_attn_gamma"], 8)
    sh["beta"] = pp(inp["L0_attn_beta"], 8)
    sh["mu_r"] = pp(inp["L0_ffn_mu_r"], 8)
    sh["mu_k"] = pp(inp["L0_ffn_mu_k"], 8)
    sh["bout"] = pp(inp["L1_attn_bout"], 8)
    for nm, key in (("lng_l", "L1_attn_ln_local_g"), ("lnb_l", "L1_attn_ln_local_b"),
                    ("lng_g", "L1_attn_ln_global_g"), ("lnb_g", "L1_attn_ln_global_b")):
        v = np.asarray(inp[key]).reshape(DH)
        sh[nm] = np.ascontiguousarray(np.concatenate([v, v]).reshape(128, 1))

    sh["ident"] = np.ascontiguousarray(np.eye(128, dtype=np.float32))
    prot = np.zeros((128, 128), np.float32)
    for hh in range(2):
        for d0 in range(32):
            prot[hh * 64 + d0 + 32, hh * 64 + d0] = -1.0
            prot[hh * 64 + d0, hh * 64 + d0 + 32] = 1.0
    sh["prot"] = prot
    sh["causal_own"] = (np.arange(W)[:, None] <= np.arange(W)[None, :]).astype(np.float32)
    hs = np.zeros((128, 2), np.float32)
    hs[:64, 0] = 1.0
    hs[64:, 1] = 1.0
    sh["halfsel"] = hs
    sh["sel2T"] = np.ascontiguousarray(hs.T)
    wp2 = np.zeros((128, 2), np.float32)
    wp = np.asarray(inp["L1_attn_Wproj"]).reshape(DH)
    wp2[:64, 0] = wp
    wp2[64:, 1] = wp
    sh["wproj2"] = wp2

    per = []
    for c in range(NCORES):
        d = {}
        t0 = TC * c
        lo = t0 - H
        sl = np.zeros((EXT, D), np.float32)
        a0 = max(0, lo)
        sl[a0 - lo:, :] = x[a0:t0 + TC, :]
        d["x0"] = np.ascontiguousarray(sl.T)
        d["prev_mask"] = np.full((128, 1), 0.0 if c == 0 else 1.0, np.float32)
        m = (np.arange(8) < c).astype(np.float32)
        d["foldm"] = np.ascontiguousarray(np.broadcast_to(m, (128, 8)))
        d["foldm1"] = np.ascontiguousarray(np.broadcast_to(1.0 - m, (128, 8)))
        tau = np.arange(t0 - H, t0 + TC)
        tau = np.maximum(tau, 0).astype(np.float64)
        invf = 1.0 / 10000.0 ** (np.arange(0, DH, 2, dtype=np.float64) / DH)
        ang = tau[None, :] * invf[:, None]
        cos = np.cos(ang).astype(np.float32)
        sin = np.sin(ang).astype(np.float32)
        cos64 = np.concatenate([cos, cos], 0)
        sinS = np.concatenate([sin, sin], 0)
        d["cos_t"] = np.ascontiguousarray(np.concatenate([cos64, cos64], 0))
        d["sin_t"] = np.ascontiguousarray(np.concatenate([sinS, sinS], 0))
        Mfull = np.zeros((4, 128, 8 * 512), np.float32)
        for j in range(4):
            qv = t0 + 512 * j + np.arange(512)
            for stile in range(8):
                segg = stile * 128 + np.arange(128)
                Mfull[j][:, stile * 512:(stile + 1) * 512] = (
                    qv[None, :] >= 16 * (segg[:, None] + 1))
        d["gmask"] = np.ascontiguousarray(Mfull)
        per.append(d)
    return sh, per


def build_program(nblocks=4, dbg=None, dbg_mode=None):
    nc = bass.Bass("TRN2", target_bir_lowering=False)
    es = contextlib.ExitStack()
    di = {}

    def IN(name, shape):
        di[name] = nc.dram_tensor(name, list(shape), F32, kind="ExternalInput")

    IN("x0", (D, EXT))
    IN("Win", (32, 128, 1024)); IN("Wa", (16, 128, INNER)); IN("Wb", (16, 128, INNER))
    IN("Wc", (16, 128, INNER)); IN("Wo", (INNER, D))
    IN("Wr", (D, D)); IN("Wk", (D, D)); IN("Wv", (D, D))
    IN("Wq", (D, D)); IN("Wkv", (D, D)); IN("Wout", (D, D)); IN("Wproj", (DH, 1))
    IN("Wrout", (D, E)); IN("Wg", (E, 16, 128, 1024)); IN("Wu", (E, 16, 128, 1024)); IN("Wd", (E, FF, D))
    IN("b_in", (128, 32)); IN("conv_w", (16, 128, 3)); IN("conv_b", (128, 16))
    IN("ba", (128, 16)); IN("bb", (128, 16)); IN("bc", (128, 16)); IN("bo", (128, 8))
    IN("gamma", (128, 8)); IN("beta", (128, 8))
    IN("mu_r", (128, 8)); IN("mu_k", (128, 8)); IN("bout", (128, 8))
    IN("lng_l", (128, 1)); IN("lnb_l", (128, 1)); IN("lng_g", (128, 1)); IN("lnb_g", (128, 1))
    IN("ident", (128, 128)); IN("prot", (128, 128)); IN("causal_own", (W, W))
    IN("prev_mask", (128, 1)); IN("foldm", (128, 8)); IN("foldm1", (128, 8))
    IN("cos_t", (128, EXT)); IN("sin_t", (128, EXT))
    IN("halfsel", (128, 2)); IN("sel2T", (2, 128)); IN("wproj2", (128, 2))
    IN("gmask", (4, 128, 8 * 512))

    out = nc.dram_tensor("out", [TC, D], F32, kind="ExternalOutput")
    dbg_out = None
    if dbg is not None:
        dbg_out = nc.dram_tensor("dbgo", [D, EXT], F32, kind="ExternalOutput")

    y_spill = nc.dram_tensor("y_spill", [16, 128, EXT], BF16)
    out_spill = nc.dram_tensor("out_spill", [8, 128, 2048], BF16)
    cp_spill = nc.dram_tensor("cp_spill", [16, 128, EXT], BF16)
    xn_spill = nc.dram_tensor("xn_spill", [8, 128, EXT], BF16)
    ag1_in = nc.dram_tensor("ag1_in", [128, 32], F32)
    ag1_out = nc.dram_tensor("ag1_out", [128 * 8, 32], F32)
    ag2_in = nc.dram_tensor("ag2_in", [NSEG_C, NH * DH], BF16)
    ag2_out = nc.dram_tensor("ag2_out", [NSEG, NH * DH], BF16)

    prog = Prog(nc, es)
    af = Arena(nc, "af", 26800, F32)
    ab = Arena(nc, "ab", 50800, BF16)

    xres = valloc(af, 128, 8 * EXT)

    def xres_t(k):
        return View(af, xres.off + k * EXT, 128, EXT)

    cst = valloc(af, 128, 600)
    co = {}
    _cc = [cst.off]

    def calloc(name, n):
        co[name] = _cc[0]
        _cc[0] += n

    for nm, n in (("b_in", 32), ("b_in_h", 32), ("conv_w", 48), ("conv_b", 16),
                  ("ba", 16), ("ba_h", 16), ("bb", 16), ("bb_h", 16), ("bc", 16),
                  ("bo", 8), ("gamma", 8), ("beta", 8), ("mu_r", 8), ("mu_r1", 8),
                  ("mu_k", 8), ("mu_k1", 8), ("bout", 8), ("lng_l", 1), ("lnb_l", 1),
                  ("lng_g", 1), ("lnb_g", 1), ("prev", 1), ("foldm", 8), ("foldm1", 8),
                  ("Sin", 16), ("exs", 16), ("exp", 16), ("onesrow", 128),
                  ("onescol", 1), ("ptail", 32), ("zero2", 2), ("wproj", 1), ("eps", 1),
                  ("maxk", 16), ("spare", 16)):
        calloc(nm, n)
    assert _cc[0] - cst.off <= 600, _cc[0] - cst.off

    def cv(name, j=0, rows=128, base=0, ncol=1):
        return bass.AP(af.t, co[name] + j + base * af.cols,
                       [[af.cols, rows], [1, ncol]])

    onescol_b = valloc(ab, 128, 16)
    ident_bf = valloc(ab, 128, 128)
    ident_f = valloc(af, 128, 128)
    prot_bf = valloc(ab, 128, 128)
    causal_bf = valloc(ab, W, W)

    ps = [es.enter_context(nc.psum_tensor("p%d" % i, [128, 512], F32)) for i in range(8)]

    def psap(i, rows=128, cols=512, r0=0, c0=0):
        return bass.AP(ps[i], c0 + r0 * 512, [[512, rows], [1, cols]])

    def gdma(dst, src, sem="wg", wait=()):
        return prog.dma("gpsimd", sem,
                        lambda E, d=dst, s=src: E.dma_start(out=d, in_=s), wait=wait)

    for nm, rows in (("b_in", 32), ("conv_b", 16), ("ba", 16), ("bb", 16), ("bc", 16),
                     ("bo", 8), ("gamma", 8), ("beta", 8), ("mu_r", 8), ("mu_k", 8),
                     ("bout", 8), ("lng_l", 1), ("lnb_l", 1), ("lng_g", 1), ("lnb_g", 1)):
        gdma(cv(nm, ncol=rows), bass.AP(di[nm], 0, [[rows, 128], [1, rows]]))
    for t in range(16):
        gdma(cv("conv_w", 3 * t, ncol=3), bass.AP(di["conv_w"], t * 384, [[3, 128], [1, 3]]))
    gdma(cv("prev"), di["prev_mask"].ap())
    gdma(cv("foldm", ncol=8), di["foldm"].ap())
    gdma(cv("foldm1", ncol=8), di["foldm1"].ap())
    gdma(cv("wproj", rows=DH), bass.AP(di["Wproj"], 0, [[1, DH], [1, 1]]))
    gdma(ident_bf[:, :], di["ident"].ap())
    gdma(ident_f[:, :], di["ident"].ap())
    gdma(prot_bf[:, :], di["prot"].ap())
    gdma(causal_bf[:, :], di["causal_own"].ap())
    for k in range(8):
        gdma(xres_t(k)[:, :], bass.AP(di["x0"], k * 128 * EXT, [[EXT, 128], [1, EXT]]))
    ld_done = prog.val("wg")

    prog.wait("vector", "wg", ld_done)
    prog.emit("vector", lambda E: E.memset(cv("onesrow", rows=1, ncol=128), 1.0), inc="v")
    prog.emit("vector", lambda E: E.memset(cv("onescol", ncol=1), 1.0), inc="v")
    prog.emit("vector", lambda E: E.memset(onescol_b[:, 0:1], 1.0), inc="v")
    prog.emit("vector", lambda E: E.memset(cv("zero2", ncol=2), 0.0), inc="v")
    prog.emit("vector", lambda E: E.memset(cv("eps", ncol=1), 1e-5), inc="v")
    prog.emit("vector", lambda E: E.tensor_scalar(
        cv("b_in_h", ncol=32), cv("b_in", ncol=32), 0.5, None, ALU.mult), inc="v")
    prog.emit("vector", lambda E: E.tensor_scalar(
        cv("ba_h", ncol=16), cv("ba", ncol=16), 0.5, None, ALU.mult), inc="v")
    prog.emit("vector", lambda E: E.tensor_scalar(
        cv("bb_h", ncol=16), cv("bb", ncol=16), 0.5, None, ALU.mult), inc="v")
    prog.emit("vector", lambda E: E.tensor_scalar(
        cv("mu_r1", ncol=8), cv("mu_r", ncol=8), -1.0, 1.0, ALU.mult, ALU.add), inc="v")
    prog.emit("vector", lambda E: E.tensor_scalar(
        cv("mu_k1", ncol=8), cv("mu_k", ncol=8), -1.0, 1.0, ALU.mult, ALU.add), inc="v")
    cdone = prog.val("v")
    prog.wait("tensor", "v", cdone)
    prog.wait("scalar", "v", cdone)
    prog.wait("sync", "v", cdone)

    st = dict(dbg_mode=dbg_mode, nc=nc, prog=prog, af=af, ab=ab, ps=ps, psap=psap, di=di, xres_t=xres_t,
              cv=cv, co=co, gdma=gdma, onescol_b=onescol_b, ident_f=ident_f, ident_bf=ident_bf, prot_bf=prot_bf, causal_bf=causal_bf,
              y_spill=y_spill, cp_spill=cp_spill, xn_spill=xn_spill,
              out_spill=out_spill,
              ag1_in=ag1_in, ag1_out=ag1_out, ag2_in=ag2_in, ag2_out=ag2_out,
              out=out, dbg_out=dbg_out)

    prog.scope = "b0"
    emit_b0(st)
    if nblocks >= 2:
        prog.scope = "b1"
        emit_b1(st)
    if nblocks >= 3:
        prog.scope = "b2"
        emit_b2(st)
    if nblocks >= 4:
        prog.scope = "b3"
        emit_b3(st)
    prog.scope = "fin"

    prog.barrier()
    if dbg is not None and dbg_mode is None:
        for k in range(8):
            dst = bass.AP(dbg_out, k * 128 * EXT, [[EXT, 128], [1, EXT]])
            prog.emit("sync", lambda E, d=dst, s=xres_t(k)[:, :]:
                      E.dma_start(out=d, in_=s), inc="st", inc_n=16)
    emit_output(st)
    prog.wait("sync", "st", prog.val("st"))
    prog.finalize()
    es.close()
    return nc


def emit_pre_ln(st, src_t, spill, n_list=None, dst_t=None):
    prog, af, ab, psap, cv = st["prog"], st["af"], st["ab"], st["psap"], st["cv"]
    n_list = list(n_list if n_list is not None else range(NN))
    af.push(); ab.push()
    sq = valloc(af, 128, 512)
    srow = valloc(af, 1, 1536)
    xt = [valloc(ab, 128, 512) for _ in range(2)]
    xt_busy = [0, 0]
    for n in n_list:
        nt, o = NT[n], NTO[n]
        sq_free = prog.val("t")
        for k in range(8):
            sv = prog.emit("scalar", lambda E, k=k, o=o, nt=nt: E.activation(
                sq[0:128, 0:nt], src_t(k)[:, o:o + nt], ACT.Square),
                wait=[("t", sq_free)], inc="s")
            prog.emit("tensor", lambda E, k=k, o=o, nt=nt: E.matmul(
                psap(6, 1, nt), cv("onescol"), src_t(k)[:, o:o + nt],
                start=(k == 0), stop=(k == 7)), inc="t")
            sq_free = prog.emit("tensor", lambda E, k=k, nt=nt: E.matmul(
                psap(7, 1, nt), cv("onescol"), sq[0:128, 0:nt],
                start=(k == 0), stop=(k == 7)), wait=[("s", sv)], inc="t")
        m_row = srow[0:1, 0:nt]
        r_row = bass.AP(srow.a.t, srow.off + 512, [[srow.a.cols, 1], [1, nt]])
        tmp = bass.AP(srow.a.t, srow.off + 1024, [[srow.a.cols, 1], [1, nt]])
        prog.emit("vector", lambda E, m_row=m_row, nt=nt: E.tensor_scalar(
            m_row, psap(6, 1, nt), 1.0 / D, None, ALU.mult),
            wait=[("t", sq_free)], inc="v")
        prog.emit("vector", lambda E, m_row=m_row, tmp=tmp: E.tensor_mul(
            tmp, m_row, m_row), inc="v")
        vv = prog.emit("vector", lambda E, r_row=r_row, tmp=tmp, nt=nt:
                       E.scalar_tensor_tensor(
            r_row, psap(7, 1, nt), 1.0 / D, tmp, ALU.mult, ALU.subtract), inc="v")
        sv = prog.emit("scalar", lambda E, r_row=r_row, tmp=tmp: E.activation(
            tmp, r_row, ACT.Ln, bias=cv("eps", rows=1)), wait=[("v", vv)], inc="s")
        sv = prog.emit("scalar", lambda E, r_row=r_row, tmp=tmp: E.activation(
            r_row, tmp, ACT.Exp, scale=-0.5), inc="s")
        if st.get("dbg_mode") == "lnrows" and n == n_list[0]:
            prog.emit("sync", lambda E: E.dma_start(
                out=bass.AP(st["dbg_out"], 0, [[EXT, 1], [1, 1536]]),
                in_=srow[0:1, 0:1536]), wait=[("s", sv)], inc="st", inc_n=16)
        tb = prog.emit("tensor", lambda E, nt=nt, m_row=m_row: E.matmul(
            psap(4, 128, nt), cv("onesrow", rows=1, ncol=128), m_row,
            start=True, stop=True), wait=[("s", sv)], inc="t")
        tb = prog.emit("tensor", lambda E, nt=nt, r_row=r_row: E.matmul(
            psap(5, 128, nt), cv("onesrow", rows=1, ncol=128), r_row,
            start=True, stop=True), inc="t")
        for k in range(8):
            if dst_t is not None:
                vv = prog.emit("vector", lambda E, k=k, o=o, nt=nt: E.tensor_sub(
                    xt[0][0:128, 0:nt], src_t(k)[:, o:o + nt], psap(4, 128, nt)),
                    wait=[("t", tb)], inc="v")
                vv = prog.emit("vector", lambda E, k=k, o=o, nt=nt: E.tensor_mul(
                    dst_t(k)[0:128, o:o + nt], xt[0][0:128, 0:nt], psap(5, 128, nt)),
                    inc="v")
                continue
            b = xt[k % 2]
            vv = prog.emit("vector", lambda E, k=k, b=b, o=o, nt=nt: E.tensor_sub(
                b[0:128, 0:nt], src_t(k)[:, o:o + nt], psap(4, 128, nt)),
                wait=[("t", tb), ("sy%d" % (k % 2), xt_busy[k % 2])], inc="v")
            vv = prog.emit("vector", lambda E, b=b, nt=nt: E.tensor_mul(
                b[0:128, 0:nt], b[0:128, 0:nt], psap(5, 128, nt)), inc="v")
            xt_busy[k % 2] = prog.emit("sync", lambda E, k=k, b=b, o=o, nt=nt:
                                       E.dma_start(
                out=bass.AP(spill, k * 128 * EXT + o, [[EXT, 128], [1, nt]]),
                in_=b[0:128, 0:nt]), wait=[("v", vv)], inc="sy%d" % (k % 2), inc_n=16)
        prog.wait("tensor", "v", prog.val("v"))
    out_done = ((prog.val("sy0"), prog.val("sy1")) if dst_t is None
                else prog.val("v"))
    af.pop(); ab.pop()
    return out_done


def emit_b0(st):
    prog, af, ab = st["prog"], st["af"], st["ab"]
    psap, cv, co = st["psap"], st["cv"], st["co"]
    gdma = st["gdma"]
    xres_t, di = st["xres_t"], st["di"]
    xn_spill = st["xn_spill"]
    y_spill, cp_spill = st["y_spill"], st["cp_spill"]

    af.push(); ab.push()
    sq = valloc(af, 128, 512)
    srow = valloc(af, 1, 2560)
    zt = [valloc(af, 128, 512) for _ in range(2)]
    z_bf = [valloc(ab, 128, 512) for _ in range(8)]
    xnt = [valloc(ab, 128, 512) for _ in range(2)]
    xnt_busy = [0, 0]

    onescol_b = st["onescol_b"]

    def stat_pass(src_fn, mcol, rcol, nt, ones_sum=None):
        if ones_sum is None:
            ones_sum = cv("onescol")
        sq_free = prog.val("t")
        for k in range(8):
            a, wv = src_fn(k)
            sv = prog.emit("scalar", lambda E, a=a, nt=nt: E.activation(
                sq[0:128, 0:nt], a, ACT.Square),
                wait=list(wv) + [("t", sq_free)], inc="s")
            prog.emit("tensor", lambda E, a=a, k=k, nt=nt, ones_sum=ones_sum: E.matmul(
                psap(6, 1, nt), ones_sum, a, start=(k == 0), stop=(k == 7)),
                wait=wv, inc="t")
            sq_free = prog.emit("tensor", lambda E, k=k, nt=nt: E.matmul(
                psap(7, 1, nt), cv("onescol"), sq[0:128, 0:nt],
                start=(k == 0), stop=(k == 7)), wait=[("s", sv)], inc="t")
        m_row = srow[0:1, mcol:mcol + nt]
        r_row = srow[0:1, rcol:rcol + nt]
        tmp = srow[0:1, 2048:2048 + nt]
        prog.emit("vector", lambda E, m_row=m_row, nt=nt: E.tensor_scalar(
            m_row, psap(6, 1, nt), 1.0 / D, None, ALU.mult),
            wait=[("t", sq_free)], inc="v")
        prog.emit("vector", lambda E, m_row=m_row, tmp=tmp: E.tensor_mul(
            tmp, m_row, m_row), inc="v")
        vv = prog.emit("vector", lambda E, r_row=r_row, tmp=tmp, nt=nt:
                       E.scalar_tensor_tensor(
            r_row, psap(7, 1, nt), 1.0 / D, tmp, ALU.mult, ALU.subtract), inc="v")
        sv = prog.emit("scalar", lambda E, r_row=r_row, tmp=tmp: E.activation(
            tmp, r_row, ACT.Ln, bias=cv("eps", rows=1)), wait=[("v", vv)], inc="s")
        sv = prog.emit("scalar", lambda E, r_row=r_row, tmp=tmp: E.activation(
            r_row, tmp, ACT.Exp, scale=-0.5), inc="s")
        return m_row, r_row, [("s", sv)]

    for n in range(NN):
        nt, o = NT[n], NTO[n]
        m1, r1, w1 = stat_pass(lambda k: (xres_t(k)[:, o:o + nt], ()), 0, 512, nt)
        tb = prog.emit("tensor", lambda E, m1=m1, nt=nt: E.matmul(
            psap(4, 128, nt), cv("onesrow", rows=1, ncol=128), m1,
            start=True, stop=True), wait=w1, inc="t")
        tb = prog.emit("tensor", lambda E, r1=r1, nt=nt: E.matmul(
            psap(5, 128, nt), cv("onesrow", rows=1, ncol=128), r1,
            start=True, stop=True), inc="t")
        zw = []
        for k in range(8):
            b = zt[k % 2]
            prog.emit("vector", lambda E, k=k, b=b, o=o, nt=nt: E.tensor_sub(
                b[0:128, 0:nt], xres_t(k)[:, o:o + nt], psap(4, 128, nt)),
                wait=[("t", tb)], inc="v")
            prog.emit("vector", lambda E, b=b, nt=nt: E.tensor_mul(
                b[0:128, 0:nt], b[0:128, 0:nt], psap(5, 128, nt)), inc="v")
            prog.emit("vector", lambda E, k=k, b=b, o=o, nt=nt: E.tensor_add(
                xres_t(k)[:, o:o + nt], xres_t(k)[:, o:o + nt], b[0:128, 0:nt]),
                inc="v")
            vv = prog.emit("vector", lambda E, k=k, b=b, nt=nt: E.tensor_copy(
                z_bf[k][0:128, 0:nt], b[0:128, 0:nt]), inc="v")
            zw.append(vv)
        m2, r2, w2 = stat_pass(
            lambda k: (z_bf[k][0:128, 0:nt], [("v", zw[k])]), 1024, 1536, nt,
            ones_sum=onescol_b[:, 0:1])
        tb2 = prog.emit("tensor", lambda E, m2=m2, nt=nt: E.matmul(
            psap(4, 128, nt), cv("onesrow", rows=1, ncol=128), m2,
            start=True, stop=True), wait=w2, inc="t")
        tb2 = prog.emit("tensor", lambda E, r2=r2, nt=nt: E.matmul(
            psap(5, 128, nt), cv("onesrow", rows=1, ncol=128), r2,
            start=True, stop=True), inc="t")
        for k in range(8):
            b = zt[k % 2]
            prog.emit("vector", lambda E, k=k, b=b, nt=nt: E.tensor_sub(
                b[0:128, 0:nt], z_bf[k][0:128, 0:nt], psap(4, 128, nt)),
                wait=[("t", tb2)], inc="v")
            prog.emit("vector", lambda E, b=b, nt=nt: E.tensor_mul(
                b[0:128, 0:nt], b[0:128, 0:nt], psap(5, 128, nt)), inc="v")
            xb = xnt[k % 2]
            vv = prog.emit("vector", lambda E, k=k, b=b, xb=xb, nt=nt: E.tensor_scalar(
                xb[0:128, 0:nt], b[0:128, 0:nt], cv("gamma", k), cv("beta", k),
                ALU.mult, ALU.add), wait=[("sy%d" % (k % 2), xnt_busy[k % 2])], inc="v")
            xnt_busy[k % 2] = prog.emit("sync", lambda E, k=k, xb=xb, o=o, nt=nt:
                                        E.dma_start(
                out=bass.AP(xn_spill, k * 128 * EXT + o, [[EXT, 128], [1, nt]]),
                in_=xb[0:128, 0:nt]), wait=[("v", vv)], inc="sy%d" % (k % 2), inc_n=16)
        prog.wait("tensor", "v", prog.val("v"))
    xn_done = (prog.val("sy0"), prog.val("sy1"))
    af.pop(); ab.pop()

    st["prog"].scope = "b0.p2"
    af.push()
    ab.push()
    proj = [valloc(af, 128, 516) for _ in range(2)]
    sg = [valloc(af, 128, 512) for _ in range(2)]
    t1 = valloc(af, 128, 512)
    t3 = valloc(af, 128, 512)
    u_bf = valloc(ab, 128, 16 * EXT)

    def u_t(m):
        return View(ab, u_bf.off + m * EXT, 128, EXT)

    st["u_t"] = u_t
    ab.push()
    xn_nt = [valloc(ab, 128, 512) for _ in range(8)]
    wbuf = [valloc(ab, 128, 2048) for _ in range(2)]
    wbuf_busy = [0, 0]
    psA_busy = [0, 0]
    pj_busy = [0, 0]
    prog.wait("gpsimd", "sy0", xn_done[0])
    prog.wait("gpsimd", "sy1", xn_done[1])
    for n in range(NN):
        nt, o = NTO[n] and (NT[n], NTO[n]) or (NT[n], NTO[n])
        nt, o = NT[n], NTO[n]
        xl = []
        for k in range(8):
            lv = gdma(xn_nt[k][0:128, 0:nt],
                      bass.AP(xn_spill, k * 128 * EXT + o, [[EXT, 128], [1, nt]]),
                      wait=[("t", st.get("_xn_rd_t", 0))] if k == 0 else (),
                      sem="wx")
            xl.append(lv)
        for mc in range(16):
            par = mc % 2
            wb = wbuf[par]
            lv = gdma(wb[0:128, 0:1024],
                      bass.AP(di["Win"], mc * 128 * 1024,
                              [[1024, 128], [1, 1024]]),
                      wait=[("t", wbuf_busy[par])], sem="w%d" % par)
            lv = gdma(wb[0:128, 1024:2048],
                      bass.AP(di["Win"], (16 + mc) * 128 * 1024,
                              [[1024, 128], [1, 1024]]),
                      sem="w%d" % par)
            prog.wait("tensor", "w%d" % par, lv)
            prog.wait("tensor", "wx", xl[-1])
            prog.wait("tensor", "s", psA_busy[par])
            prog.emit("tensor", lambda E, par=par, wb=wb, nt=nt: [E.matmul(
                psap(par, 128, nt), wb[0:128, kt * 128:kt * 128 + 128],
                xn_nt[kt][0:128, 0:nt], start=(kt == 0), stop=(kt == 7))
                for kt in range(8)][-1], inc="t")
            prog.emit("tensor", lambda E, par=par, wb=wb, nt=nt: [E.matmul(
                psap(2 + par, 128, nt),
                wb[0:128, 1024 + kt * 128:1024 + kt * 128 + 128],
                xn_nt[kt][0:128, 0:nt], start=(kt == 0), stop=(kt == 7))
                for kt in range(8)][-1], inc="t")
            tmm = prog.val("t")
            wbuf_busy[par] = tmm
            st["_xn_rd_t"] = tmm
            pj, sgb = proj[par], sg[par]
            if n > 0:
                prog.emit("vector", lambda E, pj=pj, mc=mc: E.tensor_copy(
                    pj[0:128, 0:2], cv("ptail", 2 * mc, ncol=2)),
                    wait=[("v", pj_busy[par])], inc="v")
            else:
                prog.emit("vector", lambda E, pj=pj: E.tensor_copy(
                    pj[0:128, 0:2], cv("zero2", ncol=2)),
                    wait=[("v", pj_busy[par])], inc="v")
            halo_v = prog.val("v")
            sv = prog.emit("scalar", lambda E, pj=pj, mc=mc, par=par, nt=nt:
                           E.activation(pj[0:128, 2:2 + nt], psap(par, 128, nt),
                                        ACT.Identity, bias=cv("b_in", mc)),
                           wait=[("t", tmm), ("v", halo_v)], inc="s")
            sv = prog.emit("scalar", lambda E, sgb=sgb, mc=mc, par=par, nt=nt:
                           E.activation(sgb[0:128, 0:nt], psap(2 + par, 128, nt),
                                        ACT.Tanh, scale=0.5, bias=cv("b_in_h", 16 + mc)),
                           inc="s")
            psA_busy[par] = sv
            vw = [("s", sv)]
            if n == 0:
                vv = prog.emit("vector", lambda E, pj=pj: E.tensor_scalar(
                    pj[0:128, 2:136], pj[0:128, 2:136], cv("prev"), None, ALU.mult),
                    wait=vw, inc="v")
                vw = [("v", vv)]
            vv = prog.emit("vector", lambda E, pj=pj, mc=mc, nt=nt: E.tensor_copy(
                cv("ptail", 2 * mc, ncol=2), pj[0:128, nt:nt + 2]), wait=vw, inc="v")
            vv = prog.emit("vector", lambda E, pj=pj, mc=mc, nt=nt: E.tensor_scalar(
                t1[0:128, 0:nt], pj[0:128, 1:1 + nt], cv("conv_w", 3 * mc + 1),
                None, ALU.mult), inc="v")
            vv = prog.emit("vector", lambda E, pj=pj, mc=mc, nt=nt:
                           E.scalar_tensor_tensor(
                t1[0:128, 0:nt], pj[0:128, 0:nt], cv("conv_w", 3 * mc + 0),
                t1[0:128, 0:nt], ALU.mult, ALU.add), inc="v")
            vv = prog.emit("vector", lambda E, pj=pj, mc=mc, nt=nt:
                           E.scalar_tensor_tensor(
                t3[0:128, 0:nt], pj[0:128, 2:2 + nt], cv("conv_w", 3 * mc + 2),
                t1[0:128, 0:nt], ALU.mult, ALU.add), inc="v")
            sv = prog.emit("scalar", lambda E, mc=mc, nt=nt: E.activation(
                t1[0:128, 0:nt], t3[0:128, 0:nt], ACT.Silu, bias=cv("conv_b", mc)),
                wait=[("v", vv)], inc="s")
            vv = prog.emit("vector", lambda E, sgb=sgb, nt=nt: E.tensor_scalar(
                sgb[0:128, 0:nt], sgb[0:128, 0:nt], 0.5, 0.5, ALU.mult, ALU.add),
                wait=[("s", sv)], inc="v")
            vv = prog.emit("vector", lambda E, mc=mc, o=o, nt=nt, sgb=sgb: E.tensor_mul(
                u_t(mc)[0:128, o:o + nt], t1[0:128, 0:nt], sgb[0:128, 0:nt]), inc="v")
            if n == 0:
                prog.emit("vector", lambda E, mc=mc: E.memset(
                    u_t(mc)[0:128, 0:2], 0.0), inc="v")
                vv = prog.emit("vector", lambda E, mc=mc: E.tensor_scalar(
                    u_t(mc)[0:128, 2:134], u_t(mc)[0:128, 2:134], cv("prev"),
                    None, ALU.mult), inc="v")
            pj_busy[par] = vv
    u_done = prog.val("v")
    p2_end_t = prog.val("t")
    af.pop()
    ab.pop()

    st["prog"].scope = "b0.p3"
    af.push()
    ab.push()
    a_row = valloc(af, 128, EXT)
    w_row = valloc(af, 128, EXT)
    s_row = valloc(af, 128, EXT)
    ct = valloc(af, 128, 512)
    tg = valloc(af, 128, 512)
    zrow = valloc(ab, 128, EXT)
    wabc = valloc(ab, 128, 3 * 2048)
    ytile = valloc(ab, 128, EXT)
    cptile = valloc(ab, 128, EXT)
    prog.emit("vector", lambda E: E.memset(zrow[:, :], 0.0),
              wait=[("t", p2_end_t)], inc="v")
    prog.wait("tensor", "v", u_done)

    spill_done = 0
    yv_done = 0
    ev_s = 0
    ev_v = 0
    wload = {0: {}, 1: {}, 2: {}}
    slot_busy = {0: p2_end_t, 1: p2_end_t, 2: p2_end_t}
    WNAMES = ("Wa", "Wb", "Wc")

    def load_w(wi, gi):
        wload[wi][gi] = gdma(
            wabc[0:128, wi * 2048:(wi + 1) * 2048],
            bass.AP(di[WNAMES[wi]], gi * 128 * 2048, [[2048, 128], [1, 2048]]),
            wait=[("t", slot_busy[wi])], sem="w0")

    for wi in range(3):
        load_w(wi, 0)
    for gi in range(16):
        for which in range(2):
            prog.wait("tensor", "w0", wload[which][gi])
            for n in range(NN):
                nt, o = NT[n], NTO[n]
                par = n % 2
                prog.wait("tensor", "s", ev_s)
                prog.emit("tensor", lambda E, par=par, o=o, nt=nt,
                          which=which: [E.matmul(
                    psap(par, 128, nt),
                    wabc[0:128, which * 2048 + kt * 128:
                         which * 2048 + kt * 128 + 128],
                    u_t(kt)[0:128, o:o + nt], start=(kt == 0), stop=(kt == 15))
                    for kt in range(16)][-1], inc="t")
                tmm = prog.val("t")
                bh = "ba_h" if which == 0 else "bb_h"
                sv = prog.emit("scalar", lambda E, par=par, gi=gi, nt=nt, bh=bh:
                               E.activation(tg[0:128, 0:nt], psap(par, 128, nt),
                                            ACT.Tanh, scale=0.5, bias=cv(bh, gi)),
                               wait=[("t", tmm), ("v", prog.val("v"))], inc="s")
                ev_s = sv
                if which == 0:
                    vv = prog.emit("vector", lambda E, o=o, nt=nt: E.tensor_scalar(
                        a_row[0:128, o:o + nt], tg[0:128, 0:nt], 0.5, 0.5,
                        ALU.mult, ALU.add), wait=[("s", sv)], inc="v")
                else:
                    vv = prog.emit("vector", lambda E, o=o, nt=nt: E.tensor_scalar(
                        tg[0:128, 0:nt], tg[0:128, 0:nt], 0.5, 0.5,
                        ALU.mult, ALU.add), wait=[("s", sv)], inc="v")
                    vv = prog.emit("vector", lambda E, o=o, nt=nt, gi=gi: E.tensor_mul(
                        w_row[0:128, o:o + nt], tg[0:128, 0:nt],
                        u_t(gi)[0:128, o:o + nt]), inc="v")
                ev_v = vv
            slot_busy[which] = prog.val("t")
            if gi + 1 < 16:
                load_w(which, gi + 1)
        vv = prog.emit("vector", lambda E: E.tensor_tensor_scan(
            s_row[:, :], a_row[:, :], w_row[:, :], 0.0, ALU.mult, ALU.add), inc="v")
        vv = prog.emit("vector", lambda E: E.tensor_tensor_scan(
            w_row[:, :], a_row[:, :], zrow[:, :], 1.0, ALU.mult, ALU.add), inc="v")
        vv = prog.emit("vector", lambda E, gi=gi: E.tensor_copy(
            cv("exs", gi), s_row[0:128, 2049:2050]), inc="v")
        vv = prog.emit("vector", lambda E, gi=gi: E.tensor_copy(
            cv("exp", gi), w_row[0:128, 2049:2050]), inc="v")
        scan_done = vv
        ev_s2 = 0
        prog.wait("tensor", "w0", wload[2][gi])
        for n in range(NN):
            nt, o = NT[n], NTO[n]
            par = 4 + n % 2
            prog.wait("tensor", "s", ev_s2)
            prog.emit("tensor", lambda E, par=par, o=o, nt=nt: [E.matmul(
                psap(par, 128, nt),
                wabc[0:128, 2 * 2048 + kt * 128:2 * 2048 + kt * 128 + 128],
                u_t(kt)[0:128, o:o + nt], start=(kt == 0), stop=(kt == 15))
                for kt in range(16)][-1], inc="t")
            tmm = prog.val("t")
            sv = prog.emit("scalar", lambda E, par=par, gi=gi, nt=nt: E.activation(
                ct[0:128, 0:nt], psap(par, 128, nt), ACT.Tanh, bias=cv("bc", gi)),
                wait=[("t", tmm), ("v", prog.val("v"))], inc="s")
            ev_s2 = sv
            prog.emit("vector", lambda E, o=o, nt=nt: E.tensor_mul(
                tg[0:128, 0:nt], ct[0:128, 0:nt], s_row[0:128, o:o + nt]),
                wait=[("s", sv), ("v", scan_done), ("sy", spill_done)], inc="v")
            prog.emit("vector", lambda E, o=o, nt=nt, gi=gi: E.tensor_add(
                ytile[0:128, o:o + nt], tg[0:128, 0:nt], u_t(gi)[0:128, o:o + nt]),
                inc="v")
            yv_done = prog.emit("vector", lambda E, o=o, nt=nt: E.tensor_mul(
                cptile[0:128, o:o + nt], ct[0:128, 0:nt], w_row[0:128, o:o + nt]),
                inc="v")
        prog.emit("sync", lambda E, gi=gi: E.dma_start(
            out=bass.AP(y_spill, gi * 128 * EXT, [[EXT, 128], [1, EXT]]),
            in_=ytile[:, :]), wait=[("v", yv_done)], inc="sy", inc_n=16)
        spill_done = prog.emit("sync", lambda E, gi=gi: E.dma_start(
            out=bass.AP(cp_spill, gi * 128 * EXT, [[EXT, 128], [1, EXT]]),
            in_=cptile[:, :]), inc="sy", inc_n=16)
        slot_busy[2] = prog.val("t")
        if gi + 1 < 16:
            load_w(2, gi + 1)
        ev_v = prog.val("v")
        ev_s = prog.val("s")
    p3_end_t = prog.val("t")
    af.pop()
    ab.pop()
    ab.pop()

    st["prog"].scope = "b0.ag"
    exdone = prog.val("v")
    gv = prog.emit("gpsimd", lambda E: E.dma_start(
        out=st["ag1_in"].ap(), in_=cv("exs", 0, ncol=32)),
        wait=[("v", exdone)], inc="wg", inc_n=16)
    cv1 = prog.emit("gpsimd", lambda E: E.collective_compute(
        "AllGather", ALU.bypass, replica_groups=[list(range(8))],
        ins=[st["ag1_in"].ap().opt()], outs=[st["ag1_out"].ap().opt()]),
        wait=[("wg", gv)], inc="cc", inc_n=1)
    af.push()
    agb = valloc(af, 128, 256)
    t2f = valloc(af, 128, 16)
    t3f = valloc(af, 128, 16)
    gv = prog.emit("gpsimd", lambda E: E.dma_start(
        out=agb[:, :], in_=bass.AP(st["ag1_out"], 0,
                                   [[32, 128], [128 * 32, 8], [1, 32]])),
        wait=[("cc", cv1)], inc="wg", inc_n=16)
    prog.wait("vector", "wg", gv)
    prog.emit("vector", lambda E: E.memset(cv("Sin", 0, ncol=16), 0.0), inc="v")
    for j in range(8):
        prog.emit("vector", lambda E, j=j: E.tensor_scalar(
            t2f[:, :], View(af, agb.off + j * 32 + 16, 128, 16)[:, :],
            cv("foldm", j), cv("foldm1", j), ALU.mult, ALU.add), inc="v")
        prog.emit("vector", lambda E: E.tensor_mul(
            t3f[:, :], cv("Sin", 0, ncol=16), t2f[:, :]), inc="v")
        prog.emit("vector", lambda E, j=j: E.tensor_scalar(
            t2f[:, :], View(af, agb.off + j * 32, 128, 16)[:, :],
            cv("foldm", j), None, ALU.mult), inc="v")
        prog.emit("vector", lambda E: E.tensor_add(
            cv("Sin", 0, ncol=16), t3f[:, :], t2f[:, :]), inc="v")
    fold_done = prog.val("v")
    af.pop()

    st["prog"].scope = "b0.p4"
    ab.push()
    wo_all = valloc(ab, 128, 16 * 1024)
    lv = gdma(wo_all[:, :], bass.AP(di["Wo"], 0,
                                    [[1024, 128], [128 * 1024, 16], [1, 1024]]),
              wait=[("t", p3_end_t)], sem="wx")
    ybuf = [valloc(ab, 128, 512) for _ in range(16)]
    yfin = [valloc(ab, 128, 512) for _ in range(16)]
    prog.wait("tensor", "wx", lv)
    bank_busy = [0, 0, 0, 0]
    yread_t = 0
    for n in range(NN):
        nt, o = NT[n], NTO[n]
        lw = []
        for gi in range(16):
            lv = gdma(ybuf[gi][0:128, 0:nt],
                      bass.AP(y_spill, gi * 128 * EXT + o, [[EXT, 128], [1, nt]]),
                      wait=[("sy", spill_done), ("t", yread_t)] if gi == 0 else (),
                      sem="wy")
            lv = gdma(yfin[gi][0:128, 0:nt],
                      bass.AP(cp_spill, gi * 128 * EXT + o, [[EXT, 128], [1, nt]]),
                      sem="wy")
            lw.append(lv)
        yv = []
        for gi in range(16):
            vv = prog.emit("vector", lambda E, gi=gi, nt=nt: E.scalar_tensor_tensor(
                yfin[gi][0:128, 0:nt], yfin[gi][0:128, 0:nt], cv("Sin", gi),
                ybuf[gi][0:128, 0:nt], ALU.mult, ALU.add),
                wait=[("wy", lw[-1]), ("v", fold_done)], inc="v")
            yv.append(vv)
        for mo in range(8):
            par = mo % 4
            prog.wait("tensor", "v", bank_busy[par])
            for kt in range(16):
                prog.emit("tensor", lambda E, kt=kt, mo=mo, par=par, nt=nt: E.matmul(
                    psap(par, 128, nt),
                    wo_all[0:128, kt * 1024 + mo * 128:kt * 1024 + mo * 128 + 128],
                    yfin[kt][0:128, 0:nt], start=(kt == 0), stop=(kt == 15)),
                    wait=[("v", yv[-1])] if kt == 0 else (), inc="t")
            tmm = prog.val("t")
            bank_busy[par] = prog.emit(
                "vector", lambda E, mo=mo, par=par, o=o, nt=nt:
                E.scalar_tensor_tensor(
                    xres_t(mo)[0:128, o:o + nt], psap(par, 128, nt), cv("bo", mo),
                    xres_t(mo)[0:128, o:o + nt], ALU.add, ALU.add),
                wait=[("t", tmm)], inc="v")
        yread_t = prog.val("t")
    for k in range(8):
        prog.emit("vector", lambda E, k=k: E.tensor_scalar(
            xres_t(k)[0:128, 0:H], xres_t(k)[0:128, 0:H], cv("prev"), None,
            ALU.mult), inc="v")
    ab.pop()
    x1_done = prog.val("v")
    prog.wait("tensor", "v", x1_done)
    prog.wait("scalar", "v", x1_done)
    prog.wait("gpsimd", "v", x1_done)
    prog.wait("sync", "v", x1_done)
    st["x1_done"] = x1_done


def emit_b1(st):
    prog, af, ab = st["prog"], st["af"], st["ab"]
    psap, cv, di = st["psap"], st["cv"], st["di"]
    gdma = st["gdma"]
    xres_t = st["xres_t"]
    af.push(); ab.push()
    x1n = valloc(ab, 128, 8 * EXT)

    def x1n_t(k):
        return View(ab, x1n.off + k * EXT, 128, EXT)

    emit_pre_ln(st, xres_t, None, dst_t=x1n_t)
    ln_done = prog.val("v")
    prog.wait("tensor", "v", ln_done)
    prog.wait("gpsimd", "v", ln_done)
    if st.get("dbg_mode") == "x1n":
        prog.wait("gpsimd", "v", ln_done)
        for k in range(8):
            dst = bass.AP(st["dbg_out"], k * 128 * EXT, [[EXT, 128], [1, EXT]])
            st["gdma"](dst, x1n_t(k)[:, :], sem="dbgs")
        prog.wait("gpsimd", "dbgs", prog.val("dbgs"))

    wkv = valloc(ab, 128, 2 * 8192)
    wr_s = valloc(ab, 128, 8192)
    lv = gdma(View(ab, wkv.off, 128, 8192)[:, :],
              bass.AP(di["Wk"], 0, [[1024, 128], [128 * 1024, 8], [1, 1024]]),
              sem="w0")
    lv = gdma(View(ab, wkv.off + 8192, 128, 8192)[:, :],
              bass.AP(di["Wv"], 0, [[1024, 128], [128 * 1024, 8], [1, 1024]]),
              sem="w0")
    lvr = gdma(wr_s[:, :], bass.AP(di["Wr"], 0, [[1024, 128], [128 * 1024, 8], [1, 1024]]),
               sem="w0")
    xrk = valloc(ab, 128, 8 * 512)
    ka = valloc(ab, 128, 8 * 512)
    rbuf = valloc(af, 128, 8 * 512)
    tb1 = valloc(af, 128, 512)
    tbr = valloc(af, 128, 512)
    tbk = valloc(af, 128, 512)
    prog.wait("tensor", "w0", lvr)

    def wslice(w, kt, m):
        base = wkv.off if w == "k" else (wkv.off + 8192 if w == "v" else wr_s.off)
        return View(ab, base + kt * 1024 + m * 128, 128, 128)[:, :]

    bank_s = [0] * 8
    bank_v = [0] * 8
    for n in range(NN):
        nt, o = NT[n], NTO[n]
        if n == 0:
            o, nt = 1, NT[0] - 1
        mixv = 0
        for which, (mu, mu1) in enumerate((("mu_r", "mu_r1"), ("mu_k", "mu_k1"))):
            dstv = xrk
            for k in range(8):
                prog.emit("vector", lambda E, k=k, o=o, nt=nt, mu1=mu1: E.tensor_scalar(
                    tb1[0:128, 0:nt], x1n_t(k)[0:128, o - 1:o - 1 + nt],
                    cv(mu1, k), None, ALU.mult),
                    wait=[("t", bank_v[7])] if which == 1 and k == 0 else (), inc="v")
                mixv = prog.emit("vector", lambda E, k=k, o=o, nt=nt, mu=mu, which=which:
                                 E.scalar_tensor_tensor(
                    xrk[0:128, k * 512:k * 512 + nt], x1n_t(k)[0:128, o:o + nt],
                    cv(mu, k), tb1[0:128, 0:nt], ALU.mult, ALU.add), inc="v")
            if which == 0:
                prog.wait("tensor", "v", mixv)
                for m in range(8):
                    pb = m % 4
                    prog.wait("tensor", "s", bank_s[pb])
                    prog.emit("tensor", lambda E, m=m, pb=pb, nt=nt: [E.matmul(
                        psap(pb, 128, nt), wslice("r", kt, m),
                        View(ab, xrk.off + kt * 512, 128, nt)[0:128, 0:nt],
                        start=(kt == 0), stop=(kt == 7))
                        for kt in range(8)][-1], inc="t")
                    tmm = prog.val("t")
                    bank_v[7] = tmm
                    sv = prog.emit("scalar", lambda E, m=m, pb=pb, nt=nt: E.activation(
                        tbr[0:128, 0:nt], psap(pb, 128, nt), ACT.Tanh, scale=0.5),
                        wait=[("t", tmm), ("v", prog.val("v"))], inc="s")
                    bank_s[pb] = sv
                    prog.emit("vector", lambda E, m=m, nt=nt: E.tensor_scalar(
                        View(af, rbuf.off + m * 512, 128, nt)[0:128, 0:nt],
                        tbr[0:128, 0:nt], 0.5, 0.5, ALU.mult, ALU.add),
                        wait=[("s", sv)], inc="v")
        prog.wait("tensor", "v", mixv)
        for m in range(8):
            pb = m % 4
            prog.wait("tensor", "s", bank_s[pb])
            prog.emit("tensor", lambda E, m=m, pb=pb, nt=nt: [E.matmul(
                psap(pb, 128, nt), wslice("k", kt, m),
                View(ab, xrk.off + kt * 512, 128, nt)[0:128, 0:nt],
                start=(kt == 0), stop=(kt == 7))
                for kt in range(8)][-1], inc="t")
            tmm = prog.val("t")
            sv = prog.emit("scalar", lambda E, m=m, pb=pb, nt=nt: E.activation(
                tbk[0:128, 0:nt], psap(pb, 128, nt), ACT.Relu),
                wait=[("t", tmm), ("v", prog.val("v"))], inc="s")
            sv = prog.emit("scalar", lambda E, m=m, nt=nt: E.activation(
                View(ab, ka.off + m * 512, 128, nt)[0:128, 0:nt],
                tbk[0:128, 0:nt], ACT.Square), inc="s")
            bank_s[pb] = sv
        kadone = prog.val("s")
        prog.wait("tensor", "s", kadone)
        for m in range(8):
            pb = 4 + m % 2
            prog.wait("tensor", "v", bank_v[pb])
            prog.emit("tensor", lambda E, m=m, pb=pb, nt=nt: [E.matmul(
                psap(pb, 128, nt), wslice("v", kt, m),
                View(ab, ka.off + kt * 512, 128, nt)[0:128, 0:nt],
                start=(kt == 0), stop=(kt == 7))
                for kt in range(8)][-1], inc="t")
            tmm = prog.val("t")
            vv = prog.emit("vector", lambda E, m=m, pb=pb, nt=nt: E.tensor_mul(
                tb1[0:128, 0:nt], psap(pb, 128, nt),
                View(af, rbuf.off + m * 512, 128, nt)[0:128, 0:nt]),
                wait=[("t", tmm)], inc="v")
            vv = prog.emit("vector", lambda E, m=m, o=o, nt=nt: E.tensor_add(
                xres_t(m)[0:128, o:o + nt], xres_t(m)[0:128, o:o + nt],
                tb1[0:128, 0:nt]), inc="v")
            bank_v[pb] = vv
        prog.wait("vector", "t", prog.val("t"))
    af.pop(); ab.pop()
    x2_done = prog.val("v")
    prog.wait("tensor", "v", x2_done)
    prog.wait("scalar", "v", x2_done)
    prog.wait("gpsimd", "v", x2_done)
    prog.wait("sync", "v", x2_done)


def emit_b2(st):
    prog, af, ab = st["prog"], st["af"], st["ab"]
    psap, cv, di = st["psap"], st["cv"], st["di"]
    gdma = st["gdma"]
    ps = st["ps"]
    xres_t = st["xres_t"]
    xn_spill = st["xn_spill"]
    out_spill = st["out_spill"]
    af.push(); ab.push()

    x2_done = emit_pre_ln(st, xres_t, xn_spill)
    prog.wait("gpsimd", "sy0", x2_done[0])
    prog.wait("gpsimd", "sy1", x2_done[1])

    q_rot = valloc(ab, 128, 8 * 2048)
    lkv = valloc(ab, 128, 8 * EXT)
    payload = valloc(ab, 128, 1024)
    hsel_b = valloc(ab, 128, 2)
    s2T_b = valloc(ab, 2, 128)
    wp2_b = valloc(ab, 128, 2)
    s2T_f = valloc(af, 2, 128)
    hsel_f = valloc(af, 128, 2)
    mq = valloc(af, 2, 8)
    lv = gdma(hsel_b[:, :], di["halfsel"].ap(), sem="wx")
    lv = gdma(s2T_b[0:2, :], di["sel2T"].ap(), sem="wx")
    lv = gdma(wp2_b[:, :], di["wproj2"].ap(), sem="wx")
    lv = gdma(s2T_f[0:2, :], di["sel2T"].ap(), sem="wx")
    lv = gdma(hsel_f[:, :], di["halfsel"].ap(), sem="wx")
    tabs = lv
    prog.wait("tensor", "wx", tabs)
    prog.wait("vector", "wx", tabs)
    prog.emit("vector", lambda E: E.memset(cv("spare", ncol=1, rows=2), 1.0), inc="v")
    ones2_set = prog.val("v")

    def qr(hp):
        return View(ab, q_rot.off + hp * 2048, 128, 2048)

    def lk(hp):
        return View(ab, lkv.off + hp * EXT, 128, EXT)

    KVC = [(6, 512), (518, 512), (1030, 512), (1542, 512), (2054, 128)]
    OWN = [(134 + 512 * c, 512) for c in range(4)]

    st["prog"].scope = "b2.l1"
    af.push(); ab.push()
    cos_b = valloc(ab, 128, EXT)
    sin_b = valloc(ab, 128, EXT)
    xnt = [valloc(ab, 128, 512) for _ in range(8)]
    kvrot = valloc(ab, 128, EXT)
    wslc = [valloc(ab, 128, 1024) for _ in range(2)]
    kvb = valloc(ab, 128, 512)
    srow = valloc(af, 2, 2048)
    s2 = valloc(af, 2, 2048)
    sc1 = valloc(af, 128, 512)
    sc2 = valloc(af, 128, 512)
    segt = valloc(af, 128, 35)
    comp = valloc(af, 128, 128)
    lv = gdma(cos_b[:, :], di["cos_t"].ap(), sem="wx")
    lv = gdma(sin_b[:, :], di["sin_t"].ap(), sem="wx")
    tabs2 = lv
    prog.wait("vector", "wx", tabs2)

    wq_busy = [0, 0]
    bank_v = {i: 0 for i in range(8)}
    bank_s = {i: 0 for i in range(8)}
    st["_b2xt"] = 0

    def stream_x2n(c0, nt):
        lvv = None
        for k in range(8):
            lvv = gdma(xnt[k][0:128, 0:nt],
                       bass.AP(xn_spill, k * 128 * EXT + c0, [[EXT, 128], [1, nt]]),
                       wait=[("t", st["_b2xt"])] if k == 0 else (), sem="wx")
        return lvv

    def rot_sweep(hp, wname, chunks, dst_fn, scale_q):
        par = 0 if wname == "Wkv" else 1
        wb = wslc[par]
        lvw = gdma(wb[:, :], bass.AP(di[wname], hp * 128,
                                     [[1024, 128], [128 * 1024, 8], [1, 128]]),
                   wait=[("t", wq_busy[par])], sem="w%d" % par)
        if scale_q:
            prog.emit("vector", lambda E, wb=wb: E.tensor_scalar(
                wb[:, :], wb[:, :], 0.125, None, ALU.mult),
                wait=[("w%d" % par, lvw)], inc="v")
            prog.wait("tensor", "v", prog.val("v"))
        else:
            prog.wait("tensor", "w%d" % par, lvw)
        for (c0, nt) in chunks:
            xv = stream_x2n(c0, nt)
            prog.wait("tensor", "wx", xv)
            prog.wait("tensor", "s", bank_s[0])
            prog.wait("tensor", "v", bank_v[0])
            prog.emit("tensor", lambda E, nt=nt, wb=wb: [E.matmul(
                psap(0, 128, nt), wb[0:128, kt * 128:kt * 128 + 128],
                xnt[kt][0:128, 0:nt], start=(kt == 0), stop=(kt == 7))
                for kt in range(8)][-1], inc="t")
            tmm = prog.val("t")
            st["_b2xt"] = tmm
            sv = prog.emit("scalar", lambda E, nt=nt: E.activation(
                kvb[0:128, 0:nt], psap(0, 128, nt), ACT.Copy),
                wait=[("t", tmm), ("v", bank_v[1])], inc="s")
            bank_s[0] = sv
            prog.wait("tensor", "s", sv)
            prog.wait("tensor", "v", bank_v[1])
            tv = prog.emit("tensor", lambda E, nt=nt: E.matmul(
                psap(1, 128, nt), st["prot_bf"][:, :], kvb[0:128, 0:nt],
                start=True, stop=True), inc="t")
            vv = prog.emit("vector", lambda E, c0=c0, nt=nt: E.tensor_mul(
                sc1[0:128, 0:nt], kvb[0:128, 0:nt], cos_b[0:128, c0:c0 + nt]),
                wait=[("s", sv)], inc="v")
            vv = prog.emit("vector", lambda E, c0=c0, nt=nt: E.tensor_mul(
                sc2[0:128, 0:nt], psap(1, 128, nt), sin_b[0:128, c0:c0 + nt]),
                wait=[("t", tv)], inc="v")
            vv = prog.emit("vector", lambda E, c0=c0, nt=nt, dst=dst_fn(c0, nt):
                           E.tensor_add(dst, sc1[0:128, 0:nt], sc2[0:128, 0:nt]),
                           inc="v")
            bank_v[1] = vv
        wq_busy[par] = prog.val("t")
        return prog.val("v")

    def ln64_stats(src_ap_fn, chunks, dst_write, ones_sum=None):
        osum = ones_sum if ones_sum is not None else hsel_b
        for (c0, nt) in chunks:
            srcap = src_ap_fn(c0, nt)
            sv = prog.emit("scalar", lambda E, srcap=srcap, nt=nt: E.activation(
                sc1[0:128, 0:nt], srcap, ACT.Square),
                wait=[("v", prog.val("v"))], inc="s")
            prog.wait("tensor", "s", sv)
            prog.wait("tensor", "v", bank_v[3])
            tv = prog.emit("tensor", lambda E, srcap=srcap, nt=nt, osum=osum: E.matmul(
                psap(2, 2, nt), osum[:, :], srcap, start=True, stop=True), inc="t")
            tv = prog.emit("tensor", lambda E, nt=nt: E.matmul(
                psap(3, 2, nt), hsel_f[:, :], sc1[0:128, 0:nt],
                start=True, stop=True), inc="t")
            m_row = srow[0:2, 0:nt]
            r_row = srow[0:2, 512:512 + nt]
            tmp = srow[0:2, 1024:1024 + nt]
            vv = prog.emit("vector", lambda E, m_row=m_row, nt=nt: E.tensor_scalar(
                m_row, psap(2, 2, nt), 1.0 / DH, None, ALU.mult),
                wait=[("t", tv)], inc="v")
            vv = prog.emit("vector", lambda E, m_row=m_row, tmp=tmp: E.tensor_mul(
                tmp, m_row, m_row), inc="v")
            vv = prog.emit("vector", lambda E, r_row=r_row, tmp=tmp, nt=nt:
                           E.scalar_tensor_tensor(
                r_row, psap(3, 2, nt), 1.0 / DH, tmp, ALU.mult, ALU.subtract),
                inc="v")
            sv = prog.emit("scalar", lambda E, r_row=r_row, tmp=tmp: E.activation(
                tmp, r_row, ACT.Ln, bias=cv("eps", rows=2)),
                wait=[("v", vv)], inc="s")
            sv = prog.emit("scalar", lambda E, r_row=r_row, tmp=tmp: E.activation(
                r_row, tmp, ACT.Exp, scale=-0.5), inc="s")
            prog.wait("tensor", "s", sv)
            tv = prog.emit("tensor", lambda E, m_row=m_row, nt=nt: E.matmul(
                psap(2, 128, nt), s2T_f[0:2, :], m_row, start=True, stop=True),
                inc="t")
            tv = prog.emit("tensor", lambda E, r_row=r_row, nt=nt: E.matmul(
                psap(3, 128, nt), s2T_f[0:2, :], r_row, start=True, stop=True),
                inc="t")
            vv = prog.emit("vector", lambda E, srcap=srcap, nt=nt: E.tensor_sub(
                sc1[0:128, 0:nt], srcap, psap(2, 128, nt)),
                wait=[("t", tv)], inc="v")
            vv = prog.emit("vector", lambda E, nt=nt: E.tensor_mul(
                sc1[0:128, 0:nt], sc1[0:128, 0:nt], psap(3, 128, nt)), inc="v")
            dst_write(c0, nt)
            bank_v[3] = prog.val("v")

    for hp in range(8):
        rot_sweep(hp, "Wkv", KVC, lambda c0, nt: kvrot[0:128, c0:c0 + nt], False)
        rot_sweep(hp, "Wq", OWN,
                  lambda c0, nt, hp=hp: qr(hp)[0:128, c0 - 134:c0 - 134 + nt], True)
        qv = prog.val("v")
        for ci, (c0, nt) in enumerate(OWN):
            sv = prog.emit("scalar", lambda E, c0=c0, nt=nt, hp=hp: E.activation(
                sc1[0:128, 0:nt], qr(hp)[0:128, c0 - 134:c0 - 134 + nt], ACT.Square),
                wait=[("v", qv), ("t", prog.val("t"))], inc="s")
            prog.wait("tensor", "s", sv)
            prog.wait("tensor", "v", bank_v[2])
            tv = prog.emit("tensor", lambda E, nt=nt: E.matmul(
                psap(2, 2, nt), hsel_f[:, :], sc1[0:128, 0:nt],
                start=True, stop=True), inc="t")
            bank_v[2] = prog.emit("vector", lambda E, ci=ci, nt=nt: E.reduce_max(
                srow[0:2, ci:ci + 1], psap(2, 2, nt), axis=AX.X),
                wait=[("t", tv)], inc="v")
        vv = prog.emit("vector", lambda E: E.reduce_max(
            srow[0:2, 4:5], srow[0:2, 0:4], axis=AX.X), inc="v")
        sv = prog.emit("scalar", lambda E: E.activation(
            srow[0:2, 1:2], srow[0:2, 4:5], ACT.Ln, bias=cv("eps", rows=2)),
            wait=[("v", vv)], inc="s")
        sv = prog.emit("scalar", lambda E, hp=hp: E.activation(
            mq[0:2, hp:hp + 1], srow[0:2, 1:2], ACT.Exp, scale=0.5), inc="s")
        ln64_stats(lambda c0, nt: kvrot[0:128, c0:c0 + nt], KVC,
                   lambda c0, nt, hp=hp: prog.emit(
                       "vector", lambda E, c0=c0, nt=nt, hp=hp: E.tensor_scalar(
                           lk(hp)[0:128, c0:c0 + nt], sc1[0:128, 0:nt],
                           cv("lng_l"), cv("lnb_l"), ALU.mult, ALU.add), inc="v"))
        for ci, (c0, nt) in enumerate(OWN):
            prog.wait("tensor", "v", bank_v[4])
            tv = prog.emit("tensor", lambda E, c0=c0, nt=nt: E.matmul(
                psap(4, 2, nt), wp2_b[:, :], kvrot[0:128, c0:c0 + nt],
                start=True, stop=True), inc="t")
            vv = prog.emit("vector", lambda E, ci=ci, nt=nt: E.tensor_copy(
                s2[0:2, ci * 512:ci * 512 + nt], psap(4, 2, nt)),
                wait=[("t", tv)], inc="v")
            bank_v[4] = vv
        sdone = prog.val("v")
        segdma = st.get("_segdma", 0)
        for h in range(2):
            dv = prog.emit("sync", lambda E, h=h: E.dma_start(
                out=bass.AP(af.t, segt.off, [[af.cols, 128], [1, 16]]),
                in_=bass.AP(af.t, s2.off + h * af.cols,
                            [[af.cols, 1], [16, 128], [1, 16]])),
                wait=[("v", prog.val("v"))], inc="sg", inc_n=16)
            dgv = prog.val("sg")
            vv = prog.emit("vector", lambda E: E.reduce_max(
                View(af, segt.off + 16, 128, 1)[0:128, 0:1],
                View(af, segt.off, 128, 16)[0:128, 0:16], axis=AX.X),
                wait=[("sg", dgv)], inc="v")
            vv = prog.emit("vector", lambda E: E.tensor_scalar(
                View(af, segt.off + 16, 128, 1)[0:128, 0:1],
                View(af, segt.off + 16, 128, 1)[0:128, 0:1], -1.0, None, ALU.mult),
                inc="v")
            sv = prog.emit("scalar", lambda E: E.activation(
                View(af, segt.off + 17, 128, 16)[0:128, 0:16],
                View(af, segt.off, 128, 16)[0:128, 0:16], ACT.Exp,
                bias=View(af, segt.off + 16, 128, 1)[0:128, 0:1],
                accum_out=View(af, segt.off + 33, 128, 1)[0:128, 0:1]),
                wait=[("v", vv)], inc="s")
            vv = prog.emit("vector", lambda E: E.reciprocal(
                View(af, segt.off + 34, 128, 1)[0:128, 0:1],
                View(af, segt.off + 33, 128, 1)[0:128, 0:1]),
                wait=[("s", sv)], inc="v")
            vv = prog.emit("vector", lambda E: E.tensor_scalar(
                View(af, segt.off + 17, 128, 16)[0:128, 0:16],
                View(af, segt.off + 17, 128, 16)[0:128, 0:16],
                View(af, segt.off + 34, 128, 1)[0:128, 0:1], None, ALU.mult),
                inc="v")
            dv = prog.emit("sync", lambda E, h=h: E.dma_start(
                out=bass.AP(af.t, s2.off + h * af.cols,
                            [[af.cols, 1], [16, 128], [1, 16]]),
                in_=bass.AP(af.t, segt.off + 17, [[af.cols, 128], [1, 16]])),
                wait=[("v", vv)], inc="sg", inc_n=16)
        st["_segdma"] = prog.val("sg")
        prog.wait("tensor", "sg", st["_segdma"])
        prog.wait("vector", "sg", st["_segdma"])
        for ci, (c0, nt) in enumerate(OWN):
            prog.wait("tensor", "v", bank_v[5])
            tv = prog.emit("tensor", lambda E, ci=ci, nt=nt: E.matmul(
                psap(5, 128, nt), s2T_f[0:2, :], s2[0:2, ci * 512:ci * 512 + nt],
                start=True, stop=True), inc="t")
            vv = prog.emit("vector", lambda E, c0=c0, nt=nt: E.tensor_mul(
                sc1[0:128, 0:nt], kvrot[0:128, c0:c0 + nt], psap(5, 128, nt)),
                wait=[("t", tv)], inc="v")
            vv = prog.emit("vector", lambda E, ci=ci: E.reduce_sum(
                comp[0:128, ci * 32:ci * 32 + 32],
                bass.AP(af.t, sc1.off, [[af.cols, 128], [16, 32], [1, 16]]),
                axis=AX.X), inc="v")
            bank_v[5] = vv
        def wr_gkv(c0, nt, hp=hp):
            vv = prog.emit("vector", lambda E, nt=nt: E.tensor_scalar(
                sc2[0:128, 0:nt], sc1[0:128, 0:nt], cv("lng_g"), cv("lnb_g"),
                ALU.mult, ALU.add), inc="v")
        ln64_stats(lambda c0, nt: comp[0:128, 0:128], [(0, 128)], wr_gkv,
                   ones_sum=hsel_f)
        prog.wait("tensor", "v", prog.val("v"))
        tv = prog.emit("tensor", lambda E: E.transpose(
            psap(0, 128, 128), sc2[0:128, 0:128], st["ident_f"][:, :]),
            wait=[("s", bank_s[0]), ("v", bank_v[1])], inc="t")
        vv = prog.emit("vector", lambda E, hp=hp: E.tensor_copy(
            payload[0:128, hp * 128:hp * 128 + 128], psap(0, 128, 128)),
            wait=[("t", tv)], inc="v")
        bank_v[0] = vv
    pay_done = prog.val("v")
    if st.get("dbg_mode") in ("b2q", "b2lkv", "b2pay"):
        prog.wait("gpsimd", "v", pay_done)
        mode = st["dbg_mode"]
        for hp in range(8):
            if mode == "b2q":
                srcv = qr(hp)[:, 0:2048]
                ncol = 2048
            elif mode == "b2lkv":
                srcv = lk(hp)[:, 0:EXT]
                ncol = EXT
            else:
                srcv = payload[0:128, hp * 128:hp * 128 + 128]
                ncol = 128
            dst = bass.AP(st["dbg_out"], hp * 128 * EXT, [[EXT, 128], [1, ncol]])
            gdma(dst, srcv, sem="dbgs")
        prog.wait("gpsimd", "dbgs", prog.val("dbgs"))
    af.pop(); ab.pop()

    gv = prog.emit("gpsimd", lambda E: E.dma_start(
        out=st["ag2_in"].ap(), in_=payload[:, :]),
        wait=[("v", pay_done)], inc="wg", inc_n=16)
    cv2 = prog.emit("gpsimd", lambda E: E.collective_compute(
        "AllGather", ALU.bypass, replica_groups=[list(range(8))],
        ins=[st["ag2_in"].ap().opt()], outs=[st["ag2_out"].ap().opt()]),
        wait=[("wg", gv)], inc="cc", inc_n=1)
    prog.wait("sync", "cc", cv2)

    st["prog"].scope = "b2.l2"
    af.push(); ab.push()
    gkv_fm = valloc(ab, 128, NSEG)
    vtokg = [valloc(ab, 128, 130) for _ in range(8)]
    vtokl = [valloc(ab, 128, 130) for _ in range(17)]
    gmask = valloc(ab, 128, 4096)
    attn = valloc(ab, 128, 4096)
    attl = valloc(ab, 128, 256)
    outb = valloc(ab, 128, 512)
    lsc = valloc(af, 128, 512)
    outf = valloc(af, 128, 512)
    rrow = valloc(af, 1, 512)
    gn = valloc(af, 2, 8)
    nbuf = valloc(af, 128, 1)
    for i in range(8):
        prog.emit("vector", lambda E, i=i: E.memset(vtokg[i][0:128, 64:65], 1.0), inc="v")
        prog.emit("vector", lambda E, i=i: E.memset(vtokg[i][0:128, 129:130], 1.0), inc="v")
    for i in range(17):
        prog.emit("vector", lambda E, i=i: E.memset(vtokl[i][0:128, 64:65], 1.0), inc="v")
        prog.emit("vector", lambda E, i=i: E.memset(vtokl[i][0:128, 129:130], 1.0), inc="v")
    ones_set = prog.val("v")
    prog.wait("sync", "v", ones_set)

    out_dma = 0
    gm_busy = 0
    for hp in range(8):
        dv = prog.emit("sync", lambda E, hp=hp: E.dma_start_transpose(
            gkv_fm[0:128, 0:NSEG],
            bass.AP(st["ag2_out"], hp * 128, [[1024, NSEG], [1, 128]])),
            wait=[("t", st.get("_l2t", 0))], inc="sg", inc_n=16)
        for stt in range(8):
            dv = prog.emit("sync", lambda E, hp=hp, stt=stt: E.dma_start(
                out=bass.AP(ab.t, vtokg[stt].off, [[ab.cols, 128], [65, 2], [1, 64]]),
                in_=bass.AP(st["ag2_out"], stt * 128 * 1024 + hp * 128,
                            [[1024, 128], [64, 2], [1, 64]])), inc="sg", inc_n=16)
        gld = prog.val("sg")
        for kw in range(17):
            sv = prog.emit("scalar", lambda E, kw=kw, hp=hp: E.activation(
                lsc[0:128, 0:128], lk(hp)[0:128, 6 + 128 * kw:6 + 128 * kw + 128],
                ACT.Copy), wait=[("t", prog.val("t")), ("sg", gld)], inc="s")
            prog.wait("tensor", "s", sv)
            tv = prog.emit("tensor", lambda E: E.transpose(
                psap(6, 128, 128), lsc[0:128, 0:128], st["ident_f"][:, :]),
                wait=[("v", prog.val("v"))], inc="t")
            vv = prog.emit("vector", lambda E, kw=kw: E.tensor_copy(
                bass.AP(ab.t, vtokl[kw].off, [[ab.cols, 128], [65, 2], [1, 64]]),
                bass.AP(ps[6], 0, [[512, 128], [64, 2], [1, 64]])),
                wait=[("t", tv)], inc="v")
        for ci in range(2):
            sv = prog.emit("scalar", lambda E, ci=ci: E.activation(
                lsc[0:128, 0:512], gkv_fm[0:128, ci * 512:ci * 512 + 512],
                ACT.Square), wait=[("t", prog.val("t")), ("sg", gld),
                                   ("v", prog.val("v"))], inc="s")
            prog.wait("tensor", "s", sv)
            tv = prog.emit("tensor", lambda E: E.matmul(
                psap(7, 2, 512), hsel_f[:, :], lsc[0:128, 0:512],
                start=True, stop=True), inc="t")
            prog.emit("vector", lambda E, ci=ci: E.reduce_max(
                gn[0:2, ci:ci + 1], psap(7, 2, 512), axis=AX.X),
                wait=[("t", tv)], inc="v")
        for ci, (c0, nt) in enumerate(KVC):
            sv = prog.emit("scalar", lambda E, c0=c0, nt=nt, hp=hp: E.activation(
                lsc[0:128, 0:nt], lk(hp)[0:128, c0:c0 + nt], ACT.Square),
                wait=[("t", prog.val("t")), ("v", prog.val("v"))], inc="s")
            prog.wait("tensor", "s", sv)
            tv = prog.emit("tensor", lambda E, nt=nt: E.matmul(
                psap(7, 2, nt), hsel_f[:, :], lsc[0:128, 0:nt],
                start=True, stop=True), inc="t")
            prog.emit("vector", lambda E, ci=ci, nt=nt: E.reduce_max(
                gn[0:2, 2 + ci:3 + ci], psap(7, 2, nt), axis=AX.X),
                wait=[("t", tv)], inc="v")
        vv = prog.emit("vector", lambda E: E.reduce_max(
            gn[0:2, 7:8], gn[0:2, 0:7], axis=AX.X), inc="v")
        vv = prog.emit("vector", lambda E: E.tensor_copy(
            gn[0:2, 2:3], gn[0:2, 7:8]), inc="v")
        sv = prog.emit("scalar", lambda E: E.activation(
            gn[0:2, 3:4], gn[0:2, 2:3], ACT.Ln, bias=cv("eps", rows=2)),
            wait=[("v", vv)], inc="s")
        sv = prog.emit("scalar", lambda E: E.activation(
            gn[0:2, 4:5], gn[0:2, 3:4], ACT.Exp, scale=0.5), inc="s")
        vv = prog.emit("vector", lambda E, hp=hp: E.tensor_mul(
            gn[0:2, 5:6], gn[0:2, 4:5], mq[0:2, hp:hp + 1]),
            wait=[("s", sv)], inc="v")
        prog.wait("tensor", "v", prog.val("v"))
        tv = prog.emit("tensor", lambda E: E.matmul(
            psap(7, 1, 1), cv("spare", rows=2), gn[0:2, 5:6],
            start=True, stop=True), inc="t")
        vv = prog.emit("vector", lambda E: E.tensor_scalar(
            gn[0:1, 6:7], bass.AP(ps[7], 0, [[512, 1], [1, 1]]),
            -1.0, None, ALU.mult), wait=[("t", tv)], inc="v")
        prog.wait("tensor", "v", vv)
        tv = prog.emit("tensor", lambda E: E.matmul(
            psap(7, 128, 1), cv("onesrow", rows=1, ncol=128),
            gn[0:1, 6:7], start=True, stop=True), inc="t")
        vv = prog.emit("vector", lambda E: E.tensor_copy(
            nbuf[0:128, 0:1], bass.AP(ps[7], 0, [[512, 128], [1, 1]])),
            wait=[("t", tv)], inc="v")
        nb_done = prog.val("v")
        for j in range(4):
            mv = gdma(gmask[:, :], bass.AP(di["gmask"], j * 128 * 4096,
                                           [[4096, 128], [1, 4096]]),
                      wait=[("v", gm_busy)], sem="wx")
            for h in range(2):
                hb = 64 * h
                for stt in range(8):
                    pb = stt % 2
                    prog.wait("tensor", "s", bank_s.get(("g", pb), 0))
                    tv = prog.emit("tensor", lambda E, stt=stt, h=h, hp=hp, j=j, pb=pb,
                                   hb=hb: E.matmul(
                        psap(pb, 128, 512),
                        View(ab, gkv_fm.off + stt * 128, 128, 128)[hb:hb + 64, 0:128],
                        qr(hp)[hb:hb + 64, j * 512:j * 512 + 512],
                        start=True, stop=True), inc="t")
                    sv = prog.emit("scalar", lambda E, stt=stt, pb=pb: E.activation(
                        View(ab, attn.off + stt * 512, 128, 512)[0:128, 0:512],
                        psap(pb, 128, 512), ACT.Exp, bias=nbuf[0:128, 0:1]),
                        wait=[("t", tv), ("v", nb_done)], inc="s")
                    bank_s[("g", pb)] = sv
                vv = prog.emit("vector", lambda E: E.tensor_mul(
                    attn[0:128, 0:4096], attn[0:128, 0:4096], gmask[0:128, 0:4096]),
                    wait=[("s", prog.val("s")), ("wx", mv)], inc="v")
                gm_busy_h = vv
                prog.wait("tensor", "v", vv)
                prog.wait("tensor", "s", bank_s.get("out", 0))
                prog.emit("tensor", lambda E, h=h, hb=hb: [E.matmul(
                    psap(6, 65, 512),
                    View(ab, vtokg[stt].off, 128, 130)[0:128, 65 * h:65 * h + 65],
                    View(ab, attn.off + stt * 512, 128, 512)[0:128, 0:512],
                    start=(stt == 0), stop=False)
                    for stt in range(8)][-1], inc="t")
                for w in range(4):
                    wq = 4 * j + w
                    kvc_prev = 6 + 128 * wq
                    for t2, kc in ((0, kvc_prev), (1, kvc_prev + 128)):
                        pb = 2 + t2
                        prog.wait("tensor", "s", bank_s.get(("l", pb), 0))
                        tv = prog.emit("tensor", lambda E, kc=kc, h=h, hp=hp, wq=wq,
                                       pb=pb, hb=hb: E.matmul(
                            psap(pb, 128, 128), lk(hp)[hb:hb + 64, kc:kc + 128],
                            qr(hp)[hb:hb + 64, wq * 128:wq * 128 + 128],
                            start=True, stop=True), inc="t")
                        sv = prog.emit("scalar", lambda E, t2=t2, pb=pb: E.activation(
                            attl[0:128, t2 * 128:t2 * 128 + 128],
                            psap(pb, 128, 128), ACT.Exp, bias=nbuf[0:128, 0:1]),
                            wait=[("t", tv)], inc="s")
                        bank_s[("l", pb)] = sv
                    vv = prog.emit("vector", lambda E: E.tensor_mul(
                        attl[0:128, 128:256], attl[0:128, 128:256],
                        st["causal_bf"][:, :]), wait=[("s", prog.val("s"))], inc="v")
                    if wq == 0:
                        vv = prog.emit("vector", lambda E: E.tensor_scalar(
                            attl[0:128, 0:128], attl[0:128, 0:128],
                            cv("prev"), None, ALU.mult), inc="v")
                    prog.wait("tensor", "v", vv)
                    for t2 in (0, 1):
                        prog.emit("tensor", lambda E, wq=wq, w=w, t2=t2, h=h: E.matmul(
                            bass.AP(ps[6], w * 128, [[512, 65], [1, 128]]),
                            View(ab, vtokl[wq + t2].off, 128, 130)[0:128,
                                                                   65 * h:65 * h + 65],
                            attl[0:128, t2 * 128:t2 * 128 + 128],
                            start=False, stop=(w == 3 and t2 == 1)), inc="t")
                tmm = prog.val("t")
                vv = prog.emit("vector", lambda E: E.tensor_copy(
                    View(af, outf.off, 128, 512)[0:65, 0:512],
                    bass.AP(ps[6], 0, [[512, 65], [1, 512]])),
                    wait=[("t", tmm)], inc="v")
                vv = prog.emit("vector", lambda E: E.reciprocal(
                    rrow[0:1, 0:512],
                    View(af, outf.off, 128, 512)[64:65, 0:512]), inc="v")
                prog.wait("tensor", "v", vv)
                tv = prog.emit("tensor", lambda E: E.matmul(
                    psap(7, 64, 512), cv("onesrow", rows=1, ncol=64),
                    rrow[0:1, 0:512], start=True, stop=True), inc="t")
                vv = prog.emit("vector", lambda E, h=h: E.tensor_mul(
                    outb[64 * h:64 * h + 64, 0:512],
                    View(af, outf.off, 128, 512)[0:64, 0:512], psap(7, 64, 512)),
                    wait=[("t", tv)], inc="v")
                bank_s["out"] = prog.val("s")
            gm_busy = prog.val("v")
            out_dma = prog.emit("sync", lambda E, hp=hp, j=j: E.dma_start(
                out=bass.AP(out_spill, hp * 128 * 2048 + j * 512,
                            [[2048, 128], [1, 512]]),
                in_=outb[0:128, 0:512]), wait=[("v", gm_busy)], inc="sy", inc_n=16)
            prog.wait("vector", "sy", out_dma)
        st["_l2t"] = prog.val("t")
    af.pop(); ab.pop()

    st["prog"].scope = "b2.l3"
    af.push(); ab.push()
    wout = valloc(ab, 128, 8192)
    otiles = [valloc(ab, 128, 512) for _ in range(8)]
    lv = gdma(wout[:, :], bass.AP(di["Wout"], 0,
                                  [[1024, 128], [128 * 1024, 8], [1, 1024]]),
              wait=[("t", prog.val("t"))], sem="wx")
    prog.wait("tensor", "wx", lv)
    bank_vv = [0, 0, 0, 0]
    ot_busy = 0
    for j in range(4):
        lw = None
        for kt in range(8):
            lw = gdma(otiles[kt][0:128, 0:512],
                      bass.AP(out_spill, kt * 128 * 2048 + j * 512,
                              [[2048, 128], [1, 512]]),
                      wait=[("sy", out_dma), ("t", ot_busy)] if kt == 0 else (),
                      sem="wy")
        prog.wait("tensor", "wy", lw)
        for mo in range(8):
            pb = mo % 4
            prog.wait("tensor", "v", bank_vv[pb])
            for kt in range(8):
                prog.emit("tensor", lambda E, kt=kt, mo=mo, pb=pb: E.matmul(
                    psap(pb, 128, 512),
                    wout[0:128, kt * 1024 + mo * 128:kt * 1024 + mo * 128 + 128],
                    otiles[kt][0:128, 0:512], start=(kt == 0), stop=(kt == 7)),
                    inc="t")
            tmm = prog.val("t")
            bank_vv[pb] = prog.emit(
                "vector", lambda E, mo=mo, pb=pb, j=j: E.scalar_tensor_tensor(
                    xres_t(mo)[0:128, 134 + j * 512:134 + j * 512 + 512],
                    psap(pb, 128, 512), cv("bout", mo),
                    xres_t(mo)[0:128, 134 + j * 512:134 + j * 512 + 512],
                    ALU.add, ALU.add), wait=[("t", tmm)], inc="v")
        ot_busy = prog.val("t")
    af.pop(); ab.pop()
    af.pop(); ab.pop()
    x3_done = prog.val("v")
    prog.wait("tensor", "v", x3_done)
    prog.wait("scalar", "v", x3_done)
    prog.wait("gpsimd", "v", x3_done)
    prog.wait("sync", "v", x3_done)


def emit_b3(st):
    prog, af, ab = st["prog"], st["af"], st["ab"]
    psap, cv, di = st["psap"], st["cv"], st["di"]
    gdma = st["gdma"]
    ps = st["ps"]
    xres_t = st["xres_t"]
    xn_spill = st["xn_spill"]
    af.push(); ab.push()

    x3_done = emit_pre_ln(st, xres_t, xn_spill)
    prog.wait("gpsimd", "sy0", x3_done[0])
    prog.wait("gpsimd", "sy1", x3_done[1])

    xnt = [valloc(ab, 128, 1024) for _ in range(8)]
    wrt = valloc(ab, 128, 64)
    wsl = valloc(ab, 128, 6 * 1024)
    hbuf = valloc(ab, 128, 16 * 1024)
    dwall = valloc(ab, 1, 8 * 2048)
    onesrow_b = valloc(ab, 1, 128)
    dwr = valloc(af, 8, 2048)
    sm = valloc(af, 128, 8)
    ex = valloc(af, 128, 8)
    ab8 = valloc(af, 128, 8)
    fb = valloc(af, 128, 8)
    oh = valloc(af, 128, 8)
    dwt = valloc(af, 128, 8)
    smc = valloc(af, 128, 8)
    dwb = valloc(af, 128, 512)
    scf = valloc(af, 128, 512)
    sgf = valloc(af, 128, 512)
    JEPS2 = 0.02
    BIG = 1.0e30

    def C(j):
        return smc[0:128, j:j + 1]

    lv = gdma(wrt[:, :], bass.AP(di["Wrout"], 0, [[8, 128], [1024, 8], [1, 8]]),
              sem="wx")
    prog.emit("vector", lambda E: E.memset(onesrow_b[0:1, :], 1.0), inc="v")
    st["_b3xt"] = 0

    def stream_x3n(c0, nt):
        lvv = None
        for k in range(8):
            lvv = gdma(xnt[k][0:128, 0:nt],
                       bass.AP(xn_spill, k * 128 * EXT + c0, [[EXT, 128], [1, nt]]),
                       wait=[("t", st["_b3xt"])] if k == 0 else (), sem="wx")
        return lvv

    st["prog"].scope = "b3.rt"
    prog.wait("tensor", "wx", lv)
    xv = stream_x3n(134, 1024)
    prog.wait("tensor", "wx", xv)
    tp_v = 0
    for tt in range(16):
        if tt == 8:
            xv = stream_x3n(134 + 1024, 1024)
            prog.wait("tensor", "wx", xv)
        off = (tt % 8) * 128
        pb = tt % 2
        prog.wait("tensor", "v", st.get("_b3pv%d" % pb, 0))
        for kt in range(8):
            prog.emit("tensor", lambda E, kt=kt, off=off, pb=pb: E.matmul(
                psap(pb, 128, 8), xnt[kt][0:128, off:off + 128],
                wrt[0:128, kt * 8:kt * 8 + 8], start=(kt == 0), stop=(kt == 7)),
                inc="t")
        tmm = prog.val("t")
        st["_b3xt"] = tmm
        vv = prog.emit("vector", lambda E, pb=pb: E.tensor_copy(
            sm[:, :], psap(pb, 128, 8)), wait=[("t", tmm)], inc="v")
        st["_b3pv%d" % pb] = vv
        if st.get("dbg_mode") == "b3sm":
            prog.emit("sync", lambda E, tt=tt: E.dma_start(
                out=bass.AP(st["dbg_out"], tt * 8, [[EXT, 128], [1, 8]]),
                in_=sm[:, :]), wait=[("v", vv)], inc="dbgs", inc_n=16)
            prog.wait("vector", "dbgs", prog.val("dbgs"))
        prog.emit("vector", lambda E: E.reduce_max(C(0), sm[:, :], axis=AX.X), inc="v")
        prog.emit("vector", lambda E: E.tensor_scalar(
            C(1), C(0), -1.0, None, ALU.mult), inc="v")
        prog.emit("vector", lambda E: E.tensor_scalar(
            ab8[:, :], sm[:, :], -1.0, None, ALU.mult), inc="v")
        prog.emit("vector", lambda E: E.tensor_max(ab8[:, :], ab8[:, :], sm[:, :]),
                  inc="v")
        prog.emit("vector", lambda E: E.tensor_scalar(
            fb[:, :], ab8[:, :], C(0), None, ALU.max), inc="v")
        prog.emit("vector", lambda E: E.scalar_tensor_tensor(
            fb[:, :], fb[:, :], JEPS2, sm[:, :], ALU.mult, ALU.add), inc="v")
        vv = prog.emit("vector", lambda E: E.tensor_scalar(
            fb[:, :], fb[:, :], C(0), None, ALU.is_ge), inc="v")
        sv = prog.emit("scalar", lambda E: E.activation(
            ex[:, :], sm[:, :], ACT.Exp, bias=C(1)), wait=[("v", vv)], inc="s")
        prog.emit("vector", lambda E: E.tensor_mul(ex[:, :], ex[:, :], fb[:, :]),
                  wait=[("s", sv)], inc="v")
        prog.emit("vector", lambda E: E.reduce_sum(C(2), ex[:, :], axis=AX.X), inc="v")
        prog.emit("vector", lambda E: E.reciprocal(C(3), C(2)), inc="v")
        prog.emit("vector", lambda E: E.tensor_scalar(
            oh[:, :], sm[:, :], C(0), None, ALU.is_ge), inc="v")
        prog.emit("vector", lambda E: E.tensor_scalar(
            dwt[:, :], oh[:, :], C(3), None, ALU.mult), inc="v")
        prog.emit("vector", lambda E: E.scalar_tensor_tensor(
            fb[:, :], oh[:, :], -BIG, sm[:, :], ALU.mult, ALU.add), inc="v")
        prog.emit("vector", lambda E: E.reduce_max(C(4), fb[:, :], axis=AX.X), inc="v")
        prog.emit("vector", lambda E: E.tensor_scalar(
            C(5), C(4), -1.0, None, ALU.mult), inc="v")
        prog.emit("vector", lambda E: E.tensor_scalar(
            ab8[:, :], ab8[:, :], C(4), None, ALU.max), inc="v")
        prog.emit("vector", lambda E: E.scalar_tensor_tensor(
            ab8[:, :], ab8[:, :], JEPS2, sm[:, :], ALU.mult, ALU.add), inc="v")
        prog.emit("vector", lambda E: E.tensor_scalar(
            ab8[:, :], ab8[:, :], C(4), None, ALU.is_ge), inc="v")
        prog.emit("vector", lambda E: E.tensor_scalar(
            fb[:, :], fb[:, :], C(4), None, ALU.is_ge), inc="v")
        prog.emit("vector", lambda E: E.tensor_scalar(
            oh[:, :], oh[:, :], -1.0, 1.0, ALU.mult, ALU.add), inc="v")
        vv = prog.emit("vector", lambda E: E.tensor_mul(
            ab8[:, :], ab8[:, :], oh[:, :]), inc="v")
        sv = prog.emit("scalar", lambda E: E.activation(
            ex[:, :], sm[:, :], ACT.Exp, bias=C(5)), wait=[("v", vv)], inc="s")
        prog.emit("vector", lambda E: E.tensor_mul(ex[:, :], ex[:, :], ab8[:, :]),
                  wait=[("s", sv)], inc="v")
        prog.emit("vector", lambda E: E.reduce_sum(C(6), ex[:, :], axis=AX.X), inc="v")
        prog.emit("vector", lambda E: E.reciprocal(C(7), C(6)), inc="v")
        vv = prog.emit("vector", lambda E: E.scalar_tensor_tensor(
            dwt[:, :], fb[:, :], C(7), dwt[:, :], ALU.mult, ALU.add), inc="v")
        if st.get("dbg_mode") == "b3dw":
            prog.emit("sync", lambda E, tt=tt: E.dma_start(
                out=bass.AP(st["dbg_out"], tt * 8, [[EXT, 128], [1, 8]]),
                in_=dwt[:, :]), wait=[("v", vv)], inc="dbgs", inc_n=16)
            prog.wait("vector", "dbgs", prog.val("dbgs"))
        prog.wait("tensor", "v", vv)
        prog.wait("tensor", "v", tp_v)
        tv = prog.emit("tensor", lambda E, pb=pb: E.transpose(
            psap(2, 8, 128), dwt[:, :], st["ident_f"][:, :]), inc="t")
        tp_v = prog.emit("vector", lambda E, tt=tt: E.tensor_copy(
            dwr[0:8, tt * 128:tt * 128 + 128], psap(2, 8, 128)),
            wait=[("t", tv)], inc="v")
        gdma(bass.AP(ab.t, dwall.off + tt * 128, [[ab.cols, 1], [2048, 8], [1, 128]]),
             dwr[0:8, tt * 128:tt * 128 + 128], sem="dw", wait=[("v", tp_v)])

    st["prog"].scope = "b3.ex"
    bank_busy = {i: 0 for i in range(8)}
    dwr_done = prog.val("v")
    prog.wait("tensor", "dw", prog.val("dw"))
    wslb = [0, 0]
    wdb = [0, 0]
    h_busy = 0
    for e in range(8):
        for half in range(2):
            hc0 = 134 + half * 1024
            st["_b3xt"] = prog.val("t")
            xv = stream_x3n(hc0, 1024)
            gu_load = {}

            def issue_gu(kt, e=e):
                pb = (kt % 2) * 2048
                gdma(wsl[0:128, pb:pb + 1024],
                     bass.AP(di["Wg"], (e * 16 + kt) * 128 * 1024,
                             [[1024, 128], [1, 1024]]),
                     wait=[("t", wslb[kt % 2])], sem="w0")
                gu_load[kt] = gdma(
                     wsl[0:128, pb + 1024:pb + 2048],
                     bass.AP(di["Wu"], (e * 16 + kt) * 128 * 1024,
                             [[1024, 128], [1, 1024]]), sem="w0")

            issue_gu(0)
            issue_gu(1)
            for kt in range(16):
                pb = (kt % 2) * 2048
                prog.wait("tensor", "w0", gu_load[kt])
                if kt == 0:
                    prog.wait("tensor", "wx", xv)
                for sub in range(2):
                    so = sub * 512
                    prog.wait("tensor", "s", bank_busy.get("g", 0))
                    prog.wait("tensor", "v", bank_busy.get("u", 0))
                    prog.wait("tensor", "v", bank_busy[0])
                    prog.wait("tensor", "v", bank_busy[1])
                    prog.emit("tensor", lambda E, so=so, pb=pb: [E.matmul(
                        psap(0, 128, 512),
                        wsl[0:128, pb + k8 * 128:pb + k8 * 128 + 128],
                        xnt[k8][0:128, so:so + 512],
                        start=(k8 == 0), stop=(k8 == 7))
                        for k8 in range(8)][-1], inc="t")
                    prog.emit("tensor", lambda E, so=so, pb=pb: [E.matmul(
                        psap(1, 128, 512),
                        wsl[0:128, pb + 1024 + k8 * 128:pb + 1024 + k8 * 128 + 128],
                        xnt[k8][0:128, so:so + 512],
                        start=(k8 == 0), stop=(k8 == 7))
                        for k8 in range(8)][-1], inc="t")
                    tmm = prog.val("t")
                    sv = prog.emit("scalar", lambda E: E.activation(
                        sgf[:, :], psap(0, 128, 512), ACT.Silu),
                        wait=[("t", tmm), ("v", prog.val("v"))], inc="s")
                    bank_busy["g"] = sv
                    vv = prog.emit("vector", lambda E, kt=kt, so=so: E.tensor_mul(
                        View(ab, hbuf.off + kt * 1024, 128, 1024)[0:128, so:so + 512],
                        sgf[:, :], psap(1, 128, 512)),
                        wait=[("s", sv), ("t", h_busy)], inc="v")
                    bank_busy["u"] = vv
                wslb[kt % 2] = prog.val("t")
                if kt + 2 < 16:
                    issue_gu(kt + 2)
            hv = prog.val("v")
            for sub in range(2):
                so = sub * 512
                prog.wait("tensor", "v", bank_busy.get("dwb", 0))
                prog.wait("tensor", "v", bank_busy[2])
                prog.wait("tensor", "v", dwr_done)
                tv = prog.emit("tensor", lambda E, e=e, half=half, so=so: E.matmul(
                    psap(2, 128, 512), onesrow_b[0:1, :],
                    bass.AP(ab.t, dwall.off + e * 2048 + half * 1024 + so,
                            [[ab.cols, 1], [1, 512]]),
                    start=True, stop=True), wait=[("v", tp_v)], inc="t")
                vv = prog.emit("vector", lambda E: E.tensor_copy(
                    dwb[:, :], psap(2, 128, 512)), wait=[("t", tv)], inc="v")
                bank_busy["dwb"] = vv
                prog.wait("tensor", "v", vv)
                for mo in range(8):
                    prog.wait("tensor", "v", bank_busy[mo])
                wd_load = {}

                def issue_wd(kt, e=e):
                    db = 4096 + (kt % 2) * 1024
                    wd_load[kt] = gdma(
                        wsl[0:128, db:db + 1024],
                        bass.AP(di["Wd"], e * FF * 1024 + kt * 128 * 1024,
                                [[1024, 128], [0, 1], [1, 1024]]),
                        wait=[("t", wdb[kt % 2])], sem="w1")

                issue_wd(0)
                issue_wd(1)
                for kt in range(16):
                    db = 4096 + (kt % 2) * 1024
                    prog.wait("tensor", "w1", wd_load[kt])
                    prog.emit("tensor", lambda E, kt=kt, so=so, db=db: [E.matmul(
                        psap(mo, 128, 512),
                        wsl[0:128, db + mo * 128:db + mo * 128 + 128],
                        View(ab, hbuf.off + kt * 1024, 128, 1024)[0:128,
                                                                  so:so + 512],
                        start=(kt == 0), stop=(kt == 15))
                        for mo in range(8)][-1],
                        wait=[("v", hv)] if kt == 0 else (), inc="t")
                    wdb[kt % 2] = prog.val("t")
                    if kt + 2 < 16:
                        issue_wd(kt + 2)
                for mo in range(8):
                    tmm = prog.val("t")
                    vv = prog.emit("vector", lambda E, mo=mo: E.tensor_mul(
                        scf[:, :], psap(mo, 128, 512), dwb[:, :]),
                        wait=[("t", tmm)], inc="v")
                    vv = prog.emit("vector", lambda E, mo=mo, hc0=hc0, so=so:
                                   E.tensor_add(
                        xres_t(mo)[0:128, hc0 + so:hc0 + so + 512],
                        xres_t(mo)[0:128, hc0 + so:hc0 + so + 512], scf[:, :]),
                        inc="v")
                    bank_busy[mo] = vv
            h_busy = prog.val("t")
    af.pop(); ab.pop()
    x4_done = prog.val("v")
    prog.wait("tensor", "v", x4_done)
    prog.wait("scalar", "v", x4_done)
    prog.wait("gpsimd", "v", x4_done)
    prog.wait("sync", "v", x4_done)


def emit_output(st):
    prog, af = st["prog"], st["af"]
    psap = st["psap"]
    xres_t = st["xres_t"]
    ident = st["ident_f"]
    af.push()
    otile = valloc(af, 128, 1024)
    copies = []
    for tt in range(16):
        c0 = H + tt * 128
        for k in range(8):
            pv = k % 4
            guard = copies[-4] if len(copies) >= 4 else 0
            tv = prog.emit("tensor", lambda E, k=k, c0=c0, pv=pv: E.transpose(
                psap(pv, 128, 128), xres_t(k)[:, c0:c0 + 128], ident[:, :]),
                wait=[("v", guard)], inc="t")
            vv = prog.emit("vector", lambda E, k=k, pv=pv: E.tensor_copy(
                otile[0:128, k * 128:(k + 1) * 128], psap(pv, 128, 128)),
                wait=[("t", tv)], inc="v")
            copies.append(vv)
        dst = bass.AP(st["out"], tt * 128 * D, [[D, 128], [1, D]])
        prog.emit("sync", lambda E, d=dst: E.dma_start(out=d, in_=otile[:, :]),
                  wait=[("v", copies[-1])], inc="st", inc_n=16)
        prog.wait("vector", "st", prog.val("st"))
    af.pop()


def kernel(**inputs):
    shared, per_core = host_prepare(inputs)
    nc = build_program(nblocks=4)
    in_maps = []
    for c in range(NCORES):
        m = dict(shared)
        m.update(per_core[c])
        in_maps.append(m)
    res = run_bass_kernel_spmd(nc, in_maps, core_ids=list(range(NCORES)))
    outs = [np.asarray(res.results[c]["out"]) for c in range(NCORES)]
    return np.concatenate(outs, axis=0).reshape(1, T, D)
